# version 05
# baseline (speedup 1.0000x reference)
import math
import numpy as np
from contextlib import ExitStack

import concourse.bass as bass
import concourse.mybir as mybir
from concourse.bass_utils import run_bass_kernel_spmd

F32 = mybir.dt.float32
BF16 = mybir.dt.bfloat16
I32 = mybir.dt.int32
AF = mybir.ActivationFunctionType
ALU = mybir.AluOpType
AX = mybir.AxisListType

N_CORES = 8
SCAN_PARTS = 255
SCAN_SAFE = True
B_PER_CORE = 2
T = 2048
D = 1024
NT = B_PER_CORE * T
NTILE = NT // 128
H = 16
DH = 64
FFN = 4096
LN_EPS = 1e-5


_PV_GUARD = [None]


class Tok:
    __slots__ = ("sem", "val")

    def __init__(self, sem, val):
        self.sem = sem
        self.val = val


class Ctx:

    def __init__(self, nc, stack):
        self.nc = nc
        self.stack = stack
        self.e = {
            "pe": nc.tensor,
            "act": nc.scalar,
            "dve": nc.vector,
            "pool": nc.gpsimd,
            "sp": nc.sync,
        }
        self._semn = 0
        self.chain = {}
        self.ccnt = {}
        for en in ("act", "dve", "pool"):
            self.chain[en] = self._newsem(f"ch{en}")
            self.ccnt[en] = 0
        self.pe_sem = self._newsem("pedone")
        self.pe_cnt = 0
        self.pe_unmarked = 0
        self.dmas = {}
        self.known = {en: {} for en in self.e}
        self._bc_regs = {}
        _PV_GUARD[0] = None

    def _newsem(self, name):
        s = self.stack.enter_context(self.nc.semaphore(f"{name}_{self._semn}"))
        self._semn += 1
        return s

    def op(self, en, meth, *a, **kw):
        eng = self.e[en]
        c = self.ccnt[en]
        if c:
            eng.wait_ge(self.chain[en], c)
            self.known[en][id(self.chain[en])] = c
        instr = getattr(eng, meth)(*a, **kw)
        instr.then_inc(self.chain[en], 1)
        self.ccnt[en] = c + 1
        return Tok(self.chain[en], c + 1)

    def tok(self, en):
        return Tok(self.chain[en], self.ccnt[en])

    def pe(self, meth, *a, mark=False, **kw):
        instr = getattr(self.e["pe"], meth)(*a, **kw)
        if mark:
            self.pe_cnt += 1
            instr.then_inc(self.pe_sem, 1)
            self.pe_unmarked = 0
            return Tok(self.pe_sem, self.pe_cnt)
        self.pe_unmarked += 1
        return None

    def _wait1(self, en, sem, val):
        kn = self.known[en]
        k = id(sem)
        if kn.get(k, -1) >= val:
            return
        self.e[en].wait_ge(sem, val)
        kn[k] = val

    def wait(self, en, *toks):
        best = {}
        for t in toks:
            if t is None:
                continue
            k = id(t.sem)
            if k not in best or best[k][1] < t.val:
                best[k] = (t.sem, t.val)
        for sem, val in best.values():
            self._wait1(en, sem, val)

    def dma(self, stream, out, in_, **kw):
        en = "sp" if stream.startswith("sp") else "pool"
        if stream not in self.dmas:
            self.dmas[stream] = [self._newsem("dma_" + stream), 0, en]
        rec = self.dmas[stream]
        assert rec[2] == en
        instr = self.e[en].dma_start(out=out, in_=in_, **kw)
        rec[1] += 16
        instr.then_inc(rec[0], 16)
        return Tok(rec[0], rec[1])

    def idma(self, stream, out, out_offset, in_, in_offset, **kw):
        assert stream.startswith("g")
        if stream not in self.dmas:
            self.dmas[stream] = [self._newsem("dma_" + stream), 0, "pool"]
        rec = self.dmas[stream]
        bc = kw.get("bounds_check")
        if isinstance(bc, int):
            if bc not in self._bc_regs:
                r = self.e["pool"].alloc_register()
                self.e["pool"].reg_mov(r, bc)
                self._bc_regs[bc] = r
            kw["bounds_check"] = self._bc_regs[bc]
        instr = self.e["pool"].indirect_dma_start(
            out=out, out_offset=out_offset, in_=in_, in_offset=in_offset, **kw)
        rec[1] += 16
        instr.then_inc(rec[0], 16)
        return Tok(rec[0], rec[1])

    def barrier(self, engines=("pe", "act", "dve", "pool", "sp")):
        assert self.pe_unmarked == 0, "unmarked PE work at barrier"
        toks = []
        for en in ("act", "dve", "pool"):
            if self.ccnt[en]:
                toks.append(Tok(self.chain[en], self.ccnt[en]))
        if self.pe_cnt:
            toks.append(Tok(self.pe_sem, self.pe_cnt))
        for st, (sem, cnt, _en) in self.dmas.items():
            if cnt:
                toks.append(Tok(sem, cnt))
        for en in engines:
            self.wait(en, *toks)


def ceil_div(a, b):
    return (a + b - 1) // b


def emit_select(ctx, out, mask, a, b, scratch):
    ctx.op("dve", "tensor_tensor", out=scratch, in0=a, in1=b, op=ALU.subtract)
    ctx.op("dve", "tensor_tensor", out=scratch, in0=scratch, in1=mask, op=ALU.mult)
    return ctx.op("dve", "tensor_tensor", out=out, in0=b, in1=scratch, op=ALU.add)



class Consts:

    def __init__(self, ctx: Ctx):
        nc, stack = ctx.nc, ctx.stack
        self.ident_bf = stack.enter_context(nc.sbuf_tensor("c_ident_bf", [128, 128], BF16))
        self.ident_f = stack.enter_context(nc.sbuf_tensor("c_ident_f", [128, 128], F32))
        self.tril_inc = stack.enter_context(nc.sbuf_tensor("c_tril", [128, 128], BF16))
        self.triu_strict = stack.enter_context(nc.sbuf_tensor("c_triu", [128, 128], BF16))
        self.tril_f = stack.enter_context(nc.sbuf_tensor("c_trilf", [128, 128], F32))
        self.triu_f = stack.enter_context(nc.sbuf_tensor("c_triuf", [128, 128], F32))
        self.ones_col = stack.enter_context(nc.sbuf_tensor("c_ones", [128, 128], BF16))
        self.ones_f = stack.enter_context(nc.sbuf_tensor("c_onesf", [128, 128], F32))
        self.iota_p = stack.enter_context(nc.sbuf_tensor("c_iotap", [128, 1], F32))
        sc1 = stack.enter_context(nc.sbuf_tensor("c_sc1", [128, 128], F32))
        sc2 = stack.enter_context(nc.sbuf_tensor("c_sc2", [128, 128], F32))

        t_ip = ctx.op("pool", "iota", sc1[:], [[0, 128]], base=0, channel_multiplier=1,
                      allow_small_or_imprecise_dtypes=True)
        t_if = ctx.op("pool", "iota", sc2[:], [[1, 128]], base=0, channel_multiplier=0,
                      allow_small_or_imprecise_dtypes=True)
        ctx.wait("dve", t_ip, t_if)
        ctx.op("dve", "tensor_tensor", out=self.ident_f[:], in0=sc1[:], in1=sc2[:], op=ALU.is_equal)
        ctx.op("dve", "tensor_tensor", out=self.ident_bf[:], in0=sc1[:], in1=sc2[:], op=ALU.is_equal)
        ctx.op("dve", "tensor_tensor", out=self.tril_inc[:], in0=sc1[:], in1=sc2[:], op=ALU.is_le)
        ctx.op("dve", "tensor_tensor", out=self.tril_f[:], in0=sc1[:], in1=sc2[:], op=ALU.is_le)
        ctx.op("dve", "tensor_tensor", out=self.triu_strict[:], in0=sc1[:], in1=sc2[:], op=ALU.is_gt)
        ctx.op("dve", "tensor_tensor", out=self.triu_f[:], in0=sc1[:], in1=sc2[:], op=ALU.is_gt)
        ctx.op("dve", "memset", self.ones_col[:], 1.0)
        ctx.op("dve", "memset", self.ones_f[:], 1.0)
        self.t_iota_p = ctx.op("dve", "tensor_copy", self.iota_p[:], sc1[:, 0:1])
        self.ready = ctx.tok("dve")



def prep_transposed_weight(ctx: Ctx, C: Consts, src_f32, dst_bf16, rows, cols, tag):
    nc = ctx.nc
    ctx.wait("pe", C.ready)
    rb, cb = rows // 128, cols // 128
    with ExitStack() as st_:
        nbuf = 3
        src_sb = [st_.enter_context(nc.sbuf_tensor(f"wp_src{i}_{tag}", [128, 8 * 128], BF16))
                  for i in range(nbuf)]
        out_sb = [st_.enter_context(nc.sbuf_tensor(f"wp_out{i}_{tag}", [128, 8 * 128], BF16))
                  for i in range(nbuf)]
        psum = [st_.enter_context(nc.psum_tensor(f"wp_ps{i}_{tag}", [128, 1024], BF16))
                for i in range(2)]
        free_tok = [None] * nbuf
        evac_tok = [None, None]
        tr_tok = [None] * nbuf
        n = 0
        for c in range(cb):
            for rg in range(ceil_div(rb, 8)):
                nr = min(8, rb - rg * 8)
                i = n % nbuf
                pi = n % 2
                ctx.wait("pool", tr_tok[i])
                ld = ctx.dma(f"gwp{i}",
                             out=src_sb[i][:, 0:nr * 128].rearrange("p (r c) -> p r c", r=nr),
                             in_=src_f32[rg * 1024:rg * 1024 + nr * 128,
                                         c * 128:(c + 1) * 128]
                             .rearrange("(r p) c -> p r c", p=128))
                ctx.wait("pe", ld, evac_tok[pi])
                for r in range(nr):
                    pt = ctx.pe("transpose", psum[pi][:, r * 128:(r + 1) * 128],
                                src_sb[i][:, r * 128:(r + 1) * 128], C.ident_bf[:],
                                mark=(r == nr - 1))
                tr_tok[i] = pt
                ctx.wait("dve", pt, free_tok[i])
                evac_tok[pi] = ctx.op("dve", "tensor_copy", out_sb[i][:, 0:nr * 128],
                                      psum[pi][:, 0:nr * 128])
                ctx.wait("pool", evac_tok[pi])
                free_tok[i] = ctx.dma(f"gwps{i}",
                                      out=dst_bf16[c * 128:(c + 1) * 128,
                                                   rg * 1024:rg * 1024 + nr * 128],
                                      in_=out_sb[i][:, 0:nr * 128])
                n += 1
        ctx.barrier()


def prep_colvec(ctx: Ctx, C: Consts, src_f32, n, dst_sb, col0, scale=None):
    nc = ctx.nc
    nb = ceil_div(n, 128)
    ctx.wait("pe", C.ready)
    with ExitStack() as st_:
        row = st_.enter_context(nc.sbuf_tensor(_uq("pv_row"), [nb, 128], F32))
        ps = st_.enter_context(nc.psum_tensor(_uq("pv_ps"), [128, nb], F32))
        ctx.wait("sp", _PV_GUARD[0])
        t = ctx.dma("spv", out=row[0:nb, :], in_=src_f32.rearrange("(a b) -> a b", b=128))
        ctx.wait("pe", t, _PV_GUARD[0])
        pt = ctx.pe("transpose", ps[:, 0:nb], row[0:nb, :], C.ident_f[0:nb, 0:nb], mark=True)
        ctx.wait("dve", pt)
        if scale is None:
            tok = ctx.op("dve", "tensor_copy", dst_sb[:, col0:col0 + nb], ps[:, 0:nb])
        else:
            tok = ctx.op("dve", "tensor_scalar", out=dst_sb[:, col0:col0 + nb], in0=ps[:, 0:nb],
                         scalar1=float(scale), scalar2=None, op0=ALU.mult)
        _PV_GUARD[0] = tok


def prep_bcast_row(ctx: Ctx, C: Consts, src_f32, n, dst_sb, ones_f):
    nc = ctx.nc
    ctx.wait("pe", C.ready)
    with ExitStack() as st_:
        row = st_.enter_context(nc.sbuf_tensor(_uq("pb_row"), [1, n], F32))
        ctx.wait("sp", _PV_GUARD[0])
        t = ctx.dma("spv", out=row[0:1, 0:n], in_=src_f32.unsqueeze(0))
        ctx.wait("pe", t, _PV_GUARD[0])
        for j0 in range(0, n, 512):
            w = min(512, n - j0)
            with nc.psum_tensor(_uq("pb_ps"), [128, 512], F32) as ps:
                pt = ctx.pe("matmul", ps[:, 0:w], ones_f[0:1, :], row[0:1, j0:j0 + w],
                            start=True, stop=True, mark=True)
                ctx.wait("dve", pt)
                tok = ctx.op("dve", "tensor_copy", dst_sb[:, j0:j0 + w], ps[:, 0:w])
                ctx.wait("pe", tok)
        _PV_GUARD[0] = tok



class LnBufs:
    def __init__(self, ctx, tag, nring=2, d=D):
        nc, st_ = ctx.nc, ctx.stack
        self.d = d
        self.m_neg = [st_.enter_context(nc.sbuf_tensor(f"ln_mn{i}_{tag}", [128, 1], F32)) for i in range(nring)]
        self.veps = [st_.enter_context(nc.sbuf_tensor(f"ln_ve{i}_{tag}", [128, 1], F32)) for i in range(nring)]
        self.r = [st_.enter_context(nc.sbuf_tensor(f"ln_r{i}_{tag}", [128, 1], F32)) for i in range(nring)]
        self.sq = st_.enter_context(nc.sbuf_tensor(f"ln_sq_{tag}", [128, d], BF16))
        self.ssum = [st_.enter_context(nc.sbuf_tensor(f"ln_ss{i}_{tag}", [128, 1], F32)) for i in range(nring)]


def emit_ln(ctx: Ctx, lb: LnBufs, i, x_ap, out_ap, x_tok):
    d = lb.d
    ctx.wait("dve", x_tok)
    ctx.op("dve", "tensor_reduce", out=lb.m_neg[i][:], in_=x_ap, axis=AX.X, op=ALU.add, negate=True)
    ctx.op("dve", "tensor_scalar", out=lb.m_neg[i][:], in0=lb.m_neg[i][:],
           scalar1=1.0 / d, scalar2=None, op0=ALU.mult)
    mt = ctx.tok("dve")
    ctx.wait("act", x_tok, mt)
    ctx.op("act", "activation", out=lb.sq[:, 0:d], in_=x_ap, func=AF.Square,
           bias=lb.m_neg[i][:], scale=1.0, accum_out=lb.ssum[i][:])
    st = ctx.tok("act")
    ctx.wait("dve", st)
    ctx.op("dve", "tensor_scalar", out=lb.veps[i][:], in0=lb.ssum[i][:],
           scalar1=1.0 / d, scalar2=LN_EPS, op0=ALU.mult, op1=ALU.add)
    vt = ctx.tok("dve")
    ctx.wait("act", vt)
    ctx.op("act", "activation", out=lb.r[i][:], in_=lb.veps[i][:], func=AF.Ln)
    ctx.op("act", "activation", out=lb.r[i][:], in_=lb.r[i][:], func=AF.Exp, scale=-0.5)
    sqt = ctx.tok("act")
    ctx.wait("dve", sqt)
    tout = ctx.op("dve", "tensor_scalar", out=out_ap, in0=x_ap,
                  scalar1=lb.m_neg[i][:], scalar2=lb.r[i][:], op0=ALU.add, op1=ALU.mult)
    return tout, lb.m_neg[i], lb.r[i]



def phase_ln_to_fm(ctx: Ctx, C: Consts, x_src, yt_dst, tag):
    nc = ctx.nc
    with ExitStack() as st_:
        xb = [st_.enter_context(nc.sbuf_tensor(f"s0x{i}_{tag}", [128, D], F32)) for i in range(2)]
        yb = [st_.enter_context(nc.sbuf_tensor(f"s0y{i}_{tag}", [128, D], BF16)) for i in range(2)]
        yts = [st_.enter_context(nc.sbuf_tensor(f"s0t{i}_{tag}", [128, 8 * 512], BF16)) for i in range(2)]
        ps = [st_.enter_context(nc.psum_tensor(f"s0p{i}_{tag}", [128, 1024], BF16)) for i in range(2)]
        lb = LnBufs(_StackCtx(ctx, st_), tag + "ln")
        norm_tok = [None, None]
        evac_tok = [None, None]
        tr_tok = [None, None]
        st_tok = [None, None]
        ctx.wait("pe", C.ready)
        nsup = NT // 512
        for s in range(nsup):
            for j in range(4):
                it = s * 4 + j
                i2 = it % 2
                ctx.wait("sp", norm_tok[i2])
                ldx = ctx.dma(f"spx{i2}", out=xb[i2][:], in_=x_src[it * 128:(it + 1) * 128, :])
                ctx.wait("dve", tr_tok[i2])
                tnorm, _, _ = emit_ln(ctx, lb, i2, xb[i2][:], yb[i2][:], ldx)
                norm_tok[i2] = tnorm
                ctx.wait("pe", tnorm, evac_tok[i2])
                for kc in range(8):
                    pt = ctx.pe("transpose", ps[i2][:, kc * 128:(kc + 1) * 128],
                                yb[i2][:, kc * 128:(kc + 1) * 128], C.ident_bf[:],
                                mark=(kc == 7))
                tr_tok[i2] = pt
                ctx.wait("dve", pt, st_tok[s % 2])
                evac_tok[i2] = ctx.op(
                    "dve", "tensor_copy",
                    yts[s % 2][:].rearrange("p (kc t) -> p kc t", kc=8)[:, :, j * 128:(j + 1) * 128],
                    ps[i2][:].rearrange("p (kc t) -> p kc t", kc=8))
            for kc in range(8):
                ctx.wait("pool", evac_tok[0], evac_tok[1])
                st_tok[s % 2] = ctx.dma(
                    f"gyt{s % 2}",
                    out=yt_dst[kc * 128:(kc + 1) * 128, s * 512:(s + 1) * 512],
                    in_=yts[s % 2][:, kc * 512:(kc + 1) * 512])


class _StackCtx:

    def __init__(self, ctx, st_):
        self.nc = ctx.nc
        self.stack = st_



def phase_ffn0_s1(ctx: Ctx, C: Consts, yt_src, w1t, w2t, colv, c_w1s, c_w2s, h0_dst, tag):
    nc = ctx.nc
    with ExitStack() as st_:
        w1sb = st_.enter_context(nc.sbuf_tensor(f"s1w1_{tag}", [128, 8 * FFN], BF16))
        w2sb = st_.enter_context(nc.sbuf_tensor(f"s1w2_{tag}", [128, 8 * FFN], BF16))
        ytb = [st_.enter_context(nc.sbuf_tensor(f"s1y{i}_{tag}", [128, 8 * 512], BF16)) for i in range(2)]
        t1 = [st_.enter_context(nc.sbuf_tensor(f"s1t1{i}_{tag}", [128, 512], F32)) for i in range(2)]
        t2 = [st_.enter_context(nc.sbuf_tensor(f"s1t2{i}_{tag}", [128, 512], F32)) for i in range(2)]
        h0b = [st_.enter_context(nc.sbuf_tensor(f"s1h{i}_{tag}", [128, 512], BF16)) for i in range(2)]
        pa = [st_.enter_context(nc.psum_tensor(f"s1pa{i}_{tag}", [128, 512], F32)) for i in range(2)]
        pb = [st_.enter_context(nc.psum_tensor(f"s1pb{i}_{tag}", [128, 512], F32)) for i in range(2)]
        wtoks = []
        for kc in range(8):
            wtoks.append(ctx.dma("spw", out=w1sb[:, kc * FFN:(kc + 1) * FFN],
                                 in_=w1t[kc * 128:(kc + 1) * 128, :]))
            wtoks.append(ctx.dma("spw", out=w2sb[:, kc * FFN:(kc + 1) * FFN],
                                 in_=w2t[kc * 128:(kc + 1) * 128, :]))
        ctx.wait("pe", *wtoks)
        w1v = w1sb[:].rearrange("p (kc o) -> p kc o", kc=8)
        w2v = w2sb[:].rearrange("p (kc o) -> p kc o", kc=8)
        ld_tok = [None, None]
        a_tok = [None, None]
        b_tok = [None, None]
        m_tok = [None, None]
        st_tok = [None, None]
        pe_last = [None, None]
        nsup = NT // 512
        for s in range(nsup):
            i2 = s % 2
            ctx.wait("sp", pe_last[i2])
            for kc in range(8):
                ld_tok[i2] = ctx.dma(f"spy{i2}", out=ytb[i2][:, kc * 512:(kc + 1) * 512],
                                     in_=yt_src[kc * 128:(kc + 1) * 128, s * 512:(s + 1) * 512])
            yv = ytb[i2][:].rearrange("p (kc t) -> p kc t", kc=8)
            for oc in range(32):
                o2 = oc % 2
                ctx.wait("pe", ld_tok[i2], a_tok[o2], b_tok[o2])
                for kc in range(8):
                    pat = ctx.pe("matmul", pa[o2][:], w1v[:, kc, oc * 128:(oc + 1) * 128],
                                 yv[:, kc, :], start=(kc == 0), stop=(kc == 7),
                                 mark=(kc == 7))
                for kc in range(8):
                    mm = ctx.pe("matmul", pb[o2][:], w2v[:, kc, oc * 128:(oc + 1) * 128],
                                yv[:, kc, :], start=(kc == 0), stop=(kc == 7),
                                mark=(kc == 7))
                pbt = mm
                ctx.wait("act", pat, m_tok[o2])
                a_tok[o2] = ctx.op("act", "activation", out=t1[o2][:], in_=pa[o2][:],
                                   func=AF.Silu, scale=colv[:, c_w1s + oc:c_w1s + oc + 1])
                ctx.wait("dve", pbt)
                b_tok[o2] = ctx.op("dve", "tensor_scalar", out=t2[o2][:], in0=pb[o2][:],
                                   scalar1=colv[:, c_w2s + oc:c_w2s + oc + 1], scalar2=None,
                                   op0=ALU.mult)
                ctx.wait("dve", a_tok[o2], st_tok[o2])
                m_tok[o2] = ctx.op("dve", "tensor_tensor", out=h0b[o2][:], in0=t1[o2][:],
                                   in1=t2[o2][:], op=ALU.mult)
                ctx.wait("pool", m_tok[o2])
                st_tok[o2] = ctx.dma(f"gh{o2}", out=h0_dst[oc * 128:(oc + 1) * 128,
                                                           s * 512:(s + 1) * 512],
                                     in_=h0b[o2][:])
                pe_last[i2] = pbt



def phase_ffn0_mid(ctx: Ctx, C: Consts, src, wt, colv, c_s, dst, tag):
    nc = ctx.nc
    KCH = FFN // 128
    groups = [list(range(0, 11)), list(range(11, 22)), list(range(22, 32))]
    with ExitStack() as st_:
        wsl = [st_.enter_context(nc.sbuf_tensor(f"md_w{i}_{tag}", [128, KCH * 128], BF16))
               for i in range(11)]
        rhs = [st_.enter_context(nc.sbuf_tensor(f"md_r{i}_{tag}", [128, KCH * 512], BF16))
               for i in range(2)]
        ob = [st_.enter_context(nc.sbuf_tensor(f"md_o{i}_{tag}", [128, 512], BF16)) for i in range(2)]
        ps = [st_.enter_context(nc.psum_tensor(f"md_p{i}_{tag}", [128, 512], F32)) for i in range(2)]
        g_tok = [None, None]
        st_tok = [None, None]
        ld_tok = [None, None]
        pe_last_rhs = [None, None]
        pe_last_group = None
        for gi, grp in enumerate(groups):
            wl_toks = []
            for sl, oc in enumerate(grp):
                ctx.wait("sp", pe_last_group)
                wl_toks.append(ctx.dma(
                    "spw",
                    out=wsl[sl][:].rearrange("p (kc o) -> p kc o", kc=KCH),
                    in_=wt[:, oc * 128:(oc + 1) * 128].rearrange("(kc p) o -> p kc o", p=128)))
            nsup = NT // 512
            for s in range(nsup):
                i2 = s % 2
                ctx.wait("sp", pe_last_rhs[i2])
                for kc in range(KCH):
                    ld_tok[i2] = ctx.dma(f"spy{i2}",
                                         out=rhs[i2][:, kc * 512:(kc + 1) * 512],
                                         in_=src[kc * 128:(kc + 1) * 128, s * 512:(s + 1) * 512])
                rv = rhs[i2][:].rearrange("p (kc t) -> p kc t", kc=KCH)
                for sl, oc in enumerate(grp):
                    o2 = (s * len(grp) + sl) % 2
                    ctx.wait("pe", ld_tok[i2], g_tok[o2], *(wl_toks if s == 0 else []))
                    for kc in range(KCH):
                        wv = wsl[sl][:].rearrange("p (kc o) -> p kc o", kc=KCH)
                        mm = ctx.pe("matmul", ps[o2][:], wv[:, kc, :], rv[:, kc, :],
                                    start=(kc == 0), stop=(kc == KCH - 1),
                                    mark=(kc == KCH - 1))
                    pe_last_rhs[i2] = mm
                    pe_last_group = mm
                    ctx.wait("act", mm, st_tok[o2])
                    g_tok[o2] = ctx.op("act", "activation", out=ob[o2][:], in_=ps[o2][:],
                                       func=AF.Gelu, scale=colv[:, c_s + oc:c_s + oc + 1])
                    ctx.wait("pool", g_tok[o2])
                    st_tok[o2] = ctx.dma(f"gh{o2}",
                                         out=dst[oc * 128:(oc + 1) * 128, s * 512:(s + 1) * 512],
                                         in_=ob[o2][:])



def phase_ffn0_s4(ctx: Ctx, C: Consts, src_h, w3t, colv, c_s3, x_src, x_dst, tag):
    nc = ctx.nc
    KCH = FFN // 128
    with ExitStack() as st_:
        w3sb = st_.enter_context(nc.sbuf_tensor(f"s4w_{tag}", [128, KCH * 1024], BF16))
        rhs = [st_.enter_context(nc.sbuf_tensor(f"s4r{i}_{tag}", [128, KCH * 512], BF16))
               for i in range(2)]
        ufm = [st_.enter_context(nc.sbuf_tensor(f"s4u{i}_{tag}", [128, 8 * 512], F32)) for i in range(2)]
        xs = [st_.enter_context(nc.sbuf_tensor(f"s4x{i}_{tag}", [128, D], F32)) for i in range(2)]
        xo = [st_.enter_context(nc.sbuf_tensor(f"s4xo{i}_{tag}", [128, D], F32)) for i in range(2)]
        psm = [st_.enter_context(nc.psum_tensor(f"s4pm{i}_{tag}", [128, 512], F32)) for i in range(2)]
        pst = [st_.enter_context(nc.psum_tensor(f"s4pt{i}_{tag}", [128, 1024], F32)) for i in range(2)]
        wl_toks = [ctx.dma("spw", out=w3sb[:, kc * 1024:(kc + 1) * 1024],
                           in_=w3t[kc * 128:(kc + 1) * 128, :]) for kc in range(KCH)]
        w3v = w3sb[:].rearrange("p (kc o) -> p kc o", kc=KCH)
        ld_tok = [None, None]
        ev_tok = [None, None]
        tr_tok = [None, None]
        add_tok = [None, None]
        stx_tok = [None, None]
        pe_last_rhs = [None, None]
        nsup = NT // 512
        for s in range(nsup):
            i2 = s % 2
            ctx.wait("sp", pe_last_rhs[i2])
            for kc in range(KCH):
                ld_tok[i2] = ctx.dma(f"spy{i2}", out=rhs[i2][:, kc * 512:(kc + 1) * 512],
                                     in_=src_h[kc * 128:(kc + 1) * 128, s * 512:(s + 1) * 512])
            rv = rhs[i2][:].rearrange("p (kc t) -> p kc t", kc=KCH)
            ctx.wait("dve", tr_tok[i2])
            for oc in range(8):
                o2 = oc % 2
                ctx.wait("pe", ld_tok[i2], ev_tok[o2], *(wl_toks if s == 0 else []))
                for kc in range(KCH):
                    mm = ctx.pe("matmul", psm[o2][:], w3v[:, kc, oc * 128:(oc + 1) * 128],
                                rv[:, kc, :], start=(kc == 0), stop=(kc == KCH - 1),
                                mark=(kc == KCH - 1))
                pe_last_rhs[i2] = mm
                ctx.wait("dve", mm)
                ev_tok[o2] = ctx.op("dve", "tensor_scalar",
                                    out=ufm[i2][:, oc * 512:(oc + 1) * 512], in0=psm[o2][:],
                                    scalar1=colv[:, c_s3 + oc:c_s3 + oc + 1], scalar2=None,
                                    op0=ALU.mult)
            uv = ufm[i2][:].rearrange("p (oc t) -> p oc t", oc=8)
            for j in range(4):
                it = s * 4 + j
                j2 = j % 2
                ctx.wait("sp", add_tok[j2])
                ldx = ctx.dma(f"spx{j2}", out=xs[j2][:], in_=x_src[it * 128:(it + 1) * 128, :])
                ctx.wait("pe", ev_tok[0], ev_tok[1], add_tok[j2])
                for oc in range(8):
                    pt = ctx.pe("transpose", pst[j2][:, oc * 128:(oc + 1) * 128],
                                uv[:, oc, j * 128:(j + 1) * 128], C.ident_f[:],
                                mark=(oc == 7))
                tr_tok[i2] = pt
                ctx.wait("dve", pt, ldx, stx_tok[j2])
                add_tok[j2] = ctx.op("dve", "tensor_tensor", out=xo[j2][:], in0=pst[j2][:],
                                     in1=xs[j2][:], op=ALU.add)
                ctx.wait("pool", add_tok[j2])
                stx_tok[j2] = ctx.dma(f"gx{j2}", out=x_dst[it * 128:(it + 1) * 128, :],
                                      in_=xo[j2][:])


class ColAlloc:
    def __init__(self):
        self.n = 0
        self.map = {}

    def reg(self, name, ncols):
        self.map[name] = self.n
        self.n += ncols
        return self.map[name]


def emit_prep_ffn0(ctx: Ctx, C: Consts, ext, dr, colv, cols: ColAlloc):
    prep_transposed_weight(ctx, C, ext["L0_ffn_w1_w"], dr["W1T"], FFN, D, "w1")
    prep_transposed_weight(ctx, C, ext["L0_ffn_w2_w"], dr["W2T"], FFN, D, "w2")
    prep_transposed_weight(ctx, C, ext["L0_ffn_h1_w"], dr["H1WT"], FFN, FFN, "h1")
    prep_transposed_weight(ctx, C, ext["L0_ffn_h2_w"], dr["H2WT"], FFN, FFN, "h2")
    prep_transposed_weight(ctx, C, ext["L0_ffn_w3_w"], dr["W3T"], D, FFN, "w3")
    prep_colvec(ctx, C, ext["L0_ffn_w1_s"], FFN, colv, cols.reg("w1_s", 32))
    prep_colvec(ctx, C, ext["L0_ffn_w2_s"], FFN, colv, cols.reg("w2_s", 32))
    prep_colvec(ctx, C, ext["L0_ffn_h1_s"], FFN, colv, cols.reg("h1_s", 32))
    prep_colvec(ctx, C, ext["L0_ffn_h2_s"], FFN, colv, cols.reg("h2_s", 32))
    prep_colvec(ctx, C, ext["L0_ffn_w3_s"], D, colv, cols.reg("w3_s", 8), scale=0.5)


def emit_ffn0_app(ctx: Ctx, C: Consts, dr, colv, cols: ColAlloc, x_src, x_dst, tag):
    phase_ln_to_fm(ctx, C, x_src, dr["YT"], tag + "a")
    ctx.barrier()
    phase_ffn0_s1(ctx, C, dr["YT"], dr["W1T"], dr["W2T"], colv,
                  cols.map["w1_s"], cols.map["w2_s"], dr["HA"], tag + "b")
    ctx.barrier()
    phase_ffn0_mid(ctx, C, dr["HA"], dr["H1WT"], colv, cols.map["h1_s"], dr["HB"], tag + "c")
    ctx.barrier()
    phase_ffn0_mid(ctx, C, dr["HB"], dr["H2WT"], colv, cols.map["h2_s"], dr["HA"], tag + "d")
    ctx.barrier()
    phase_out_ffn(ctx, C, dr["HA"], dr["W3T"], False, colv, cols.map["w3_s"], None,
                  x_src, None, x_dst, tag + "e")
    ctx.barrier()



def phase_ln2(ctx: Ctx, C: Consts, x_src, yt_dst, tag, y1_dst=None, double_ln=False,
              colv=None, w_cols=None, b_cols=None, ytok_dst=None):
    nc = ctx.nc
    with ExitStack() as st_:
        xb = [st_.enter_context(nc.sbuf_tensor(f"l2x{i}_{tag}", [128, D], F32)) for i in range(2)]
        y1b = [st_.enter_context(nc.sbuf_tensor(f"l2y1{i}_{tag}", [128, D], F32)) for i in range(2)] \
            if double_ln else None
        yb = [st_.enter_context(nc.sbuf_tensor(f"l2y{i}_{tag}", [128, D], BF16)) for i in range(2)]
        yts = [st_.enter_context(nc.sbuf_tensor(f"l2t{i}_{tag}", [128, 8 * 512], BF16)) for i in range(2)]
        ps = [st_.enter_context(nc.psum_tensor(f"l2p{i}_{tag}", [128, 1024], BF16)) for i in range(2)]
        lb = LnBufs(_StackCtx(ctx, st_), tag + "ln")
        lb2 = LnBufs(_StackCtx(ctx, st_), tag + "l2") if double_ln else None
        norm_tok = [None, None]
        evac_tok = [None, None]
        tr_tok = [None, None]
        st_tok = [None, None]
        sty_tok = [None, None]
        stk_tok = [None, None]
        ctx.wait("pe", C.ready)
        nsup = NT // 512
        for s in range(nsup):
            for j in range(4):
                it = s * 4 + j
                i2 = it % 2
                ctx.wait("sp", norm_tok[i2])
                ldx = ctx.dma(f"spx{i2}", out=xb[i2][:], in_=x_src[it * 128:(it + 1) * 128, :])
                ctx.wait("dve", tr_tok[i2])
                if double_ln:
                    ctx.wait("dve", sty_tok[i2])
                    t1, _, _ = emit_ln(ctx, lb, i2, xb[i2][:], y1b[i2][:], ldx)
                    ctx.wait("pool", t1)
                    sty_tok[i2] = ctx.dma(f"gy1{i2}", out=y1_dst[it * 128:(it + 1) * 128, :],
                                          in_=y1b[i2][:])
                    tnorm, _, _ = emit_ln(ctx, lb2, i2, y1b[i2][:], yb[i2][:], t1)
                else:
                    tnorm, _, _ = emit_ln(ctx, lb, i2, xb[i2][:], yb[i2][:], ldx)
                norm_tok[i2] = tnorm
                if ytok_dst is not None:
                    ctx.wait("pool", tnorm)
                    stk_tok[i2] = ctx.dma(f"gyk{i2}", out=ytok_dst[it * 128:(it + 1) * 128, :],
                                          in_=yb[i2][:])
                ctx.wait("pe", tnorm, evac_tok[i2])
                for kc in range(8):
                    pt = ctx.pe("transpose", ps[i2][:, kc * 128:(kc + 1) * 128],
                                yb[i2][:, kc * 128:(kc + 1) * 128], C.ident_bf[:],
                                mark=(kc == 7))
                tr_tok[i2] = pt
                ctx.wait("dve", pt, st_tok[s % 2])
                pv = ps[i2][:].rearrange("p (kc t) -> p kc t", kc=8)
                yv = yts[s % 2][:].rearrange("p (kc t) -> p kc t", kc=8)
                if w_cols is not None:
                    for kc in range(8):
                        evac_tok[i2] = ctx.op(
                            "dve", "tensor_scalar",
                            out=yv[:, kc, j * 128:(j + 1) * 128],
                            in0=pv[:, kc, :],
                            scalar1=colv[:, w_cols + kc:w_cols + kc + 1],
                            scalar2=colv[:, b_cols + kc:b_cols + kc + 1],
                            op0=ALU.mult, op1=ALU.add)
                else:
                    evac_tok[i2] = ctx.op("dve", "tensor_copy",
                                          yv[:, :, j * 128:(j + 1) * 128], pv[:, :, :])
            for kc in range(8):
                ctx.wait("pool", evac_tok[0], evac_tok[1])
                st_tok[s % 2] = ctx.dma(
                    f"gyt{s % 2}",
                    out=yt_dst[kc * 128:(kc + 1) * 128, s * 512:(s + 1) * 512],
                    in_=yts[s % 2][:, kc * 512:(kc + 1) * 512])



def emit_phi(ctx, psum_ap, bias_col, out_ap, tmpe, tmpm, phi_scale=1.0):
    ctx.op("dve", "tensor_scalar", out=tmpm, in0=psum_ap, scalar1=bias_col,
           scalar2=None, op0=ALU.add)
    t = ctx.tok("dve")
    ctx.wait("act", t)
    if phi_scale != 1.0:
        ctx.op("dve", "tensor_scalar", out=tmpm, in0=tmpm, scalar1=float(phi_scale),
               scalar2=None, op0=ALU.mult)
    ctx.op("dve", "tensor_scalar", out=tmpe, in0=tmpm, scalar1=0.0, scalar2=None, op0=ALU.min)
    t = ctx.tok("dve")
    ctx.wait("act", t)
    ctx.op("act", "activation", out=tmpe, in_=tmpe, func=AF.Exp)
    ta = ctx.tok("act")
    ctx.wait("dve", ta)
    ctx.op("dve", "tensor_scalar", out=tmpm, in0=tmpm, scalar1=0.0, scalar2=None, op0=ALU.max)
    return ctx.op("dve", "tensor_tensor", out=out_ap, in0=tmpe, in1=tmpm, op=ALU.add)


def phase_proj_fm(ctx: Ctx, C: Consts, xnt, w_src, dst, tag, colv=None, bias_cols=None,
                  phi=False, phi_scale=1.0, w_transposed_src=False):
    nc = ctx.nc
    O = 1024
    with ExitStack() as st_:
        wsb = st_.enter_context(nc.sbuf_tensor(f"pj_w_{tag}", [128, 8 * O], BF16))
        xtb = [st_.enter_context(nc.sbuf_tensor(f"pj_x{i}_{tag}", [128, 8 * 512], BF16)) for i in range(2)]
        ob = [st_.enter_context(nc.sbuf_tensor(f"pj_o{i}_{tag}", [128, 512], BF16)) for i in range(2)]
        te = [st_.enter_context(nc.sbuf_tensor(f"pj_e{i}_{tag}", [128, 512], F32)) for i in range(2)]
        tm = [st_.enter_context(nc.sbuf_tensor(f"pj_m{i}_{tag}", [128, 512], F32)) for i in range(2)]
        ps = [st_.enter_context(nc.psum_tensor(f"pj_p{i}_{tag}", [128, 512], F32)) for i in range(2)]
        wv = wsb[:].rearrange("p (kc o) -> p kc o", kc=8)
        if not w_transposed_src:
            wts = [ctx.dma(f"gw{kc % 2}", out=wv[:, kc, :], in_=w_src[kc * 128:(kc + 1) * 128, :])
                   for kc in range(8)]
            ctx.wait("pe", *wts)
        else:
            with ExitStack() as st2:
                blk = [st2.enter_context(nc.sbuf_tensor(f"pj_tb{i}_{tag}", [128, 128], BF16))
                       for i in range(2)]
                pst = [st2.enter_context(nc.psum_tensor(f"pj_tp{i}_{tag}", [128, 128], BF16))
                       for i in range(2)]
                ctx.wait("pe", C.ready)
                ev = [None, None]
                tt = [None, None]
                for n in range(64):
                    r, c = n // 8, n % 8
                    i = n % 2
                    ctx.wait("pool", tt[i])
                    lt = ctx.dma(f"gw{i}", out=blk[i][:],
                                 in_=w_src[r * 128:(r + 1) * 128, c * 128:(c + 1) * 128])
                    ctx.wait("pe", lt, ev[i])
                    tt[i] = ctx.pe("transpose", pst[i][:], blk[i][:], C.ident_bf[:], mark=True)
                    ctx.wait("dve", tt[i])
                    ev[i] = ctx.op("dve", "tensor_copy", wv[:, c, r * 128:(r + 1) * 128], pst[i][:])
                ctx.wait("pe", ev[0], ev[1])
        ld_tok = [None, None]
        e_tok = [None, None]
        st_tok = [None, None]
        pe_last = [None, None]
        nsup = NT // 512
        for s in range(nsup):
            i2 = s % 2
            ctx.wait("sp", pe_last[i2])
            for kc in range(8):
                ld_tok[i2] = ctx.dma(f"spy{i2}", out=xtb[i2][:, kc * 512:(kc + 1) * 512],
                                     in_=xnt[kc * 128:(kc + 1) * 128, s * 512:(s + 1) * 512])
            xv = xtb[i2][:].rearrange("p (kc t) -> p kc t", kc=8)
            for oc in range(8):
                o2 = (s * 8 + oc) % 2
                ctx.wait("pe", ld_tok[i2], e_tok[o2])
                for kc in range(8):
                    mm = ctx.pe("matmul", ps[o2][:], wv[:, kc, oc * 128:(oc + 1) * 128],
                                xv[:, kc, :], start=(kc == 0), stop=(kc == 7), mark=(kc == 7))
                pe_last[i2] = mm
                bias_col = colv[:, bias_cols + oc:bias_cols + oc + 1] if bias_cols is not None else 0.0
                ctx.wait("dve", mm, st_tok[o2])
                if phi:
                    e_tok[o2] = emit_phi(ctx, ps[o2][:], bias_col, ob[o2][:],
                                         te[o2][:], tm[o2][:], phi_scale)
                else:
                    e_tok[o2] = ctx.op("dve", "tensor_scalar", out=ob[o2][:], in0=ps[o2][:],
                                       scalar1=bias_col, scalar2=None, op0=ALU.add)
                ctx.wait("pool", e_tok[o2])
                st_tok[o2] = ctx.dma(f"go{o2}", out=dst[oc * 128:(oc + 1) * 128,
                                                       s * 512:(s + 1) * 512],
                                     in_=ob[o2][:])



def phase_proj_tok(ctx: Ctx, C: Consts, xnt, wv_src, vbias_row, dst_v, wd_src, dbias_row,
                   lam_sb, tag, wv_transposed=False, wd_transposed=False):
    nc = ctx.nc
    with ExitStack() as st_:
        wsb = st_.enter_context(nc.sbuf_tensor(f"pt_w_{tag}", [128, 8 * 1024], BF16))
        wdb = st_.enter_context(nc.sbuf_tensor(f"pt_wd_{tag}", [128, 8 * 16], BF16)) \
            if wd_src is not None else None
        xtb = [st_.enter_context(nc.sbuf_tensor(f"pt_x{i}_{tag}", [128, 8 * 128], BF16)) for i in range(2)]
        vb = [st_.enter_context(nc.sbuf_tensor(f"pt_v{i}_{tag}", [128, 512], BF16)) for i in range(2)]
        ps = [st_.enter_context(nc.psum_tensor(f"pt_p{i}_{tag}", [128, 512], F32)) for i in range(2)]
        psd = st_.enter_context(nc.psum_tensor(f"pt_pd_{tag}", [128, 16], F32)) \
            if wd_src is not None else None
        wv_ = wsb[:].rearrange("p (kc o) -> p kc o", kc=8)
        if not wv_transposed:
            wts = [ctx.dma(f"gw{kc % 2}", out=wv_[:, kc, :], in_=wv_src[kc * 128:(kc + 1) * 128, :])
                   for kc in range(8)]
        else:
            wts = []
            _load_transposed_into(ctx, C, wv_src, wv_, 1024, tag + "v")
        if wd_src is not None:
            wd_ = wdb[:].rearrange("p (kc o) -> p kc o", kc=8)
            if not wd_transposed:
                wts += [ctx.dma("gwd", out=wd_[:, kc, :], in_=wd_src[kc * 128:(kc + 1) * 128, :])
                        for kc in range(8)]
            else:
                _load_transposed_into(ctx, C, wd_src, wd_, 16, tag + "d")
        ctx.wait("pe", *wts)
        ld_tok = [None, None]
        e_tok = [None, None]
        d_tok = None
        st_tok = [None, None]
        pe_last = [None, None]
        for tt in range(NTILE):
            i2 = tt % 2
            ctx.wait("sp", pe_last[i2])
            for kc in range(8):
                ld_tok[i2] = ctx.dma(f"spy{i2}", out=xtb[i2][:, kc * 128:(kc + 1) * 128],
                                     in_=xnt[kc * 128:(kc + 1) * 128, tt * 128:(tt + 1) * 128])
            xv = xtb[i2][:].rearrange("p (kc t) -> p kc t", kc=8)
            for oh in range(2):
                o2 = (tt * 2 + oh) % 2
                ctx.wait("pe", ld_tok[i2], e_tok[o2])
                for kc in range(8):
                    mm = ctx.pe("matmul", ps[o2][:], xv[:, kc, :],
                                wv_[:, kc, oh * 512:(oh + 1) * 512],
                                start=(kc == 0), stop=(kc == 7), mark=(kc == 7))
                pe_last[i2] = mm
                ctx.wait("dve", mm, st_tok[o2])
                e_tok[o2] = ctx.op("dve", "tensor_tensor", out=vb[o2][:], in0=ps[o2][:],
                                   in1=vbias_row[:, oh * 512:(oh + 1) * 512], op=ALU.add)
                ctx.wait("pool", e_tok[o2])
                st_tok[o2] = ctx.dma(f"go{o2}", out=dst_v[tt * 128:(tt + 1) * 128,
                                                         oh * 512:(oh + 1) * 512],
                                     in_=vb[o2][:])
            if wd_src is not None:
                ctx.wait("pe", d_tok)
                for kc in range(8):
                    mm = ctx.pe("matmul", psd[:], xv[:, kc, :], wd_[:, kc, :],
                                start=(kc == 0), stop=(kc == 7), mark=(kc == 7))
                pe_last[i2] = mm
                ctx.wait("dve", mm)
                ctx.op("dve", "tensor_tensor", out=lam_sb[:, tt * 16:(tt + 1) * 16],
                       in0=psd[:], in1=dbias_row[:, 0:16], op=ALU.add)
                t = ctx.tok("dve")
                ctx.wait("act", t)
                d_tok = ctx.op("act", "activation", out=lam_sb[:, tt * 16:(tt + 1) * 16],
                               in_=lam_sb[:, tt * 16:(tt + 1) * 16], func=AF.Sigmoid)


def _load_transposed_into(ctx: Ctx, C: Consts, w_src, wv_dst3, O, tag):
    nc = ctx.nc
    ob = ceil_div(O, 128)
    with ExitStack() as st2:
        blk = [st2.enter_context(nc.sbuf_tensor(f"lt_b{i}_{tag}", [128, 128], BF16)) for i in range(2)]
        pst = [st2.enter_context(nc.psum_tensor(f"lt_p{i}_{tag}", [128, 128], BF16)) for i in range(2)]
        ctx.wait("pe", C.ready)
        ev = [None, None]
        tt = [None, None]
        n = 0
        for r in range(ob):
            rr = min(128, O - r * 128)
            for c in range(8):
                i = n % 2
                ctx.wait("pool", tt[i])
                lt = ctx.dma(f"gw{i}", out=blk[i][0:rr, :],
                             in_=w_src[r * 128:r * 128 + rr, c * 128:(c + 1) * 128])
                ctx.wait("pe", lt, ev[i])
                tt[i] = ctx.pe("transpose", pst[i][:, 0:rr], blk[i][0:rr, :],
                               C.ident_bf[0:rr, 0:rr], mark=True)
                ctx.wait("dve", tt[i])
                ev[i] = ctx.op("dve", "tensor_copy", wv_dst3[:, c, r * 128:r * 128 + rr],
                               pst[i][:, 0:rr])
                n += 1
        ctx.barrier()



def phase_fm_to_tok(ctx: Ctx, C: Consts, src_fm, dst_tok, tag):
    nc = ctx.nc
    with ExitStack() as st_:
        xb = [st_.enter_context(nc.sbuf_tensor(f"ft_x{i}_{tag}", [128, 8 * 128], BF16)) for i in range(2)]
        ob = [st_.enter_context(nc.sbuf_tensor(f"ft_o{i}_{tag}", [128, 1024], BF16)) for i in range(2)]
        ps = [st_.enter_context(nc.psum_tensor(f"ft_p{i}_{tag}", [128, 1024], BF16)) for i in range(2)]
        ctx.wait("pe", C.ready)
        tr_tok = [None, None]
        ev_tok = [None, None]
        st_tok = [None, None]
        for tt in range(NTILE):
            i2 = tt % 2
            ctx.wait("sp", tr_tok[i2])
            ld = None
            for kc in range(8):
                ld = ctx.dma(f"spy{i2}", out=xb[i2][:, kc * 128:(kc + 1) * 128],
                             in_=src_fm[kc * 128:(kc + 1) * 128, tt * 128:(tt + 1) * 128])
            ctx.wait("pe", ld, ev_tok[i2])
            for kc in range(8):
                pt = ctx.pe("transpose", ps[i2][:, kc * 128:(kc + 1) * 128],
                            xb[i2][:, kc * 128:(kc + 1) * 128], C.ident_bf[:], mark=(kc == 7))
            tr_tok[i2] = pt
            ctx.wait("dve", pt, st_tok[i2])
            ev_tok[i2] = ctx.op("dve", "tensor_copy", ob[i2][:], ps[i2][:])
            ctx.wait("pool", ev_tok[i2])
            st_tok[i2] = ctx.dma(f"go{i2}", out=dst_tok[tt * 128:(tt + 1) * 128, :], in_=ob[i2][:])



SLOT65 = 65


class ScanApp:

    def __init__(self, ctx: Ctx, tag, den=True):
        nc, st_ = ctx.nc, ctx.stack
        self.den = den
        self.w = SLOT65 if den else 64
        w = self.w
        ch = T // 128
        self.ch = ch
        self.out_sb = st_.enter_context(nc.sbuf_tensor(f"sc_out_{tag}", [128, ch * H * w], F32))
        self.lam = st_.enter_context(nc.sbuf_tensor(f"sc_lam_{tag}", [128, NTILE * H], F32))
        self.Sf = st_.enter_context(nc.sbuf_tensor(f"sc_sf_{tag}", [64, H * w], F32))
        self.Sb = st_.enter_context(nc.sbuf_tensor(f"sc_sb_{tag}", [128, H * w], BF16))
        self.eq = st_.enter_context(nc.sbuf_tensor(f"sc_eq_{tag}", [128, ch * H], F32))
        self.ek = st_.enter_context(nc.sbuf_tensor(f"sc_ek_{tag}", [128, ch * H], F32))
        self.gb = st_.enter_context(nc.sbuf_tensor(f"sc_gb_{tag}", [64, ch * H], F32))
        self.lhs16 = st_.enter_context(nc.sbuf_tensor(f"sc_lh_{tag}", [16, ch * 128], F32))
        self.bd16 = st_.enter_context(nc.sbuf_tensor(f"sc_bd_{tag}", [16, H * 128], F32))
        self.negone = st_.enter_context(nc.sbuf_tensor(f"sc_no_{tag}", [1, 128], F32))
        self.maskt = st_.enter_context(nc.sbuf_tensor(f"sc_mk_{tag}", [128, 128], F32))


_UNIQ = [0]


def _uq(s):
    _UNIQ[0] += 1
    return f"{s}{_UNIQ[0]}"


def build_scan_consts(ctx: Ctx, C: Consts, app: ScanApp, causal_mask):
    nc = ctx.nc
    ctx.barrier()
    with ExitStack() as st_:
        iorow = st_.enter_context(nc.sbuf_tensor(_uq("bd_ior"), [16, H * 128], F32))
        col = st_.enter_context(nc.sbuf_tensor(_uq("bd_col"), [16, 1], F32))
        t = ctx.op("pool", "iota", iorow[:], [[1, H * 128]], base=0, channel_multiplier=0,
                   allow_small_or_imprecise_dtypes=True)
        ctx.wait("dve", t, C.t_iota_p)
        ctx.op("dve", "tensor_scalar", out=col[:], in0=C.iota_p[0:16, :], scalar1=128.0,
               scalar2=None, op0=ALU.mult)
        ctx.op("dve", "tensor_scalar", out=app.bd16[:], in0=iorow[:], scalar1=col[:],
               scalar2=None, op0=ALU.is_ge)
        ctx.op("dve", "tensor_scalar", out=iorow[:], in0=iorow[:], scalar1=col[:],
               scalar2=128.0, op0=ALU.subtract, op1=ALU.is_lt)
        ctx.op("dve", "tensor_tensor", out=app.bd16[:], in0=app.bd16[:], in1=iorow[:], op=ALU.mult)
        ctx.op("dve", "memset", app.negone[:], -1.0)
        if causal_mask:
            sc1 = st_.enter_context(nc.sbuf_tensor(_uq("bd_s1"), [128, 128], F32))
            t2 = ctx.op("pool", "iota", sc1[:], [[1, 128]], base=0, channel_multiplier=0,
                        allow_small_or_imprecise_dtypes=True)
            ctx.wait("dve", t2)
            ctx.op("dve", "tensor_scalar", out=app.maskt[:], in0=sc1[:], scalar1=C.iota_p[:],
                   scalar2=None, op0=ALU.is_lt)
            ctx.op("dve", "tensor_scalar", out=app.maskt[:], in0=app.maskt[:], scalar1=-1e30,
                   scalar2=None, op0=ALU.mult)
        else:
            ctx.op("dve", "memset", app.maskt[:], 0.0)
        ctx.barrier()


def emit_scan_prelude(ctx: Ctx, C: Consts, app: ScanApp, b, prow_dram, clamp_min, tag):
    nc = ctx.nc
    CW = app.ch * H
    with ExitStack() as st_:
        lgl = st_.enter_context(nc.sbuf_tensor(f"pr_lgl_{tag}", [128, CW], F32))
        pcol = st_.enter_context(nc.sbuf_tensor(f"pr_pc_{tag}", [128, CW], F32))
        trow = st_.enter_context(nc.sbuf_tensor(f"pr_tr_{tag}", [128, ceil_div(CW, 128) * 128], F32))
        grow = st_.enter_context(nc.sbuf_tensor(f"pr_gr_{tag}", [1, CW], F32))
        pp = st_.enter_context(nc.psum_tensor(f"pr_pp_{tag}", [128, CW], F32))
        pp2 = st_.enter_context(nc.psum_tensor(f"pr_p2_{tag}", [128, CW], F32))
        ppg = st_.enter_context(nc.psum_tensor(f"pr_pg_{tag}", [1, CW], F32))
        ppb = st_.enter_context(nc.psum_tensor(f"pr_pb_{tag}", [64, CW], F32))
        ppt = [st_.enter_context(nc.psum_tensor(f"pr_pt{i}_{tag}", [128, 128], F32))
               for i in range(2)]
        lam_sl = app.lam[:, b * CW:(b + 1) * CW]
        ctx.op("dve", "tensor_scalar", out=lgl[:], in0=lam_sl, scalar1=float(clamp_min),
               scalar2=None, op0=ALU.max)
        t = ctx.tok("dve")
        ctx.wait("act", t)
        tl = ctx.op("act", "activation", out=lgl[:], in_=lgl[:], func=AF.Ln)
        ctx.wait("pe", tl, C.ready)
        t1 = ctx.pe("matmul", pp[:], C.tril_f[:], lgl[:], start=True, stop=True, mark=True)
        t2 = ctx.pe("matmul", pp2[:], C.triu_f[:], lgl[:], start=True, stop=True, mark=True)
        t3 = ctx.pe("matmul", ppg[:], C.ones_f[:, 0:1], lgl[:], start=True, stop=True, mark=True)
        ctx.wait("act", t1, t2, t3)
        ctx.op("act", "activation", out=app.eq[:], in_=pp[:], func=AF.Exp)
        ctx.op("act", "activation", out=app.ek[:], in_=pp2[:], func=AF.Exp)
        tg = ctx.op("act", "activation", out=grow[:], in_=ppg[:], func=AF.Exp)
        ctx.wait("dve", t1)
        tpc = ctx.op("dve", "tensor_copy", pcol[:], pp[:])
        ctx.wait("pe", tg)
        t4 = ctx.pe("matmul", ppb[:], C.ones_f[0:1, 0:64], grow[0:1, :], start=True, stop=True,
                    mark=True)
        ctx.wait("dve", t4)
        ctx.op("dve", "tensor_copy", app.gb[:], ppb[:])
        ctx.wait("pe", tpc)
        st_toks = []
        for half in range(ceil_div(CW, 128)):
            wdt = min(128, CW - half * 128)
            th = ctx.pe("transpose", ppt[half % 2][0:wdt, :],
                        pcol[:, half * 128:half * 128 + wdt],
                        C.ident_f[:], mark=True)
            ctx.wait("dve", th)
            ev = ctx.op("dve", "tensor_scalar",
                        out=trow[0:wdt, half * 128:half * 128 + 128],
                        in0=ppt[half % 2][0:wdt, :], scalar1=-1.0, scalar2=None, op0=ALU.mult)
            ctx.wait("pool", ev)
            st_toks.append(ctx.dma("gprow",
                                   out=prow_dram[half * 128:half * 128 + wdt, :],
                                   in_=trow[0:wdt, half * 128:half * 128 + 128]))
        ctx.wait("pool", *st_toks)
        ctx.dma("gprow", out=app.lhs16[:].rearrange("p (c i) -> p c i", c=app.ch),
                in_=prow_dram[:].rearrange("(c h) i -> h c i", h=16))
        ctx.barrier()


def emit_scan_pass(ctx: Ctx, C: Consts, app: ScanApp, b, qf, kf, ktok, vtok, prow_dram,
                   forward, causal, tag):
    nc = ctx.nc
    W = app.w
    den = app.den
    with ExitStack() as st_:
        qft = [st_.enter_context(nc.sbuf_tensor(f"sp_q{i}_{tag}", [128, 8 * 128], BF16)) for i in range(2)]
        kft = [st_.enter_context(nc.sbuf_tensor(f"sp_k{i}_{tag}", [128, 8 * 128], BF16)) for i in range(2)] \
            if forward else None
        ktt = [st_.enter_context(nc.sbuf_tensor(f"sp_kt{i}_{tag}", [128, 1024], BF16)) for i in range(2)]
        vtt = [st_.enter_context(nc.sbuf_tensor(f"sp_vt{i}_{tag}", [128, 1024], BF16)) for i in range(2)]
        pflat = [st_.enter_context(nc.sbuf_tensor(f"sp_pf{i}_{tag}", [1, 2048], F32)) for i in range(2)] \
            if forward else None
        cwn = st_.enter_context(nc.sbuf_tensor(f"sp_cn_{tag}", [128, 1024], F32)) if forward else None
        cw = st_.enter_context(nc.sbuf_tensor(f"sp_cw_{tag}", [128, 1024], F32)) if forward else None
        cs = [st_.enter_context(nc.sbuf_tensor(f"sp_cs{i}_{tag}", [128, 1024], BF16)) for i in range(2)] \
            if forward else None
        k2 = st_.enter_context(nc.sbuf_tensor(f"sp_k2_{tag}", [128, 1024], BF16))
        SW = W if W % 2 == 0 else W + 1
        tmp = [st_.enter_context(nc.sbuf_tensor(f"sp_tm{i}_{tag}", [128, 4 * SW], F32)) for i in range(2)]
        onesb = st_.enter_context(nc.sbuf_tensor(f"sp_on_{tag}", [128, 2], BF16))
        ps_big = st_.enter_context(nc.psum_tensor(f"sp_pb_{tag}", [128, 1024], F32)) if forward else None
        ps_a = [st_.enter_context(nc.psum_tensor(f"sp_pa{i}_{tag}", [128, 4 * SW], F32)) for i in range(2)] \
            if forward else None
        ps_ib = [st_.enter_context(nc.psum_tensor(f"sp_pi{i}_{tag}", [128, 4 * SW], F32)) for i in range(2)]
        ps_u = [st_.enter_context(nc.psum_tensor(f"sp_pu{i}_{tag}", [64, 4 * SW], F32)) for i in range(2)]
        ctx.op("dve", "memset", onesb[:], 1.0)
        ctx.op("dve", "memset", app.Sf[:], 0.0)
        sb_cast_tok = ctx.op("dve", "memset", app.Sb[:], 0.0)
        ld_q = [None, None]
        ld_k = [None, None]
        ld_kt = [None, None]
        ld_vt = [None, None]
        ld_pf = [None, None]
        cfree_tok = None
        cs_tok = [None, None]
        a_ev = [None, None]
        i_ev = [None, None]
        u_ev = [None, None]
        pe_q = [None, None]
        pe_kt = [None, None]
        pe_vt = [None, None]
        CH = app.ch
        eqv = app.eq[:].rearrange("p (c h) -> p c h", c=CH)
        ekv = app.ek[:].rearrange("p (c h) -> p c h", c=CH)
        gbv = app.gb[:].rearrange("p (c h) -> p c h", c=CH)
        outv = app.out_sb[:].rearrange("p (c h w) -> p c h w", c=CH, h=H)
        lhsv = app.lhs16[:].rearrange("p (c i) -> p c i", c=CH)
        chunks = range(CH) if forward else range(CH - 1, -1, -1)
        cs_war = [None, None]
        k2_war = None
        for ci, c in enumerate(chunks):
            i2 = ci % 2
            t0 = b * T + c * 128
            ctx.wait("sp", pe_q[i2])
            for fc in range(8):
                ld_q[i2] = ctx.dma(f"spq{i2}", out=qft[i2][:, fc * 128:(fc + 1) * 128],
                                   in_=qf[fc * 128:(fc + 1) * 128, t0:t0 + 128])
            if forward:
                for fc in range(8):
                    ld_k[i2] = ctx.dma(f"spk{i2}", out=kft[i2][:, fc * 128:(fc + 1) * 128],
                                       in_=kf[fc * 128:(fc + 1) * 128, t0:t0 + 128])
                ctx.wait("sp", cfree_tok)
                ld_pf[i2] = ctx.dma(f"sppf{i2}", out=pflat[i2][0:1, :],
                                    in_=prow_dram[c * 16:(c + 1) * 16, :].rearrange("a b -> (a b)").unsqueeze(0))
            ctx.wait("sp", pe_kt[i2], pe_vt[i2])
            ld_kt[i2] = ctx.dma(f"spkt{i2}", out=ktt[i2][:], in_=ktok[t0:t0 + 128, :])
            ld_vt[i2] = ctx.dma(f"spvt{i2}", out=vtt[i2][:], in_=vtok[t0:t0 + 128, :])
            inter_marks = []
            atok = None
            for g in range(2):
                if forward:
                    ctx.wait("pe", ld_pf[i2], cs_tok[0], cs_tok[1])
                    for sl in range(2 if SCAN_PARTS & 1 else 0):
                        ctx.pe("matmul", ps_big[:, sl * 512:(sl + 1) * 512],
                               lhsv[:, c, :], app.bd16[:, g * 1024 + sl * 512:g * 1024 + (sl + 1) * 512],
                               start=True, stop=False)
                        mtok = ctx.pe("matmul", ps_big[:, sl * 512:(sl + 1) * 512],
                                      app.negone[0:1, :],
                                      pflat[i2][0:1, g * 1024 + sl * 512:g * 1024 + (sl + 1) * 512],
                                      start=False, stop=True, mark=True)
                    if not SCAN_PARTS & 2:
                        mtok2 = mtok if SCAN_PARTS & 1 else None
                        ctx.wait("dve", mtok2)
                        ce_tok = ctx.op("dve", "memset", cw[:], 1.0)
                        cfree_tok = ce_tok
                    elif causal:
                        ctx.wait("dve", mtok)
                        mvk = app.maskt[:].unsqueeze(1).broadcast_to([128, 8, 128])
                        ctx.op("dve", "tensor_tensor",
                               out=cw[:].rearrange("p (h i) -> p h i", h=8),
                               in0=ps_big[:].rearrange("p (h i) -> p h i", h=8),
                               in1=mvk, op=ALU.add)
                    else:
                        ctx.wait("dve", mtok)
                        ctx.op("dve", "tensor_scalar", out=cwn[:], in0=ps_big[:],
                               scalar1=-1.0, scalar2=None, op0=ALU.mult)
                        ctx.op("dve", "tensor_tensor", out=cw[:], in0=ps_big[:], in1=cwn[:],
                               op=ALU.min)
                    if SCAN_PARTS & 2:
                        cfree_tok = ctx.tok("dve")
                        ctx.wait("act", cfree_tok)
                        ce_tok = ctx.op("act", "activation", out=cw[:], in_=cw[:], func=AF.Exp)
                    ctx.wait("pe", cfree_tok, ld_q[i2], ld_k[i2])
                    stok = None
                    for hh in ([0, 5, 2, 7, 1, 4, 3, 6] if SCAN_PARTS & 4 else []):
                        h = g * 8 + hh
                        p0 = (h % 2) * 64
                        fc = h // 2
                        stok = ctx.pe("matmul", ps_big[:, hh * 128:(hh + 1) * 128],
                                      kft[i2][p0:p0 + 64, fc * 128:(fc + 1) * 128],
                                      qft[i2][p0:p0 + 64, fc * 128:(fc + 1) * 128],
                                      start=True, stop=True, mark=(SCAN_SAFE or hh == 6))
                        if SCAN_SAFE:
                            ctx.wait("pe", stok)
                    ctx.wait("dve", stok, ce_tok, cs_war[g])
                    if SCAN_PARTS & 8:
                        cs_tok[g] = ctx.op("dve", "tensor_tensor", out=cs[g][:], in0=ps_big[:],
                                           in1=cw[:], op=ALU.mult)
                    else:
                        cs_tok[g] = ctx.op("dve", "memset", cs[g][:], 0.01)
                    ctx.wait("pe", cs_tok[g], ld_vt[i2], a_ev[0], a_ev[1])
                    for hh in (range(8) if SCAN_PARTS & 16 else []):
                        h = g * 8 + hh
                        half, pos = hh // 4, hh % 4
                        ctx.pe("matmul", ps_a[half][:, pos * SW:pos * SW + 64],
                               cs[g][:, hh * 128:(hh + 1) * 128],
                               vtt[i2][:, h * 64:(h + 1) * 64],
                               start=True, stop=True,
                               mark=(hh == 7 and not den))
                        if den:
                            ctx.pe("matmul", ps_a[half][:, pos * SW + 64:pos * SW + 66],
                                   cs[g][:, hh * 128:(hh + 1) * 128], onesb[:, 0:2],
                                   start=True, stop=True, mark=(hh == 7))
                    atok = Tok(ctx.pe_sem, ctx.pe_cnt)
                    cs_war[g] = atok
                    pe_vt[i2] = atok
                ctx.wait("pe", ld_q[i2], sb_cast_tok, i_ev[0], i_ev[1])
                state = app.Sb
                btok = None
                for hh in ([0, 4, 1, 5, 2, 6, 3, 7] if SCAN_PARTS & 32 else []):
                    h = g * 8 + hh
                    p0 = (h % 2) * 64
                    fc = h // 2
                    half, pos = hh // 4, hh % 4
                    btok = ctx.pe("matmul", ps_ib[half][:, pos * SW:pos * SW + W],
                                  qft[i2][p0:p0 + 64, fc * 128:(fc + 1) * 128],
                                  state[p0:p0 + 64, h * W:(h + 1) * W],
                                  start=True, stop=True, mark=(SCAN_SAFE or hh == 7))
                    if SCAN_SAFE:
                        ctx.wait("pe", btok)
                inter_marks.append(btok)
                pe_q[i2] = btok
                fac = eqv if forward else ekv
                for q in (range(2) if SCAN_PARTS & 64 else []):
                    hb = g * 8 + q * 4
                    ctx.wait("dve", btok, (atok if forward else None))
                    fap = fac[:, c, hb:hb + 4].unsqueeze(2).broadcast_to([128, 4, W])
                    i_ev[q] = ctx.op("dve", "tensor_tensor",
                                     out=tmp[q][:].rearrange("p (h w) -> p h w", h=4)[:, :, 0:W],
                                     in0=ps_ib[q][:].rearrange("p (h w) -> p h w", h=4)[:, :, 0:W],
                                     in1=fap, op=ALU.mult)
                    if forward:
                        a_ev[q] = ctx.op("dve", "tensor_tensor",
                                         out=outv[:, c, hb:hb + 4, :],
                                         in0=ps_a[q][:].rearrange("p (h w) -> p h w", h=4)[:, :, 0:W],
                                         in1=tmp[q][:].rearrange("p (h w) -> p h w", h=4)[:, :, 0:W],
                                         op=ALU.add)
                    else:
                        a_ev[q] = ctx.op("dve", "tensor_tensor",
                                         out=outv[:, c, hb:hb + 4, :],
                                         in0=outv[:, c, hb:hb + 4, :],
                                         in1=tmp[q][:].rearrange("p (h w) -> p h w", h=4)[:, :, 0:W],
                                         op=ALU.add)
            if not SCAN_PARTS & 128:
                continue
            kfac = ekv if forward else eqv
            ctx.wait("dve", ld_kt[i2], k2_war)
            k2tok = ctx.op("dve", "tensor_tensor",
                           out=k2[:].rearrange("p (h k) -> p h k", h=H),
                           in0=ktt[i2][:].rearrange("p (h k) -> p h k", h=H),
                           in1=kfac[:, c, :].unsqueeze(2).broadcast_to([128, H, 64]),
                           op=ALU.mult)
            pe_kt[i2] = k2tok
            for quarter in range(4):
                pu = ps_u[quarter % 2]
                ctx.wait("pe", k2tok, ld_vt[i2], u_ev[quarter % 2], *inter_marks)
                for hh4 in range(4):
                    h = quarter * 4 + hh4
                    ctx.pe("matmul", pu[:, hh4 * SW:hh4 * SW + 64],
                           k2[:, h * 64:(h + 1) * 64], vtt[i2][:, h * 64:(h + 1) * 64],
                           start=True, stop=True, mark=(hh4 == 3 and not den))
                    if den:
                        ctx.pe("matmul", pu[:, hh4 * SW + 64:hh4 * SW + 66],
                               k2[:, h * 64:(h + 1) * 64], onesb[:, 0:2],
                               start=True, stop=True, mark=(hh4 == 3))
                utok = Tok(ctx.pe_sem, ctx.pe_cnt)
                pe_vt[i2] = utok
                k2_war = utok
                sf4 = app.Sf[:].rearrange("p (h w) -> p h w", h=H)[:, quarter * 4:quarter * 4 + 4, :]
                gap = gbv[:, c, quarter * 4:quarter * 4 + 4].unsqueeze(2).broadcast_to([64, 4, W])
                ctx.wait("dve", utok, *inter_marks)
                ctx.op("dve", "tensor_tensor", out=sf4, in0=sf4, in1=gap, op=ALU.mult)
                u_ev[quarter % 2] = ctx.op(
                    "dve", "tensor_tensor", out=sf4, in0=sf4,
                    in1=pu[:].rearrange("p (h w) -> p h w", h=4)[:, :, 0:W], op=ALU.add)
                ctx.op(
                    "dve", "tensor_copy",
                    app.Sb[0:64, :].rearrange("p (h w) -> p h w", h=H)[:, quarter * 4:quarter * 4 + 4, :],
                    sf4)
                sb_cast_tok = ctx.op(
                    "dve", "tensor_copy",
                    app.Sb[64:128, :].rearrange("p (h w) -> p h w", h=H)[:, quarter * 4:quarter * 4 + 4, :],
                    sf4)


def emit_a3_l0(ctx: Ctx, C: Consts, app: ScanApp, b, ow_src, ob_row, y1_dram, z_dram,
               x2t_dram, colv, c_ln2w, c_ln2b, tag):
    nc = ctx.nc
    W = app.w
    with ExitStack() as st_:
        ow_sb = st_.enter_context(nc.sbuf_tensor(f"a3_ow_{tag}", [128, 8 * 1024], BF16))
        atile = [st_.enter_context(nc.sbuf_tensor(f"a3_at{i}_{tag}", [128, 1024], BF16)) for i in range(2)]
        afm = [st_.enter_context(nc.sbuf_tensor(f"a3_af{i}_{tag}", [128, 1024], BF16)) for i in range(2)]
        rden = [st_.enter_context(nc.sbuf_tensor(f"a3_rd{i}_{tag}", [128, 16], F32)) for i in range(2)]
        y1t = [st_.enter_context(nc.sbuf_tensor(f"a3_y1{i}_{tag}", [128, 1024], F32)) for i in range(2)]
        zt = [st_.enter_context(nc.sbuf_tensor(f"a3_z{i}_{tag}", [128, 1024], F32)) for i in range(2)]
        zn = [st_.enter_context(nc.sbuf_tensor(f"a3_zn{i}_{tag}", [128, 1024], BF16)) for i in range(2)]
        x2s = [st_.enter_context(nc.sbuf_tensor(f"a3_x2{i}_{tag}", [128, 1024], BF16)) for i in range(2)]
        ps_t = [st_.enter_context(nc.psum_tensor(f"a3_pt{i}_{tag}", [128, 1024], BF16)) for i in range(2)]
        ps_o = [st_.enter_context(nc.psum_tensor(f"a3_po{i}_{tag}", [128, 512], F32)) for i in range(2)]
        ps_t2 = [st_.enter_context(nc.psum_tensor(f"a3_p2{i}_{tag}", [128, 1024], BF16)) for i in range(2)]
        lnb = LnBufs(_StackCtx(ctx, st_), tag + "ln")
        wv = ow_sb[:].rearrange("p (kc o) -> p kc o", kc=8)
        wts = [ctx.dma(f"gw{kc % 2}", out=wv[:, kc, :], in_=ow_src[kc * 128:(kc + 1) * 128, :])
               for kc in range(8)]
        ctx.wait("pe", *wts)
        outv = app.out_sb[:].rearrange("p (c h w) -> p c h w", c=app.ch, h=H)
        tr_tok = [None, None]
        afm_ev = [None, None]
        o_ev = [None, None]
        z_tok = [None, None]
        stz_tok = [None, None]
        ln_tok = [None, None]
        tr2_tok = [None, None]
        x2_ev = [None, None]
        stx_tok = [None, None]
        pe_o_last = [None, None]
        for ti in range(app.ch):
            tt = b * app.ch + ti
            i2 = ti % 2
            ctx.wait("dve", tr_tok[i2])
            if app.den:
                ctx.op("dve", "tensor_scalar", out=rden[i2][:],
                       in0=outv[:, ti, :, 64], scalar1=1e-6, scalar2=None, op0=ALU.max)
                ctx.op("dve", "reciprocal", out=rden[i2][:], in_=rden[i2][:])
                at_tok = ctx.op("dve", "tensor_tensor",
                                out=atile[i2][:].rearrange("p (h k) -> p h k", h=H),
                                in0=outv[:, ti, :, 0:64],
                                in1=rden[i2][:].unsqueeze(2).broadcast_to([128, H, 64]),
                                op=ALU.mult)
            else:
                at_tok = ctx.op("dve", "tensor_copy", atile[i2][:], app.out_sb[
                    :, ti * H * W:(ti + 1) * H * W])
            ctx.wait("pe", at_tok, afm_ev[i2])
            for fc in range(8):
                ptok = ctx.pe("transpose", ps_t[i2][:, fc * 128:(fc + 1) * 128],
                              atile[i2][:, fc * 128:(fc + 1) * 128], C.ident_bf[:],
                              mark=(fc == 7))
            tr_tok[i2] = ptok
            ctx.wait("dve", ptok, pe_o_last[i2])
            afm_ev[i2] = ctx.op("dve", "tensor_copy", afm[i2][:], ps_t[i2][:])
            ctx.wait("sp", z_tok[i2])
            ldy = ctx.dma(f"spx{i2}", out=y1t[i2][:], in_=y1_dram[tt * 128:(tt + 1) * 128, :])
            for oh in range(2):
                ctx.wait("pe", afm_ev[i2], o_ev[oh])
                for kc in range(8):
                    mm = ctx.pe("matmul", ps_o[oh][:], afm[i2][:, kc * 128:(kc + 1) * 128],
                                wv[:, kc, oh * 512:(oh + 1) * 512],
                                start=(kc == 0), stop=(kc == 7), mark=(kc == 7))
                pe_o_last[i2] = mm
                ctx.wait("dve", mm, stz_tok[i2], ln_tok[i2])
                o_ev[oh] = ctx.op("dve", "tensor_tensor", out=zt[i2][:, oh * 512:(oh + 1) * 512],
                                  in0=ps_o[oh][:], in1=ob_row[:, oh * 512:(oh + 1) * 512],
                                  op=ALU.add)
            ctx.wait("dve", ldy)
            z_tok[i2] = ctx.op("dve", "tensor_tensor", out=zt[i2][:], in0=zt[i2][:],
                               in1=y1t[i2][:], op=ALU.add)
            ctx.wait("pool", z_tok[i2])
            stz_tok[i2] = ctx.dma(f"gz{i2}", out=z_dram[tt * 128:(tt + 1) * 128, :], in_=zt[i2][:])
            ctx.wait("dve", tr2_tok[i2])
            ln_tok[i2], _, _ = emit_ln(ctx, lnb, i2, zt[i2][:], zn[i2][:], z_tok[i2])
            ctx.wait("pe", ln_tok[i2], x2_ev[i2])
            for fc in range(8):
                p2 = ctx.pe("transpose", ps_t2[i2][:, fc * 128:(fc + 1) * 128],
                            zn[i2][:, fc * 128:(fc + 1) * 128], C.ident_bf[:], mark=(fc == 7))
            tr2_tok[i2] = p2
            ctx.wait("dve", p2, stx_tok[i2])
            for fc in range(8):
                x2_ev[i2] = ctx.op("dve", "tensor_scalar",
                                   out=x2s[i2][:, fc * 128:(fc + 1) * 128],
                                   in0=ps_t2[i2][:, fc * 128:(fc + 1) * 128],
                                   scalar1=colv[:, c_ln2w + fc:c_ln2w + fc + 1],
                                   scalar2=colv[:, c_ln2b + fc:c_ln2b + fc + 1],
                                   op0=ALU.mult, op1=ALU.add)
            ctx.wait("pool", x2_ev[i2])
            for fc in range(8):
                stx_tok[i2] = ctx.dma(f"gx{i2}", out=x2t_dram[fc * 128:(fc + 1) * 128,
                                                            tt * 128:(tt + 1) * 128],
                                      in_=x2s[i2][:, fc * 128:(fc + 1) * 128])



def phase_lin_fm(ctx: Ctx, C: Consts, y_src, w_src, w_f32, colv, c_bias, dst, func, tag,
                 kd=D, O=FFN):
    nc = ctx.nc
    kch = kd // 128
    och = O // 128
    with ExitStack() as st_:
        wsb = st_.enter_context(nc.sbuf_tensor(f"lf_w_{tag}", [128, kch * O], BF16))
        ytb = [st_.enter_context(nc.sbuf_tensor(f"lf_y{i}_{tag}", [128, kch * 512], BF16)) for i in range(2)]
        ob = [st_.enter_context(nc.sbuf_tensor(f"lf_o{i}_{tag}", [128, 512], BF16)) for i in range(2)]
        ps = [st_.enter_context(nc.psum_tensor(f"lf_p{i}_{tag}", [128, 512], F32)) for i in range(2)]
        wv = wsb[:].rearrange("p (kc o) -> p kc o", kc=kch)
        if w_f32:
            wts = [ctx.dma(f"gw{kc % 2}", out=wv[:, kc, :], in_=w_src[kc * 128:(kc + 1) * 128, :])
                   for kc in range(kch)]
        else:
            wts = [ctx.dma("spw", out=wv[:, kc, :], in_=w_src[kc * 128:(kc + 1) * 128, :])
                   for kc in range(kch)]
        ctx.wait("pe", *wts)
        ld_tok = [None, None]
        e_tok = [None, None]
        st_tok = [None, None]
        pe_last = [None, None]
        nsup = NT // 512
        for s in range(nsup):
            i2 = s % 2
            ctx.wait("sp", pe_last[i2])
            for kc in range(kch):
                ld_tok[i2] = ctx.dma(f"spy{i2}", out=ytb[i2][:, kc * 512:(kc + 1) * 512],
                                     in_=y_src[kc * 128:(kc + 1) * 128, s * 512:(s + 1) * 512])
            yv = ytb[i2][:].rearrange("p (kc t) -> p kc t", kc=kch)
            for oc in range(och):
                o2 = (s * och + oc) % 2
                ctx.wait("pe", ld_tok[i2], e_tok[o2])
                for kc in range(kch):
                    mm = ctx.pe("matmul", ps[o2][:], wv[:, kc, oc * 128:(oc + 1) * 128],
                                yv[:, kc, :], start=(kc == 0), stop=(kc == kch - 1),
                                mark=(kc == kch - 1))
                pe_last[i2] = mm
                ctx.wait("act", mm, st_tok[o2])
                e_tok[o2] = ctx.op("act", "activation", out=ob[o2][:], in_=ps[o2][:],
                                   func=func, bias=colv[:, c_bias + oc:c_bias + oc + 1],
                                   scale=1.0)
                ctx.wait("pool", e_tok[o2])
                st_tok[o2] = ctx.dma(f"go{o2}", out=dst[oc * 128:(oc + 1) * 128,
                                                       s * 512:(s + 1) * 512],
                                     in_=ob[o2][:])



def phase_out_ffn(ctx: Ctx, C: Consts, src_h, wt, w_f32, colv, c_scale, c_bias,
                  x_src, extra_src, x_dst, tag):
    nc = ctx.nc
    KCH = FFN // 128
    with ExitStack() as st_:
        w3sb = st_.enter_context(nc.sbuf_tensor(f"of_w_{tag}", [128, KCH * 1024], BF16))
        rhs = [st_.enter_context(nc.sbuf_tensor(f"of_r{i}_{tag}", [128, KCH * 512], BF16))
               for i in range(2)]
        ufm = [st_.enter_context(nc.sbuf_tensor(f"of_u{i}_{tag}", [128, 8 * 512], F32)) for i in range(2)]
        xs = [st_.enter_context(nc.sbuf_tensor(f"of_x{i}_{tag}", [128, D], F32)) for i in range(2)]
        es = [st_.enter_context(nc.sbuf_tensor(f"of_e{i}_{tag}", [128, D], F32)) for i in range(2)] \
            if extra_src is not None else None
        xo = [st_.enter_context(nc.sbuf_tensor(f"of_xo{i}_{tag}", [128, D], F32)) for i in range(2)]
        psm = [st_.enter_context(nc.psum_tensor(f"of_pm{i}_{tag}", [128, 512], F32)) for i in range(2)]
        pst = [st_.enter_context(nc.psum_tensor(f"of_pt{i}_{tag}", [128, 1024], F32)) for i in range(2)]
        w3v = w3sb[:].rearrange("p (kc o) -> p kc o", kc=KCH)
        if w_f32:
            wl_toks = [ctx.dma(f"gw{kc % 2}", out=w3v[:, kc, :],
                               in_=wt[kc * 128:(kc + 1) * 128, :]) for kc in range(KCH)]
        else:
            wl_toks = [ctx.dma("spw", out=w3v[:, kc, :],
                               in_=wt[kc * 128:(kc + 1) * 128, :]) for kc in range(KCH)]
        ld_tok = [None, None]
        ev_tok = [None, None]
        tr_tok = [None, None]
        add_tok = [None, None]
        stx_tok = [None, None]
        pe_last_rhs = [None, None]
        nsup = NT // 512
        for s in range(nsup):
            i2 = s % 2
            ctx.wait("sp", pe_last_rhs[i2])
            for kc in range(KCH):
                ld_tok[i2] = ctx.dma(f"spy{i2}", out=rhs[i2][:, kc * 512:(kc + 1) * 512],
                                     in_=src_h[kc * 128:(kc + 1) * 128, s * 512:(s + 1) * 512])
            rv = rhs[i2][:].rearrange("p (kc t) -> p kc t", kc=KCH)
            ctx.wait("dve", tr_tok[i2])
            for oc in range(8):
                o2 = oc % 2
                ctx.wait("pe", ld_tok[i2], ev_tok[o2], *(wl_toks if s == 0 else []))
                for kc in range(KCH):
                    mm = ctx.pe("matmul", psm[o2][:], w3v[:, kc, oc * 128:(oc + 1) * 128],
                                rv[:, kc, :], start=(kc == 0), stop=(kc == KCH - 1),
                                mark=(kc == KCH - 1))
                pe_last_rhs[i2] = mm
                ctx.wait("dve", mm)
                if c_scale is not None:
                    ev_tok[o2] = ctx.op("dve", "tensor_scalar",
                                        out=ufm[i2][:, oc * 512:(oc + 1) * 512], in0=psm[o2][:],
                                        scalar1=colv[:, c_scale + oc:c_scale + oc + 1],
                                        scalar2=None, op0=ALU.mult)
                else:
                    ev_tok[o2] = ctx.op("dve", "tensor_scalar",
                                        out=ufm[i2][:, oc * 512:(oc + 1) * 512], in0=psm[o2][:],
                                        scalar1=colv[:, c_bias + oc:c_bias + oc + 1],
                                        scalar2=None, op0=ALU.add)
            uv = ufm[i2][:].rearrange("p (oc t) -> p oc t", oc=8)
            for j in range(4):
                it = s * 4 + j
                j2 = j % 2
                ctx.wait("sp", add_tok[j2])
                ldx = ctx.dma(f"spx{j2}", out=xs[j2][:], in_=x_src[it * 128:(it + 1) * 128, :])
                ldx2 = ctx.dma(f"spe{j2}", out=es[j2][:],
                               in_=extra_src[it * 128:(it + 1) * 128, :])                     if extra_src is not None else None
                ctx.wait("pe", ev_tok[0], ev_tok[1], add_tok[j2])
                for oc in range(8):
                    pt = ctx.pe("transpose", pst[j2][:, oc * 128:(oc + 1) * 128],
                                uv[:, oc, j * 128:(j + 1) * 128], C.ident_f[:],
                                mark=(oc == 7))
                tr_tok[i2] = pt
                ctx.wait("dve", pt, ldx, ldx2, stx_tok[j2])
                if extra_src is not None:
                    ctx.op("dve", "tensor_tensor", out=xo[j2][:], in0=pst[j2][:],
                           in1=es[j2][:], op=ALU.add)
                    add_tok[j2] = ctx.op("dve", "tensor_tensor", out=xo[j2][:], in0=xo[j2][:],
                                         in1=xs[j2][:], op=ALU.add)
                else:
                    add_tok[j2] = ctx.op("dve", "tensor_tensor", out=xo[j2][:], in0=pst[j2][:],
                                         in1=xs[j2][:], op=ALU.add)
                ctx.wait("pool", add_tok[j2])
                stx_tok[j2] = ctx.dma(f"gxo{j2}", out=x_dst[it * 128:(it + 1) * 128, :],
                                      in_=xo[j2][:])


def emit_prep_attn0(ctx: Ctx, C: Consts, ext, colv, cols: ColAlloc):
    prep_colvec(ctx, C, ext["L0_attn_q_b"], D, colv, cols.reg("q_b", 8))
    prep_colvec(ctx, C, ext["L0_attn_k_b"], D, colv, cols.reg("k_b", 8))
    prep_colvec(ctx, C, ext["L0_attn_ln1_w"], D, colv, cols.reg("ln1_w", 8))
    prep_colvec(ctx, C, ext["L0_attn_ln1_b"], D, colv, cols.reg("ln1_b", 8))
    prep_colvec(ctx, C, ext["L0_attn_ln2_w"], D, colv, cols.reg("ln2_w", 8))
    prep_colvec(ctx, C, ext["L0_attn_ln2_b"], D, colv, cols.reg("ln2_b", 8))
    prep_colvec(ctx, C, ext["L0_attn_f1_b"], FFN, colv, cols.reg("f1_b", 32))
    prep_colvec(ctx, C, ext["L0_attn_f2_b"], D, colv, cols.reg("f2_b", 8))
    ctx.barrier()


def emit_attn0_app(ctx: Ctx, C: Consts, ext, dr, colv, cols: ColAlloc, x_src, x_dst, tag,
                   upto=99):
    nc = ctx.nc
    step = [0]

    def go():
        step[0] += 1
        return step[0] <= upto
    with ExitStack() as appst:
        sctx = _StackCtx(ctx, appst)
        app = ScanApp(sctx, tag, den=True)
        vb_row = appst.enter_context(nc.sbuf_tensor(f"a_vbr_{tag}", [128, D], F32))
        ob_row = appst.enter_context(nc.sbuf_tensor(f"a_obr_{tag}", [128, D], F32))
        dec_row = appst.enter_context(nc.sbuf_tensor(f"a_dbr_{tag}", [128, 16], F32))
        prep_bcast_row(ctx, C, ext["L0_attn_v_b"], D, vb_row, C.ones_f)
        prep_bcast_row(ctx, C, ext["L0_attn_o_b"], D, ob_row, C.ones_f)
        prep_bcast_row(ctx, C, ext["L0_attn_dec_b"], 16, dec_row, C.ones_f)
        build_scan_consts(ctx, C, app, causal_mask=False)
        ctx.barrier()
        if go():
            phase_ln2(ctx, C, x_src, dr["YT"], tag + "a", y1_dst=dr["Y1"], double_ln=True,
                      colv=colv, w_cols=cols.map["ln1_w"], b_cols=cols.map["ln1_b"])
            ctx.barrier()
        if go():
            phase_proj_fm(ctx, C, dr["YT"], ext["L0_attn_q_w"], dr["QF"], tag + "q",
                          colv=colv, bias_cols=cols.map["q_b"], phi=True)
            ctx.barrier()
        if go():
            phase_proj_fm(ctx, C, dr["YT"], ext["L0_attn_k_w"], dr["KF"], tag + "k",
                          colv=colv, bias_cols=cols.map["k_b"], phi=True)
            ctx.barrier()
        if go():
            phase_proj_tok(ctx, C, dr["YT"], ext["L0_attn_v_w"], vb_row, dr["VT"],
                           ext["L0_attn_dec_w"], dec_row, app.lam, tag + "v")
            ctx.barrier()
        if go():
            phase_fm_to_tok(ctx, C, dr["KF"], dr["KTOK"], tag + "t")
            ctx.barrier()
        for b in range(B_PER_CORE):
            if go():
                emit_scan_prelude(ctx, C, app, b, dr["PROW"], 1e-8, tag + f"p{b}")
            if go():
                emit_scan_pass(ctx, C, app, b, dr["QF"], dr["KF"], dr["KTOK"], dr["VT"],
                               dr["PROW"], True, False, tag + f"f{b}")
                ctx.barrier()
            if go():
                emit_scan_pass(ctx, C, app, b, dr["QF"], None, dr["KTOK"], dr["VT"],
                               dr["PROW"], False, False, tag + f"b{b}")
                ctx.barrier()
            if go():
                emit_a3_l0(ctx, C, app, b, ext["L0_attn_o_w"], ob_row, dr["Y1"], dr["Z"],
                           dr["X2T"], colv, cols.map["ln2_w"], cols.map["ln2_b"], tag + f"o{b}")
                ctx.barrier()
    if step[0] + 1 <= upto:
        step[0] += 1
        phase_lin_fm(ctx, C, dr["X2T"], ext["L0_attn_f1_w"], True, colv, cols.map["f1_b"],
                     dr["HA"], AF.Gelu, tag + "m")
        ctx.barrier()
    if step[0] + 1 <= upto:
        step[0] += 1
        phase_out_ffn(ctx, C, dr["HA"], ext["L0_attn_f2_w"], True, colv, None,
                      cols.map["f2_b"], x_src, dr["Z"], x_dst, tag + "n")
        ctx.barrier()
    else:
        phase_copy_tok(ctx, C, x_src, x_dst, tag + "cp")
        ctx.barrier()


def phase_copy_tok(ctx: Ctx, C: Consts, src, dst, tag):
    nc = ctx.nc
    with ExitStack() as st_:
        xb = [st_.enter_context(nc.sbuf_tensor(f"cp{i}_{tag}", [128, D], F32)) for i in range(2)]
        cp = [None, None]
        st = [None, None]
        for tt in range(NTILE):
            i2 = tt % 2
            ctx.wait("sp", cp[i2])
            ld = ctx.dma(f"spx{i2}", out=xb[i2][:], in_=src[tt * 128:(tt + 1) * 128, :])
            ctx.wait("dve", ld, st[i2])
            cp[i2] = ctx.op("dve", "tensor_copy", xb[i2][:], xb[i2][:])
            ctx.wait("pool", cp[i2])
            st[i2] = ctx.dma(f"gxo{i2}", out=dst[tt * 128:(tt + 1) * 128, :], in_=xb[i2][:])



CAP = 512
NE = 8


class MoeApp:
    def __init__(self, ctx, tag):
        nc, st_ = ctx.nc, ctx.stack
        self.probs = st_.enter_context(nc.sbuf_tensor(f"mo_pr_{tag}", [128, NTILE * NE], F32))
        self.cpart = st_.enter_context(nc.sbuf_tensor(f"mo_cp_{tag}", [128, 128], F32))
        self.cfree = st_.enter_context(nc.sbuf_tensor(f"mo_cf_{tag}", [128, 128], F32))
        self.epart = st_.enter_context(nc.sbuf_tensor(f"mo_ep_{tag}", [128, 128], F32))
        self.efree = st_.enter_context(nc.sbuf_tensor(f"mo_ef_{tag}", [128, 128], F32))
        self.bmat = st_.enter_context(nc.sbuf_tensor(f"mo_bm_{tag}", [128, 128], F32))
        self.eoff = st_.enter_context(nc.sbuf_tensor(f"mo_eo_{tag}", [128, 128], F32))
        self.big = st_.enter_context(nc.sbuf_tensor(f"mo_bg_{tag}", [128, 128], F32))
        self.trils = st_.enter_context(nc.sbuf_tensor(f"mo_ts_{tag}", [128, 128], F32))
        self.bigi = st_.enter_context(nc.sbuf_tensor(f"mo_bi_{tag}", [128, 64], I32))
        self.psum_guard = None


def build_moe_consts(ctx: Ctx, C: Consts, mo: MoeApp, nch):
    nc = ctx.nc
    ctx.barrier()
    ctx.wait("pe", C.ready)
    with ExitStack() as st_:
        ps = st_.enter_context(nc.psum_tensor(_uq("moc_ps"), [128, 128], F32))
        ctx.op("dve", "memset", mo.cfree[:], 0.0)
        ctx.op("dve", "memset", mo.efree[:], 0.0)
        tz = ctx.tok("dve")
        ctx.wait("pool", tz)
        t1 = ctx.op("pool", "iota",
                    mo.cfree[:, 0:nch * NE].rearrange("p (c e) -> p c e", c=nch),
                    [[1, nch], [0, NE]], base=0, channel_multiplier=0,
                    allow_small_or_imprecise_dtypes=True)
        t2 = ctx.op("pool", "iota",
                    mo.efree[:, 0:nch * NE].rearrange("p (c e) -> p c e", c=nch),
                    [[0, nch], [1, NE]], base=0, channel_multiplier=0,
                    allow_small_or_imprecise_dtypes=True)
        ctx.wait("pe", t1, t2)
        pt = ctx.pe("transpose", ps[:], mo.cfree[:], C.ident_f[:], mark=True)
        ctx.wait("dve", pt)
        tc = ctx.op("dve", "tensor_copy", mo.cpart[:], ps[:])
        ctx.wait("pe", tc)
        pt2 = ctx.pe("transpose", ps[:], mo.efree[:], C.ident_f[:], mark=True)
        ctx.wait("dve", pt2)
        ctx.op("dve", "tensor_copy", mo.epart[:], ps[:])
        ctx.op("dve", "tensor_tensor", out=mo.bmat[:], in0=mo.cpart[:], in1=mo.cfree[:],
               op=ALU.is_lt)
        ctx.op("dve", "tensor_tensor", out=mo.big[:], in0=mo.epart[:], in1=mo.efree[:],
               op=ALU.is_equal)
        ctx.op("dve", "tensor_tensor", out=mo.bmat[:], in0=mo.bmat[:], in1=mo.big[:],
               op=ALU.mult)
        ctx.op("dve", "tensor_scalar", out=mo.eoff[:], in0=mo.efree[:], scalar1=float(CAP),
               scalar2=None, op0=ALU.mult)
        ctx.op("dve", "memset", mo.big[:], 100000.0)
        ctx.op("dve", "memset", mo.bigi[:], 100000)
        sc1 = st_.enter_context(nc.sbuf_tensor(_uq("moc_s1"), [128, 128], F32))
        t3 = ctx.op("pool", "iota", sc1[:], [[1, 128]], base=0, channel_multiplier=0,
                    allow_small_or_imprecise_dtypes=True)
        ctx.wait("dve", t3, C.t_iota_p)
        ctx.op("dve", "tensor_scalar", out=mo.trils[:], in0=sc1[:], scalar1=C.iota_p[:],
               scalar2=None, op0=ALU.is_gt)
        ctx.barrier()


def phase_moe_m0(ctx: Ctx, C: Consts, mo: MoeApp, x_src, gate_w, ytok_dst, pr_dram, tag):
    nc = ctx.nc
    with ExitStack() as st_:
        xb = [st_.enter_context(nc.sbuf_tensor(f"m0x{i}_{tag}", [128, D], F32)) for i in range(2)]
        yb = [st_.enter_context(nc.sbuf_tensor(f"m0y{i}_{tag}", [128, D], BF16)) for i in range(2)]
        y3t = [st_.enter_context(nc.sbuf_tensor(f"m0t{i}_{tag}", [128, 8 * 128], BF16)) for i in range(2)]
        gsb = st_.enter_context(nc.sbuf_tensor(f"m0g_{tag}", [128, 8 * NE], BF16))
        prb = [st_.enter_context(nc.sbuf_tensor(f"m0p{i}_{tag}", [128, NE], F32)) for i in range(2)]
        mx = [st_.enter_context(nc.sbuf_tensor(f"m0m{i}_{tag}", [128, 1], F32)) for i in range(2)]
        ps = [st_.enter_context(nc.psum_tensor(f"m0ps{i}_{tag}", [128, 1024], BF16)) for i in range(2)]
        psc = [st_.enter_context(nc.psum_tensor(f"m0pc{i}_{tag}", [128, NE], F32)) for i in range(2)]
        lb = LnBufs(_StackCtx(ctx, st_), tag + "ln")
        gv = gsb[:].rearrange("p (kc e) -> p kc e", kc=8)
        wts = [ctx.dma("gw0", out=gv[:, kc, :], in_=gate_w[kc * 128:(kc + 1) * 128, :])
               for kc in range(8)]
        ctx.wait("pe", *wts, C.ready)
        norm_tok = [None, None]
        tr_tok = [None, None]
        ev_tok = [None, None]
        sm_tok = [None, None]
        sty = [None, None]
        stp = [None, None]
        for tt in range(NTILE):
            i2 = tt % 2
            ctx.wait("sp", norm_tok[i2])
            ldx = ctx.dma(f"spx{i2}", out=xb[i2][:], in_=x_src[tt * 128:(tt + 1) * 128, :])
            ctx.wait("dve", tr_tok[i2], sty[i2])
            tnorm, _, _ = emit_ln(ctx, lb, i2, xb[i2][:], yb[i2][:], ldx)
            norm_tok[i2] = tnorm
            ctx.wait("pool", tnorm)
            sty[i2] = ctx.dma(f"gyk{i2}", out=ytok_dst[tt * 128:(tt + 1) * 128, :], in_=yb[i2][:])
            ctx.wait("pe", tnorm, ev_tok[i2])
            for kc in range(8):
                pt = ctx.pe("transpose", ps[i2][:, kc * 128:(kc + 1) * 128],
                            yb[i2][:, kc * 128:(kc + 1) * 128], C.ident_bf[:], mark=(kc == 7))
            tr_tok[i2] = pt
            ctx.wait("dve", pt)
            ev_tok[i2] = ctx.op("dve", "tensor_copy", y3t[i2][:], ps[i2][:])
            ctx.wait("pe", ev_tok[i2], sm_tok[i2])
            for kc in range(8):
                mm = ctx.pe("matmul", psc[i2][:], y3t[i2][:, kc * 128:(kc + 1) * 128],
                            gv[:, kc, :], start=(kc == 0), stop=(kc == 7), mark=(kc == 7))
            ctx.wait("dve", mm, stp[i2])
            ctx.op("dve", "tensor_reduce", out=mx[i2][:], in_=psc[i2][:], axis=AX.X,
                   op=ALU.max, negate=True)
            ctx.op("dve", "tensor_scalar", out=prb[i2][:], in0=psc[i2][:],
                   scalar1=mx[i2][:], scalar2=None, op0=ALU.add)
            tsm = ctx.tok("dve")
            ctx.wait("act", tsm)
            ctx.op("act", "activation", out=prb[i2][:], in_=prb[i2][:], func=AF.Exp,
                   accum_out=mx[i2][:])
            ta = ctx.tok("act")
            ctx.wait("dve", ta)
            ctx.op("dve", "reciprocal", out=mx[i2][:], in_=mx[i2][:])
            sm_tok[i2] = ctx.op("dve", "tensor_scalar", out=mo.probs[:, tt * NE:(tt + 1) * NE],
                                in0=prb[i2][:], scalar1=mx[i2][:], scalar2=None, op0=ALU.mult)
            ctx.wait("pool", sm_tok[i2])
            stp[i2] = ctx.dma(f"gpr{i2}", out=pr_dram[tt * 128:(tt + 1) * 128, :],
                              in_=mo.probs[:, tt * NE:(tt + 1) * NE])


def phase_moe_route(ctx: Ctx, C: Consts, mo: MoeApp, b, ytok, gx_dram, tok_dram, pv_dram,
                    appst, tag, n_iter=40):
    nc = ctx.nc
    nch = T // 128
    CW = nch * NE
    st_ = appst
    if True:
        lo = st_.enter_context(nc.sbuf_tensor(f"rt_lo_{tag}", [128, NE], F32))
        hi = st_.enter_context(nc.sbuf_tensor(f"rt_hi_{tag}", [128, NE], F32))
        thr = st_.enter_context(nc.sbuf_tensor(f"rt_th_{tag}", [128, NE], F32))
        sel = st_.enter_context(nc.sbuf_tensor(f"rt_se_{tag}", [128, NE], F32))
        cmp_ = st_.enter_context(nc.sbuf_tensor(f"rt_cm_{tag}", [128, CW], F32))
        part = st_.enter_context(nc.sbuf_tensor(f"rt_pa_{tag}", [128, NE], F32))
        cnt1 = st_.enter_context(nc.sbuf_tensor(f"rt_c1_{tag}", [1, NE], F32))
        cbc = st_.enter_context(nc.sbuf_tensor(f"rt_cb_{tag}", [128, NE], F32))
        mask = st_.enter_context(nc.sbuf_tensor(f"rt_mk_{tag}", [128, CW], F32))
        csr = st_.enter_context(nc.sbuf_tensor(f"rt_cs_{tag}", [1, CW], F32))
        csc = st_.enter_context(nc.sbuf_tensor(f"rt_cc_{tag}", [128, 1], F32))
        offr = st_.enter_context(nc.sbuf_tensor(f"rt_of_{tag}", [1, CW], F32))
        rank = st_.enter_context(nc.sbuf_tensor(f"rt_rk_{tag}", [128, CW], F32))
        dd = st_.enter_context(nc.sbuf_tensor(f"rt_dd_{tag}", [128, CW], F32))
        di = st_.enter_context(nc.sbuf_tensor(f"rt_di_{tag}", [128, CW], I32))
        tcol = st_.enter_context(nc.sbuf_tensor(f"rt_tc_{tag}", [128, 1], F32))
        tcoli = st_.enter_context(nc.sbuf_tensor(f"rt_ti_{tag}", [128, nch], I32))
        yt = [st_.enter_context(nc.sbuf_tensor(f"rt_yt{i}_{tag}", [128, D], BF16)) for i in range(2)]
        pstack = ExitStack()
        pp1 = pstack.enter_context(nc.psum_tensor(f"rt_p1_{tag}", [1, CW], F32))
        pp2 = pstack.enter_context(nc.psum_tensor(f"rt_p2_{tag}", [128, CW], F32))
        ppc = pstack.enter_context(nc.psum_tensor(f"rt_pc_{tag}", [128, 1], F32))
        prs = mo.probs[:, b * CW:(b + 1) * CW]
        prv = prs.rearrange("p (c e) -> p c e", c=nch)
        ctx.wait("pe", C.ready, mo.psum_guard)
        ctx.op("dve", "memset", lo[:], 0.0)
        ctx.op("dve", "memset", hi[:], 1.0)
        ctx.op("dve", "tensor_scalar", out=thr[:], in0=hi[:], scalar1=0.5, scalar2=None,
               op0=ALU.mult)
        for it in range(n_iter):
            ctx.op("dve", "tensor_tensor",
                   out=cmp_[:].rearrange("p (e c) -> p c e", e=NE),
                   in0=prv, in1=thr[:].unsqueeze(1).broadcast_to([128, nch, NE]), op=ALU.is_ge)
            ctx.op("dve", "tensor_reduce", out=part[:],
                   in_=cmp_[:].rearrange("p (e c) -> p e c", e=NE), axis=AX.X, op=ALU.add)
            tpart = ctx.tok("dve")
            ctx.wait("pe", tpart)
            m2 = ctx.pe("matmul", pp2[:, 0:NE], C.ones_f[:, :], part[:], start=True,
                        stop=True, mark=True)
            ctx.wait("dve", m2)
            ctx.op("dve", "tensor_scalar", out=sel[:], in0=pp2[:, 0:NE], scalar1=float(CAP),
                   scalar2=None, op0=ALU.is_ge)
            emit_select(ctx, lo[:], sel[:], thr[:], lo[:], part[:])
            emit_select(ctx, hi[:], sel[:], hi[:], thr[:], part[:])
            ctx.op("dve", "tensor_tensor", out=thr[:], in0=lo[:], in1=hi[:], op=ALU.add)
            ctx.op("dve", "tensor_scalar", out=thr[:], in0=thr[:], scalar1=0.5, scalar2=None,
                   op0=ALU.mult)
        ctx.op("dve", "tensor_tensor", out=mask[:].rearrange("p (c e) -> p c e", c=nch),
               in0=prv, in1=lo[:].unsqueeze(1).broadcast_to([128, nch, NE]), op=ALU.is_ge)
        tmk = ctx.tok("dve")
        ctx.wait("pe", tmk)
        m3 = ctx.pe("matmul", pp1[0:1, :], C.ones_f[:, 0:1], mask[:], start=True, stop=True,
                    mark=True)
        ctx.wait("dve", m3)
        tcs = ctx.op("dve", "tensor_copy", csr[:], pp1[0:1, :])
        ctx.wait("pe", tcs)
        m4 = ctx.pe("transpose", ppc[0:CW, 0:1], csr[0:1, 0:CW], C.ident_f[0:1, 0:1], mark=True)
        ctx.wait("dve", m4)
        tcc = ctx.op("dve", "tensor_copy", csc[0:CW, :], ppc[0:CW, 0:1])
        ctx.wait("pe", tcc)
        m5 = ctx.pe("matmul", pp1[0:1, :], csc[0:CW, 0:1], mo.bmat[0:CW, 0:CW], start=True,
                    stop=True, mark=True)
        ctx.wait("dve", m5)
        tof = ctx.op("dve", "tensor_copy", offr[:], pp1[0:1, :])
        ctx.wait("pe", tof)
        ctx.pe("matmul", pp2[:], mo.trils[:], mask[:], start=True, stop=False)
        m6 = ctx.pe("matmul", pp2[:], C.ones_f[0:1, :], offr[0:1, :], start=False, stop=True,
                    mark=True)
        ctx.wait("dve", m6)
        ctx.op("dve", "tensor_copy", rank[:], pp2[:])
        ctx.op("dve", "tensor_scalar", out=cmp_[:], in0=rank[:], scalar1=float(CAP),
               scalar2=None, op0=ALU.is_lt)
        ctx.op("dve", "tensor_tensor", out=cmp_[:], in0=cmp_[:], in1=mask[:], op=ALU.mult)
        ctx.op("dve", "tensor_tensor", out=rank[:],
               in0=rank[:], in1=mo.eoff[0:128, 0:CW], op=ALU.add)
        emit_select(ctx, dd[:], cmp_[:], rank[:], mo.big[0:128, 0:CW], rank[:])
        tdi = ctx.op("dve", "tensor_copy", di[:], dd[:])
        mo.psum_guard = tdi
        pstack.close()
        for c in range(nch):
            ctx.op("dve", "tensor_scalar", out=tcol[:], in0=C.iota_p[:],
                   scalar1=float(b * T + c * 128), scalar2=None, op0=ALU.add)
            ctx.op("dve", "tensor_copy", tcoli[:, c:c + 1], tcol[:])
        ttc = ctx.tok("dve")
        ctx.wait("pool", tdi, ttc)
        it_tok = ctx.dma(f"gtk{b}", out=tok_dram[:].rearrange("(a b) x -> a (b x)", a=128),
                         in_=mo.bigi[:, 0:(NE * CAP) // 128])
        div = di[:].rearrange("p (c e) -> p c e", c=nch)
        e_toks = []
        sc_par = [None, None]
        for e in range(NE):
            ld_tok = [None, None]
            for c in range(nch):
                i2 = c % 2
                ctx.wait("sp", sc_par[i2])
                ld_tok[i2] = ctx.dma(f"sprt{b}{i2}", out=yt[i2][:],
                                     in_=ytok[(b * nch + c) * 128:(b * nch + c + 1) * 128, :])
                ctx.wait("pool", ld_tok[i2], it_tok)
                off = bass.IndirectOffsetOnAxis(div[:, c, e:e + 1], 0)
                stream = f"gsc{b}{e % 2}{c % 2}"
                ctx.idma(stream, out=gx_dram, out_offset=off, in_=yt[i2][:],
                         in_offset=None, bounds_check=NE * CAP - 1, oob_is_err=False)
                ctx.idma(stream, out=tok_dram, out_offset=off,
                         in_=tcoli[:, c:c + 1], in_offset=None,
                         bounds_check=NE * CAP - 1, oob_is_err=False)
                sc_par[i2] = ctx.idma(stream, out=pv_dram, out_offset=off,
                                      in_=prs[:, c * NE + e:c * NE + e + 1], in_offset=None,
                                      bounds_check=NE * CAP - 1, oob_is_err=False)
            e_toks.append((sc_par[0], sc_par[1]))
        return e_toks


def phase_moe_experts(ctx: Ctx, C: Consts, mo: MoeApp, w1_dram, w2_dram, gx_dram, tok_dram,
                      pv_dram, mout_dram, tag):
    nc = ctx.nc
    with ExitStack() as st_:
        w1sb = st_.enter_context(nc.sbuf_tensor(f"ex_w1_{tag}", [128, 8 * FFN], BF16))
        w2sb = st_.enter_context(nc.sbuf_tensor(f"ex_w2_{tag}", [128, 32 * 1024], BF16))
        xt = [st_.enter_context(nc.sbuf_tensor(f"ex_x{i}_{tag}", [128, 1024], BF16)) for i in range(2)]
        xfm = st_.enter_context(nc.sbuf_tensor(f"ex_xf_{tag}", [128, 8 * CAP], BF16))
        ht = st_.enter_context(nc.sbuf_tensor(f"ex_h_{tag}", [128, 32 * CAP], BF16))
        ysb = [st_.enter_context(nc.sbuf_tensor(f"ex_y{i}_{tag}", [128, 1024], F32)) for i in range(2)]
        pcol = [st_.enter_context(nc.sbuf_tensor(f"ex_p{i}_{tag}", [128, 1], F32)) for i in range(2)]
        tcol = [st_.enter_context(nc.sbuf_tensor(f"ex_t{i}_{tag}", [128, 1], I32)) for i in range(2)]
        pst = [st_.enter_context(nc.psum_tensor(f"ex_pt{i}_{tag}", [128, 1024], BF16)) for i in range(2)]
        psm = [st_.enter_context(nc.psum_tensor(f"ex_pm{i}_{tag}", [128, 512], F32)) for i in range(2)]
        w1v = w1sb[:].rearrange("p (kc o) -> p kc o", kc=8)
        w2v = w2sb[:].rearrange("p (kc o) -> p kc o", kc=32)
        xfv = xfm[:].rearrange("p (kc t) -> p kc t", kc=8)
        htv = ht[:].rearrange("p (oc t) -> p oc t", oc=32)
        w_tok = None
        pe_last_w = None
        tr_tok = [None, None]
        ev_x = [None, None]
        g_tok = [None, None]
        y_ev = [None, None]
        sc_tok = None
        pe_h_last = None
        pe_y_last = None
        for e in range(NE):
            ctx.wait("pool", pe_last_w)
            wt = []
            for kc in range(8):
                wt.append(ctx.dma(f"gw{kc % 2}", out=w1v[:, kc, :],
                                  in_=w1_dram[e, kc * 128:(kc + 1) * 128, :]))
            for kc in range(32):
                wt.append(ctx.dma(f"gw{kc % 2}", out=w2v[:, kc, :],
                                  in_=w2_dram[e, kc * 128:(kc + 1) * 128, :]))
            RB = CAP // 128
            for rb in range(RB):
                i2 = rb % 2
                ctx.wait("sp", pe_h_last, tr_tok[i2])
                ld = ctx.dma(f"spy{i2}", out=xt[i2][:],
                             in_=gx_dram[e * CAP + rb * 128:e * CAP + (rb + 1) * 128, :])
                ctx.wait("pe", ld, ev_x[i2], C.ready)
                for kc in range(8):
                    pt = ctx.pe("transpose", pst[i2][:, kc * 128:(kc + 1) * 128],
                                xt[i2][:, kc * 128:(kc + 1) * 128], C.ident_bf[:],
                                mark=(kc == 7))
                tr_tok[i2] = pt
                ctx.wait("dve", pt)
                ev_x[i2] = ctx.op("dve", "tensor_copy",
                                  xfv[:, :, rb * 128:(rb + 1) * 128],
                                  pst[i2][:].rearrange("p (kc t) -> p kc t", kc=8))
            for oc in range(32):
                o2 = oc % 2
                ctx.wait("pe", ev_x[0], ev_x[1], g_tok[o2], *wt)
                for kc in range(8):
                    mm = ctx.pe("matmul", psm[o2][:, 0:CAP], w1v[:, kc, oc * 128:(oc + 1) * 128],
                                xfv[:, kc, :], start=(kc == 0), stop=(kc == 7), mark=(kc == 7))
                pe_h_last = mm
                pe_last_w = mm
                ctx.wait("act", mm, pe_y_last)
                g_tok[o2] = ctx.op("act", "activation", out=htv[:, oc, :], in_=psm[o2][:, 0:CAP],
                                   func=AF.Gelu)
            for rb in range(RB):
                for oh in range(2):
                    o2 = (rb * 2 + oh) % 2
                    ctx.wait("pe", g_tok[0], g_tok[1], y_ev[o2])
                    for kc in range(32):
                        mm = ctx.pe("matmul", psm[o2][:],
                                    htv[:, kc, rb * 128:(rb + 1) * 128],
                                    w2v[:, kc, oh * 512:(oh + 1) * 512],
                                    start=(kc == 0), stop=(kc == 31), mark=(kc == 31))
                    pe_y_last = mm
                    pe_last_w = mm
                    if oh == 0:
                        ctx.wait("sp", y_ev[0], y_ev[1], sc_tok)
                        ldp = ctx.dma(f"spp{rb % 2}", out=pcol[rb % 2][:],
                                      in_=pv_dram[e * CAP + rb * 128:e * CAP + (rb + 1) * 128, :])
                        ldt = ctx.dma(f"spt{rb % 2}", out=tcol[rb % 2][:],
                                      in_=tok_dram[e * CAP + rb * 128:e * CAP + (rb + 1) * 128, :])
                        ctx.wait("dve", ldp)
                        ctx.op("dve", "tensor_scalar", out=pcol[rb % 2][:], in0=pcol[rb % 2][:],
                               scalar1=0.5, scalar2=None, op0=ALU.mult)
                    ctx.wait("dve", mm, sc_tok)
                    y_ev[o2] = ctx.op("dve", "tensor_scalar",
                                      out=ysb[rb % 2][:, oh * 512:(oh + 1) * 512],
                                      in0=psm[o2][:], scalar1=pcol[rb % 2][:], scalar2=None,
                                      op0=ALU.mult)
                ctx.wait("pool", y_ev[0], y_ev[1], ldt)
                sc_tok = ctx.idma("gsc", out=mout_dram,
                                  out_offset=bass.IndirectOffsetOnAxis(tcol[rb % 2][:], 0),
                                  in_=ysb[rb % 2][:], in_offset=None,
                                  bounds_check=NT - 1, oob_is_err=False,
                                  compute_op=ALU.add)
        ctx.barrier()


def phase_zero_dram(ctx: Ctx, dst, rows, tag):
    nc = ctx.nc
    with ExitStack() as st_:
        z = st_.enter_context(nc.sbuf_tensor(f"z_{tag}", [128, D], F32))
        t = ctx.op("dve", "memset", z[:], 0.0)
        ctx.wait("pool", t)
        for r in range(rows // 128):
            ctx.dma("gz0", out=dst[r * 128:(r + 1) * 128, :], in_=z[:])
        ctx.barrier()


def phase_add_from(ctx: Ctx, C: Consts, x_src, add_src, x_dst, tag):
    nc = ctx.nc
    with ExitStack() as st_:
        a = [st_.enter_context(nc.sbuf_tensor(f"ad_a{i}_{tag}", [128, D], F32)) for i in range(2)]
        bb = [st_.enter_context(nc.sbuf_tensor(f"ad_b{i}_{tag}", [128, D], F32)) for i in range(2)]
        o = [st_.enter_context(nc.sbuf_tensor(f"ad_o{i}_{tag}", [128, D], F32)) for i in range(2)]
        add_tok = [None, None]
        st_tok = [None, None]
        for tt in range(NTILE):
            i2 = tt % 2
            ctx.wait("sp", add_tok[i2])
            l1 = ctx.dma(f"spx{i2}", out=a[i2][:], in_=x_src[tt * 128:(tt + 1) * 128, :])
            l2 = ctx.dma(f"spe{i2}", out=bb[i2][:], in_=add_src[tt * 128:(tt + 1) * 128, :])
            ctx.wait("dve", l1, l2, st_tok[i2])
            add_tok[i2] = ctx.op("dve", "tensor_tensor", out=o[i2][:], in0=a[i2][:],
                                 in1=bb[i2][:], op=ALU.add)
            ctx.wait("pool", add_tok[i2])
            st_tok[i2] = ctx.dma(f"gxo{i2}", out=x_dst[tt * 128:(tt + 1) * 128, :], in_=o[i2][:])


def emit_moe_app(ctx: Ctx, C: Consts, ext, dr, x_src, x_dst, tag, upto=99):
    nc = ctx.nc
    nch = T // 128
    step = [0]

    def go():
        step[0] += 1
        return step[0] <= upto
    with ExitStack() as appst:
        sctx = _StackCtx(ctx, appst)
        mo = MoeApp(sctx, tag)
        build_moe_consts(ctx, C, mo, nch)
        if go():
            phase_zero_dram(ctx, dr["MOUT"], NT, tag + "z")
        if go():
            phase_moe_m0(ctx, C, mo, x_src, ext["L1_ffn_gate_w"], dr["YTOK"], dr["PR"],
                         tag + "a")
            ctx.barrier()
        route_toks = []
        for b in range(B_PER_CORE):
            if go():
                route_toks.append(phase_moe_route(
                    ctx, C, mo, b, dr["YTOK"], dr[f"GX{b}"], dr[f"TOK{b}"],
                    dr[f"PV{b}"], sctx.stack, tag + f"r{b}"))
        if go():
            phase_moe_experts2(ctx, C, mo, dr["MW1C"], dr["MW2C"],
                               [dr["GX0"], dr["GX1"]], [dr["TOK0"], dr["TOK1"]],
                               [dr["PV0"], dr["PV1"]], dr["MOUT"], dr["_mw_cast_tok"],
                               tag + "e", route_toks=route_toks if len(route_toks) == 2 else None)
    phase_add_from(ctx, C, x_src, dr["MOUT"], x_dst, tag + "f")
    ctx.barrier()



def emit_a3_l1(ctx: Ctx, C: Consts, app: ScanApp, b, wo_src, x_src, x_dst, tag):
    nc = ctx.nc
    W = app.w
    with ExitStack() as st_:
        ow_sb = st_.enter_context(nc.sbuf_tensor(f"b3_ow_{tag}", [128, 8 * 1024], BF16))
        atile = [st_.enter_context(nc.sbuf_tensor(f"b3_at{i}_{tag}", [128, 1024], BF16)) for i in range(2)]
        afm = [st_.enter_context(nc.sbuf_tensor(f"b3_af{i}_{tag}", [128, 1024], BF16)) for i in range(2)]
        xs = [st_.enter_context(nc.sbuf_tensor(f"b3_x{i}_{tag}", [128, 1024], F32)) for i in range(2)]
        zt = [st_.enter_context(nc.sbuf_tensor(f"b3_z{i}_{tag}", [128, 1024], F32)) for i in range(2)]
        ps_t = [st_.enter_context(nc.psum_tensor(f"b3_pt{i}_{tag}", [128, 1024], BF16)) for i in range(2)]
        ps_o = [st_.enter_context(nc.psum_tensor(f"b3_po{i}_{tag}", [128, 512], F32)) for i in range(2)]
        wv = ow_sb[:].rearrange("p (kc o) -> p kc o", kc=8)
        _load_transposed_into(ctx, C, wo_src, wv, 1024, tag + "w")
        outv = app.out_sb[:].rearrange("p (c h w) -> p c h w", c=app.ch, h=H)
        tr_tok = [None, None]
        afm_ev = [None, None]
        o_ev = [None, None]
        z_tok = [None, None]
        st_tok = [None, None]
        pe_o_last = [None, None]
        for ti in range(app.ch):
            tt = b * app.ch + ti
            i2 = ti % 2
            ctx.wait("dve", tr_tok[i2])
            at_tok = ctx.op("dve", "tensor_copy", atile[i2][:],
                            app.out_sb[:, ti * H * W:(ti + 1) * H * W])
            ctx.wait("pe", at_tok, afm_ev[i2])
            for fc in range(8):
                ptok = ctx.pe("transpose", ps_t[i2][:, fc * 128:(fc + 1) * 128],
                              atile[i2][:, fc * 128:(fc + 1) * 128], C.ident_bf[:],
                              mark=(fc == 7))
            tr_tok[i2] = ptok
            ctx.wait("dve", ptok, pe_o_last[i2])
            afm_ev[i2] = ctx.op("dve", "tensor_copy", afm[i2][:], ps_t[i2][:])
            ctx.wait("sp", z_tok[i2])
            ldx = ctx.dma(f"spx{i2}", out=xs[i2][:], in_=x_src[tt * 128:(tt + 1) * 128, :])
            for oh in range(2):
                ctx.wait("pe", afm_ev[i2], o_ev[oh])
                for kc in range(8):
                    mm = ctx.pe("matmul", ps_o[oh][:], afm[i2][:, kc * 128:(kc + 1) * 128],
                                wv[:, kc, oh * 512:(oh + 1) * 512],
                                start=(kc == 0), stop=(kc == 7), mark=(kc == 7))
                pe_o_last[i2] = mm
                ctx.wait("dve", mm, st_tok[i2], ldx)
                o_ev[oh] = ctx.op("dve", "tensor_tensor", out=zt[i2][:, oh * 512:(oh + 1) * 512],
                                  in0=ps_o[oh][:], in1=xs[i2][:, oh * 512:(oh + 1) * 512],
                                  op=ALU.add)
            z_tok[i2] = o_ev[1]
            ctx.wait("pool", z_tok[i2])
            st_tok[i2] = ctx.dma(f"gxo{i2}", out=x_dst[tt * 128:(tt + 1) * 128, :], in_=zt[i2][:])


def emit_attn1_app(ctx: Ctx, C: Consts, ext, dr, x_src, x_dst, tag):
    nc = ctx.nc
    with ExitStack() as appst:
        sctx = _StackCtx(ctx, appst)
        app = ScanApp(sctx, tag, den=False)
        zrow = appst.enter_context(nc.sbuf_tensor(f"b_zr_{tag}", [128, D], F32))
        grow = appst.enter_context(nc.sbuf_tensor(f"b_gr_{tag}", [128, 16], F32))
        ctx.op("dve", "memset", zrow[:], 0.0)
        prep_bcast_row(ctx, C, ext["L1_attn_bg"], 16, grow, C.ones_f)
        build_scan_consts(ctx, C, app, causal_mask=True)
        ctx.barrier()
        phase_ln2(ctx, C, x_src, dr["YT"], tag + "a")
        ctx.barrier()
        phase_proj_fm(ctx, C, dr["YT"], ext["L1_attn_Wq"], dr["QF"], tag + "q",
                      phi=True, phi_scale=0.125, w_transposed_src=True)
        ctx.barrier()
        phase_proj_fm(ctx, C, dr["YT"], ext["L1_attn_Wk"], dr["KF"], tag + "k",
                      phi=True, w_transposed_src=True)
        ctx.barrier()
        phase_proj_tok(ctx, C, dr["YT"], ext["L1_attn_Wv"], zrow, dr["VT"],
                       ext["L1_attn_Wg"], grow, app.lam, tag + "v",
                       wv_transposed=True, wd_transposed=True)
        ctx.barrier()
        phase_fm_to_tok(ctx, C, dr["KF"], dr["KTOK"], tag + "t")
        ctx.barrier()
        for b in range(B_PER_CORE):
            emit_scan_prelude(ctx, C, app, b, dr["PROW"], 1e-30, tag + f"p{b}")
            emit_scan_pass(ctx, C, app, b, dr["QF"], dr["KF"], dr["KTOK"], dr["VT"],
                           dr["PROW"], True, True, tag + f"f{b}")
            ctx.barrier()
            emit_a3_l1(ctx, C, app, b, ext["L1_attn_Wo"], x_src, x_dst, tag + f"o{b}")
            ctx.barrier()



def build_core_program(marks=None):
    def mark(name):
        if marks is not None:
            marks.append((name, nc.n_instructions()))
    nc = bass.Bass("TRN2", target_bir_lowering=False)
    ext = {}
    ext["L0_attn_x"] = nc.dram_tensor("L0_attn_x", [NT, D], F32, kind="ExternalInput").ap()
    specs = {
        "L0_attn_q_w": [D, D], "L0_attn_q_b": [D], "L0_attn_k_w": [D, D], "L0_attn_k_b": [D],
        "L0_attn_v_w": [D, D], "L0_attn_v_b": [D], "L0_attn_o_w": [D, D], "L0_attn_o_b": [D],
        "L0_attn_ln1_w": [D], "L0_attn_ln1_b": [D], "L0_attn_ln2_w": [D], "L0_attn_ln2_b": [D],
        "L0_attn_dec_w": [D, 16], "L0_attn_dec_b": [16],
        "L0_attn_f1_w": [D, FFN], "L0_attn_f1_b": [FFN],
        "L0_attn_f2_w": [FFN, D], "L0_attn_f2_b": [D],
        "L0_ffn_w1_w": [FFN, D], "L0_ffn_w1_s": [FFN], "L0_ffn_w2_w": [FFN, D],
        "L0_ffn_w2_s": [FFN], "L0_ffn_h1_w": [FFN, FFN], "L0_ffn_h1_s": [FFN],
        "L0_ffn_h2_w": [FFN, FFN], "L0_ffn_h2_s": [FFN], "L0_ffn_w3_w": [D, FFN],
        "L0_ffn_w3_s": [D],
        "L1_attn_Wq": [D, D], "L1_attn_Wk": [D, D], "L1_attn_Wv": [D, D],
        "L1_attn_Wo": [D, D], "L1_attn_Wg": [16, D], "L1_attn_bg": [16],
        "L1_ffn_gate_w": [D, NE], "L1_ffn_w1": [NE, D, FFN], "L1_ffn_w2": [NE, FFN, D],
    }
    for nm, s in specs.items():
        ext[nm] = nc.dram_tensor(nm, s, F32, kind="ExternalInput").ap()
    xout = nc.dram_tensor("xout", [NT, D], F32, kind="ExternalOutput").ap()

    dr = {}
    dr["XA"] = nc.dram_tensor("XA", [NT, D], F32, kind="Internal").ap()
    dr["XB"] = nc.dram_tensor("XB", [NT, D], F32, kind="Internal").ap()
    dr["YT"] = nc.dram_tensor("YT", [D, NT], BF16, kind="Internal").ap()
    dr["Y1"] = nc.dram_tensor("Y1", [NT, D], F32, kind="Internal").ap()
    dr["Z"] = nc.dram_tensor("Z", [NT, D], F32, kind="Internal").ap()
    dr["X2T"] = nc.dram_tensor("X2T", [D, NT], BF16, kind="Internal").ap()
    dr["QF"] = nc.dram_tensor("QF", [D, NT], BF16, kind="Internal").ap()
    dr["KF"] = nc.dram_tensor("KF", [D, NT], BF16, kind="Internal").ap()
    dr["KTOK"] = nc.dram_tensor("KTOK", [NT, D], BF16, kind="Internal").ap()
    dr["VT"] = nc.dram_tensor("VT", [NT, D], BF16, kind="Internal").ap()
    dr["HA"] = nc.dram_tensor("HA", [FFN, NT], BF16, kind="Internal").ap()
    dr["HB"] = nc.dram_tensor("HB", [FFN, NT], BF16, kind="Internal").ap()
    dr["PROW"] = nc.dram_tensor("PROW", [(T // 128) * 16, 128], F32, kind="Internal").ap()
    dr["YTOK"] = nc.dram_tensor("YTOK", [NT, D], BF16, kind="Internal").ap()
    dr["PR"] = nc.dram_tensor("PR", [NT, NE], F32, kind="Internal").ap()
    for b in range(B_PER_CORE):
        dr[f"GX{b}"] = nc.dram_tensor(f"GX{b}", [NE * CAP, D], BF16, kind="Internal").ap()
        dr[f"TOK{b}"] = nc.dram_tensor(f"TOK{b}", [NE * CAP, 1], I32, kind="Internal").ap()
        dr[f"PV{b}"] = nc.dram_tensor(f"PV{b}", [NE * CAP, 1], F32, kind="Internal").ap()
    dr["MW1C"] = nc.dram_tensor("MW1C", [NE * D, FFN], BF16, kind="Internal").ap()
    dr["MW2C"] = nc.dram_tensor("MW2C", [NE * FFN, D], BF16, kind="Internal").ap()
    dr["MOUT"] = nc.dram_tensor("MOUT", [NT, D], F32, kind="Internal").ap()
    dr["W1T"] = nc.dram_tensor("W1T", [D, FFN], BF16, kind="Internal").ap()
    dr["W2T"] = nc.dram_tensor("W2T", [D, FFN], BF16, kind="Internal").ap()
    dr["H1WT"] = nc.dram_tensor("H1WT", [FFN, FFN], BF16, kind="Internal").ap()
    dr["H2WT"] = nc.dram_tensor("H2WT", [FFN, FFN], BF16, kind="Internal").ap()
    dr["W3T"] = nc.dram_tensor("W3T", [FFN, D], BF16, kind="Internal").ap()

    with ExitStack() as stack:
        ctx = Ctx(nc, stack)
        colv = stack.enter_context(nc.sbuf_tensor("colv", [128, 256], F32))
        C = Consts(ctx)
        cols = ColAlloc()
        dr["_mw_cast_tok"] = emit_moe_weight_cast(ctx, ext, dr)
        mark("emit_prep_ffn0:")
        emit_prep_ffn0(ctx, C, ext, dr, colv, cols)
        mark("emit_prep_attn0:")
        emit_prep_attn0(ctx, C, ext, colv, cols)
        mark("emit_ffn0_app:f1")
        emit_ffn0_app(ctx, C, dr, colv, cols, ext["L0_attn_x"], dr["XA"], "f1")
        mark("emit_attn0_app:a0")
        emit_attn0_app(ctx, C, ext, dr, colv, cols, dr["XA"], dr["XB"], "a0")
        mark("emit_ffn0_app:f2")
        emit_ffn0_app(ctx, C, dr, colv, cols, dr["XB"], dr["XA"], "f2")
        mark("emit_moe_app:m1")
        emit_moe_app(ctx, C, ext, dr, dr["XA"], dr["XB"], "m1")
        mark("emit_attn1_app:b1")
        emit_attn1_app(ctx, C, ext, dr, dr["XB"], dr["XA"], "b1")
        mark("emit_moe_app:m2")
        emit_moe_app(ctx, C, ext, dr, dr["XA"], xout, "m2")
        ctx.barrier()
        mark("end")
    return nc


_PROG_CACHE = {}


def kernel(**inputs):
    if "nc" not in _PROG_CACHE:
        _PROG_CACHE["nc"] = build_core_program()
    nc = _PROG_CACHE["nc"]
    x = np.ascontiguousarray(np.asarray(inputs["L0_attn_x"]).reshape(16, T, D))
    in_maps = []
    for c in range(N_CORES):
        m = {"L0_attn_x": np.ascontiguousarray(
            x[c * B_PER_CORE:(c + 1) * B_PER_CORE].reshape(NT, D))}
        for nm, v in inputs.items():
            if nm != "L0_attn_x":
                m[nm] = np.ascontiguousarray(np.asarray(v, dtype=np.float32))
        in_maps.append(m)
    res = run_bass_kernel_spmd(nc, in_maps, core_ids=list(range(N_CORES)))
    out = np.concatenate([np.asarray(r["xout"]).reshape(B_PER_CORE, T, D)
                          for r in res.results], axis=0)
    return out


def emit_moe_weight_cast(ctx: Ctx, ext, dr):
    toks = []
    for e in range(NE):
        toks.append(ctx.dma("gmw", out=dr["MW1C"][e * D:(e + 1) * D, :],
                            in_=ext["L1_ffn_w1"][e]))
        toks.append(ctx.dma("gmw", out=dr["MW2C"][e * FFN:(e + 1) * FFN, :],
                            in_=ext["L1_ffn_w2"][e]))
    return toks[-1]


def phase_moe_experts2(ctx: Ctx, C: Consts, mo: MoeApp, w1c, w2c, gx, tok, pv,
                       mout_dram, cast_tok, tag, route_toks=None):
    nc = ctx.nc
    RB = CAP // 128
    with ExitStack() as st_:
        w1sb = st_.enter_context(nc.sbuf_tensor(f"e2_w1_{tag}", [128, 8 * FFN], BF16))
        w2sb = st_.enter_context(nc.sbuf_tensor(f"e2_w2_{tag}", [128, 32 * 1024], BF16))
        xt = [st_.enter_context(nc.sbuf_tensor(f"e2_x{i}_{tag}", [128, 1024], BF16)) for i in range(2)]
        xfm = st_.enter_context(nc.sbuf_tensor(f"e2_xf_{tag}", [128, 8 * CAP], BF16))
        ht = st_.enter_context(nc.sbuf_tensor(f"e2_h_{tag}", [128, 32 * CAP], BF16))
        ysb = [st_.enter_context(nc.sbuf_tensor(f"e2_y{i}_{tag}", [128, 1024], F32)) for i in range(2)]
        pcol = [st_.enter_context(nc.sbuf_tensor(f"e2_p{i}_{tag}", [128, 1], F32)) for i in range(2)]
        tcol = [st_.enter_context(nc.sbuf_tensor(f"e2_t{i}_{tag}", [128, 1], I32)) for i in range(2)]
        pst = [st_.enter_context(nc.psum_tensor(f"e2_pt{i}_{tag}", [128, 1024], BF16)) for i in range(2)]
        psm = [st_.enter_context(nc.psum_tensor(f"e2_pm{i}_{tag}", [128, 512], F32)) for i in range(2)]
        w1v = w1sb[:].rearrange("p (kc o) -> p kc o", kc=8)
        w2v = w2sb[:].rearrange("p (kc o) -> p kc o", kc=32)
        xfv = xfm[:].rearrange("p (kc t) -> p kc t", kc=8)
        htv = ht[:].rearrange("p (oc t) -> p oc t", oc=32)

        w1_tok = [None] * (NE + 1)
        w2_tok = [None] * (NE + 1)
        ctx.wait("pe", mo.psum_guard)
        ctx.wait("sp", cast_tok)
        for kc in range(8):
            w1_tok[0] = ctx.dma("spw1", out=w1v[:, kc, :], in_=w1c[kc * 128:(kc + 1) * 128, :])
        for kc in range(32):
            w2_tok[0] = ctx.dma("spw2", out=w2v[:, kc, :], in_=w2c[kc * 128:(kc + 1) * 128, :])
        tr_tok = [None, None]
        ev_x = [None, None]
        g_tok = [None, None]
        y_ev = [None, None]
        sc_tok = None
        pe_h_last = None
        pe_y_last = None
        for e in range(NE):
            for b in range(B_PER_CORE):
                rt = route_toks[b][e] if route_toks is not None else (None, None)
                for rb in range(RB):
                    i2 = rb % 2
                    ctx.wait("sp", pe_h_last, tr_tok[i2], rt[0], rt[1])
                    ld = ctx.dma(f"spgx{i2}", out=xt[i2][:],
                                 in_=gx[b][e * CAP + rb * 128:e * CAP + (rb + 1) * 128, :])
                    ctx.wait("pe", ld, ev_x[i2], C.ready)
                    for kc in range(8):
                        pt = ctx.pe("transpose", pst[i2][:, kc * 128:(kc + 1) * 128],
                                    xt[i2][:, kc * 128:(kc + 1) * 128], C.ident_bf[:],
                                    mark=(kc == 7))
                    tr_tok[i2] = pt
                    ctx.wait("dve", pt)
                    ev_x[i2] = ctx.op("dve", "tensor_copy",
                                      xfv[:, :, rb * 128:(rb + 1) * 128],
                                      pst[i2][:].rearrange("p (kc t) -> p kc t", kc=8))
                for oc in range(32):
                    o2 = oc % 2
                    ctx.wait("pe", ev_x[0], ev_x[1], g_tok[o2], w1_tok[e])
                    for kc in range(8):
                        mm = ctx.pe("matmul", psm[o2][:, 0:CAP],
                                    w1v[:, kc, oc * 128:(oc + 1) * 128],
                                    xfv[:, kc, :], start=(kc == 0), stop=(kc == 7),
                                    mark=(kc == 7))
                    pe_h_last = mm
                    ctx.wait("act", mm, pe_y_last)
                    g_tok[o2] = ctx.op("act", "activation", out=htv[:, oc, :],
                                       in_=psm[o2][:, 0:CAP], func=AF.Gelu)
                if b == B_PER_CORE - 1 and e + 1 < NE:
                    ctx.wait("sp", pe_h_last)
                    for kc in range(8):
                        w1_tok[e + 1] = ctx.dma("spw1", out=w1v[:, kc, :],
                                                in_=w1c[(e + 1) * D + kc * 128:
                                                        (e + 1) * D + (kc + 1) * 128, :])
                for rb in range(RB):
                    for oh in range(2):
                        o2 = (rb * 2 + oh) % 2
                        ctx.wait("pe", g_tok[0], g_tok[1], y_ev[o2], w2_tok[e])
                        for kc in range(32):
                            mm = ctx.pe("matmul", psm[o2][:],
                                        htv[:, kc, rb * 128:(rb + 1) * 128],
                                        w2v[:, kc, oh * 512:(oh + 1) * 512],
                                        start=(kc == 0), stop=(kc == 31), mark=(kc == 31))
                        pe_y_last = mm
                        if oh == 0:
                            ctx.wait("sp", y_ev[0], y_ev[1], sc_tok, rt[0], rt[1])
                            ldp = ctx.dma(f"spp{rb % 2}", out=pcol[rb % 2][:],
                                          in_=pv[b][e * CAP + rb * 128:
                                                    e * CAP + (rb + 1) * 128, :])
                            ldt = ctx.dma(f"spt{rb % 2}", out=tcol[rb % 2][:],
                                          in_=tok[b][e * CAP + rb * 128:
                                                     e * CAP + (rb + 1) * 128, :])
                            ctx.wait("dve", ldp)
                            ctx.op("dve", "tensor_scalar", out=pcol[rb % 2][:],
                                   in0=pcol[rb % 2][:], scalar1=0.5, scalar2=None,
                                   op0=ALU.mult)
                        ctx.wait("dve", mm, sc_tok)
                        y_ev[o2] = ctx.op("dve", "tensor_scalar",
                                          out=ysb[rb % 2][:, oh * 512:(oh + 1) * 512],
                                          in0=psm[o2][:], scalar1=pcol[rb % 2][:],
                                          scalar2=None, op0=ALU.mult)
                    ctx.wait("pool", y_ev[0], y_ev[1], ldt)
                    sc_tok = ctx.idma("gsa", out=mout_dram,
                                      out_offset=bass.IndirectOffsetOnAxis(tcol[rb % 2][:], 0),
                                      in_=ysb[rb % 2][:], in_offset=None,
                                      bounds_check=NT - 1, oob_is_err=False,
                                      compute_op=ALU.add)
            if e + 1 < NE:
                ctx.wait("sp", pe_y_last)
                for kc in range(32):
                    w2_tok[e + 1] = ctx.dma("spw2", out=w2v[:, kc, :],
                                            in_=w2c[(e + 1) * FFN + kc * 128:
                                                    (e + 1) * FFN + (kc + 1) * 128, :])
        ctx.barrier()


# version 06 (final)
# speedup vs baseline: 1.0155x; 1.0155x over previous
import math
import numpy as np
from contextlib import ExitStack

import concourse.bass as bass
import concourse.mybir as mybir
from concourse.bass_utils import run_bass_kernel_spmd

F32 = mybir.dt.float32
BF16 = mybir.dt.bfloat16
I32 = mybir.dt.int32
AF = mybir.ActivationFunctionType
ALU = mybir.AluOpType
AX = mybir.AxisListType

N_CORES = 8
SCAN_PARTS = 255
SCAN_SAFE = True
B_PER_CORE = 2
T = 2048
D = 1024
NT = B_PER_CORE * T
NTILE = NT // 128
H = 16
DH = 64
FFN = 4096
LN_EPS = 1e-5


class Tok:
    __slots__ = ("sem", "val")

    def __init__(self, sem, val):
        self.sem = sem
        self.val = val


class Ctx:

    def __init__(self, nc, stack):
        self.nc = nc
        self.stack = stack
        self.e = {
            "pe": nc.tensor,
            "act": nc.scalar,
            "dve": nc.vector,
            "pool": nc.gpsimd,
            "sp": nc.sync,
        }
        self._semn = 0
        self.chain = {}
        self.ccnt = {}
        for en in ("act", "dve", "pool"):
            self.chain[en] = self._newsem(f"ch{en}")
            self.ccnt[en] = 0
        self.pe_sem = self._newsem("pedone")
        self.pe_cnt = 0
        self.pe_unmarked = 0
        self.dmas = {}
        self.known = {en: {} for en in self.e}
        self._bc_regs = {}

    def _newsem(self, name):
        s = self.stack.enter_context(self.nc.semaphore(f"{name}_{self._semn}"))
        self._semn += 1
        return s

    def op(self, en, meth, *a, **kw):
        eng = self.e[en]
        c = self.ccnt[en]
        if c:
            eng.wait_ge(self.chain[en], c)
            self.known[en][id(self.chain[en])] = c
        instr = getattr(eng, meth)(*a, **kw)
        instr.then_inc(self.chain[en], 1)
        self.ccnt[en] = c + 1
        return Tok(self.chain[en], c + 1)

    def tok(self, en):
        return Tok(self.chain[en], self.ccnt[en])

    def pe(self, meth, *a, mark=False, **kw):
        instr = getattr(self.e["pe"], meth)(*a, **kw)
        if mark:
            self.pe_cnt += 1
            instr.then_inc(self.pe_sem, 1)
            self.pe_unmarked = 0
            return Tok(self.pe_sem, self.pe_cnt)
        self.pe_unmarked += 1
        return None

    def _wait1(self, en, sem, val):
        kn = self.known[en]
        k = id(sem)
        if kn.get(k, -1) >= val:
            return
        self.e[en].wait_ge(sem, val)
        kn[k] = val

    def wait(self, en, *toks):
        best = {}
        for t in toks:
            if t is None:
                continue
            k = id(t.sem)
            if k not in best or best[k][1] < t.val:
                best[k] = (t.sem, t.val)
        for sem, val in best.values():
            self._wait1(en, sem, val)

    def dma(self, stream, out, in_, **kw):
        en = "sp" if stream.startswith("sp") else "pool"
        if stream not in self.dmas:
            self.dmas[stream] = [self._newsem("dma_" + stream), 0, en]
        rec = self.dmas[stream]
        assert rec[2] == en
        instr = self.e[en].dma_start(out=out, in_=in_, **kw)
        rec[1] += 16
        instr.then_inc(rec[0], 16)
        return Tok(rec[0], rec[1])

    def idma(self, stream, out, out_offset, in_, in_offset, **kw):
        assert stream.startswith("g")
        if stream not in self.dmas:
            self.dmas[stream] = [self._newsem("dma_" + stream), 0, "pool"]
        rec = self.dmas[stream]
        bc = kw.get("bounds_check")
        if isinstance(bc, int):
            if bc not in self._bc_regs:
                r = self.e["pool"].alloc_register()
                self.e["pool"].reg_mov(r, bc)
                self._bc_regs[bc] = r
            kw["bounds_check"] = self._bc_regs[bc]
        instr = self.e["pool"].indirect_dma_start(
            out=out, out_offset=out_offset, in_=in_, in_offset=in_offset, **kw)
        rec[1] += 16
        instr.then_inc(rec[0], 16)
        return Tok(rec[0], rec[1])

    def barrier(self, engines=("pe", "act", "dve", "pool", "sp")):
        assert self.pe_unmarked == 0, "unmarked PE work at barrier"
        toks = []
        for en in ("act", "dve", "pool"):
            if self.ccnt[en]:
                toks.append(Tok(self.chain[en], self.ccnt[en]))
        if self.pe_cnt:
            toks.append(Tok(self.pe_sem, self.pe_cnt))
        for st, (sem, cnt, _en) in self.dmas.items():
            if cnt:
                toks.append(Tok(sem, cnt))
        for en in engines:
            self.wait(en, *toks)


def ceil_div(a, b):
    return (a + b - 1) // b


def emit_select(ctx, out, mask, a, b, scratch):
    ctx.op("dve", "tensor_tensor", out=scratch, in0=a, in1=b, op=ALU.subtract)
    ctx.op("dve", "tensor_tensor", out=scratch, in0=scratch, in1=mask, op=ALU.mult)
    return ctx.op("dve", "tensor_tensor", out=out, in0=b, in1=scratch, op=ALU.add)



class Consts:

    def __init__(self, ctx: Ctx):
        nc, stack = ctx.nc, ctx.stack
        self.ident_bf = stack.enter_context(nc.sbuf_tensor("c_ident_bf", [128, 128], BF16))
        self.ident_f = stack.enter_context(nc.sbuf_tensor("c_ident_f", [128, 128], F32))
        self.tril_inc = stack.enter_context(nc.sbuf_tensor("c_tril", [128, 128], BF16))
        self.triu_strict = stack.enter_context(nc.sbuf_tensor("c_triu", [128, 128], BF16))
        self.tril_f = stack.enter_context(nc.sbuf_tensor("c_trilf", [128, 128], F32))
        self.triu_f = stack.enter_context(nc.sbuf_tensor("c_triuf", [128, 128], F32))
        self.ones_col = stack.enter_context(nc.sbuf_tensor("c_ones", [128, 128], BF16))
        self.ones_f = stack.enter_context(nc.sbuf_tensor("c_onesf", [128, 128], F32))
        self.iota_p = stack.enter_context(nc.sbuf_tensor("c_iotap", [128, 1], F32))
        sc1 = stack.enter_context(nc.sbuf_tensor("c_sc1", [128, 128], F32))
        sc2 = stack.enter_context(nc.sbuf_tensor("c_sc2", [128, 128], F32))

        t_ip = ctx.op("pool", "iota", sc1[:], [[0, 128]], base=0, channel_multiplier=1,
                      allow_small_or_imprecise_dtypes=True)
        t_if = ctx.op("pool", "iota", sc2[:], [[1, 128]], base=0, channel_multiplier=0,
                      allow_small_or_imprecise_dtypes=True)
        ctx.wait("dve", t_ip, t_if)
        ctx.op("dve", "tensor_tensor", out=self.ident_f[:], in0=sc1[:], in1=sc2[:], op=ALU.is_equal)
        ctx.op("dve", "tensor_tensor", out=self.ident_bf[:], in0=sc1[:], in1=sc2[:], op=ALU.is_equal)
        ctx.op("dve", "tensor_tensor", out=self.tril_inc[:], in0=sc1[:], in1=sc2[:], op=ALU.is_le)
        ctx.op("dve", "tensor_tensor", out=self.tril_f[:], in0=sc1[:], in1=sc2[:], op=ALU.is_le)
        ctx.op("dve", "tensor_tensor", out=self.triu_strict[:], in0=sc1[:], in1=sc2[:], op=ALU.is_gt)
        ctx.op("dve", "tensor_tensor", out=self.triu_f[:], in0=sc1[:], in1=sc2[:], op=ALU.is_gt)
        ctx.op("dve", "memset", self.ones_col[:], 1.0)
        ctx.op("dve", "memset", self.ones_f[:], 1.0)
        self.t_iota_p = ctx.op("dve", "tensor_copy", self.iota_p[:], sc1[:, 0:1])
        self.ready = ctx.tok("dve")



def prep_transposed_weight(ctx: Ctx, C: Consts, src_f32, dst_bf16, rows, cols, tag):
    nc = ctx.nc
    ctx.wait("pe", C.ready)
    rb, cb = rows // 128, cols // 128
    with ExitStack() as st_:
        nbuf = 3
        src_sb = [st_.enter_context(nc.sbuf_tensor(f"wp_src{i}_{tag}", [128, 8 * 128], BF16))
                  for i in range(nbuf)]
        out_sb = [st_.enter_context(nc.sbuf_tensor(f"wp_out{i}_{tag}", [128, 8 * 128], BF16))
                  for i in range(nbuf)]
        psum = [st_.enter_context(nc.psum_tensor(f"wp_ps{i}_{tag}", [128, 1024], BF16))
                for i in range(2)]
        free_tok = [None] * nbuf
        evac_tok = [None, None]
        tr_tok = [None] * nbuf
        n = 0
        for c in range(cb):
            for rg in range(ceil_div(rb, 8)):
                nr = min(8, rb - rg * 8)
                i = n % nbuf
                pi = n % 2
                ctx.wait("pool", tr_tok[i])
                ld = ctx.dma(f"gwp{i}",
                             out=src_sb[i][:, 0:nr * 128].rearrange("p (r c) -> p r c", r=nr),
                             in_=src_f32[rg * 1024:rg * 1024 + nr * 128,
                                         c * 128:(c + 1) * 128]
                             .rearrange("(r p) c -> p r c", p=128))
                ctx.wait("pe", ld, evac_tok[pi])
                for r in range(nr):
                    pt = ctx.pe("transpose", psum[pi][:, r * 128:(r + 1) * 128],
                                src_sb[i][:, r * 128:(r + 1) * 128], C.ident_bf[:],
                                mark=(r == nr - 1))
                tr_tok[i] = pt
                ctx.wait("dve", pt, free_tok[i])
                evac_tok[pi] = ctx.op("dve", "tensor_copy", out_sb[i][:, 0:nr * 128],
                                      psum[pi][:, 0:nr * 128])
                ctx.wait("pool", evac_tok[pi])
                free_tok[i] = ctx.dma(f"gwps{i}",
                                      out=dst_bf16[c * 128:(c + 1) * 128,
                                                   rg * 1024:rg * 1024 + nr * 128],
                                      in_=out_sb[i][:, 0:nr * 128])
                n += 1
        ctx.barrier()


def prep_colvec(ctx: Ctx, C: Consts, src_f32, n, dst_sb, col0, scale=None):
    nc = ctx.nc
    nb = ceil_div(n, 128)
    ctx.wait("pe", C.ready)
    with ExitStack() as st_:
        row = st_.enter_context(nc.sbuf_tensor(_uq("pv_row"), [nb, 128], F32))
        ps = st_.enter_context(nc.psum_tensor(_uq("pv_ps"), [128, nb], F32))
        t = ctx.dma("spv", out=row[0:nb, :], in_=src_f32.rearrange("(a b) -> a b", b=128))
        ctx.wait("pe", t)
        pt = ctx.pe("transpose", ps[:, 0:nb], row[0:nb, :], C.ident_f[0:nb, 0:nb], mark=True)
        ctx.wait("dve", pt)
        if scale is None:
            ctx.op("dve", "tensor_copy", dst_sb[:, col0:col0 + nb], ps[:, 0:nb])
        else:
            ctx.op("dve", "tensor_scalar", out=dst_sb[:, col0:col0 + nb], in0=ps[:, 0:nb],
                   scalar1=float(scale), scalar2=None, op0=ALU.mult)
        ctx.barrier()


def prep_bcast_row(ctx: Ctx, C: Consts, src_f32, n, dst_sb, ones_f):
    nc = ctx.nc
    ctx.wait("pe", C.ready)
    with ExitStack() as st_:
        row = st_.enter_context(nc.sbuf_tensor(_uq("pb_row"), [1, n], F32))
        t = ctx.dma("spv", out=row[0:1, 0:n], in_=src_f32.unsqueeze(0))
        ctx.wait("pe", t)
        for j0 in range(0, n, 512):
            w = min(512, n - j0)
            with nc.psum_tensor(_uq("pb_ps"), [128, 512], F32) as ps:
                pt = ctx.pe("matmul", ps[:, 0:w], ones_f[0:1, :], row[0:1, j0:j0 + w],
                            start=True, stop=True, mark=True)
                ctx.wait("dve", pt)
                ctx.op("dve", "tensor_copy", dst_sb[:, j0:j0 + w], ps[:, 0:w])
                ctx.barrier()
        ctx.barrier()



class LnBufs:
    def __init__(self, ctx, tag, nring=2, d=D):
        nc, st_ = ctx.nc, ctx.stack
        self.d = d
        self.m_neg = [st_.enter_context(nc.sbuf_tensor(f"ln_mn{i}_{tag}", [128, 1], F32)) for i in range(nring)]
        self.veps = [st_.enter_context(nc.sbuf_tensor(f"ln_ve{i}_{tag}", [128, 1], F32)) for i in range(nring)]
        self.r = [st_.enter_context(nc.sbuf_tensor(f"ln_r{i}_{tag}", [128, 1], F32)) for i in range(nring)]
        self.sq = st_.enter_context(nc.sbuf_tensor(f"ln_sq_{tag}", [128, d], BF16))
        self.ssum = [st_.enter_context(nc.sbuf_tensor(f"ln_ss{i}_{tag}", [128, 1], F32)) for i in range(nring)]


def emit_ln(ctx: Ctx, lb: LnBufs, i, x_ap, out_ap, x_tok):
    d = lb.d
    ctx.wait("dve", x_tok)
    ctx.op("dve", "tensor_reduce", out=lb.m_neg[i][:], in_=x_ap, axis=AX.X, op=ALU.add, negate=True)
    ctx.op("dve", "tensor_scalar", out=lb.m_neg[i][:], in0=lb.m_neg[i][:],
           scalar1=1.0 / d, scalar2=None, op0=ALU.mult)
    mt = ctx.tok("dve")
    ctx.wait("act", x_tok, mt)
    ctx.op("act", "activation", out=lb.sq[:, 0:d], in_=x_ap, func=AF.Square,
           bias=lb.m_neg[i][:], scale=1.0, accum_out=lb.ssum[i][:])
    st = ctx.tok("act")
    ctx.wait("dve", st)
    ctx.op("dve", "tensor_scalar", out=lb.veps[i][:], in0=lb.ssum[i][:],
           scalar1=1.0 / d, scalar2=LN_EPS, op0=ALU.mult, op1=ALU.add)
    vt = ctx.tok("dve")
    ctx.wait("act", vt)
    ctx.op("act", "activation", out=lb.r[i][:], in_=lb.veps[i][:], func=AF.Ln)
    ctx.op("act", "activation", out=lb.r[i][:], in_=lb.r[i][:], func=AF.Exp, scale=-0.5)
    sqt = ctx.tok("act")
    ctx.wait("dve", sqt)
    tout = ctx.op("dve", "tensor_scalar", out=out_ap, in0=x_ap,
                  scalar1=lb.m_neg[i][:], scalar2=lb.r[i][:], op0=ALU.add, op1=ALU.mult)
    return tout, lb.m_neg[i], lb.r[i]



def phase_ln_to_fm(ctx: Ctx, C: Consts, x_src, yt_dst, tag):
    nc = ctx.nc
    with ExitStack() as st_:
        xb = [st_.enter_context(nc.sbuf_tensor(f"s0x{i}_{tag}", [128, D], F32)) for i in range(2)]
        yb = [st_.enter_context(nc.sbuf_tensor(f"s0y{i}_{tag}", [128, D], BF16)) for i in range(2)]
        yts = [st_.enter_context(nc.sbuf_tensor(f"s0t{i}_{tag}", [128, 8 * 512], BF16)) for i in range(2)]
        ps = [st_.enter_context(nc.psum_tensor(f"s0p{i}_{tag}", [128, 1024], BF16)) for i in range(2)]
        lb = LnBufs(_StackCtx(ctx, st_), tag + "ln")
        norm_tok = [None, None]
        evac_tok = [None, None]
        tr_tok = [None, None]
        st_tok = [None, None]
        ctx.wait("pe", C.ready)
        nsup = NT // 512
        for s in range(nsup):
            for j in range(4):
                it = s * 4 + j
                i2 = it % 2
                ctx.wait("sp", norm_tok[i2])
                ldx = ctx.dma(f"spx{i2}", out=xb[i2][:], in_=x_src[it * 128:(it + 1) * 128, :])
                ctx.wait("dve", tr_tok[i2])
                tnorm, _, _ = emit_ln(ctx, lb, i2, xb[i2][:], yb[i2][:], ldx)
                norm_tok[i2] = tnorm
                ctx.wait("pe", tnorm, evac_tok[i2])
                for kc in range(8):
                    pt = ctx.pe("transpose", ps[i2][:, kc * 128:(kc + 1) * 128],
                                yb[i2][:, kc * 128:(kc + 1) * 128], C.ident_bf[:],
                                mark=(kc == 7))
                tr_tok[i2] = pt
                ctx.wait("dve", pt, st_tok[s % 2])
                evac_tok[i2] = ctx.op(
                    "dve", "tensor_copy",
                    yts[s % 2][:].rearrange("p (kc t) -> p kc t", kc=8)[:, :, j * 128:(j + 1) * 128],
                    ps[i2][:].rearrange("p (kc t) -> p kc t", kc=8))
            for kc in range(8):
                ctx.wait("pool", evac_tok[0], evac_tok[1])
                st_tok[s % 2] = ctx.dma(
                    f"gyt{s % 2}",
                    out=yt_dst[kc * 128:(kc + 1) * 128, s * 512:(s + 1) * 512],
                    in_=yts[s % 2][:, kc * 512:(kc + 1) * 512])


class _StackCtx:

    def __init__(self, ctx, st_):
        self.nc = ctx.nc
        self.stack = st_



def phase_ffn0_s1(ctx: Ctx, C: Consts, yt_src, w1t, w2t, colv, c_w1s, c_w2s, h0_dst, tag):
    nc = ctx.nc
    with ExitStack() as st_:
        w1sb = st_.enter_context(nc.sbuf_tensor(f"s1w1_{tag}", [128, 8 * FFN], BF16))
        w2sb = st_.enter_context(nc.sbuf_tensor(f"s1w2_{tag}", [128, 8 * FFN], BF16))
        ytb = [st_.enter_context(nc.sbuf_tensor(f"s1y{i}_{tag}", [128, 8 * 512], BF16)) for i in range(2)]
        t1 = [st_.enter_context(nc.sbuf_tensor(f"s1t1{i}_{tag}", [128, 512], F32)) for i in range(2)]
        t2 = [st_.enter_context(nc.sbuf_tensor(f"s1t2{i}_{tag}", [128, 512], F32)) for i in range(2)]
        h0b = [st_.enter_context(nc.sbuf_tensor(f"s1h{i}_{tag}", [128, 512], BF16)) for i in range(2)]
        pa = [st_.enter_context(nc.psum_tensor(f"s1pa{i}_{tag}", [128, 512], F32)) for i in range(2)]
        pb = [st_.enter_context(nc.psum_tensor(f"s1pb{i}_{tag}", [128, 512], F32)) for i in range(2)]
        wtoks = []
        for kc in range(8):
            wtoks.append(ctx.dma("spw", out=w1sb[:, kc * FFN:(kc + 1) * FFN],
                                 in_=w1t[kc * 128:(kc + 1) * 128, :]))
            wtoks.append(ctx.dma("spw", out=w2sb[:, kc * FFN:(kc + 1) * FFN],
                                 in_=w2t[kc * 128:(kc + 1) * 128, :]))
        ctx.wait("pe", *wtoks)
        w1v = w1sb[:].rearrange("p (kc o) -> p kc o", kc=8)
        w2v = w2sb[:].rearrange("p (kc o) -> p kc o", kc=8)
        ld_tok = [None, None]
        a_tok = [None, None]
        b_tok = [None, None]
        m_tok = [None, None]
        st_tok = [None, None]
        pe_last = [None, None]
        nsup = NT // 512
        for s in range(nsup):
            i2 = s % 2
            ctx.wait("sp", pe_last[i2])
            for kc in range(8):
                ld_tok[i2] = ctx.dma(f"spy{i2}", out=ytb[i2][:, kc * 512:(kc + 1) * 512],
                                     in_=yt_src[kc * 128:(kc + 1) * 128, s * 512:(s + 1) * 512])
            yv = ytb[i2][:].rearrange("p (kc t) -> p kc t", kc=8)
            for oc in range(32):
                o2 = oc % 2
                ctx.wait("pe", ld_tok[i2], a_tok[o2], b_tok[o2])
                for kc in range(8):
                    pat = ctx.pe("matmul", pa[o2][:], w1v[:, kc, oc * 128:(oc + 1) * 128],
                                 yv[:, kc, :], start=(kc == 0), stop=(kc == 7),
                                 mark=(kc == 7))
                for kc in range(8):
                    mm = ctx.pe("matmul", pb[o2][:], w2v[:, kc, oc * 128:(oc + 1) * 128],
                                yv[:, kc, :], start=(kc == 0), stop=(kc == 7),
                                mark=(kc == 7))
                pbt = mm
                ctx.wait("act", pat, m_tok[o2])
                a_tok[o2] = ctx.op("act", "activation", out=t1[o2][:], in_=pa[o2][:],
                                   func=AF.Silu, scale=colv[:, c_w1s + oc:c_w1s + oc + 1])
                ctx.wait("dve", pbt)
                b_tok[o2] = ctx.op("dve", "tensor_scalar", out=t2[o2][:], in0=pb[o2][:],
                                   scalar1=colv[:, c_w2s + oc:c_w2s + oc + 1], scalar2=None,
                                   op0=ALU.mult)
                ctx.wait("dve", a_tok[o2], st_tok[o2])
                m_tok[o2] = ctx.op("dve", "tensor_tensor", out=h0b[o2][:], in0=t1[o2][:],
                                   in1=t2[o2][:], op=ALU.mult)
                ctx.wait("pool", m_tok[o2])
                st_tok[o2] = ctx.dma(f"gh{o2}", out=h0_dst[oc * 128:(oc + 1) * 128,
                                                           s * 512:(s + 1) * 512],
                                     in_=h0b[o2][:])
                pe_last[i2] = pbt



def phase_ffn0_mid(ctx: Ctx, C: Consts, src, wt, colv, c_s, dst, tag):
    nc = ctx.nc
    KCH = FFN // 128
    groups = [list(range(0, 11)), list(range(11, 22)), list(range(22, 32))]
    with ExitStack() as st_:
        wsl = [st_.enter_context(nc.sbuf_tensor(f"md_w{i}_{tag}", [128, KCH * 128], BF16))
               for i in range(11)]
        rhs = [st_.enter_context(nc.sbuf_tensor(f"md_r{i}_{tag}", [128, KCH * 512], BF16))
               for i in range(2)]
        ob = [st_.enter_context(nc.sbuf_tensor(f"md_o{i}_{tag}", [128, 512], BF16)) for i in range(2)]
        ps = [st_.enter_context(nc.psum_tensor(f"md_p{i}_{tag}", [128, 512], F32)) for i in range(2)]
        g_tok = [None, None]
        st_tok = [None, None]
        ld_tok = [None, None]
        pe_last_rhs = [None, None]
        pe_last_group = None
        for gi, grp in enumerate(groups):
            wl_toks = []
            for sl, oc in enumerate(grp):
                ctx.wait("sp", pe_last_group)
                wl_toks.append(ctx.dma(
                    "spw",
                    out=wsl[sl][:].rearrange("p (kc o) -> p kc o", kc=KCH),
                    in_=wt[:, oc * 128:(oc + 1) * 128].rearrange("(kc p) o -> p kc o", p=128)))
            nsup = NT // 512
            for s in range(nsup):
                i2 = s % 2
                ctx.wait("sp", pe_last_rhs[i2])
                for kc in range(KCH):
                    ld_tok[i2] = ctx.dma(f"spy{i2}",
                                         out=rhs[i2][:, kc * 512:(kc + 1) * 512],
                                         in_=src[kc * 128:(kc + 1) * 128, s * 512:(s + 1) * 512])
                rv = rhs[i2][:].rearrange("p (kc t) -> p kc t", kc=KCH)
                for sl, oc in enumerate(grp):
                    o2 = (s * len(grp) + sl) % 2
                    ctx.wait("pe", ld_tok[i2], g_tok[o2], *(wl_toks if s == 0 else []))
                    for kc in range(KCH):
                        wv = wsl[sl][:].rearrange("p (kc o) -> p kc o", kc=KCH)
                        mm = ctx.pe("matmul", ps[o2][:], wv[:, kc, :], rv[:, kc, :],
                                    start=(kc == 0), stop=(kc == KCH - 1),
                                    mark=(kc == KCH - 1))
                    pe_last_rhs[i2] = mm
                    pe_last_group = mm
                    ctx.wait("act", mm, st_tok[o2])
                    g_tok[o2] = ctx.op("act", "activation", out=ob[o2][:], in_=ps[o2][:],
                                       func=AF.Gelu, scale=colv[:, c_s + oc:c_s + oc + 1])
                    ctx.wait("pool", g_tok[o2])
                    st_tok[o2] = ctx.dma(f"gh{o2}",
                                         out=dst[oc * 128:(oc + 1) * 128, s * 512:(s + 1) * 512],
                                         in_=ob[o2][:])



def phase_ffn0_s4(ctx: Ctx, C: Consts, src_h, w3t, colv, c_s3, x_src, x_dst, tag):
    nc = ctx.nc
    KCH = FFN // 128
    with ExitStack() as st_:
        w3sb = st_.enter_context(nc.sbuf_tensor(f"s4w_{tag}", [128, KCH * 1024], BF16))
        rhs = [st_.enter_context(nc.sbuf_tensor(f"s4r{i}_{tag}", [128, KCH * 512], BF16))
               for i in range(2)]
        ufm = [st_.enter_context(nc.sbuf_tensor(f"s4u{i}_{tag}", [128, 8 * 512], F32)) for i in range(2)]
        xs = [st_.enter_context(nc.sbuf_tensor(f"s4x{i}_{tag}", [128, D], F32)) for i in range(2)]
        xo = [st_.enter_context(nc.sbuf_tensor(f"s4xo{i}_{tag}", [128, D], F32)) for i in range(2)]
        psm = [st_.enter_context(nc.psum_tensor(f"s4pm{i}_{tag}", [128, 512], F32)) for i in range(2)]
        pst = [st_.enter_context(nc.psum_tensor(f"s4pt{i}_{tag}", [128, 1024], F32)) for i in range(2)]
        wl_toks = [ctx.dma("spw", out=w3sb[:, kc * 1024:(kc + 1) * 1024],
                           in_=w3t[kc * 128:(kc + 1) * 128, :]) for kc in range(KCH)]
        w3v = w3sb[:].rearrange("p (kc o) -> p kc o", kc=KCH)
        ld_tok = [None, None]
        ev_tok = [None, None]
        tr_tok = [None, None]
        add_tok = [None, None]
        stx_tok = [None, None]
        pe_last_rhs = [None, None]
        nsup = NT // 512
        for s in range(nsup):
            i2 = s % 2
            ctx.wait("sp", pe_last_rhs[i2])
            for kc in range(KCH):
                ld_tok[i2] = ctx.dma(f"spy{i2}", out=rhs[i2][:, kc * 512:(kc + 1) * 512],
                                     in_=src_h[kc * 128:(kc + 1) * 128, s * 512:(s + 1) * 512])
            rv = rhs[i2][:].rearrange("p (kc t) -> p kc t", kc=KCH)
            ctx.wait("dve", tr_tok[i2])
            for oc in range(8):
                o2 = oc % 2
                ctx.wait("pe", ld_tok[i2], ev_tok[o2], *(wl_toks if s == 0 else []))
                for kc in range(KCH):
                    mm = ctx.pe("matmul", psm[o2][:], w3v[:, kc, oc * 128:(oc + 1) * 128],
                                rv[:, kc, :], start=(kc == 0), stop=(kc == KCH - 1),
                                mark=(kc == KCH - 1))
                pe_last_rhs[i2] = mm
                ctx.wait("dve", mm)
                ev_tok[o2] = ctx.op("dve", "tensor_scalar",
                                    out=ufm[i2][:, oc * 512:(oc + 1) * 512], in0=psm[o2][:],
                                    scalar1=colv[:, c_s3 + oc:c_s3 + oc + 1], scalar2=None,
                                    op0=ALU.mult)
            uv = ufm[i2][:].rearrange("p (oc t) -> p oc t", oc=8)
            for j in range(4):
                it = s * 4 + j
                j2 = j % 2
                ctx.wait("sp", add_tok[j2])
                ldx = ctx.dma(f"spx{j2}", out=xs[j2][:], in_=x_src[it * 128:(it + 1) * 128, :])
                ctx.wait("pe", ev_tok[0], ev_tok[1], add_tok[j2])
                for oc in range(8):
                    pt = ctx.pe("transpose", pst[j2][:, oc * 128:(oc + 1) * 128],
                                uv[:, oc, j * 128:(j + 1) * 128], C.ident_f[:],
                                mark=(oc == 7))
                tr_tok[i2] = pt
                ctx.wait("dve", pt, ldx, stx_tok[j2])
                add_tok[j2] = ctx.op("dve", "tensor_tensor", out=xo[j2][:], in0=pst[j2][:],
                                     in1=xs[j2][:], op=ALU.add)
                ctx.wait("pool", add_tok[j2])
                stx_tok[j2] = ctx.dma(f"gx{j2}", out=x_dst[it * 128:(it + 1) * 128, :],
                                      in_=xo[j2][:])


class ColAlloc:
    def __init__(self):
        self.n = 0
        self.map = {}

    def reg(self, name, ncols):
        self.map[name] = self.n
        self.n += ncols
        return self.map[name]


def emit_prep_ffn0(ctx: Ctx, C: Consts, ext, dr, colv, cols: ColAlloc):
    prep_transposed_weight(ctx, C, ext["L0_ffn_w1_w"], dr["W1T"], FFN, D, "w1")
    prep_transposed_weight(ctx, C, ext["L0_ffn_w2_w"], dr["W2T"], FFN, D, "w2")
    prep_transposed_weight(ctx, C, ext["L0_ffn_h1_w"], dr["H1WT"], FFN, FFN, "h1")
    prep_transposed_weight(ctx, C, ext["L0_ffn_h2_w"], dr["H2WT"], FFN, FFN, "h2")
    prep_transposed_weight(ctx, C, ext["L0_ffn_w3_w"], dr["W3T"], D, FFN, "w3")
    prep_colvec(ctx, C, ext["L0_ffn_w1_s"], FFN, colv, cols.reg("w1_s", 32))
    prep_colvec(ctx, C, ext["L0_ffn_w2_s"], FFN, colv, cols.reg("w2_s", 32))
    prep_colvec(ctx, C, ext["L0_ffn_h1_s"], FFN, colv, cols.reg("h1_s", 32))
    prep_colvec(ctx, C, ext["L0_ffn_h2_s"], FFN, colv, cols.reg("h2_s", 32))
    prep_colvec(ctx, C, ext["L0_ffn_w3_s"], D, colv, cols.reg("w3_s", 8), scale=0.5)
    ctx.barrier()


def emit_ffn0_app(ctx: Ctx, C: Consts, dr, colv, cols: ColAlloc, x_src, x_dst, tag):
    phase_ln_to_fm(ctx, C, x_src, dr["YT"], tag + "a")
    ctx.barrier()
    phase_ffn0_s1(ctx, C, dr["YT"], dr["W1T"], dr["W2T"], colv,
                  cols.map["w1_s"], cols.map["w2_s"], dr["HA"], tag + "b")
    ctx.barrier()
    phase_ffn0_mid(ctx, C, dr["HA"], dr["H1WT"], colv, cols.map["h1_s"], dr["HB"], tag + "c")
    ctx.barrier()
    phase_ffn0_mid(ctx, C, dr["HB"], dr["H2WT"], colv, cols.map["h2_s"], dr["HA"], tag + "d")
    ctx.barrier()
    phase_out_ffn(ctx, C, dr["HA"], dr["W3T"], False, colv, cols.map["w3_s"], None,
                  x_src, None, x_dst, tag + "e")
    ctx.barrier()



def phase_ln2(ctx: Ctx, C: Consts, x_src, yt_dst, tag, y1_dst=None, double_ln=False,
              colv=None, w_cols=None, b_cols=None, ytok_dst=None):
    nc = ctx.nc
    with ExitStack() as st_:
        xb = [st_.enter_context(nc.sbuf_tensor(f"l2x{i}_{tag}", [128, D], F32)) for i in range(2)]
        y1b = [st_.enter_context(nc.sbuf_tensor(f"l2y1{i}_{tag}", [128, D], F32)) for i in range(2)] \
            if double_ln else None
        yb = [st_.enter_context(nc.sbuf_tensor(f"l2y{i}_{tag}", [128, D], BF16)) for i in range(2)]
        yts = [st_.enter_context(nc.sbuf_tensor(f"l2t{i}_{tag}", [128, 8 * 512], BF16)) for i in range(2)]
        ps = [st_.enter_context(nc.psum_tensor(f"l2p{i}_{tag}", [128, 1024], BF16)) for i in range(2)]
        lb = LnBufs(_StackCtx(ctx, st_), tag + "ln")
        lb2 = LnBufs(_StackCtx(ctx, st_), tag + "l2") if double_ln else None
        norm_tok = [None, None]
        evac_tok = [None, None]
        tr_tok = [None, None]
        st_tok = [None, None]
        sty_tok = [None, None]
        stk_tok = [None, None]
        ctx.wait("pe", C.ready)
        nsup = NT // 512
        for s in range(nsup):
            for j in range(4):
                it = s * 4 + j
                i2 = it % 2
                ctx.wait("sp", norm_tok[i2])
                ldx = ctx.dma(f"spx{i2}", out=xb[i2][:], in_=x_src[it * 128:(it + 1) * 128, :])
                ctx.wait("dve", tr_tok[i2])
                if double_ln:
                    ctx.wait("dve", sty_tok[i2])
                    t1, _, _ = emit_ln(ctx, lb, i2, xb[i2][:], y1b[i2][:], ldx)
                    ctx.wait("pool", t1)
                    sty_tok[i2] = ctx.dma(f"gy1{i2}", out=y1_dst[it * 128:(it + 1) * 128, :],
                                          in_=y1b[i2][:])
                    tnorm, _, _ = emit_ln(ctx, lb2, i2, y1b[i2][:], yb[i2][:], t1)
                else:
                    tnorm, _, _ = emit_ln(ctx, lb, i2, xb[i2][:], yb[i2][:], ldx)
                norm_tok[i2] = tnorm
                if ytok_dst is not None:
                    ctx.wait("pool", tnorm)
                    stk_tok[i2] = ctx.dma(f"gyk{i2}", out=ytok_dst[it * 128:(it + 1) * 128, :],
                                          in_=yb[i2][:])
                ctx.wait("pe", tnorm, evac_tok[i2])
                for kc in range(8):
                    pt = ctx.pe("transpose", ps[i2][:, kc * 128:(kc + 1) * 128],
                                yb[i2][:, kc * 128:(kc + 1) * 128], C.ident_bf[:],
                                mark=(kc == 7))
                tr_tok[i2] = pt
                ctx.wait("dve", pt, st_tok[s % 2])
                pv = ps[i2][:].rearrange("p (kc t) -> p kc t", kc=8)
                yv = yts[s % 2][:].rearrange("p (kc t) -> p kc t", kc=8)
                if w_cols is not None:
                    for kc in range(8):
                        evac_tok[i2] = ctx.op(
                            "dve", "tensor_scalar",
                            out=yv[:, kc, j * 128:(j + 1) * 128],
                            in0=pv[:, kc, :],
                            scalar1=colv[:, w_cols + kc:w_cols + kc + 1],
                            scalar2=colv[:, b_cols + kc:b_cols + kc + 1],
                            op0=ALU.mult, op1=ALU.add)
                else:
                    evac_tok[i2] = ctx.op("dve", "tensor_copy",
                                          yv[:, :, j * 128:(j + 1) * 128], pv[:, :, :])
            for kc in range(8):
                ctx.wait("pool", evac_tok[0], evac_tok[1])
                st_tok[s % 2] = ctx.dma(
                    f"gyt{s % 2}",
                    out=yt_dst[kc * 128:(kc + 1) * 128, s * 512:(s + 1) * 512],
                    in_=yts[s % 2][:, kc * 512:(kc + 1) * 512])



def emit_phi(ctx, psum_ap, bias_col, out_ap, tmpe, tmpm, phi_scale=1.0):
    ctx.op("dve", "tensor_scalar", out=tmpm, in0=psum_ap, scalar1=bias_col,
           scalar2=None, op0=ALU.add)
    t = ctx.tok("dve")
    ctx.wait("act", t)
    if phi_scale != 1.0:
        ctx.op("dve", "tensor_scalar", out=tmpm, in0=tmpm, scalar1=float(phi_scale),
               scalar2=None, op0=ALU.mult)
    ctx.op("dve", "tensor_scalar", out=tmpe, in0=tmpm, scalar1=0.0, scalar2=None, op0=ALU.min)
    t = ctx.tok("dve")
    ctx.wait("act", t)
    ctx.op("act", "activation", out=tmpe, in_=tmpe, func=AF.Exp)
    ta = ctx.tok("act")
    ctx.wait("dve", ta)
    ctx.op("dve", "tensor_scalar", out=tmpm, in0=tmpm, scalar1=0.0, scalar2=None, op0=ALU.max)
    return ctx.op("dve", "tensor_tensor", out=out_ap, in0=tmpe, in1=tmpm, op=ALU.add)


def phase_proj_fm(ctx: Ctx, C: Consts, xnt, w_src, dst, tag, colv=None, bias_cols=None,
                  phi=False, phi_scale=1.0, w_transposed_src=False):
    nc = ctx.nc
    O = 1024
    with ExitStack() as st_:
        wsb = st_.enter_context(nc.sbuf_tensor(f"pj_w_{tag}", [128, 8 * O], BF16))
        xtb = [st_.enter_context(nc.sbuf_tensor(f"pj_x{i}_{tag}", [128, 8 * 512], BF16)) for i in range(2)]
        ob = [st_.enter_context(nc.sbuf_tensor(f"pj_o{i}_{tag}", [128, 512], BF16)) for i in range(2)]
        te = [st_.enter_context(nc.sbuf_tensor(f"pj_e{i}_{tag}", [128, 512], F32)) for i in range(2)]
        tm = [st_.enter_context(nc.sbuf_tensor(f"pj_m{i}_{tag}", [128, 512], F32)) for i in range(2)]
        ps = [st_.enter_context(nc.psum_tensor(f"pj_p{i}_{tag}", [128, 512], F32)) for i in range(2)]
        wv = wsb[:].rearrange("p (kc o) -> p kc o", kc=8)
        if not w_transposed_src:
            wts = [ctx.dma(f"gw{kc % 2}", out=wv[:, kc, :], in_=w_src[kc * 128:(kc + 1) * 128, :])
                   for kc in range(8)]
            ctx.wait("pe", *wts)
        else:
            with ExitStack() as st2:
                blk = [st2.enter_context(nc.sbuf_tensor(f"pj_tb{i}_{tag}", [128, 128], BF16))
                       for i in range(2)]
                pst = [st2.enter_context(nc.psum_tensor(f"pj_tp{i}_{tag}", [128, 128], BF16))
                       for i in range(2)]
                ctx.wait("pe", C.ready)
                ev = [None, None]
                tt = [None, None]
                for n in range(64):
                    r, c = n // 8, n % 8
                    i = n % 2
                    ctx.wait("pool", tt[i])
                    lt = ctx.dma(f"gw{i}", out=blk[i][:],
                                 in_=w_src[r * 128:(r + 1) * 128, c * 128:(c + 1) * 128])
                    ctx.wait("pe", lt, ev[i])
                    tt[i] = ctx.pe("transpose", pst[i][:], blk[i][:], C.ident_bf[:], mark=True)
                    ctx.wait("dve", tt[i])
                    ev[i] = ctx.op("dve", "tensor_copy", wv[:, c, r * 128:(r + 1) * 128], pst[i][:])
                ctx.wait("pe", ev[0], ev[1])
        ld_tok = [None, None]
        e_tok = [None, None]
        st_tok = [None, None]
        pe_last = [None, None]
        nsup = NT // 512
        for s in range(nsup):
            i2 = s % 2
            ctx.wait("sp", pe_last[i2])
            for kc in range(8):
                ld_tok[i2] = ctx.dma(f"spy{i2}", out=xtb[i2][:, kc * 512:(kc + 1) * 512],
                                     in_=xnt[kc * 128:(kc + 1) * 128, s * 512:(s + 1) * 512])
            xv = xtb[i2][:].rearrange("p (kc t) -> p kc t", kc=8)
            for oc in range(8):
                o2 = (s * 8 + oc) % 2
                ctx.wait("pe", ld_tok[i2], e_tok[o2])
                for kc in range(8):
                    mm = ctx.pe("matmul", ps[o2][:], wv[:, kc, oc * 128:(oc + 1) * 128],
                                xv[:, kc, :], start=(kc == 0), stop=(kc == 7), mark=(kc == 7))
                pe_last[i2] = mm
                bias_col = colv[:, bias_cols + oc:bias_cols + oc + 1] if bias_cols is not None else 0.0
                ctx.wait("dve", mm, st_tok[o2])
                if phi:
                    e_tok[o2] = emit_phi(ctx, ps[o2][:], bias_col, ob[o2][:],
                                         te[o2][:], tm[o2][:], phi_scale)
                else:
                    e_tok[o2] = ctx.op("dve", "tensor_scalar", out=ob[o2][:], in0=ps[o2][:],
                                       scalar1=bias_col, scalar2=None, op0=ALU.add)
                ctx.wait("pool", e_tok[o2])
                st_tok[o2] = ctx.dma(f"go{o2}", out=dst[oc * 128:(oc + 1) * 128,
                                                       s * 512:(s + 1) * 512],
                                     in_=ob[o2][:])



def phase_proj_tok(ctx: Ctx, C: Consts, xnt, wv_src, vbias_row, dst_v, wd_src, dbias_row,
                   lam_sb, tag, wv_transposed=False, wd_transposed=False):
    nc = ctx.nc
    with ExitStack() as st_:
        wsb = st_.enter_context(nc.sbuf_tensor(f"pt_w_{tag}", [128, 8 * 1024], BF16))
        wdb = st_.enter_context(nc.sbuf_tensor(f"pt_wd_{tag}", [128, 8 * 16], BF16)) \
            if wd_src is not None else None
        xtb = [st_.enter_context(nc.sbuf_tensor(f"pt_x{i}_{tag}", [128, 8 * 128], BF16)) for i in range(2)]
        vb = [st_.enter_context(nc.sbuf_tensor(f"pt_v{i}_{tag}", [128, 512], BF16)) for i in range(2)]
        ps = [st_.enter_context(nc.psum_tensor(f"pt_p{i}_{tag}", [128, 512], F32)) for i in range(2)]
        psd = st_.enter_context(nc.psum_tensor(f"pt_pd_{tag}", [128, 16], F32)) \
            if wd_src is not None else None
        wv_ = wsb[:].rearrange("p (kc o) -> p kc o", kc=8)
        if not wv_transposed:
            wts = [ctx.dma(f"gw{kc % 2}", out=wv_[:, kc, :], in_=wv_src[kc * 128:(kc + 1) * 128, :])
                   for kc in range(8)]
        else:
            wts = []
            _load_transposed_into(ctx, C, wv_src, wv_, 1024, tag + "v")
        if wd_src is not None:
            wd_ = wdb[:].rearrange("p (kc o) -> p kc o", kc=8)
            if not wd_transposed:
                wts += [ctx.dma("gwd", out=wd_[:, kc, :], in_=wd_src[kc * 128:(kc + 1) * 128, :])
                        for kc in range(8)]
            else:
                _load_transposed_into(ctx, C, wd_src, wd_, 16, tag + "d")
        ctx.wait("pe", *wts)
        ld_tok = [None, None]
        e_tok = [None, None]
        d_tok = None
        st_tok = [None, None]
        pe_last = [None, None]
        for tt in range(NTILE):
            i2 = tt % 2
            ctx.wait("sp", pe_last[i2])
            for kc in range(8):
                ld_tok[i2] = ctx.dma(f"spy{i2}", out=xtb[i2][:, kc * 128:(kc + 1) * 128],
                                     in_=xnt[kc * 128:(kc + 1) * 128, tt * 128:(tt + 1) * 128])
            xv = xtb[i2][:].rearrange("p (kc t) -> p kc t", kc=8)
            for oh in range(2):
                o2 = (tt * 2 + oh) % 2
                ctx.wait("pe", ld_tok[i2], e_tok[o2])
                for kc in range(8):
                    mm = ctx.pe("matmul", ps[o2][:], xv[:, kc, :],
                                wv_[:, kc, oh * 512:(oh + 1) * 512],
                                start=(kc == 0), stop=(kc == 7), mark=(kc == 7))
                pe_last[i2] = mm
                ctx.wait("dve", mm, st_tok[o2])
                e_tok[o2] = ctx.op("dve", "tensor_tensor", out=vb[o2][:], in0=ps[o2][:],
                                   in1=vbias_row[:, oh * 512:(oh + 1) * 512], op=ALU.add)
                ctx.wait("pool", e_tok[o2])
                st_tok[o2] = ctx.dma(f"go{o2}", out=dst_v[tt * 128:(tt + 1) * 128,
                                                         oh * 512:(oh + 1) * 512],
                                     in_=vb[o2][:])
            if wd_src is not None:
                ctx.wait("pe", d_tok)
                for kc in range(8):
                    mm = ctx.pe("matmul", psd[:], xv[:, kc, :], wd_[:, kc, :],
                                start=(kc == 0), stop=(kc == 7), mark=(kc == 7))
                pe_last[i2] = mm
                ctx.wait("dve", mm)
                ctx.op("dve", "tensor_tensor", out=lam_sb[:, tt * 16:(tt + 1) * 16],
                       in0=psd[:], in1=dbias_row[:, 0:16], op=ALU.add)
                t = ctx.tok("dve")
                ctx.wait("act", t)
                d_tok = ctx.op("act", "activation", out=lam_sb[:, tt * 16:(tt + 1) * 16],
                               in_=lam_sb[:, tt * 16:(tt + 1) * 16], func=AF.Sigmoid)


def _load_transposed_into(ctx: Ctx, C: Consts, w_src, wv_dst3, O, tag):
    nc = ctx.nc
    ob = ceil_div(O, 128)
    with ExitStack() as st2:
        blk = [st2.enter_context(nc.sbuf_tensor(f"lt_b{i}_{tag}", [128, 128], BF16)) for i in range(2)]
        pst = [st2.enter_context(nc.psum_tensor(f"lt_p{i}_{tag}", [128, 128], BF16)) for i in range(2)]
        ctx.wait("pe", C.ready)
        ev = [None, None]
        tt = [None, None]
        n = 0
        for r in range(ob):
            rr = min(128, O - r * 128)
            for c in range(8):
                i = n % 2
                ctx.wait("pool", tt[i])
                lt = ctx.dma(f"gw{i}", out=blk[i][0:rr, :],
                             in_=w_src[r * 128:r * 128 + rr, c * 128:(c + 1) * 128])
                ctx.wait("pe", lt, ev[i])
                tt[i] = ctx.pe("transpose", pst[i][:, 0:rr], blk[i][0:rr, :],
                               C.ident_bf[0:rr, 0:rr], mark=True)
                ctx.wait("dve", tt[i])
                ev[i] = ctx.op("dve", "tensor_copy", wv_dst3[:, c, r * 128:r * 128 + rr],
                               pst[i][:, 0:rr])
                n += 1
        ctx.barrier()



def phase_fm_to_tok(ctx: Ctx, C: Consts, src_fm, dst_tok, tag):
    nc = ctx.nc
    with ExitStack() as st_:
        xb = [st_.enter_context(nc.sbuf_tensor(f"ft_x{i}_{tag}", [128, 8 * 128], BF16)) for i in range(2)]
        ob = [st_.enter_context(nc.sbuf_tensor(f"ft_o{i}_{tag}", [128, 1024], BF16)) for i in range(2)]
        ps = [st_.enter_context(nc.psum_tensor(f"ft_p{i}_{tag}", [128, 1024], BF16)) for i in range(2)]
        ctx.wait("pe", C.ready)
        tr_tok = [None, None]
        ev_tok = [None, None]
        st_tok = [None, None]
        for tt in range(NTILE):
            i2 = tt % 2
            ctx.wait("sp", tr_tok[i2])
            ld = None
            for kc in range(8):
                ld = ctx.dma(f"spy{i2}", out=xb[i2][:, kc * 128:(kc + 1) * 128],
                             in_=src_fm[kc * 128:(kc + 1) * 128, tt * 128:(tt + 1) * 128])
            ctx.wait("pe", ld, ev_tok[i2])
            for kc in range(8):
                pt = ctx.pe("transpose", ps[i2][:, kc * 128:(kc + 1) * 128],
                            xb[i2][:, kc * 128:(kc + 1) * 128], C.ident_bf[:], mark=(kc == 7))
            tr_tok[i2] = pt
            ctx.wait("dve", pt, st_tok[i2])
            ev_tok[i2] = ctx.op("dve", "tensor_copy", ob[i2][:], ps[i2][:])
            ctx.wait("pool", ev_tok[i2])
            st_tok[i2] = ctx.dma(f"go{i2}", out=dst_tok[tt * 128:(tt + 1) * 128, :], in_=ob[i2][:])



SLOT65 = 65


class ScanApp:

    def __init__(self, ctx: Ctx, tag, den=True):
        nc, st_ = ctx.nc, ctx.stack
        self.den = den
        self.w = SLOT65 if den else 64
        w = self.w
        ch = T // 128
        self.ch = ch
        self.out_sb = st_.enter_context(nc.sbuf_tensor(f"sc_out_{tag}", [128, ch * H * w], F32))
        self.lam = st_.enter_context(nc.sbuf_tensor(f"sc_lam_{tag}", [128, NTILE * H], F32))
        self.Sf = st_.enter_context(nc.sbuf_tensor(f"sc_sf_{tag}", [64, H * w], F32))
        self.Sb = st_.enter_context(nc.sbuf_tensor(f"sc_sb_{tag}", [128, H * w], BF16))
        self.eq = st_.enter_context(nc.sbuf_tensor(f"sc_eq_{tag}", [128, ch * H], F32))
        self.ek = st_.enter_context(nc.sbuf_tensor(f"sc_ek_{tag}", [128, ch * H], F32))
        self.gb = st_.enter_context(nc.sbuf_tensor(f"sc_gb_{tag}", [64, ch * H], F32))
        self.lhs16 = st_.enter_context(nc.sbuf_tensor(f"sc_lh_{tag}", [16, ch * 128], F32))
        self.bd16 = st_.enter_context(nc.sbuf_tensor(f"sc_bd_{tag}", [16, H * 128], F32))
        self.negone = st_.enter_context(nc.sbuf_tensor(f"sc_no_{tag}", [1, 128], F32))
        self.maskt = st_.enter_context(nc.sbuf_tensor(f"sc_mk_{tag}", [128, 128], F32))


_UNIQ = [0]


def _uq(s):
    _UNIQ[0] += 1
    return f"{s}{_UNIQ[0]}"


def build_scan_consts(ctx: Ctx, C: Consts, app: ScanApp, causal_mask):
    nc = ctx.nc
    with ExitStack() as st_:
        iorow = st_.enter_context(nc.sbuf_tensor(_uq("bd_ior"), [16, H * 128], F32))
        col = st_.enter_context(nc.sbuf_tensor(_uq("bd_col"), [16, 1], F32))
        t = ctx.op("pool", "iota", iorow[:], [[1, H * 128]], base=0, channel_multiplier=0,
                   allow_small_or_imprecise_dtypes=True)
        ctx.wait("dve", t, C.t_iota_p)
        ctx.op("dve", "tensor_scalar", out=col[:], in0=C.iota_p[0:16, :], scalar1=128.0,
               scalar2=None, op0=ALU.mult)
        ctx.op("dve", "tensor_scalar", out=app.bd16[:], in0=iorow[:], scalar1=col[:],
               scalar2=None, op0=ALU.is_ge)
        ctx.op("dve", "tensor_scalar", out=iorow[:], in0=iorow[:], scalar1=col[:],
               scalar2=128.0, op0=ALU.subtract, op1=ALU.is_lt)
        ctx.op("dve", "tensor_tensor", out=app.bd16[:], in0=app.bd16[:], in1=iorow[:], op=ALU.mult)
        ctx.op("dve", "memset", app.negone[:], -1.0)
        if causal_mask:
            sc1 = st_.enter_context(nc.sbuf_tensor(_uq("bd_s1"), [128, 128], F32))
            t2 = ctx.op("pool", "iota", sc1[:], [[1, 128]], base=0, channel_multiplier=0,
                        allow_small_or_imprecise_dtypes=True)
            ctx.wait("dve", t2)
            ctx.op("dve", "tensor_scalar", out=app.maskt[:], in0=sc1[:], scalar1=C.iota_p[:],
                   scalar2=None, op0=ALU.is_lt)
            ctx.op("dve", "tensor_scalar", out=app.maskt[:], in0=app.maskt[:], scalar1=-1e30,
                   scalar2=None, op0=ALU.mult)
        else:
            ctx.op("dve", "memset", app.maskt[:], 0.0)
        ctx.barrier()


def emit_scan_prelude(ctx: Ctx, C: Consts, app: ScanApp, b, prow_dram, clamp_min, tag):
    nc = ctx.nc
    CW = app.ch * H
    with ExitStack() as st_:
        lgl = st_.enter_context(nc.sbuf_tensor(f"pr_lgl_{tag}", [128, CW], F32))
        pcol = st_.enter_context(nc.sbuf_tensor(f"pr_pc_{tag}", [128, CW], F32))
        trow = st_.enter_context(nc.sbuf_tensor(f"pr_tr_{tag}", [128, ceil_div(CW, 128) * 128], F32))
        grow = st_.enter_context(nc.sbuf_tensor(f"pr_gr_{tag}", [1, CW], F32))
        pp = st_.enter_context(nc.psum_tensor(f"pr_pp_{tag}", [128, CW], F32))
        pp2 = st_.enter_context(nc.psum_tensor(f"pr_p2_{tag}", [128, CW], F32))
        ppg = st_.enter_context(nc.psum_tensor(f"pr_pg_{tag}", [1, CW], F32))
        ppb = st_.enter_context(nc.psum_tensor(f"pr_pb_{tag}", [64, CW], F32))
        ppt = [st_.enter_context(nc.psum_tensor(f"pr_pt{i}_{tag}", [128, 128], F32))
               for i in range(2)]
        lam_sl = app.lam[:, b * CW:(b + 1) * CW]
        ctx.op("dve", "tensor_scalar", out=lgl[:], in0=lam_sl, scalar1=float(clamp_min),
               scalar2=None, op0=ALU.max)
        t = ctx.tok("dve")
        ctx.wait("act", t)
        tl = ctx.op("act", "activation", out=lgl[:], in_=lgl[:], func=AF.Ln)
        ctx.wait("pe", tl, C.ready)
        t1 = ctx.pe("matmul", pp[:], C.tril_f[:], lgl[:], start=True, stop=True, mark=True)
        t2 = ctx.pe("matmul", pp2[:], C.triu_f[:], lgl[:], start=True, stop=True, mark=True)
        t3 = ctx.pe("matmul", ppg[:], C.ones_f[:, 0:1], lgl[:], start=True, stop=True, mark=True)
        ctx.wait("act", t1, t2, t3)
        ctx.op("act", "activation", out=app.eq[:], in_=pp[:], func=AF.Exp)
        ctx.op("act", "activation", out=app.ek[:], in_=pp2[:], func=AF.Exp)
        tg = ctx.op("act", "activation", out=grow[:], in_=ppg[:], func=AF.Exp)
        ctx.wait("dve", t1)
        tpc = ctx.op("dve", "tensor_copy", pcol[:], pp[:])
        ctx.wait("pe", tg)
        t4 = ctx.pe("matmul", ppb[:], C.ones_f[0:1, 0:64], grow[0:1, :], start=True, stop=True,
                    mark=True)
        ctx.wait("dve", t4)
        ctx.op("dve", "tensor_copy", app.gb[:], ppb[:])
        ctx.wait("pe", tpc)
        st_toks = []
        for half in range(ceil_div(CW, 128)):
            wdt = min(128, CW - half * 128)
            th = ctx.pe("transpose", ppt[half % 2][0:wdt, :],
                        pcol[:, half * 128:half * 128 + wdt],
                        C.ident_f[:], mark=True)
            ctx.wait("dve", th)
            ev = ctx.op("dve", "tensor_scalar",
                        out=trow[0:wdt, half * 128:half * 128 + 128],
                        in0=ppt[half % 2][0:wdt, :], scalar1=-1.0, scalar2=None, op0=ALU.mult)
            ctx.wait("pool", ev)
            st_toks.append(ctx.dma("gprow",
                                   out=prow_dram[half * 128:half * 128 + wdt, :],
                                   in_=trow[0:wdt, half * 128:half * 128 + 128]))
        ctx.wait("pool", *st_toks)
        ctx.dma("gprow", out=app.lhs16[:].rearrange("p (c i) -> p c i", c=app.ch),
                in_=prow_dram[:].rearrange("(c h) i -> h c i", h=16))
        ctx.barrier()


def emit_scan_pass(ctx: Ctx, C: Consts, app: ScanApp, b, qf, kf, ktok, vtok, prow_dram,
                   forward, causal, tag):
    nc = ctx.nc
    W = app.w
    den = app.den
    with ExitStack() as st_:
        qft = [st_.enter_context(nc.sbuf_tensor(f"sp_q{i}_{tag}", [128, 8 * 128], BF16)) for i in range(2)]
        kft = [st_.enter_context(nc.sbuf_tensor(f"sp_k{i}_{tag}", [128, 8 * 128], BF16)) for i in range(2)] \
            if forward else None
        ktt = [st_.enter_context(nc.sbuf_tensor(f"sp_kt{i}_{tag}", [128, 1024], BF16)) for i in range(2)]
        vtt = [st_.enter_context(nc.sbuf_tensor(f"sp_vt{i}_{tag}", [128, 1024], BF16)) for i in range(2)]
        pflat = [st_.enter_context(nc.sbuf_tensor(f"sp_pf{i}_{tag}", [1, 2048], F32)) for i in range(2)] \
            if forward else None
        cwn = st_.enter_context(nc.sbuf_tensor(f"sp_cn_{tag}", [128, 1024], F32)) if forward else None
        cw = st_.enter_context(nc.sbuf_tensor(f"sp_cw_{tag}", [128, 1024], F32)) if forward else None
        cs = [st_.enter_context(nc.sbuf_tensor(f"sp_cs{i}_{tag}", [128, 1024], BF16)) for i in range(2)] \
            if forward else None
        k2 = st_.enter_context(nc.sbuf_tensor(f"sp_k2_{tag}", [128, 1024], BF16))
        SW = W if W % 2 == 0 else W + 1
        tmp = [st_.enter_context(nc.sbuf_tensor(f"sp_tm{i}_{tag}", [128, 4 * SW], F32)) for i in range(2)]
        onesb = st_.enter_context(nc.sbuf_tensor(f"sp_on_{tag}", [128, 2], BF16))
        ps_big = st_.enter_context(nc.psum_tensor(f"sp_pb_{tag}", [128, 1024], F32)) if forward else None
        ps_a = [st_.enter_context(nc.psum_tensor(f"sp_pa{i}_{tag}", [128, 4 * SW], F32)) for i in range(2)] \
            if forward else None
        ps_ib = [st_.enter_context(nc.psum_tensor(f"sp_pi{i}_{tag}", [128, 4 * SW], F32)) for i in range(2)]
        ps_u = [st_.enter_context(nc.psum_tensor(f"sp_pu{i}_{tag}", [64, 4 * SW], F32)) for i in range(2)]
        ctx.op("dve", "memset", onesb[:], 1.0)
        ctx.op("dve", "memset", app.Sf[:], 0.0)
        sb_cast_tok = ctx.op("dve", "memset", app.Sb[:], 0.0)
        ld_q = [None, None]
        ld_k = [None, None]
        ld_kt = [None, None]
        ld_vt = [None, None]
        ld_pf = [None, None]
        cfree_tok = None
        cs_tok = [None, None]
        a_ev = [None, None]
        i_ev = [None, None]
        u_ev = [None, None]
        pe_q = [None, None]
        pe_kt = [None, None]
        pe_vt = [None, None]
        CH = app.ch
        eqv = app.eq[:].rearrange("p (c h) -> p c h", c=CH)
        ekv = app.ek[:].rearrange("p (c h) -> p c h", c=CH)
        gbv = app.gb[:].rearrange("p (c h) -> p c h", c=CH)
        outv = app.out_sb[:].rearrange("p (c h w) -> p c h w", c=CH, h=H)
        lhsv = app.lhs16[:].rearrange("p (c i) -> p c i", c=CH)
        chunks = range(CH) if forward else range(CH - 1, -1, -1)
        cs_war = [None, None]
        k2_war = None
        for ci, c in enumerate(chunks):
            i2 = ci % 2
            t0 = b * T + c * 128
            ctx.wait("sp", pe_q[i2])
            for fc in range(8):
                ld_q[i2] = ctx.dma(f"spq{i2}", out=qft[i2][:, fc * 128:(fc + 1) * 128],
                                   in_=qf[fc * 128:(fc + 1) * 128, t0:t0 + 128])
            if forward:
                for fc in range(8):
                    ld_k[i2] = ctx.dma(f"spk{i2}", out=kft[i2][:, fc * 128:(fc + 1) * 128],
                                       in_=kf[fc * 128:(fc + 1) * 128, t0:t0 + 128])
                ctx.wait("sp", cfree_tok)
                ld_pf[i2] = ctx.dma(f"sppf{i2}", out=pflat[i2][0:1, :],
                                    in_=prow_dram[c * 16:(c + 1) * 16, :].rearrange("a b -> (a b)").unsqueeze(0))
            ctx.wait("sp", pe_kt[i2], pe_vt[i2])
            ld_kt[i2] = ctx.dma(f"spkt{i2}", out=ktt[i2][:], in_=ktok[t0:t0 + 128, :])
            ld_vt[i2] = ctx.dma(f"spvt{i2}", out=vtt[i2][:], in_=vtok[t0:t0 + 128, :])
            inter_marks = []
            atok = None
            for g in range(2):
                if forward:
                    ctx.wait("pe", ld_pf[i2], cs_tok[0], cs_tok[1])
                    for sl in range(2 if SCAN_PARTS & 1 else 0):
                        ctx.pe("matmul", ps_big[:, sl * 512:(sl + 1) * 512],
                               lhsv[:, c, :], app.bd16[:, g * 1024 + sl * 512:g * 1024 + (sl + 1) * 512],
                               start=True, stop=False)
                        mtok = ctx.pe("matmul", ps_big[:, sl * 512:(sl + 1) * 512],
                                      app.negone[0:1, :],
                                      pflat[i2][0:1, g * 1024 + sl * 512:g * 1024 + (sl + 1) * 512],
                                      start=False, stop=True, mark=True)
                    if not SCAN_PARTS & 2:
                        mtok2 = mtok if SCAN_PARTS & 1 else None
                        ctx.wait("dve", mtok2)
                        ce_tok = ctx.op("dve", "memset", cw[:], 1.0)
                        cfree_tok = ce_tok
                    elif causal:
                        ctx.wait("dve", mtok)
                        mvk = app.maskt[:].unsqueeze(1).broadcast_to([128, 8, 128])
                        ctx.op("dve", "tensor_tensor",
                               out=cw[:].rearrange("p (h i) -> p h i", h=8),
                               in0=ps_big[:].rearrange("p (h i) -> p h i", h=8),
                               in1=mvk, op=ALU.add)
                    else:
                        ctx.wait("dve", mtok)
                        ctx.op("dve", "tensor_scalar", out=cwn[:], in0=ps_big[:],
                               scalar1=-1.0, scalar2=None, op0=ALU.mult)
                        ctx.op("dve", "tensor_tensor", out=cw[:], in0=ps_big[:], in1=cwn[:],
                               op=ALU.min)
                    if SCAN_PARTS & 2:
                        cfree_tok = ctx.tok("dve")
                        ctx.wait("act", cfree_tok)
                        ce_tok = ctx.op("act", "activation", out=cw[:], in_=cw[:], func=AF.Exp)
                    ctx.wait("pe", cfree_tok, ld_q[i2], ld_k[i2])
                    stok = None
                    for hh in ([0, 5, 2, 7, 1, 4, 3, 6] if SCAN_PARTS & 4 else []):
                        h = g * 8 + hh
                        p0 = (h % 2) * 64
                        fc = h // 2
                        stok = ctx.pe("matmul", ps_big[:, hh * 128:(hh + 1) * 128],
                                      kft[i2][p0:p0 + 64, fc * 128:(fc + 1) * 128],
                                      qft[i2][p0:p0 + 64, fc * 128:(fc + 1) * 128],
                                      start=True, stop=True, mark=(SCAN_SAFE or hh == 6))
                        if SCAN_SAFE:
                            ctx.wait("pe", stok)
                    ctx.wait("dve", stok, ce_tok, cs_war[g])
                    if SCAN_PARTS & 8:
                        cs_tok[g] = ctx.op("dve", "tensor_tensor", out=cs[g][:], in0=ps_big[:],
                                           in1=cw[:], op=ALU.mult)
                    else:
                        cs_tok[g] = ctx.op("dve", "memset", cs[g][:], 0.01)
                    ctx.wait("pe", cs_tok[g], ld_vt[i2], a_ev[0], a_ev[1])
                    for hh in (range(8) if SCAN_PARTS & 16 else []):
                        h = g * 8 + hh
                        half, pos = hh // 4, hh % 4
                        ctx.pe("matmul", ps_a[half][:, pos * SW:pos * SW + 64],
                               cs[g][:, hh * 128:(hh + 1) * 128],
                               vtt[i2][:, h * 64:(h + 1) * 64],
                               start=True, stop=True,
                               mark=(hh == 7 and not den))
                        if den:
                            ctx.pe("matmul", ps_a[half][:, pos * SW + 64:pos * SW + 66],
                                   cs[g][:, hh * 128:(hh + 1) * 128], onesb[:, 0:2],
                                   start=True, stop=True, mark=(hh == 7))
                    atok = Tok(ctx.pe_sem, ctx.pe_cnt)
                    cs_war[g] = atok
                    pe_vt[i2] = atok
                ctx.wait("pe", ld_q[i2], sb_cast_tok, i_ev[0], i_ev[1])
                state = app.Sb
                btok = None
                for hh in ([0, 4, 1, 5, 2, 6, 3, 7] if SCAN_PARTS & 32 else []):
                    h = g * 8 + hh
                    p0 = (h % 2) * 64
                    fc = h // 2
                    half, pos = hh // 4, hh % 4
                    btok = ctx.pe("matmul", ps_ib[half][:, pos * SW:pos * SW + W],
                                  qft[i2][p0:p0 + 64, fc * 128:(fc + 1) * 128],
                                  state[p0:p0 + 64, h * W:(h + 1) * W],
                                  start=True, stop=True, mark=(SCAN_SAFE or hh == 7))
                    if SCAN_SAFE:
                        ctx.wait("pe", btok)
                inter_marks.append(btok)
                pe_q[i2] = btok
                fac = eqv if forward else ekv
                for q in (range(2) if SCAN_PARTS & 64 else []):
                    hb = g * 8 + q * 4
                    ctx.wait("dve", btok, (atok if forward else None))
                    fap = fac[:, c, hb:hb + 4].unsqueeze(2).broadcast_to([128, 4, W])
                    i_ev[q] = ctx.op("dve", "tensor_tensor",
                                     out=tmp[q][:].rearrange("p (h w) -> p h w", h=4)[:, :, 0:W],
                                     in0=ps_ib[q][:].rearrange("p (h w) -> p h w", h=4)[:, :, 0:W],
                                     in1=fap, op=ALU.mult)
                    if forward:
                        a_ev[q] = ctx.op("dve", "tensor_tensor",
                                         out=outv[:, c, hb:hb + 4, :],
                                         in0=ps_a[q][:].rearrange("p (h w) -> p h w", h=4)[:, :, 0:W],
                                         in1=tmp[q][:].rearrange("p (h w) -> p h w", h=4)[:, :, 0:W],
                                         op=ALU.add)
                    else:
                        a_ev[q] = ctx.op("dve", "tensor_tensor",
                                         out=outv[:, c, hb:hb + 4, :],
                                         in0=outv[:, c, hb:hb + 4, :],
                                         in1=tmp[q][:].rearrange("p (h w) -> p h w", h=4)[:, :, 0:W],
                                         op=ALU.add)
            if not SCAN_PARTS & 128:
                continue
            kfac = ekv if forward else eqv
            ctx.wait("dve", ld_kt[i2], k2_war)
            k2tok = ctx.op("dve", "tensor_tensor",
                           out=k2[:].rearrange("p (h k) -> p h k", h=H),
                           in0=ktt[i2][:].rearrange("p (h k) -> p h k", h=H),
                           in1=kfac[:, c, :].unsqueeze(2).broadcast_to([128, H, 64]),
                           op=ALU.mult)
            pe_kt[i2] = k2tok
            for quarter in range(4):
                pu = ps_u[quarter % 2]
                ctx.wait("pe", k2tok, ld_vt[i2], u_ev[quarter % 2], *inter_marks)
                for hh4 in range(4):
                    h = quarter * 4 + hh4
                    ctx.pe("matmul", pu[:, hh4 * SW:hh4 * SW + 64],
                           k2[:, h * 64:(h + 1) * 64], vtt[i2][:, h * 64:(h + 1) * 64],
                           start=True, stop=True, mark=(hh4 == 3 and not den))
                    if den:
                        ctx.pe("matmul", pu[:, hh4 * SW + 64:hh4 * SW + 66],
                               k2[:, h * 64:(h + 1) * 64], onesb[:, 0:2],
                               start=True, stop=True, mark=(hh4 == 3))
                utok = Tok(ctx.pe_sem, ctx.pe_cnt)
                pe_vt[i2] = utok
                k2_war = utok
                sf4 = app.Sf[:].rearrange("p (h w) -> p h w", h=H)[:, quarter * 4:quarter * 4 + 4, :]
                gap = gbv[:, c, quarter * 4:quarter * 4 + 4].unsqueeze(2).broadcast_to([64, 4, W])
                ctx.wait("dve", utok, *inter_marks)
                ctx.op("dve", "tensor_tensor", out=sf4, in0=sf4, in1=gap, op=ALU.mult)
                u_ev[quarter % 2] = ctx.op(
                    "dve", "tensor_tensor", out=sf4, in0=sf4,
                    in1=pu[:].rearrange("p (h w) -> p h w", h=4)[:, :, 0:W], op=ALU.add)
                ctx.op(
                    "dve", "tensor_copy",
                    app.Sb[0:64, :].rearrange("p (h w) -> p h w", h=H)[:, quarter * 4:quarter * 4 + 4, :],
                    sf4)
                sb_cast_tok = ctx.op(
                    "dve", "tensor_copy",
                    app.Sb[64:128, :].rearrange("p (h w) -> p h w", h=H)[:, quarter * 4:quarter * 4 + 4, :],
                    sf4)


def emit_a3_l0(ctx: Ctx, C: Consts, app: ScanApp, b, ow_src, ob_row, y1_dram, z_dram,
               x2t_dram, colv, c_ln2w, c_ln2b, tag):
    nc = ctx.nc
    W = app.w
    with ExitStack() as st_:
        ow_sb = st_.enter_context(nc.sbuf_tensor(f"a3_ow_{tag}", [128, 8 * 1024], BF16))
        atile = [st_.enter_context(nc.sbuf_tensor(f"a3_at{i}_{tag}", [128, 1024], BF16)) for i in range(2)]
        afm = [st_.enter_context(nc.sbuf_tensor(f"a3_af{i}_{tag}", [128, 1024], BF16)) for i in range(2)]
        rden = [st_.enter_context(nc.sbuf_tensor(f"a3_rd{i}_{tag}", [128, 16], F32)) for i in range(2)]
        y1t = [st_.enter_context(nc.sbuf_tensor(f"a3_y1{i}_{tag}", [128, 1024], F32)) for i in range(2)]
        zt = [st_.enter_context(nc.sbuf_tensor(f"a3_z{i}_{tag}", [128, 1024], F32)) for i in range(2)]
        zn = [st_.enter_context(nc.sbuf_tensor(f"a3_zn{i}_{tag}", [128, 1024], BF16)) for i in range(2)]
        x2s = [st_.enter_context(nc.sbuf_tensor(f"a3_x2{i}_{tag}", [128, 1024], BF16)) for i in range(2)]
        ps_t = [st_.enter_context(nc.psum_tensor(f"a3_pt{i}_{tag}", [128, 1024], BF16)) for i in range(2)]
        ps_o = [st_.enter_context(nc.psum_tensor(f"a3_po{i}_{tag}", [128, 512], F32)) for i in range(2)]
        ps_t2 = [st_.enter_context(nc.psum_tensor(f"a3_p2{i}_{tag}", [128, 1024], BF16)) for i in range(2)]
        lnb = LnBufs(_StackCtx(ctx, st_), tag + "ln")
        wv = ow_sb[:].rearrange("p (kc o) -> p kc o", kc=8)
        wts = [ctx.dma(f"gw{kc % 2}", out=wv[:, kc, :], in_=ow_src[kc * 128:(kc + 1) * 128, :])
               for kc in range(8)]
        ctx.wait("pe", *wts)
        outv = app.out_sb[:].rearrange("p (c h w) -> p c h w", c=app.ch, h=H)
        tr_tok = [None, None]
        afm_ev = [None, None]
        o_ev = [None, None]
        z_tok = [None, None]
        stz_tok = [None, None]
        ln_tok = [None, None]
        tr2_tok = [None, None]
        x2_ev = [None, None]
        stx_tok = [None, None]
        pe_o_last = [None, None]
        for ti in range(app.ch):
            tt = b * app.ch + ti
            i2 = ti % 2
            ctx.wait("dve", tr_tok[i2])
            if app.den:
                ctx.op("dve", "tensor_scalar", out=rden[i2][:],
                       in0=outv[:, ti, :, 64], scalar1=1e-6, scalar2=None, op0=ALU.max)
                ctx.op("dve", "reciprocal", out=rden[i2][:], in_=rden[i2][:])
                at_tok = ctx.op("dve", "tensor_tensor",
                                out=atile[i2][:].rearrange("p (h k) -> p h k", h=H),
                                in0=outv[:, ti, :, 0:64],
                                in1=rden[i2][:].unsqueeze(2).broadcast_to([128, H, 64]),
                                op=ALU.mult)
            else:
                at_tok = ctx.op("dve", "tensor_copy", atile[i2][:], app.out_sb[
                    :, ti * H * W:(ti + 1) * H * W])
            ctx.wait("pe", at_tok, afm_ev[i2])
            for fc in range(8):
                ptok = ctx.pe("transpose", ps_t[i2][:, fc * 128:(fc + 1) * 128],
                              atile[i2][:, fc * 128:(fc + 1) * 128], C.ident_bf[:],
                              mark=(fc == 7))
            tr_tok[i2] = ptok
            ctx.wait("dve", ptok, pe_o_last[i2])
            afm_ev[i2] = ctx.op("dve", "tensor_copy", afm[i2][:], ps_t[i2][:])
            ctx.wait("sp", z_tok[i2])
            ldy = ctx.dma(f"spx{i2}", out=y1t[i2][:], in_=y1_dram[tt * 128:(tt + 1) * 128, :])
            for oh in range(2):
                ctx.wait("pe", afm_ev[i2], o_ev[oh])
                for kc in range(8):
                    mm = ctx.pe("matmul", ps_o[oh][:], afm[i2][:, kc * 128:(kc + 1) * 128],
                                wv[:, kc, oh * 512:(oh + 1) * 512],
                                start=(kc == 0), stop=(kc == 7), mark=(kc == 7))
                pe_o_last[i2] = mm
                ctx.wait("dve", mm, stz_tok[i2], ln_tok[i2])
                o_ev[oh] = ctx.op("dve", "tensor_tensor", out=zt[i2][:, oh * 512:(oh + 1) * 512],
                                  in0=ps_o[oh][:], in1=ob_row[:, oh * 512:(oh + 1) * 512],
                                  op=ALU.add)
            ctx.wait("dve", ldy)
            z_tok[i2] = ctx.op("dve", "tensor_tensor", out=zt[i2][:], in0=zt[i2][:],
                               in1=y1t[i2][:], op=ALU.add)
            ctx.wait("pool", z_tok[i2])
            stz_tok[i2] = ctx.dma(f"gz{i2}", out=z_dram[tt * 128:(tt + 1) * 128, :], in_=zt[i2][:])
            ctx.wait("dve", tr2_tok[i2])
            ln_tok[i2], _, _ = emit_ln(ctx, lnb, i2, zt[i2][:], zn[i2][:], z_tok[i2])
            ctx.wait("pe", ln_tok[i2], x2_ev[i2])
            for fc in range(8):
                p2 = ctx.pe("transpose", ps_t2[i2][:, fc * 128:(fc + 1) * 128],
                            zn[i2][:, fc * 128:(fc + 1) * 128], C.ident_bf[:], mark=(fc == 7))
            tr2_tok[i2] = p2
            ctx.wait("dve", p2, stx_tok[i2])
            for fc in range(8):
                x2_ev[i2] = ctx.op("dve", "tensor_scalar",
                                   out=x2s[i2][:, fc * 128:(fc + 1) * 128],
                                   in0=ps_t2[i2][:, fc * 128:(fc + 1) * 128],
                                   scalar1=colv[:, c_ln2w + fc:c_ln2w + fc + 1],
                                   scalar2=colv[:, c_ln2b + fc:c_ln2b + fc + 1],
                                   op0=ALU.mult, op1=ALU.add)
            ctx.wait("pool", x2_ev[i2])
            for fc in range(8):
                stx_tok[i2] = ctx.dma(f"gx{i2}", out=x2t_dram[fc * 128:(fc + 1) * 128,
                                                            tt * 128:(tt + 1) * 128],
                                      in_=x2s[i2][:, fc * 128:(fc + 1) * 128])



def phase_lin_fm(ctx: Ctx, C: Consts, y_src, w_src, w_f32, colv, c_bias, dst, func, tag,
                 kd=D, O=FFN):
    nc = ctx.nc
    kch = kd // 128
    och = O // 128
    with ExitStack() as st_:
        wsb = st_.enter_context(nc.sbuf_tensor(f"lf_w_{tag}", [128, kch * O], BF16))
        ytb = [st_.enter_context(nc.sbuf_tensor(f"lf_y{i}_{tag}", [128, kch * 512], BF16)) for i in range(2)]
        ob = [st_.enter_context(nc.sbuf_tensor(f"lf_o{i}_{tag}", [128, 512], BF16)) for i in range(2)]
        ps = [st_.enter_context(nc.psum_tensor(f"lf_p{i}_{tag}", [128, 512], F32)) for i in range(2)]
        wv = wsb[:].rearrange("p (kc o) -> p kc o", kc=kch)
        if w_f32:
            wts = [ctx.dma(f"gw{kc % 2}", out=wv[:, kc, :], in_=w_src[kc * 128:(kc + 1) * 128, :])
                   for kc in range(kch)]
        else:
            wts = [ctx.dma("spw", out=wv[:, kc, :], in_=w_src[kc * 128:(kc + 1) * 128, :])
                   for kc in range(kch)]
        ctx.wait("pe", *wts)
        ld_tok = [None, None]
        e_tok = [None, None]
        st_tok = [None, None]
        pe_last = [None, None]
        nsup = NT // 512
        for s in range(nsup):
            i2 = s % 2
            ctx.wait("sp", pe_last[i2])
            for kc in range(kch):
                ld_tok[i2] = ctx.dma(f"spy{i2}", out=ytb[i2][:, kc * 512:(kc + 1) * 512],
                                     in_=y_src[kc * 128:(kc + 1) * 128, s * 512:(s + 1) * 512])
            yv = ytb[i2][:].rearrange("p (kc t) -> p kc t", kc=kch)
            for oc in range(och):
                o2 = (s * och + oc) % 2
                ctx.wait("pe", ld_tok[i2], e_tok[o2])
                for kc in range(kch):
                    mm = ctx.pe("matmul", ps[o2][:], wv[:, kc, oc * 128:(oc + 1) * 128],
                                yv[:, kc, :], start=(kc == 0), stop=(kc == kch - 1),
                                mark=(kc == kch - 1))
                pe_last[i2] = mm
                ctx.wait("act", mm, st_tok[o2])
                e_tok[o2] = ctx.op("act", "activation", out=ob[o2][:], in_=ps[o2][:],
                                   func=func, bias=colv[:, c_bias + oc:c_bias + oc + 1],
                                   scale=1.0)
                ctx.wait("pool", e_tok[o2])
                st_tok[o2] = ctx.dma(f"go{o2}", out=dst[oc * 128:(oc + 1) * 128,
                                                       s * 512:(s + 1) * 512],
                                     in_=ob[o2][:])



def phase_out_ffn(ctx: Ctx, C: Consts, src_h, wt, w_f32, colv, c_scale, c_bias,
                  x_src, extra_src, x_dst, tag):
    nc = ctx.nc
    KCH = FFN // 128
    with ExitStack() as st_:
        w3sb = st_.enter_context(nc.sbuf_tensor(f"of_w_{tag}", [128, KCH * 1024], BF16))
        rhs = [st_.enter_context(nc.sbuf_tensor(f"of_r{i}_{tag}", [128, KCH * 512], BF16))
               for i in range(2)]
        ufm = [st_.enter_context(nc.sbuf_tensor(f"of_u{i}_{tag}", [128, 8 * 512], F32)) for i in range(2)]
        xs = [st_.enter_context(nc.sbuf_tensor(f"of_x{i}_{tag}", [128, D], F32)) for i in range(2)]
        es = [st_.enter_context(nc.sbuf_tensor(f"of_e{i}_{tag}", [128, D], F32)) for i in range(2)] \
            if extra_src is not None else None
        xo = [st_.enter_context(nc.sbuf_tensor(f"of_xo{i}_{tag}", [128, D], F32)) for i in range(2)]
        psm = [st_.enter_context(nc.psum_tensor(f"of_pm{i}_{tag}", [128, 512], F32)) for i in range(2)]
        pst = [st_.enter_context(nc.psum_tensor(f"of_pt{i}_{tag}", [128, 1024], F32)) for i in range(2)]
        w3v = w3sb[:].rearrange("p (kc o) -> p kc o", kc=KCH)
        if w_f32:
            wl_toks = [ctx.dma(f"gw{kc % 2}", out=w3v[:, kc, :],
                               in_=wt[kc * 128:(kc + 1) * 128, :]) for kc in range(KCH)]
        else:
            wl_toks = [ctx.dma("spw", out=w3v[:, kc, :],
                               in_=wt[kc * 128:(kc + 1) * 128, :]) for kc in range(KCH)]
        ld_tok = [None, None]
        ev_tok = [None, None]
        tr_tok = [None, None]
        add_tok = [None, None]
        stx_tok = [None, None]
        pe_last_rhs = [None, None]
        nsup = NT // 512
        for s in range(nsup):
            i2 = s % 2
            ctx.wait("sp", pe_last_rhs[i2])
            for kc in range(KCH):
                ld_tok[i2] = ctx.dma(f"spy{i2}", out=rhs[i2][:, kc * 512:(kc + 1) * 512],
                                     in_=src_h[kc * 128:(kc + 1) * 128, s * 512:(s + 1) * 512])
            rv = rhs[i2][:].rearrange("p (kc t) -> p kc t", kc=KCH)
            ctx.wait("dve", tr_tok[i2])
            for oc in range(8):
                o2 = oc % 2
                ctx.wait("pe", ld_tok[i2], ev_tok[o2], *(wl_toks if s == 0 else []))
                for kc in range(KCH):
                    mm = ctx.pe("matmul", psm[o2][:], w3v[:, kc, oc * 128:(oc + 1) * 128],
                                rv[:, kc, :], start=(kc == 0), stop=(kc == KCH - 1),
                                mark=(kc == KCH - 1))
                pe_last_rhs[i2] = mm
                ctx.wait("dve", mm)
                if c_scale is not None:
                    ev_tok[o2] = ctx.op("dve", "tensor_scalar",
                                        out=ufm[i2][:, oc * 512:(oc + 1) * 512], in0=psm[o2][:],
                                        scalar1=colv[:, c_scale + oc:c_scale + oc + 1],
                                        scalar2=None, op0=ALU.mult)
                else:
                    ev_tok[o2] = ctx.op("dve", "tensor_scalar",
                                        out=ufm[i2][:, oc * 512:(oc + 1) * 512], in0=psm[o2][:],
                                        scalar1=colv[:, c_bias + oc:c_bias + oc + 1],
                                        scalar2=None, op0=ALU.add)
            uv = ufm[i2][:].rearrange("p (oc t) -> p oc t", oc=8)
            for j in range(4):
                it = s * 4 + j
                j2 = j % 2
                ctx.wait("sp", add_tok[j2])
                ldx = ctx.dma(f"spx{j2}", out=xs[j2][:], in_=x_src[it * 128:(it + 1) * 128, :])
                ldx2 = ctx.dma(f"spe{j2}", out=es[j2][:],
                               in_=extra_src[it * 128:(it + 1) * 128, :])                     if extra_src is not None else None
                ctx.wait("pe", ev_tok[0], ev_tok[1], add_tok[j2])
                for oc in range(8):
                    pt = ctx.pe("transpose", pst[j2][:, oc * 128:(oc + 1) * 128],
                                uv[:, oc, j * 128:(j + 1) * 128], C.ident_f[:],
                                mark=(oc == 7))
                tr_tok[i2] = pt
                ctx.wait("dve", pt, ldx, ldx2, stx_tok[j2])
                if extra_src is not None:
                    ctx.op("dve", "tensor_tensor", out=xo[j2][:], in0=pst[j2][:],
                           in1=es[j2][:], op=ALU.add)
                    add_tok[j2] = ctx.op("dve", "tensor_tensor", out=xo[j2][:], in0=xo[j2][:],
                                         in1=xs[j2][:], op=ALU.add)
                else:
                    add_tok[j2] = ctx.op("dve", "tensor_tensor", out=xo[j2][:], in0=pst[j2][:],
                                         in1=xs[j2][:], op=ALU.add)
                ctx.wait("pool", add_tok[j2])
                stx_tok[j2] = ctx.dma(f"gxo{j2}", out=x_dst[it * 128:(it + 1) * 128, :],
                                      in_=xo[j2][:])


def emit_prep_attn0(ctx: Ctx, C: Consts, ext, colv, cols: ColAlloc):
    prep_colvec(ctx, C, ext["L0_attn_q_b"], D, colv, cols.reg("q_b", 8))
    prep_colvec(ctx, C, ext["L0_attn_k_b"], D, colv, cols.reg("k_b", 8))
    prep_colvec(ctx, C, ext["L0_attn_ln1_w"], D, colv, cols.reg("ln1_w", 8))
    prep_colvec(ctx, C, ext["L0_attn_ln1_b"], D, colv, cols.reg("ln1_b", 8))
    prep_colvec(ctx, C, ext["L0_attn_ln2_w"], D, colv, cols.reg("ln2_w", 8))
    prep_colvec(ctx, C, ext["L0_attn_ln2_b"], D, colv, cols.reg("ln2_b", 8))
    prep_colvec(ctx, C, ext["L0_attn_f1_b"], FFN, colv, cols.reg("f1_b", 32))
    prep_colvec(ctx, C, ext["L0_attn_f2_b"], D, colv, cols.reg("f2_b", 8))
    ctx.barrier()


def emit_attn0_app(ctx: Ctx, C: Consts, ext, dr, colv, cols: ColAlloc, x_src, x_dst, tag,
                   upto=99):
    nc = ctx.nc
    step = [0]

    def go():
        step[0] += 1
        return step[0] <= upto
    with ExitStack() as appst:
        sctx = _StackCtx(ctx, appst)
        app = ScanApp(sctx, tag, den=True)
        vb_row = appst.enter_context(nc.sbuf_tensor(f"a_vbr_{tag}", [128, D], F32))
        ob_row = appst.enter_context(nc.sbuf_tensor(f"a_obr_{tag}", [128, D], F32))
        dec_row = appst.enter_context(nc.sbuf_tensor(f"a_dbr_{tag}", [128, 16], F32))
        prep_bcast_row(ctx, C, ext["L0_attn_v_b"], D, vb_row, C.ones_f)
        prep_bcast_row(ctx, C, ext["L0_attn_o_b"], D, ob_row, C.ones_f)
        prep_bcast_row(ctx, C, ext["L0_attn_dec_b"], 16, dec_row, C.ones_f)
        build_scan_consts(ctx, C, app, causal_mask=False)
        ctx.barrier()
        if go():
            phase_ln2(ctx, C, x_src, dr["YT"], tag + "a", y1_dst=dr["Y1"], double_ln=True,
                      colv=colv, w_cols=cols.map["ln1_w"], b_cols=cols.map["ln1_b"])
            ctx.barrier()
        if go():
            phase_proj_fm(ctx, C, dr["YT"], ext["L0_attn_q_w"], dr["QF"], tag + "q",
                          colv=colv, bias_cols=cols.map["q_b"], phi=True)
            ctx.barrier()
        if go():
            phase_proj_fm(ctx, C, dr["YT"], ext["L0_attn_k_w"], dr["KF"], tag + "k",
                          colv=colv, bias_cols=cols.map["k_b"], phi=True)
            ctx.barrier()
        if go():
            phase_proj_tok(ctx, C, dr["YT"], ext["L0_attn_v_w"], vb_row, dr["VT"],
                           ext["L0_attn_dec_w"], dec_row, app.lam, tag + "v")
            ctx.barrier()
        if go():
            phase_fm_to_tok(ctx, C, dr["KF"], dr["KTOK"], tag + "t")
            ctx.barrier()
        for b in range(B_PER_CORE):
            if go():
                emit_scan_prelude(ctx, C, app, b, dr["PROW"], 1e-8, tag + f"p{b}")
            if go():
                emit_scan_pass(ctx, C, app, b, dr["QF"], dr["KF"], dr["KTOK"], dr["VT"],
                               dr["PROW"], True, False, tag + f"f{b}")
                ctx.barrier()
            if go():
                emit_scan_pass(ctx, C, app, b, dr["QF"], None, dr["KTOK"], dr["VT"],
                               dr["PROW"], False, False, tag + f"b{b}")
                ctx.barrier()
            if go():
                emit_a3_l0(ctx, C, app, b, ext["L0_attn_o_w"], ob_row, dr["Y1"], dr["Z"],
                           dr["X2T"], colv, cols.map["ln2_w"], cols.map["ln2_b"], tag + f"o{b}")
                ctx.barrier()
    if step[0] + 1 <= upto:
        step[0] += 1
        phase_lin_fm(ctx, C, dr["X2T"], ext["L0_attn_f1_w"], True, colv, cols.map["f1_b"],
                     dr["HA"], AF.Gelu, tag + "m")
        ctx.barrier()
    if step[0] + 1 <= upto:
        step[0] += 1
        phase_out_ffn(ctx, C, dr["HA"], ext["L0_attn_f2_w"], True, colv, None,
                      cols.map["f2_b"], x_src, dr["Z"], x_dst, tag + "n")
        ctx.barrier()
    else:
        phase_copy_tok(ctx, C, x_src, x_dst, tag + "cp")
        ctx.barrier()


def phase_copy_tok(ctx: Ctx, C: Consts, src, dst, tag):
    nc = ctx.nc
    with ExitStack() as st_:
        xb = [st_.enter_context(nc.sbuf_tensor(f"cp{i}_{tag}", [128, D], F32)) for i in range(2)]
        cp = [None, None]
        st = [None, None]
        for tt in range(NTILE):
            i2 = tt % 2
            ctx.wait("sp", cp[i2])
            ld = ctx.dma(f"spx{i2}", out=xb[i2][:], in_=src[tt * 128:(tt + 1) * 128, :])
            ctx.wait("dve", ld, st[i2])
            cp[i2] = ctx.op("dve", "tensor_copy", xb[i2][:], xb[i2][:])
            ctx.wait("pool", cp[i2])
            st[i2] = ctx.dma(f"gxo{i2}", out=dst[tt * 128:(tt + 1) * 128, :], in_=xb[i2][:])



CAP = 512
NE = 8


class MoeApp:
    def __init__(self, ctx, tag):
        nc, st_ = ctx.nc, ctx.stack
        self.probs = st_.enter_context(nc.sbuf_tensor(f"mo_pr_{tag}", [128, NTILE * NE], F32))
        self.cpart = st_.enter_context(nc.sbuf_tensor(f"mo_cp_{tag}", [128, 128], F32))
        self.cfree = st_.enter_context(nc.sbuf_tensor(f"mo_cf_{tag}", [128, 128], F32))
        self.epart = st_.enter_context(nc.sbuf_tensor(f"mo_ep_{tag}", [128, 128], F32))
        self.efree = st_.enter_context(nc.sbuf_tensor(f"mo_ef_{tag}", [128, 128], F32))
        self.bmat = st_.enter_context(nc.sbuf_tensor(f"mo_bm_{tag}", [128, 128], F32))
        self.eoff = st_.enter_context(nc.sbuf_tensor(f"mo_eo_{tag}", [128, 128], F32))
        self.big = st_.enter_context(nc.sbuf_tensor(f"mo_bg_{tag}", [128, 128], F32))
        self.trils = st_.enter_context(nc.sbuf_tensor(f"mo_ts_{tag}", [128, 128], F32))
        self.bigi = st_.enter_context(nc.sbuf_tensor(f"mo_bi_{tag}", [128, 64], I32))
        self.psum_guard = None


def build_moe_consts(ctx: Ctx, C: Consts, mo: MoeApp, nch):
    nc = ctx.nc
    ctx.wait("pe", C.ready)
    with ExitStack() as st_:
        ps = st_.enter_context(nc.psum_tensor(_uq("moc_ps"), [128, 128], F32))
        ctx.op("dve", "memset", mo.cfree[:], 0.0)
        ctx.op("dve", "memset", mo.efree[:], 0.0)
        tz = ctx.tok("dve")
        ctx.wait("pool", tz)
        t1 = ctx.op("pool", "iota",
                    mo.cfree[:, 0:nch * NE].rearrange("p (c e) -> p c e", c=nch),
                    [[1, nch], [0, NE]], base=0, channel_multiplier=0,
                    allow_small_or_imprecise_dtypes=True)
        t2 = ctx.op("pool", "iota",
                    mo.efree[:, 0:nch * NE].rearrange("p (c e) -> p c e", c=nch),
                    [[0, nch], [1, NE]], base=0, channel_multiplier=0,
                    allow_small_or_imprecise_dtypes=True)
        ctx.wait("pe", t1, t2)
        pt = ctx.pe("transpose", ps[:], mo.cfree[:], C.ident_f[:], mark=True)
        ctx.wait("dve", pt)
        tc = ctx.op("dve", "tensor_copy", mo.cpart[:], ps[:])
        ctx.wait("pe", tc)
        pt2 = ctx.pe("transpose", ps[:], mo.efree[:], C.ident_f[:], mark=True)
        ctx.wait("dve", pt2)
        ctx.op("dve", "tensor_copy", mo.epart[:], ps[:])
        ctx.op("dve", "tensor_tensor", out=mo.bmat[:], in0=mo.cpart[:], in1=mo.cfree[:],
               op=ALU.is_lt)
        ctx.op("dve", "tensor_tensor", out=mo.big[:], in0=mo.epart[:], in1=mo.efree[:],
               op=ALU.is_equal)
        ctx.op("dve", "tensor_tensor", out=mo.bmat[:], in0=mo.bmat[:], in1=mo.big[:],
               op=ALU.mult)
        ctx.op("dve", "tensor_scalar", out=mo.eoff[:], in0=mo.efree[:], scalar1=float(CAP),
               scalar2=None, op0=ALU.mult)
        ctx.op("dve", "memset", mo.big[:], 100000.0)
        ctx.op("dve", "memset", mo.bigi[:], 100000)
        sc1 = st_.enter_context(nc.sbuf_tensor(_uq("moc_s1"), [128, 128], F32))
        t3 = ctx.op("pool", "iota", sc1[:], [[1, 128]], base=0, channel_multiplier=0,
                    allow_small_or_imprecise_dtypes=True)
        ctx.wait("dve", t3, C.t_iota_p)
        ctx.op("dve", "tensor_scalar", out=mo.trils[:], in0=sc1[:], scalar1=C.iota_p[:],
               scalar2=None, op0=ALU.is_gt)
        ctx.barrier()


def phase_moe_m0(ctx: Ctx, C: Consts, mo: MoeApp, x_src, gate_w, ytok_dst, pr_dram, tag):
    nc = ctx.nc
    with ExitStack() as st_:
        xb = [st_.enter_context(nc.sbuf_tensor(f"m0x{i}_{tag}", [128, D], F32)) for i in range(2)]
        yb = [st_.enter_context(nc.sbuf_tensor(f"m0y{i}_{tag}", [128, D], BF16)) for i in range(2)]
        y3t = [st_.enter_context(nc.sbuf_tensor(f"m0t{i}_{tag}", [128, 8 * 128], BF16)) for i in range(2)]
        gsb = st_.enter_context(nc.sbuf_tensor(f"m0g_{tag}", [128, 8 * NE], BF16))
        prb = [st_.enter_context(nc.sbuf_tensor(f"m0p{i}_{tag}", [128, NE], F32)) for i in range(2)]
        mx = [st_.enter_context(nc.sbuf_tensor(f"m0m{i}_{tag}", [128, 1], F32)) for i in range(2)]
        ps = [st_.enter_context(nc.psum_tensor(f"m0ps{i}_{tag}", [128, 1024], BF16)) for i in range(2)]
        psc = [st_.enter_context(nc.psum_tensor(f"m0pc{i}_{tag}", [128, NE], F32)) for i in range(2)]
        lb = LnBufs(_StackCtx(ctx, st_), tag + "ln")
        gv = gsb[:].rearrange("p (kc e) -> p kc e", kc=8)
        wts = [ctx.dma("gw0", out=gv[:, kc, :], in_=gate_w[kc * 128:(kc + 1) * 128, :])
               for kc in range(8)]
        ctx.wait("pe", *wts, C.ready)
        norm_tok = [None, None]
        tr_tok = [None, None]
        ev_tok = [None, None]
        sm_tok = [None, None]
        sty = [None, None]
        stp = [None, None]
        for tt in range(NTILE):
            i2 = tt % 2
            ctx.wait("sp", norm_tok[i2])
            ldx = ctx.dma(f"spx{i2}", out=xb[i2][:], in_=x_src[tt * 128:(tt + 1) * 128, :])
            ctx.wait("dve", tr_tok[i2], sty[i2])
            tnorm, _, _ = emit_ln(ctx, lb, i2, xb[i2][:], yb[i2][:], ldx)
            norm_tok[i2] = tnorm
            ctx.wait("pool", tnorm)
            sty[i2] = ctx.dma(f"gyk{i2}", out=ytok_dst[tt * 128:(tt + 1) * 128, :], in_=yb[i2][:])
            ctx.wait("pe", tnorm, ev_tok[i2])
            for kc in range(8):
                pt = ctx.pe("transpose", ps[i2][:, kc * 128:(kc + 1) * 128],
                            yb[i2][:, kc * 128:(kc + 1) * 128], C.ident_bf[:], mark=(kc == 7))
            tr_tok[i2] = pt
            ctx.wait("dve", pt)
            ev_tok[i2] = ctx.op("dve", "tensor_copy", y3t[i2][:], ps[i2][:])
            ctx.wait("pe", ev_tok[i2], sm_tok[i2])
            for kc in range(8):
                mm = ctx.pe("matmul", psc[i2][:], y3t[i2][:, kc * 128:(kc + 1) * 128],
                            gv[:, kc, :], start=(kc == 0), stop=(kc == 7), mark=(kc == 7))
            ctx.wait("dve", mm, stp[i2])
            ctx.op("dve", "tensor_reduce", out=mx[i2][:], in_=psc[i2][:], axis=AX.X,
                   op=ALU.max, negate=True)
            ctx.op("dve", "tensor_scalar", out=prb[i2][:], in0=psc[i2][:],
                   scalar1=mx[i2][:], scalar2=None, op0=ALU.add)
            tsm = ctx.tok("dve")
            ctx.wait("act", tsm)
            ctx.op("act", "activation", out=prb[i2][:], in_=prb[i2][:], func=AF.Exp,
                   accum_out=mx[i2][:])
            ta = ctx.tok("act")
            ctx.wait("dve", ta)
            ctx.op("dve", "reciprocal", out=mx[i2][:], in_=mx[i2][:])
            sm_tok[i2] = ctx.op("dve", "tensor_scalar", out=mo.probs[:, tt * NE:(tt + 1) * NE],
                                in0=prb[i2][:], scalar1=mx[i2][:], scalar2=None, op0=ALU.mult)
            ctx.wait("pool", sm_tok[i2])
            stp[i2] = ctx.dma(f"gpr{i2}", out=pr_dram[tt * 128:(tt + 1) * 128, :],
                              in_=mo.probs[:, tt * NE:(tt + 1) * NE])


def phase_moe_route(ctx: Ctx, C: Consts, mo: MoeApp, b, ytok, gx_dram, tok_dram, pv_dram,
                    appst, tag, n_iter=40):
    nc = ctx.nc
    nch = T // 128
    CW = nch * NE
    st_ = appst
    if True:
        lo = st_.enter_context(nc.sbuf_tensor(f"rt_lo_{tag}", [128, NE], F32))
        hi = st_.enter_context(nc.sbuf_tensor(f"rt_hi_{tag}", [128, NE], F32))
        thr = st_.enter_context(nc.sbuf_tensor(f"rt_th_{tag}", [128, NE], F32))
        sel = st_.enter_context(nc.sbuf_tensor(f"rt_se_{tag}", [128, NE], F32))
        cmp_ = st_.enter_context(nc.sbuf_tensor(f"rt_cm_{tag}", [128, CW], F32))
        part = st_.enter_context(nc.sbuf_tensor(f"rt_pa_{tag}", [128, NE], F32))
        cnt1 = st_.enter_context(nc.sbuf_tensor(f"rt_c1_{tag}", [1, NE], F32))
        cbc = st_.enter_context(nc.sbuf_tensor(f"rt_cb_{tag}", [128, NE], F32))
        mask = st_.enter_context(nc.sbuf_tensor(f"rt_mk_{tag}", [128, CW], F32))
        csr = st_.enter_context(nc.sbuf_tensor(f"rt_cs_{tag}", [1, CW], F32))
        csc = st_.enter_context(nc.sbuf_tensor(f"rt_cc_{tag}", [128, 1], F32))
        offr = st_.enter_context(nc.sbuf_tensor(f"rt_of_{tag}", [1, CW], F32))
        rank = st_.enter_context(nc.sbuf_tensor(f"rt_rk_{tag}", [128, CW], F32))
        dd = st_.enter_context(nc.sbuf_tensor(f"rt_dd_{tag}", [128, CW], F32))
        di = st_.enter_context(nc.sbuf_tensor(f"rt_di_{tag}", [128, CW], I32))
        tcol = st_.enter_context(nc.sbuf_tensor(f"rt_tc_{tag}", [128, 1], F32))
        tcoli = st_.enter_context(nc.sbuf_tensor(f"rt_ti_{tag}", [128, nch], I32))
        yt = [st_.enter_context(nc.sbuf_tensor(f"rt_yt{i}_{tag}", [128, D], BF16)) for i in range(2)]
        pstack = ExitStack()
        pp1 = pstack.enter_context(nc.psum_tensor(f"rt_p1_{tag}", [1, CW], F32))
        pp2 = pstack.enter_context(nc.psum_tensor(f"rt_p2_{tag}", [128, CW], F32))
        ppc = pstack.enter_context(nc.psum_tensor(f"rt_pc_{tag}", [128, 1], F32))
        prs = mo.probs[:, b * CW:(b + 1) * CW]
        prv = prs.rearrange("p (c e) -> p c e", c=nch)
        ctx.wait("pe", C.ready, mo.psum_guard)
        ctx.op("dve", "memset", lo[:], 0.0)
        ctx.op("dve", "memset", hi[:], 1.0)
        ctx.op("dve", "tensor_scalar", out=thr[:], in0=hi[:], scalar1=0.5, scalar2=None,
               op0=ALU.mult)
        for it in range(n_iter):
            ctx.op("dve", "tensor_tensor",
                   out=cmp_[:].rearrange("p (e c) -> p c e", e=NE),
                   in0=prv, in1=thr[:].unsqueeze(1).broadcast_to([128, nch, NE]), op=ALU.is_ge)
            ctx.op("dve", "tensor_reduce", out=part[:],
                   in_=cmp_[:].rearrange("p (e c) -> p e c", e=NE), axis=AX.X, op=ALU.add)
            tpart = ctx.tok("dve")
            ctx.wait("pe", tpart)
            m2 = ctx.pe("matmul", pp2[:, 0:NE], C.ones_f[:, :], part[:], start=True,
                        stop=True, mark=True)
            ctx.wait("dve", m2)
            ctx.op("dve", "tensor_scalar", out=sel[:], in0=pp2[:, 0:NE], scalar1=float(CAP),
                   scalar2=None, op0=ALU.is_ge)
            emit_select(ctx, lo[:], sel[:], thr[:], lo[:], part[:])
            emit_select(ctx, hi[:], sel[:], hi[:], thr[:], part[:])
            ctx.op("dve", "tensor_tensor", out=thr[:], in0=lo[:], in1=hi[:], op=ALU.add)
            ctx.op("dve", "tensor_scalar", out=thr[:], in0=thr[:], scalar1=0.5, scalar2=None,
                   op0=ALU.mult)
        ctx.op("dve", "tensor_tensor", out=mask[:].rearrange("p (c e) -> p c e", c=nch),
               in0=prv, in1=lo[:].unsqueeze(1).broadcast_to([128, nch, NE]), op=ALU.is_ge)
        tmk = ctx.tok("dve")
        ctx.wait("pe", tmk)
        m3 = ctx.pe("matmul", pp1[0:1, :], C.ones_f[:, 0:1], mask[:], start=True, stop=True,
                    mark=True)
        ctx.wait("dve", m3)
        tcs = ctx.op("dve", "tensor_copy", csr[:], pp1[0:1, :])
        ctx.wait("pe", tcs)
        m4 = ctx.pe("transpose", ppc[0:CW, 0:1], csr[0:1, 0:CW], C.ident_f[0:1, 0:1], mark=True)
        ctx.wait("dve", m4)
        tcc = ctx.op("dve", "tensor_copy", csc[0:CW, :], ppc[0:CW, 0:1])
        ctx.wait("pe", tcc)
        m5 = ctx.pe("matmul", pp1[0:1, :], csc[0:CW, 0:1], mo.bmat[0:CW, 0:CW], start=True,
                    stop=True, mark=True)
        ctx.wait("dve", m5)
        tof = ctx.op("dve", "tensor_copy", offr[:], pp1[0:1, :])
        ctx.wait("pe", tof)
        ctx.pe("matmul", pp2[:], mo.trils[:], mask[:], start=True, stop=False)
        m6 = ctx.pe("matmul", pp2[:], C.ones_f[0:1, :], offr[0:1, :], start=False, stop=True,
                    mark=True)
        ctx.wait("dve", m6)
        ctx.op("dve", "tensor_copy", rank[:], pp2[:])
        ctx.op("dve", "tensor_scalar", out=cmp_[:], in0=rank[:], scalar1=float(CAP),
               scalar2=None, op0=ALU.is_lt)
        ctx.op("dve", "tensor_tensor", out=cmp_[:], in0=cmp_[:], in1=mask[:], op=ALU.mult)
        ctx.op("dve", "tensor_tensor", out=rank[:],
               in0=rank[:], in1=mo.eoff[0:128, 0:CW], op=ALU.add)
        emit_select(ctx, dd[:], cmp_[:], rank[:], mo.big[0:128, 0:CW], rank[:])
        tdi = ctx.op("dve", "tensor_copy", di[:], dd[:])
        mo.psum_guard = tdi
        pstack.close()
        for c in range(nch):
            ctx.op("dve", "tensor_scalar", out=tcol[:], in0=C.iota_p[:],
                   scalar1=float(b * T + c * 128), scalar2=None, op0=ALU.add)
            ctx.op("dve", "tensor_copy", tcoli[:, c:c + 1], tcol[:])
        ttc = ctx.tok("dve")
        ctx.wait("pool", tdi, ttc)
        it_tok = ctx.dma(f"gtk{b}", out=tok_dram[:].rearrange("(a b) x -> a (b x)", a=128),
                         in_=mo.bigi[:, 0:(NE * CAP) // 128])
        div = di[:].rearrange("p (c e) -> p c e", c=nch)
        e_toks = []
        sc_par = [None, None]
        for e in range(NE):
            ld_tok = [None, None]
            for c in range(nch):
                i2 = c % 2
                ctx.wait("sp", sc_par[i2])
                ld_tok[i2] = ctx.dma(f"sprt{b}{i2}", out=yt[i2][:],
                                     in_=ytok[(b * nch + c) * 128:(b * nch + c + 1) * 128, :])
                ctx.wait("pool", ld_tok[i2], it_tok)
                off = bass.IndirectOffsetOnAxis(div[:, c, e:e + 1], 0)
                stream = f"gsc{b}{e % 2}{c % 2}"
                ctx.idma(stream, out=gx_dram, out_offset=off, in_=yt[i2][:],
                         in_offset=None, bounds_check=NE * CAP - 1, oob_is_err=False)
                ctx.idma(stream, out=tok_dram, out_offset=off,
                         in_=tcoli[:, c:c + 1], in_offset=None,
                         bounds_check=NE * CAP - 1, oob_is_err=False)
                sc_par[i2] = ctx.idma(stream, out=pv_dram, out_offset=off,
                                      in_=prs[:, c * NE + e:c * NE + e + 1], in_offset=None,
                                      bounds_check=NE * CAP - 1, oob_is_err=False)
            e_toks.append((sc_par[0], sc_par[1]))
        return e_toks


def phase_moe_experts(ctx: Ctx, C: Consts, mo: MoeApp, w1_dram, w2_dram, gx_dram, tok_dram,
                      pv_dram, mout_dram, tag):
    nc = ctx.nc
    with ExitStack() as st_:
        w1sb = st_.enter_context(nc.sbuf_tensor(f"ex_w1_{tag}", [128, 8 * FFN], BF16))
        w2sb = st_.enter_context(nc.sbuf_tensor(f"ex_w2_{tag}", [128, 32 * 1024], BF16))
        xt = [st_.enter_context(nc.sbuf_tensor(f"ex_x{i}_{tag}", [128, 1024], BF16)) for i in range(2)]
        xfm = st_.enter_context(nc.sbuf_tensor(f"ex_xf_{tag}", [128, 8 * CAP], BF16))
        ht = st_.enter_context(nc.sbuf_tensor(f"ex_h_{tag}", [128, 32 * CAP], BF16))
        ysb = [st_.enter_context(nc.sbuf_tensor(f"ex_y{i}_{tag}", [128, 1024], F32)) for i in range(2)]
        pcol = [st_.enter_context(nc.sbuf_tensor(f"ex_p{i}_{tag}", [128, 1], F32)) for i in range(2)]
        tcol = [st_.enter_context(nc.sbuf_tensor(f"ex_t{i}_{tag}", [128, 1], I32)) for i in range(2)]
        pst = [st_.enter_context(nc.psum_tensor(f"ex_pt{i}_{tag}", [128, 1024], BF16)) for i in range(2)]
        psm = [st_.enter_context(nc.psum_tensor(f"ex_pm{i}_{tag}", [128, 512], F32)) for i in range(2)]
        w1v = w1sb[:].rearrange("p (kc o) -> p kc o", kc=8)
        w2v = w2sb[:].rearrange("p (kc o) -> p kc o", kc=32)
        xfv = xfm[:].rearrange("p (kc t) -> p kc t", kc=8)
        htv = ht[:].rearrange("p (oc t) -> p oc t", oc=32)
        w_tok = None
        pe_last_w = None
        tr_tok = [None, None]
        ev_x = [None, None]
        g_tok = [None, None]
        y_ev = [None, None]
        sc_tok = None
        pe_h_last = None
        pe_y_last = None
        for e in range(NE):
            ctx.wait("pool", pe_last_w)
            wt = []
            for kc in range(8):
                wt.append(ctx.dma(f"gw{kc % 2}", out=w1v[:, kc, :],
                                  in_=w1_dram[e, kc * 128:(kc + 1) * 128, :]))
            for kc in range(32):
                wt.append(ctx.dma(f"gw{kc % 2}", out=w2v[:, kc, :],
                                  in_=w2_dram[e, kc * 128:(kc + 1) * 128, :]))
            RB = CAP // 128
            for rb in range(RB):
                i2 = rb % 2
                ctx.wait("sp", pe_h_last, tr_tok[i2])
                ld = ctx.dma(f"spy{i2}", out=xt[i2][:],
                             in_=gx_dram[e * CAP + rb * 128:e * CAP + (rb + 1) * 128, :])
                ctx.wait("pe", ld, ev_x[i2], C.ready)
                for kc in range(8):
                    pt = ctx.pe("transpose", pst[i2][:, kc * 128:(kc + 1) * 128],
                                xt[i2][:, kc * 128:(kc + 1) * 128], C.ident_bf[:],
                                mark=(kc == 7))
                tr_tok[i2] = pt
                ctx.wait("dve", pt)
                ev_x[i2] = ctx.op("dve", "tensor_copy",
                                  xfv[:, :, rb * 128:(rb + 1) * 128],
                                  pst[i2][:].rearrange("p (kc t) -> p kc t", kc=8))
            for oc in range(32):
                o2 = oc % 2
                ctx.wait("pe", ev_x[0], ev_x[1], g_tok[o2], *wt)
                for kc in range(8):
                    mm = ctx.pe("matmul", psm[o2][:, 0:CAP], w1v[:, kc, oc * 128:(oc + 1) * 128],
                                xfv[:, kc, :], start=(kc == 0), stop=(kc == 7), mark=(kc == 7))
                pe_h_last = mm
                pe_last_w = mm
                ctx.wait("act", mm, pe_y_last)
                g_tok[o2] = ctx.op("act", "activation", out=htv[:, oc, :], in_=psm[o2][:, 0:CAP],
                                   func=AF.Gelu)
            for rb in range(RB):
                for oh in range(2):
                    o2 = (rb * 2 + oh) % 2
                    ctx.wait("pe", g_tok[0], g_tok[1], y_ev[o2])
                    for kc in range(32):
                        mm = ctx.pe("matmul", psm[o2][:],
                                    htv[:, kc, rb * 128:(rb + 1) * 128],
                                    w2v[:, kc, oh * 512:(oh + 1) * 512],
                                    start=(kc == 0), stop=(kc == 31), mark=(kc == 31))
                    pe_y_last = mm
                    pe_last_w = mm
                    if oh == 0:
                        ctx.wait("sp", y_ev[0], y_ev[1], sc_tok)
                        ldp = ctx.dma(f"spp{rb % 2}", out=pcol[rb % 2][:],
                                      in_=pv_dram[e * CAP + rb * 128:e * CAP + (rb + 1) * 128, :])
                        ldt = ctx.dma(f"spt{rb % 2}", out=tcol[rb % 2][:],
                                      in_=tok_dram[e * CAP + rb * 128:e * CAP + (rb + 1) * 128, :])
                        ctx.wait("dve", ldp)
                        ctx.op("dve", "tensor_scalar", out=pcol[rb % 2][:], in0=pcol[rb % 2][:],
                               scalar1=0.5, scalar2=None, op0=ALU.mult)
                    ctx.wait("dve", mm, sc_tok)
                    y_ev[o2] = ctx.op("dve", "tensor_scalar",
                                      out=ysb[rb % 2][:, oh * 512:(oh + 1) * 512],
                                      in0=psm[o2][:], scalar1=pcol[rb % 2][:], scalar2=None,
                                      op0=ALU.mult)
                ctx.wait("pool", y_ev[0], y_ev[1], ldt)
                sc_tok = ctx.idma("gsc", out=mout_dram,
                                  out_offset=bass.IndirectOffsetOnAxis(tcol[rb % 2][:], 0),
                                  in_=ysb[rb % 2][:], in_offset=None,
                                  bounds_check=NT - 1, oob_is_err=False,
                                  compute_op=ALU.add)
        ctx.barrier()


def phase_zero_dram(ctx: Ctx, dst, rows, tag):
    nc = ctx.nc
    with ExitStack() as st_:
        z = st_.enter_context(nc.sbuf_tensor(f"z_{tag}", [128, D], F32))
        t = ctx.op("dve", "memset", z[:], 0.0)
        ctx.wait("pool", t)
        for r in range(rows // 128):
            ctx.dma("gz0", out=dst[r * 128:(r + 1) * 128, :], in_=z[:])
        ctx.barrier()


def phase_add_from(ctx: Ctx, C: Consts, x_src, add_src, x_dst, tag):
    nc = ctx.nc
    with ExitStack() as st_:
        a = [st_.enter_context(nc.sbuf_tensor(f"ad_a{i}_{tag}", [128, D], F32)) for i in range(2)]
        bb = [st_.enter_context(nc.sbuf_tensor(f"ad_b{i}_{tag}", [128, D], F32)) for i in range(2)]
        o = [st_.enter_context(nc.sbuf_tensor(f"ad_o{i}_{tag}", [128, D], F32)) for i in range(2)]
        add_tok = [None, None]
        st_tok = [None, None]
        for tt in range(NTILE):
            i2 = tt % 2
            ctx.wait("sp", add_tok[i2])
            l1 = ctx.dma(f"spx{i2}", out=a[i2][:], in_=x_src[tt * 128:(tt + 1) * 128, :])
            l2 = ctx.dma(f"spe{i2}", out=bb[i2][:], in_=add_src[tt * 128:(tt + 1) * 128, :])
            ctx.wait("dve", l1, l2, st_tok[i2])
            add_tok[i2] = ctx.op("dve", "tensor_tensor", out=o[i2][:], in0=a[i2][:],
                                 in1=bb[i2][:], op=ALU.add)
            ctx.wait("pool", add_tok[i2])
            st_tok[i2] = ctx.dma(f"gxo{i2}", out=x_dst[tt * 128:(tt + 1) * 128, :], in_=o[i2][:])


def emit_moe_app(ctx: Ctx, C: Consts, ext, dr, x_src, x_dst, tag, upto=99):
    nc = ctx.nc
    nch = T // 128
    step = [0]

    def go():
        step[0] += 1
        return step[0] <= upto
    with ExitStack() as appst:
        sctx = _StackCtx(ctx, appst)
        mo = MoeApp(sctx, tag)
        build_moe_consts(ctx, C, mo, nch)
        if go():
            phase_zero_dram(ctx, dr["MOUT"], NT, tag + "z")
        if go():
            phase_moe_m0(ctx, C, mo, x_src, ext["L1_ffn_gate_w"], dr["YTOK"], dr["PR"],
                         tag + "a")
            ctx.barrier()
        route_toks = []
        for b in range(B_PER_CORE):
            if go():
                route_toks.append(phase_moe_route(
                    ctx, C, mo, b, dr["YTOK"], dr[f"GX{b}"], dr[f"TOK{b}"],
                    dr[f"PV{b}"], sctx.stack, tag + f"r{b}"))
        if go():
            phase_moe_experts2(ctx, C, mo, dr["MW1C"], dr["MW2C"],
                               [dr["GX0"], dr["GX1"]], [dr["TOK0"], dr["TOK1"]],
                               [dr["PV0"], dr["PV1"]], dr["MOUT"], dr["_mw_cast_tok"],
                               tag + "e", route_toks=route_toks if len(route_toks) == 2 else None)
    phase_add_from(ctx, C, x_src, dr["MOUT"], x_dst, tag + "f")
    ctx.barrier()



def emit_a3_l1(ctx: Ctx, C: Consts, app: ScanApp, b, wo_src, x_src, x_dst, tag):
    nc = ctx.nc
    W = app.w
    with ExitStack() as st_:
        ow_sb = st_.enter_context(nc.sbuf_tensor(f"b3_ow_{tag}", [128, 8 * 1024], BF16))
        atile = [st_.enter_context(nc.sbuf_tensor(f"b3_at{i}_{tag}", [128, 1024], BF16)) for i in range(2)]
        afm = [st_.enter_context(nc.sbuf_tensor(f"b3_af{i}_{tag}", [128, 1024], BF16)) for i in range(2)]
        xs = [st_.enter_context(nc.sbuf_tensor(f"b3_x{i}_{tag}", [128, 1024], F32)) for i in range(2)]
        zt = [st_.enter_context(nc.sbuf_tensor(f"b3_z{i}_{tag}", [128, 1024], F32)) for i in range(2)]
        ps_t = [st_.enter_context(nc.psum_tensor(f"b3_pt{i}_{tag}", [128, 1024], BF16)) for i in range(2)]
        ps_o = [st_.enter_context(nc.psum_tensor(f"b3_po{i}_{tag}", [128, 512], F32)) for i in range(2)]
        wv = ow_sb[:].rearrange("p (kc o) -> p kc o", kc=8)
        _load_transposed_into(ctx, C, wo_src, wv, 1024, tag + "w")
        outv = app.out_sb[:].rearrange("p (c h w) -> p c h w", c=app.ch, h=H)
        tr_tok = [None, None]
        afm_ev = [None, None]
        o_ev = [None, None]
        z_tok = [None, None]
        st_tok = [None, None]
        pe_o_last = [None, None]
        for ti in range(app.ch):
            tt = b * app.ch + ti
            i2 = ti % 2
            ctx.wait("dve", tr_tok[i2])
            at_tok = ctx.op("dve", "tensor_copy", atile[i2][:],
                            app.out_sb[:, ti * H * W:(ti + 1) * H * W])
            ctx.wait("pe", at_tok, afm_ev[i2])
            for fc in range(8):
                ptok = ctx.pe("transpose", ps_t[i2][:, fc * 128:(fc + 1) * 128],
                              atile[i2][:, fc * 128:(fc + 1) * 128], C.ident_bf[:],
                              mark=(fc == 7))
            tr_tok[i2] = ptok
            ctx.wait("dve", ptok, pe_o_last[i2])
            afm_ev[i2] = ctx.op("dve", "tensor_copy", afm[i2][:], ps_t[i2][:])
            ctx.wait("sp", z_tok[i2])
            ldx = ctx.dma(f"spx{i2}", out=xs[i2][:], in_=x_src[tt * 128:(tt + 1) * 128, :])
            for oh in range(2):
                ctx.wait("pe", afm_ev[i2], o_ev[oh])
                for kc in range(8):
                    mm = ctx.pe("matmul", ps_o[oh][:], afm[i2][:, kc * 128:(kc + 1) * 128],
                                wv[:, kc, oh * 512:(oh + 1) * 512],
                                start=(kc == 0), stop=(kc == 7), mark=(kc == 7))
                pe_o_last[i2] = mm
                ctx.wait("dve", mm, st_tok[i2], ldx)
                o_ev[oh] = ctx.op("dve", "tensor_tensor", out=zt[i2][:, oh * 512:(oh + 1) * 512],
                                  in0=ps_o[oh][:], in1=xs[i2][:, oh * 512:(oh + 1) * 512],
                                  op=ALU.add)
            z_tok[i2] = o_ev[1]
            ctx.wait("pool", z_tok[i2])
            st_tok[i2] = ctx.dma(f"gxo{i2}", out=x_dst[tt * 128:(tt + 1) * 128, :], in_=zt[i2][:])


def emit_attn1_app(ctx: Ctx, C: Consts, ext, dr, x_src, x_dst, tag):
    nc = ctx.nc
    with ExitStack() as appst:
        sctx = _StackCtx(ctx, appst)
        app = ScanApp(sctx, tag, den=False)
        zrow = appst.enter_context(nc.sbuf_tensor(f"b_zr_{tag}", [128, D], F32))
        grow = appst.enter_context(nc.sbuf_tensor(f"b_gr_{tag}", [128, 16], F32))
        ctx.op("dve", "memset", zrow[:], 0.0)
        prep_bcast_row(ctx, C, ext["L1_attn_bg"], 16, grow, C.ones_f)
        build_scan_consts(ctx, C, app, causal_mask=True)
        ctx.barrier()
        phase_ln2(ctx, C, x_src, dr["YT"], tag + "a")
        ctx.barrier()
        phase_proj_fm(ctx, C, dr["YT"], ext["L1_attn_Wq"], dr["QF"], tag + "q",
                      phi=True, phi_scale=0.125, w_transposed_src=True)
        ctx.barrier()
        phase_proj_fm(ctx, C, dr["YT"], ext["L1_attn_Wk"], dr["KF"], tag + "k",
                      phi=True, w_transposed_src=True)
        ctx.barrier()
        phase_proj_tok(ctx, C, dr["YT"], ext["L1_attn_Wv"], zrow, dr["VT"],
                       ext["L1_attn_Wg"], grow, app.lam, tag + "v",
                       wv_transposed=True, wd_transposed=True)
        ctx.barrier()
        phase_fm_to_tok(ctx, C, dr["KF"], dr["KTOK"], tag + "t")
        ctx.barrier()
        for b in range(B_PER_CORE):
            emit_scan_prelude(ctx, C, app, b, dr["PROW"], 1e-30, tag + f"p{b}")
            emit_scan_pass(ctx, C, app, b, dr["QF"], dr["KF"], dr["KTOK"], dr["VT"],
                           dr["PROW"], True, True, tag + f"f{b}")
            ctx.barrier()
            emit_a3_l1(ctx, C, app, b, ext["L1_attn_Wo"], x_src, x_dst, tag + f"o{b}")
            ctx.barrier()



def build_core_program(marks=None):
    def mark(name):
        if marks is not None:
            marks.append((name, nc.n_instructions()))
    nc = bass.Bass("TRN2", target_bir_lowering=False)
    ext = {}
    ext["L0_attn_x"] = nc.dram_tensor("L0_attn_x", [NT, D], F32, kind="ExternalInput").ap()
    specs = {
        "L0_attn_q_w": [D, D], "L0_attn_q_b": [D], "L0_attn_k_w": [D, D], "L0_attn_k_b": [D],
        "L0_attn_v_w": [D, D], "L0_attn_v_b": [D], "L0_attn_o_w": [D, D], "L0_attn_o_b": [D],
        "L0_attn_ln1_w": [D], "L0_attn_ln1_b": [D], "L0_attn_ln2_w": [D], "L0_attn_ln2_b": [D],
        "L0_attn_dec_w": [D, 16], "L0_attn_dec_b": [16],
        "L0_attn_f1_w": [D, FFN], "L0_attn_f1_b": [FFN],
        "L0_attn_f2_w": [FFN, D], "L0_attn_f2_b": [D],
        "L0_ffn_w1_w": [FFN, D], "L0_ffn_w1_s": [FFN], "L0_ffn_w2_w": [FFN, D],
        "L0_ffn_w2_s": [FFN], "L0_ffn_h1_w": [FFN, FFN], "L0_ffn_h1_s": [FFN],
        "L0_ffn_h2_w": [FFN, FFN], "L0_ffn_h2_s": [FFN], "L0_ffn_w3_w": [D, FFN],
        "L0_ffn_w3_s": [D],
        "L1_attn_Wq": [D, D], "L1_attn_Wk": [D, D], "L1_attn_Wv": [D, D],
        "L1_attn_Wo": [D, D], "L1_attn_Wg": [16, D], "L1_attn_bg": [16],
        "L1_ffn_gate_w": [D, NE], "L1_ffn_w1": [NE, D, FFN], "L1_ffn_w2": [NE, FFN, D],
    }
    for nm, s in specs.items():
        ext[nm] = nc.dram_tensor(nm, s, F32, kind="ExternalInput").ap()
    xout = nc.dram_tensor("xout", [NT, D], F32, kind="ExternalOutput").ap()

    dr = {}
    dr["XA"] = nc.dram_tensor("XA", [NT, D], F32, kind="Internal").ap()
    dr["XB"] = nc.dram_tensor("XB", [NT, D], F32, kind="Internal").ap()
    dr["YT"] = nc.dram_tensor("YT", [D, NT], BF16, kind="Internal").ap()
    dr["Y1"] = nc.dram_tensor("Y1", [NT, D], F32, kind="Internal").ap()
    dr["Z"] = nc.dram_tensor("Z", [NT, D], F32, kind="Internal").ap()
    dr["X2T"] = nc.dram_tensor("X2T", [D, NT], BF16, kind="Internal").ap()
    dr["QF"] = nc.dram_tensor("QF", [D, NT], BF16, kind="Internal").ap()
    dr["KF"] = nc.dram_tensor("KF", [D, NT], BF16, kind="Internal").ap()
    dr["KTOK"] = nc.dram_tensor("KTOK", [NT, D], BF16, kind="Internal").ap()
    dr["VT"] = nc.dram_tensor("VT", [NT, D], BF16, kind="Internal").ap()
    dr["HA"] = nc.dram_tensor("HA", [FFN, NT], BF16, kind="Internal").ap()
    dr["HB"] = nc.dram_tensor("HB", [FFN, NT], BF16, kind="Internal").ap()
    dr["PROW"] = nc.dram_tensor("PROW", [(T // 128) * 16, 128], F32, kind="Internal").ap()
    dr["YTOK"] = nc.dram_tensor("YTOK", [NT, D], BF16, kind="Internal").ap()
    dr["PR"] = nc.dram_tensor("PR", [NT, NE], F32, kind="Internal").ap()
    for b in range(B_PER_CORE):
        dr[f"GX{b}"] = nc.dram_tensor(f"GX{b}", [NE * CAP, D], BF16, kind="Internal").ap()
        dr[f"TOK{b}"] = nc.dram_tensor(f"TOK{b}", [NE * CAP, 1], I32, kind="Internal").ap()
        dr[f"PV{b}"] = nc.dram_tensor(f"PV{b}", [NE * CAP, 1], F32, kind="Internal").ap()
    dr["MW1C"] = nc.dram_tensor("MW1C", [NE * D, FFN], BF16, kind="Internal").ap()
    dr["MW2C"] = nc.dram_tensor("MW2C", [NE * FFN, D], BF16, kind="Internal").ap()
    dr["MOUT"] = nc.dram_tensor("MOUT", [NT, D], F32, kind="Internal").ap()
    dr["W1T"] = nc.dram_tensor("W1T", [D, FFN], BF16, kind="Internal").ap()
    dr["W2T"] = nc.dram_tensor("W2T", [D, FFN], BF16, kind="Internal").ap()
    dr["H1WT"] = nc.dram_tensor("H1WT", [FFN, FFN], BF16, kind="Internal").ap()
    dr["H2WT"] = nc.dram_tensor("H2WT", [FFN, FFN], BF16, kind="Internal").ap()
    dr["W3T"] = nc.dram_tensor("W3T", [FFN, D], BF16, kind="Internal").ap()

    with ExitStack() as stack:
        ctx = Ctx(nc, stack)
        colv = stack.enter_context(nc.sbuf_tensor("colv", [128, 256], F32))
        C = Consts(ctx)
        cols = ColAlloc()
        dr["_mw_cast_tok"] = emit_moe_weight_cast(ctx, ext, dr)
        mark("emit_prep_ffn0:")
        emit_prep_ffn0(ctx, C, ext, dr, colv, cols)
        mark("emit_prep_attn0:")
        emit_prep_attn0(ctx, C, ext, colv, cols)
        mark("emit_ffn0_app:f1")
        emit_ffn0_app(ctx, C, dr, colv, cols, ext["L0_attn_x"], dr["XA"], "f1")
        mark("emit_attn0_app:a0")
        emit_attn0_app(ctx, C, ext, dr, colv, cols, dr["XA"], dr["XB"], "a0")
        mark("emit_ffn0_app:f2")
        emit_ffn0_app(ctx, C, dr, colv, cols, dr["XB"], dr["XA"], "f2")
        mark("emit_moe_app:m1")
        emit_moe_app(ctx, C, ext, dr, dr["XA"], dr["XB"], "m1")
        mark("emit_attn1_app:b1")
        emit_attn1_app(ctx, C, ext, dr, dr["XB"], dr["XA"], "b1")
        mark("emit_moe_app:m2")
        emit_moe_app(ctx, C, ext, dr, dr["XA"], xout, "m2")
        ctx.barrier()
        mark("end")
    return nc


_PROG_CACHE = {}


def kernel(**inputs):
    if "nc" not in _PROG_CACHE:
        _PROG_CACHE["nc"] = build_core_program()
    nc = _PROG_CACHE["nc"]
    x = np.ascontiguousarray(np.asarray(inputs["L0_attn_x"]).reshape(16, T, D))
    in_maps = []
    for c in range(N_CORES):
        m = {"L0_attn_x": np.ascontiguousarray(
            x[c * B_PER_CORE:(c + 1) * B_PER_CORE].reshape(NT, D))}
        for nm, v in inputs.items():
            if nm != "L0_attn_x":
                m[nm] = np.ascontiguousarray(np.asarray(v, dtype=np.float32))
        in_maps.append(m)
    res = run_bass_kernel_spmd(nc, in_maps, core_ids=list(range(N_CORES)))
    out = np.concatenate([np.asarray(r["xout"]).reshape(B_PER_CORE, T, D)
                          for r in res.results], axis=0)
    return out


def emit_moe_weight_cast(ctx: Ctx, ext, dr):
    toks = []
    for e in range(NE):
        toks.append(ctx.dma("gmw", out=dr["MW1C"][e * D:(e + 1) * D, :],
                            in_=ext["L1_ffn_w1"][e]))
        toks.append(ctx.dma("gmw", out=dr["MW2C"][e * FFN:(e + 1) * FFN, :],
                            in_=ext["L1_ffn_w2"][e]))
    return toks[-1]


def phase_moe_experts2(ctx: Ctx, C: Consts, mo: MoeApp, w1c, w2c, gx, tok, pv,
                       mout_dram, cast_tok, tag, route_toks=None):
    nc = ctx.nc
    RB = CAP // 128
    with ExitStack() as st_:
        w1sb = st_.enter_context(nc.sbuf_tensor(f"e2_w1_{tag}", [128, 8 * FFN], BF16))
        w2sb = st_.enter_context(nc.sbuf_tensor(f"e2_w2_{tag}", [128, 32 * 1024], BF16))
        xt = [st_.enter_context(nc.sbuf_tensor(f"e2_x{i}_{tag}", [128, 1024], BF16)) for i in range(2)]
        xfm = st_.enter_context(nc.sbuf_tensor(f"e2_xf_{tag}", [128, 8 * CAP], BF16))
        ht = st_.enter_context(nc.sbuf_tensor(f"e2_h_{tag}", [128, 32 * CAP], BF16))
        ysb = [st_.enter_context(nc.sbuf_tensor(f"e2_y{i}_{tag}", [128, 1024], F32)) for i in range(2)]
        pcol = [st_.enter_context(nc.sbuf_tensor(f"e2_p{i}_{tag}", [128, 1], F32)) for i in range(2)]
        tcol = [st_.enter_context(nc.sbuf_tensor(f"e2_t{i}_{tag}", [128, 1], I32)) for i in range(2)]
        pst = [st_.enter_context(nc.psum_tensor(f"e2_pt{i}_{tag}", [128, 1024], BF16)) for i in range(2)]
        psm = [st_.enter_context(nc.psum_tensor(f"e2_pm{i}_{tag}", [128, 512], F32)) for i in range(2)]
        w1v = w1sb[:].rearrange("p (kc o) -> p kc o", kc=8)
        w2v = w2sb[:].rearrange("p (kc o) -> p kc o", kc=32)
        xfv = xfm[:].rearrange("p (kc t) -> p kc t", kc=8)
        htv = ht[:].rearrange("p (oc t) -> p oc t", oc=32)

        w1_tok = [None] * (NE + 1)
        w2_tok = [None] * (NE + 1)
        ctx.wait("pe", mo.psum_guard)
        ctx.wait("sp", cast_tok)
        for kc in range(8):
            w1_tok[0] = ctx.dma("spw1", out=w1v[:, kc, :], in_=w1c[kc * 128:(kc + 1) * 128, :])
        for kc in range(32):
            w2_tok[0] = ctx.dma("spw2", out=w2v[:, kc, :], in_=w2c[kc * 128:(kc + 1) * 128, :])
        tr_tok = [None, None]
        ev_x = [None, None]
        g_tok = [None, None]
        y_ev = [None, None]
        sc_tok = None
        pe_h_last = None
        pe_y_last = None
        for e in range(NE):
            for b in range(B_PER_CORE):
                rt = route_toks[b][e] if route_toks is not None else (None, None)
                for rb in range(RB):
                    i2 = rb % 2
                    ctx.wait("sp", pe_h_last, tr_tok[i2], rt[0], rt[1])
                    ld = ctx.dma(f"spgx{i2}", out=xt[i2][:],
                                 in_=gx[b][e * CAP + rb * 128:e * CAP + (rb + 1) * 128, :])
                    ctx.wait("pe", ld, ev_x[i2], C.ready)
                    for kc in range(8):
                        pt = ctx.pe("transpose", pst[i2][:, kc * 128:(kc + 1) * 128],
                                    xt[i2][:, kc * 128:(kc + 1) * 128], C.ident_bf[:],
                                    mark=(kc == 7))
                    tr_tok[i2] = pt
                    ctx.wait("dve", pt)
                    ev_x[i2] = ctx.op("dve", "tensor_copy",
                                      xfv[:, :, rb * 128:(rb + 1) * 128],
                                      pst[i2][:].rearrange("p (kc t) -> p kc t", kc=8))
                for oc in range(32):
                    o2 = oc % 2
                    ctx.wait("pe", ev_x[0], ev_x[1], g_tok[o2], w1_tok[e])
                    for kc in range(8):
                        mm = ctx.pe("matmul", psm[o2][:, 0:CAP],
                                    w1v[:, kc, oc * 128:(oc + 1) * 128],
                                    xfv[:, kc, :], start=(kc == 0), stop=(kc == 7),
                                    mark=(kc == 7))
                    pe_h_last = mm
                    ctx.wait("act", mm, pe_y_last)
                    g_tok[o2] = ctx.op("act", "activation", out=htv[:, oc, :],
                                       in_=psm[o2][:, 0:CAP], func=AF.Gelu)
                if b == B_PER_CORE - 1 and e + 1 < NE:
                    ctx.wait("sp", pe_h_last)
                    for kc in range(8):
                        w1_tok[e + 1] = ctx.dma("spw1", out=w1v[:, kc, :],
                                                in_=w1c[(e + 1) * D + kc * 128:
                                                        (e + 1) * D + (kc + 1) * 128, :])
                for rb in range(RB):
                    for oh in range(2):
                        o2 = (rb * 2 + oh) % 2
                        ctx.wait("pe", g_tok[0], g_tok[1], y_ev[o2], w2_tok[e])
                        for kc in range(32):
                            mm = ctx.pe("matmul", psm[o2][:],
                                        htv[:, kc, rb * 128:(rb + 1) * 128],
                                        w2v[:, kc, oh * 512:(oh + 1) * 512],
                                        start=(kc == 0), stop=(kc == 31), mark=(kc == 31))
                        pe_y_last = mm
                        if oh == 0:
                            ctx.wait("sp", y_ev[0], y_ev[1], sc_tok, rt[0], rt[1])
                            ldp = ctx.dma(f"spp{rb % 2}", out=pcol[rb % 2][:],
                                          in_=pv[b][e * CAP + rb * 128:
                                                    e * CAP + (rb + 1) * 128, :])
                            ldt = ctx.dma(f"spt{rb % 2}", out=tcol[rb % 2][:],
                                          in_=tok[b][e * CAP + rb * 128:
                                                     e * CAP + (rb + 1) * 128, :])
                            ctx.wait("dve", ldp)
                            ctx.op("dve", "tensor_scalar", out=pcol[rb % 2][:],
                                   in0=pcol[rb % 2][:], scalar1=0.5, scalar2=None,
                                   op0=ALU.mult)
                        ctx.wait("dve", mm, sc_tok)
                        y_ev[o2] = ctx.op("dve", "tensor_scalar",
                                          out=ysb[rb % 2][:, oh * 512:(oh + 1) * 512],
                                          in0=psm[o2][:], scalar1=pcol[rb % 2][:],
                                          scalar2=None, op0=ALU.mult)
                    ctx.wait("pool", y_ev[0], y_ev[1], ldt)
                    sc_tok = ctx.idma("gsa", out=mout_dram,
                                      out_offset=bass.IndirectOffsetOnAxis(tcol[rb % 2][:], 0),
                                      in_=ysb[rb % 2][:], in_offset=None,
                                      bounds_check=NT - 1, oob_is_err=False,
                                      compute_op=ALU.add)
            if e + 1 < NE:
                ctx.wait("sp", pe_y_last)
                for kc in range(32):
                    w2_tok[e + 1] = ctx.dma("spw2", out=w2v[:, kc, :],
                                            in_=w2c[(e + 1) * FFN + kc * 128:
                                                    (e + 1) * FFN + (kc + 1) * 128, :])
        ctx.barrier()


# version 07
# speedup vs baseline: 1.0406x; 1.0247x over previous
import math
import numpy as np
from contextlib import ExitStack

import concourse.bass as bass
import concourse.mybir as mybir
from concourse.bass_utils import run_bass_kernel_spmd

F32 = mybir.dt.float32
BF16 = mybir.dt.bfloat16
I32 = mybir.dt.int32
AF = mybir.ActivationFunctionType
ALU = mybir.AluOpType
AX = mybir.AxisListType

N_CORES = 8
SCAN_PARTS = 255
SCAN_SAFE = True
B_PER_CORE = 2
T = 2048
D = 1024
NT = B_PER_CORE * T
NTILE = NT // 128
H = 16
DH = 64
FFN = 4096
LN_EPS = 1e-5


class Tok:
    __slots__ = ("sem", "val")

    def __init__(self, sem, val):
        self.sem = sem
        self.val = val


class Ctx:

    def __init__(self, nc, stack):
        self.nc = nc
        self.stack = stack
        self.e = {
            "pe": nc.tensor,
            "act": nc.scalar,
            "dve": nc.vector,
            "pool": nc.gpsimd,
            "sp": nc.sync,
        }
        self._semn = 0
        self.chain = {}
        self.ccnt = {}
        for en in ("act", "dve", "pool"):
            self.chain[en] = self._newsem(f"ch{en}")
            self.ccnt[en] = 0
        self.pe_sem = self._newsem("pedone")
        self.pe_cnt = 0
        self.pe_unmarked = 0
        self.dmas = {}
        self.known = {en: {} for en in self.e}
        self._bc_regs = {}

    def _newsem(self, name):
        s = self.stack.enter_context(self.nc.semaphore(f"{name}_{self._semn}"))
        self._semn += 1
        return s

    def op(self, en, meth, *a, **kw):
        eng = self.e[en]
        c = self.ccnt[en]
        if c:
            eng.wait_ge(self.chain[en], c)
            self.known[en][id(self.chain[en])] = c
        instr = getattr(eng, meth)(*a, **kw)
        instr.then_inc(self.chain[en], 1)
        self.ccnt[en] = c + 1
        return Tok(self.chain[en], c + 1)

    def tok(self, en):
        return Tok(self.chain[en], self.ccnt[en])

    def pe(self, meth, *a, mark=False, **kw):
        instr = getattr(self.e["pe"], meth)(*a, **kw)
        if mark:
            self.pe_cnt += 1
            instr.then_inc(self.pe_sem, 1)
            self.pe_unmarked = 0
            return Tok(self.pe_sem, self.pe_cnt)
        self.pe_unmarked += 1
        return None

    def _wait1(self, en, sem, val):
        kn = self.known[en]
        k = id(sem)
        if kn.get(k, -1) >= val:
            return
        self.e[en].wait_ge(sem, val)
        kn[k] = val

    def wait(self, en, *toks):
        best = {}
        for t in toks:
            if t is None:
                continue
            k = id(t.sem)
            if k not in best or best[k][1] < t.val:
                best[k] = (t.sem, t.val)
        for sem, val in best.values():
            self._wait1(en, sem, val)

    def dma(self, stream, out, in_, **kw):
        en = "sp" if stream.startswith("sp") else "pool"
        if stream not in self.dmas:
            self.dmas[stream] = [self._newsem("dma_" + stream), 0, en]
        rec = self.dmas[stream]
        assert rec[2] == en
        instr = self.e[en].dma_start(out=out, in_=in_, **kw)
        rec[1] += 16
        instr.then_inc(rec[0], 16)
        return Tok(rec[0], rec[1])

    def idma(self, stream, out, out_offset, in_, in_offset, **kw):
        assert stream.startswith("g")
        if stream not in self.dmas:
            self.dmas[stream] = [self._newsem("dma_" + stream), 0, "pool"]
        rec = self.dmas[stream]
        bc = kw.get("bounds_check")
        if isinstance(bc, int):
            if bc not in self._bc_regs:
                r = self.e["pool"].alloc_register()
                self.e["pool"].reg_mov(r, bc)
                self._bc_regs[bc] = r
            kw["bounds_check"] = self._bc_regs[bc]
        instr = self.e["pool"].indirect_dma_start(
            out=out, out_offset=out_offset, in_=in_, in_offset=in_offset, **kw)
        rec[1] += 16
        instr.then_inc(rec[0], 16)
        return Tok(rec[0], rec[1])

    def barrier(self, engines=("pe", "act", "dve", "pool", "sp")):
        assert self.pe_unmarked == 0, "unmarked PE work at barrier"
        toks = []
        for en in ("act", "dve", "pool"):
            if self.ccnt[en]:
                toks.append(Tok(self.chain[en], self.ccnt[en]))
        if self.pe_cnt:
            toks.append(Tok(self.pe_sem, self.pe_cnt))
        for st, (sem, cnt, _en) in self.dmas.items():
            if cnt:
                toks.append(Tok(sem, cnt))
        for en in engines:
            self.wait(en, *toks)


def ceil_div(a, b):
    return (a + b - 1) // b


def emit_select(ctx, out, mask, a, b, scratch):
    ctx.op("dve", "tensor_tensor", out=scratch, in0=a, in1=b, op=ALU.subtract)
    ctx.op("dve", "tensor_tensor", out=scratch, in0=scratch, in1=mask, op=ALU.mult)
    return ctx.op("dve", "tensor_tensor", out=out, in0=b, in1=scratch, op=ALU.add)



class Consts:

    def __init__(self, ctx: Ctx):
        nc, stack = ctx.nc, ctx.stack
        self.ident_bf = stack.enter_context(nc.sbuf_tensor("c_ident_bf", [128, 128], BF16))
        self.ident_f = stack.enter_context(nc.sbuf_tensor("c_ident_f", [128, 128], F32))
        self.tril_inc = stack.enter_context(nc.sbuf_tensor("c_tril", [128, 128], BF16))
        self.triu_strict = stack.enter_context(nc.sbuf_tensor("c_triu", [128, 128], BF16))
        self.tril_f = stack.enter_context(nc.sbuf_tensor("c_trilf", [128, 128], F32))
        self.triu_f = stack.enter_context(nc.sbuf_tensor("c_triuf", [128, 128], F32))
        self.ones_col = stack.enter_context(nc.sbuf_tensor("c_ones", [128, 128], BF16))
        self.ones_f = stack.enter_context(nc.sbuf_tensor("c_onesf", [128, 128], F32))
        self.iota_p = stack.enter_context(nc.sbuf_tensor("c_iotap", [128, 1], F32))
        sc1 = stack.enter_context(nc.sbuf_tensor("c_sc1", [128, 128], F32))
        sc2 = stack.enter_context(nc.sbuf_tensor("c_sc2", [128, 128], F32))

        t_ip = ctx.op("pool", "iota", sc1[:], [[0, 128]], base=0, channel_multiplier=1,
                      allow_small_or_imprecise_dtypes=True)
        t_if = ctx.op("pool", "iota", sc2[:], [[1, 128]], base=0, channel_multiplier=0,
                      allow_small_or_imprecise_dtypes=True)
        ctx.wait("dve", t_ip, t_if)
        ctx.op("dve", "tensor_tensor", out=self.ident_f[:], in0=sc1[:], in1=sc2[:], op=ALU.is_equal)
        ctx.op("dve", "tensor_tensor", out=self.ident_bf[:], in0=sc1[:], in1=sc2[:], op=ALU.is_equal)
        ctx.op("dve", "tensor_tensor", out=self.tril_inc[:], in0=sc1[:], in1=sc2[:], op=ALU.is_le)
        ctx.op("dve", "tensor_tensor", out=self.tril_f[:], in0=sc1[:], in1=sc2[:], op=ALU.is_le)
        ctx.op("dve", "tensor_tensor", out=self.triu_strict[:], in0=sc1[:], in1=sc2[:], op=ALU.is_gt)
        ctx.op("dve", "tensor_tensor", out=self.triu_f[:], in0=sc1[:], in1=sc2[:], op=ALU.is_gt)
        ctx.op("dve", "memset", self.ones_col[:], 1.0)
        ctx.op("dve", "memset", self.ones_f[:], 1.0)
        self.t_iota_p = ctx.op("dve", "tensor_copy", self.iota_p[:], sc1[:, 0:1])
        self.ready = ctx.tok("dve")



def prep_transposed_weight(ctx: Ctx, C: Consts, src_f32, dst_bf16, rows, cols, tag):
    nc = ctx.nc
    ctx.wait("pe", C.ready)
    rb, cb = rows // 128, cols // 128
    with ExitStack() as st_:
        nbuf = 3
        src_sb = [st_.enter_context(nc.sbuf_tensor(f"wp_src{i}_{tag}", [128, 8 * 128], BF16))
                  for i in range(nbuf)]
        out_sb = [st_.enter_context(nc.sbuf_tensor(f"wp_out{i}_{tag}", [128, 8 * 128], BF16))
                  for i in range(nbuf)]
        psum = [st_.enter_context(nc.psum_tensor(f"wp_ps{i}_{tag}", [128, 1024], BF16))
                for i in range(2)]
        free_tok = [None] * nbuf
        evac_tok = [None, None]
        tr_tok = [None] * nbuf
        n = 0
        for c in range(cb):
            for rg in range(ceil_div(rb, 8)):
                nr = min(8, rb - rg * 8)
                i = n % nbuf
                pi = n % 2
                ctx.wait("pool", tr_tok[i])
                ld = ctx.dma(f"gwp{i}",
                             out=src_sb[i][:, 0:nr * 128].rearrange("p (r c) -> p r c", r=nr),
                             in_=src_f32[rg * 1024:rg * 1024 + nr * 128,
                                         c * 128:(c + 1) * 128]
                             .rearrange("(r p) c -> p r c", p=128))
                ctx.wait("pe", ld, evac_tok[pi])
                for r in range(nr):
                    pt = ctx.pe("transpose", psum[pi][:, r * 128:(r + 1) * 128],
                                src_sb[i][:, r * 128:(r + 1) * 128], C.ident_bf[:],
                                mark=(r == nr - 1))
                tr_tok[i] = pt
                ctx.wait("dve", pt, free_tok[i])
                evac_tok[pi] = ctx.op("dve", "tensor_copy", out_sb[i][:, 0:nr * 128],
                                      psum[pi][:, 0:nr * 128])
                ctx.wait("pool", evac_tok[pi])
                free_tok[i] = ctx.dma(f"gwps{i}",
                                      out=dst_bf16[c * 128:(c + 1) * 128,
                                                   rg * 1024:rg * 1024 + nr * 128],
                                      in_=out_sb[i][:, 0:nr * 128])
                n += 1
        ctx.barrier()


def prep_colvec(ctx: Ctx, C: Consts, src_f32, n, dst_sb, col0, scale=None):
    nc = ctx.nc
    nb = ceil_div(n, 128)
    ctx.wait("pe", C.ready)
    with ExitStack() as st_:
        row = st_.enter_context(nc.sbuf_tensor(_uq("pv_row"), [nb, 128], F32))
        ps = st_.enter_context(nc.psum_tensor(_uq("pv_ps"), [128, nb], F32))
        t = ctx.dma("spv", out=row[0:nb, :], in_=src_f32.rearrange("(a b) -> a b", b=128))
        ctx.wait("pe", t)
        pt = ctx.pe("transpose", ps[:, 0:nb], row[0:nb, :], C.ident_f[0:nb, 0:nb], mark=True)
        ctx.wait("dve", pt)
        if scale is None:
            ctx.op("dve", "tensor_copy", dst_sb[:, col0:col0 + nb], ps[:, 0:nb])
        else:
            ctx.op("dve", "tensor_scalar", out=dst_sb[:, col0:col0 + nb], in0=ps[:, 0:nb],
                   scalar1=float(scale), scalar2=None, op0=ALU.mult)
        ctx.barrier()


def prep_bcast_row(ctx: Ctx, C: Consts, src_f32, n, dst_sb, ones_f):
    nc = ctx.nc
    ctx.wait("pe", C.ready)
    with ExitStack() as st_:
        row = st_.enter_context(nc.sbuf_tensor(_uq("pb_row"), [1, n], F32))
        t = ctx.dma("spv", out=row[0:1, 0:n], in_=src_f32.unsqueeze(0))
        ctx.wait("pe", t)
        for j0 in range(0, n, 512):
            w = min(512, n - j0)
            with nc.psum_tensor(_uq("pb_ps"), [128, 512], F32) as ps:
                pt = ctx.pe("matmul", ps[:, 0:w], ones_f[0:1, :], row[0:1, j0:j0 + w],
                            start=True, stop=True, mark=True)
                ctx.wait("dve", pt)
                ctx.op("dve", "tensor_copy", dst_sb[:, j0:j0 + w], ps[:, 0:w])
                ctx.barrier()
        ctx.barrier()



class LnBufs:
    def __init__(self, ctx, tag, nring=2, d=D):
        nc, st_ = ctx.nc, ctx.stack
        self.d = d
        self.m_neg = [st_.enter_context(nc.sbuf_tensor(f"ln_mn{i}_{tag}", [128, 1], F32)) for i in range(nring)]
        self.veps = [st_.enter_context(nc.sbuf_tensor(f"ln_ve{i}_{tag}", [128, 1], F32)) for i in range(nring)]
        self.r = [st_.enter_context(nc.sbuf_tensor(f"ln_r{i}_{tag}", [128, 1], F32)) for i in range(nring)]
        self.sq = st_.enter_context(nc.sbuf_tensor(f"ln_sq_{tag}", [128, d], BF16))
        self.ssum = [st_.enter_context(nc.sbuf_tensor(f"ln_ss{i}_{tag}", [128, 1], F32)) for i in range(nring)]


def emit_ln(ctx: Ctx, lb: LnBufs, i, x_ap, out_ap, x_tok):
    d = lb.d
    ctx.wait("dve", x_tok)
    ctx.op("dve", "tensor_reduce", out=lb.m_neg[i][:], in_=x_ap, axis=AX.X, op=ALU.add, negate=True)
    ctx.op("dve", "tensor_scalar", out=lb.m_neg[i][:], in0=lb.m_neg[i][:],
           scalar1=1.0 / d, scalar2=None, op0=ALU.mult)
    mt = ctx.tok("dve")
    ctx.wait("act", x_tok, mt)
    ctx.op("act", "activation", out=lb.sq[:, 0:d], in_=x_ap, func=AF.Square,
           bias=lb.m_neg[i][:], scale=1.0, accum_out=lb.ssum[i][:])
    st = ctx.tok("act")
    ctx.wait("dve", st)
    ctx.op("dve", "tensor_scalar", out=lb.veps[i][:], in0=lb.ssum[i][:],
           scalar1=1.0 / d, scalar2=LN_EPS, op0=ALU.mult, op1=ALU.add)
    vt = ctx.tok("dve")
    ctx.wait("act", vt)
    ctx.op("act", "activation", out=lb.r[i][:], in_=lb.veps[i][:], func=AF.Ln)
    ctx.op("act", "activation", out=lb.r[i][:], in_=lb.r[i][:], func=AF.Exp, scale=-0.5)
    sqt = ctx.tok("act")
    ctx.wait("dve", sqt)
    tout = ctx.op("dve", "tensor_scalar", out=out_ap, in0=x_ap,
                  scalar1=lb.m_neg[i][:], scalar2=lb.r[i][:], op0=ALU.add, op1=ALU.mult)
    return tout, lb.m_neg[i], lb.r[i]



def phase_ln_to_fm(ctx: Ctx, C: Consts, x_src, yt_dst, tag):
    nc = ctx.nc
    with ExitStack() as st_:
        xb = [st_.enter_context(nc.sbuf_tensor(f"s0x{i}_{tag}", [128, D], F32)) for i in range(2)]
        yb = [st_.enter_context(nc.sbuf_tensor(f"s0y{i}_{tag}", [128, D], BF16)) for i in range(2)]
        yts = [st_.enter_context(nc.sbuf_tensor(f"s0t{i}_{tag}", [128, 8 * 512], BF16)) for i in range(2)]
        ps = [st_.enter_context(nc.psum_tensor(f"s0p{i}_{tag}", [128, 1024], BF16)) for i in range(2)]
        lb = LnBufs(_StackCtx(ctx, st_), tag + "ln")
        norm_tok = [None, None]
        evac_tok = [None, None]
        tr_tok = [None, None]
        st_tok = [None, None]
        ctx.wait("pe", C.ready)
        nsup = NT // 512
        for s in range(nsup):
            for j in range(4):
                it = s * 4 + j
                i2 = it % 2
                ctx.wait("sp", norm_tok[i2])
                ldx = ctx.dma(f"spx{i2}", out=xb[i2][:], in_=x_src[it * 128:(it + 1) * 128, :])
                ctx.wait("dve", tr_tok[i2])
                tnorm, _, _ = emit_ln(ctx, lb, i2, xb[i2][:], yb[i2][:], ldx)
                norm_tok[i2] = tnorm
                ctx.wait("pe", tnorm, evac_tok[i2])
                for kc in range(8):
                    pt = ctx.pe("transpose", ps[i2][:, kc * 128:(kc + 1) * 128],
                                yb[i2][:, kc * 128:(kc + 1) * 128], C.ident_bf[:],
                                mark=(kc == 7))
                tr_tok[i2] = pt
                ctx.wait("dve", pt, st_tok[s % 2])
                evac_tok[i2] = ctx.op(
                    "dve", "tensor_copy",
                    yts[s % 2][:].rearrange("p (kc t) -> p kc t", kc=8)[:, :, j * 128:(j + 1) * 128],
                    ps[i2][:].rearrange("p (kc t) -> p kc t", kc=8))
            for kc in range(8):
                ctx.wait("pool", evac_tok[0], evac_tok[1])
                st_tok[s % 2] = ctx.dma(
                    f"gyt{s % 2}",
                    out=yt_dst[kc * 128:(kc + 1) * 128, s * 512:(s + 1) * 512],
                    in_=yts[s % 2][:, kc * 512:(kc + 1) * 512])


class _StackCtx:

    def __init__(self, ctx, st_):
        self.nc = ctx.nc
        self.stack = st_



def phase_ffn0_s1(ctx: Ctx, C: Consts, yt_src, w1t, w2t, colv, c_w1s, c_w2s, h0_dst, tag):
    nc = ctx.nc
    with ExitStack() as st_:
        w1sb = st_.enter_context(nc.sbuf_tensor(f"s1w1_{tag}", [128, 8 * FFN], BF16))
        w2sb = st_.enter_context(nc.sbuf_tensor(f"s1w2_{tag}", [128, 8 * FFN], BF16))
        ytb = [st_.enter_context(nc.sbuf_tensor(f"s1y{i}_{tag}", [128, 8 * 512], BF16)) for i in range(2)]
        t1 = [st_.enter_context(nc.sbuf_tensor(f"s1t1{i}_{tag}", [128, 512], F32)) for i in range(2)]
        t2 = [st_.enter_context(nc.sbuf_tensor(f"s1t2{i}_{tag}", [128, 512], F32)) for i in range(2)]
        h0b = [st_.enter_context(nc.sbuf_tensor(f"s1h{i}_{tag}", [128, 512], BF16)) for i in range(2)]
        pa = [st_.enter_context(nc.psum_tensor(f"s1pa{i}_{tag}", [128, 512], F32)) for i in range(2)]
        pb = [st_.enter_context(nc.psum_tensor(f"s1pb{i}_{tag}", [128, 512], F32)) for i in range(2)]
        wtoks = []
        for kc in range(8):
            wtoks.append(ctx.dma("spw", out=w1sb[:, kc * FFN:(kc + 1) * FFN],
                                 in_=w1t[kc * 128:(kc + 1) * 128, :]))
            wtoks.append(ctx.dma("spw", out=w2sb[:, kc * FFN:(kc + 1) * FFN],
                                 in_=w2t[kc * 128:(kc + 1) * 128, :]))
        ctx.wait("pe", *wtoks)
        w1v = w1sb[:].rearrange("p (kc o) -> p kc o", kc=8)
        w2v = w2sb[:].rearrange("p (kc o) -> p kc o", kc=8)
        ld_tok = [None, None]
        a_tok = [None, None]
        b_tok = [None, None]
        m_tok = [None, None]
        st_tok = [None, None]
        pe_last = [None, None]
        nsup = NT // 512
        for s in range(nsup):
            i2 = s % 2
            ctx.wait("sp", pe_last[i2])
            for kc in range(8):
                ld_tok[i2] = ctx.dma(f"spy{i2}", out=ytb[i2][:, kc * 512:(kc + 1) * 512],
                                     in_=yt_src[kc * 128:(kc + 1) * 128, s * 512:(s + 1) * 512])
            yv = ytb[i2][:].rearrange("p (kc t) -> p kc t", kc=8)
            for oc in range(32):
                o2 = oc % 2
                ctx.wait("pe", ld_tok[i2], a_tok[o2], b_tok[o2])
                for kc in range(8):
                    pat = ctx.pe("matmul", pa[o2][:], w1v[:, kc, oc * 128:(oc + 1) * 128],
                                 yv[:, kc, :], start=(kc == 0), stop=(kc == 7),
                                 mark=(kc == 7))
                for kc in range(8):
                    mm = ctx.pe("matmul", pb[o2][:], w2v[:, kc, oc * 128:(oc + 1) * 128],
                                yv[:, kc, :], start=(kc == 0), stop=(kc == 7),
                                mark=(kc == 7))
                pbt = mm
                ctx.wait("act", pat, m_tok[o2])
                a_tok[o2] = ctx.op("act", "activation", out=t1[o2][:], in_=pa[o2][:],
                                   func=AF.Silu, scale=colv[:, c_w1s + oc:c_w1s + oc + 1])
                ctx.wait("dve", pbt)
                b_tok[o2] = ctx.op("dve", "tensor_scalar", out=t2[o2][:], in0=pb[o2][:],
                                   scalar1=colv[:, c_w2s + oc:c_w2s + oc + 1], scalar2=None,
                                   op0=ALU.mult)
                ctx.wait("dve", a_tok[o2], st_tok[o2])
                m_tok[o2] = ctx.op("dve", "tensor_tensor", out=h0b[o2][:], in0=t1[o2][:],
                                   in1=t2[o2][:], op=ALU.mult)
                ctx.wait("pool", m_tok[o2])
                st_tok[o2] = ctx.dma(f"gh{o2}", out=h0_dst[oc * 128:(oc + 1) * 128,
                                                           s * 512:(s + 1) * 512],
                                     in_=h0b[o2][:])
                pe_last[i2] = pbt



def phase_ffn0_mid(ctx: Ctx, C: Consts, src, wt, colv, c_s, dst, tag):
    nc = ctx.nc
    KCH = FFN // 128
    groups = [list(range(0, 11)), list(range(11, 22)), list(range(22, 32))]
    with ExitStack() as st_:
        wsl = [st_.enter_context(nc.sbuf_tensor(f"md_w{i}_{tag}", [128, KCH * 128], BF16))
               for i in range(11)]
        rhs = [st_.enter_context(nc.sbuf_tensor(f"md_r{i}_{tag}", [128, KCH * 512], BF16))
               for i in range(2)]
        ob = [st_.enter_context(nc.sbuf_tensor(f"md_o{i}_{tag}", [128, 512], BF16)) for i in range(2)]
        ps = [st_.enter_context(nc.psum_tensor(f"md_p{i}_{tag}", [128, 512], F32)) for i in range(2)]
        g_tok = [None, None]
        st_tok = [None, None]
        ld_tok = [None, None]
        pe_last_rhs = [None, None]
        pe_last_group = None
        for gi, grp in enumerate(groups):
            wl_toks = []
            for sl, oc in enumerate(grp):
                ctx.wait("sp", pe_last_group)
                wl_toks.append(ctx.dma(
                    "spw",
                    out=wsl[sl][:].rearrange("p (kc o) -> p kc o", kc=KCH),
                    in_=wt[:, oc * 128:(oc + 1) * 128].rearrange("(kc p) o -> p kc o", p=128)))
            nsup = NT // 512
            for s in range(nsup):
                i2 = s % 2
                ctx.wait("sp", pe_last_rhs[i2])
                for kc in range(KCH):
                    ld_tok[i2] = ctx.dma(f"spy{i2}",
                                         out=rhs[i2][:, kc * 512:(kc + 1) * 512],
                                         in_=src[kc * 128:(kc + 1) * 128, s * 512:(s + 1) * 512])
                rv = rhs[i2][:].rearrange("p (kc t) -> p kc t", kc=KCH)
                for sl, oc in enumerate(grp):
                    o2 = (s * len(grp) + sl) % 2
                    ctx.wait("pe", ld_tok[i2], g_tok[o2], *(wl_toks if s == 0 else []))
                    for kc in range(KCH):
                        wv = wsl[sl][:].rearrange("p (kc o) -> p kc o", kc=KCH)
                        mm = ctx.pe("matmul", ps[o2][:], wv[:, kc, :], rv[:, kc, :],
                                    start=(kc == 0), stop=(kc == KCH - 1),
                                    mark=(kc == KCH - 1))
                    pe_last_rhs[i2] = mm
                    pe_last_group = mm
                    ctx.wait("act", mm, st_tok[o2])
                    g_tok[o2] = ctx.op("act", "activation", out=ob[o2][:], in_=ps[o2][:],
                                       func=AF.Gelu, scale=colv[:, c_s + oc:c_s + oc + 1])
                    ctx.wait("pool", g_tok[o2])
                    st_tok[o2] = ctx.dma(f"gh{o2}",
                                         out=dst[oc * 128:(oc + 1) * 128, s * 512:(s + 1) * 512],
                                         in_=ob[o2][:])



def phase_ffn0_s4(ctx: Ctx, C: Consts, src_h, w3t, colv, c_s3, x_src, x_dst, tag):
    nc = ctx.nc
    KCH = FFN // 128
    with ExitStack() as st_:
        w3sb = st_.enter_context(nc.sbuf_tensor(f"s4w_{tag}", [128, KCH * 1024], BF16))
        rhs = [st_.enter_context(nc.sbuf_tensor(f"s4r{i}_{tag}", [128, KCH * 512], BF16))
               for i in range(2)]
        ufm = [st_.enter_context(nc.sbuf_tensor(f"s4u{i}_{tag}", [128, 8 * 512], F32)) for i in range(2)]
        xs = [st_.enter_context(nc.sbuf_tensor(f"s4x{i}_{tag}", [128, D], F32)) for i in range(2)]
        xo = [st_.enter_context(nc.sbuf_tensor(f"s4xo{i}_{tag}", [128, D], F32)) for i in range(2)]
        psm = [st_.enter_context(nc.psum_tensor(f"s4pm{i}_{tag}", [128, 512], F32)) for i in range(2)]
        pst = [st_.enter_context(nc.psum_tensor(f"s4pt{i}_{tag}", [128, 1024], F32)) for i in range(2)]
        wl_toks = [ctx.dma("spw", out=w3sb[:, kc * 1024:(kc + 1) * 1024],
                           in_=w3t[kc * 128:(kc + 1) * 128, :]) for kc in range(KCH)]
        w3v = w3sb[:].rearrange("p (kc o) -> p kc o", kc=KCH)
        ld_tok = [None, None]
        ev_tok = [None, None]
        tr_tok = [None, None]
        add_tok = [None, None]
        stx_tok = [None, None]
        pe_last_rhs = [None, None]
        nsup = NT // 512
        for s in range(nsup):
            i2 = s % 2
            ctx.wait("sp", pe_last_rhs[i2])
            for kc in range(KCH):
                ld_tok[i2] = ctx.dma(f"spy{i2}", out=rhs[i2][:, kc * 512:(kc + 1) * 512],
                                     in_=src_h[kc * 128:(kc + 1) * 128, s * 512:(s + 1) * 512])
            rv = rhs[i2][:].rearrange("p (kc t) -> p kc t", kc=KCH)
            ctx.wait("dve", tr_tok[i2])
            for oc in range(8):
                o2 = oc % 2
                ctx.wait("pe", ld_tok[i2], ev_tok[o2], *(wl_toks if s == 0 else []))
                for kc in range(KCH):
                    mm = ctx.pe("matmul", psm[o2][:], w3v[:, kc, oc * 128:(oc + 1) * 128],
                                rv[:, kc, :], start=(kc == 0), stop=(kc == KCH - 1),
                                mark=(kc == KCH - 1))
                pe_last_rhs[i2] = mm
                ctx.wait("dve", mm)
                ev_tok[o2] = ctx.op("dve", "tensor_scalar",
                                    out=ufm[i2][:, oc * 512:(oc + 1) * 512], in0=psm[o2][:],
                                    scalar1=colv[:, c_s3 + oc:c_s3 + oc + 1], scalar2=None,
                                    op0=ALU.mult)
            uv = ufm[i2][:].rearrange("p (oc t) -> p oc t", oc=8)
            for j in range(4):
                it = s * 4 + j
                j2 = j % 2
                ctx.wait("sp", add_tok[j2])
                ldx = ctx.dma(f"spx{j2}", out=xs[j2][:], in_=x_src[it * 128:(it + 1) * 128, :])
                ctx.wait("pe", ev_tok[0], ev_tok[1], add_tok[j2])
                for oc in range(8):
                    pt = ctx.pe("transpose", pst[j2][:, oc * 128:(oc + 1) * 128],
                                uv[:, oc, j * 128:(j + 1) * 128], C.ident_f[:],
                                mark=(oc == 7))
                tr_tok[i2] = pt
                ctx.wait("dve", pt, ldx, stx_tok[j2])
                add_tok[j2] = ctx.op("dve", "tensor_tensor", out=xo[j2][:], in0=pst[j2][:],
                                     in1=xs[j2][:], op=ALU.add)
                ctx.wait("pool", add_tok[j2])
                stx_tok[j2] = ctx.dma(f"gx{j2}", out=x_dst[it * 128:(it + 1) * 128, :],
                                      in_=xo[j2][:])


class ColAlloc:
    def __init__(self):
        self.n = 0
        self.map = {}

    def reg(self, name, ncols):
        self.map[name] = self.n
        self.n += ncols
        return self.map[name]


def emit_prep_ffn0(ctx: Ctx, C: Consts, ext, dr, colv, cols: ColAlloc):
    prep_transposed_weight(ctx, C, ext["L0_ffn_w1_w"], dr["W1T"], FFN, D, "w1")
    prep_transposed_weight(ctx, C, ext["L0_ffn_w2_w"], dr["W2T"], FFN, D, "w2")
    prep_transposed_weight(ctx, C, ext["L0_ffn_h1_w"], dr["H1WT"], FFN, FFN, "h1")
    prep_transposed_weight(ctx, C, ext["L0_ffn_h2_w"], dr["H2WT"], FFN, FFN, "h2")
    prep_transposed_weight(ctx, C, ext["L0_ffn_w3_w"], dr["W3T"], D, FFN, "w3")
    prep_colvec(ctx, C, ext["L0_ffn_w1_s"], FFN, colv, cols.reg("w1_s", 32))
    prep_colvec(ctx, C, ext["L0_ffn_w2_s"], FFN, colv, cols.reg("w2_s", 32))
    prep_colvec(ctx, C, ext["L0_ffn_h1_s"], FFN, colv, cols.reg("h1_s", 32))
    prep_colvec(ctx, C, ext["L0_ffn_h2_s"], FFN, colv, cols.reg("h2_s", 32))
    prep_colvec(ctx, C, ext["L0_ffn_w3_s"], D, colv, cols.reg("w3_s", 8), scale=0.5)
    ctx.barrier()


def emit_ffn0_app(ctx: Ctx, C: Consts, dr, colv, cols: ColAlloc, x_src, x_dst, tag):
    phase_ln_to_fm(ctx, C, x_src, dr["YT"], tag + "a")
    ctx.barrier()
    phase_ffn0_s1(ctx, C, dr["YT"], dr["W1T"], dr["W2T"], colv,
                  cols.map["w1_s"], cols.map["w2_s"], dr["HA"], tag + "b")
    ctx.barrier()
    phase_ffn0_mid(ctx, C, dr["HA"], dr["H1WT"], colv, cols.map["h1_s"], dr["HB"], tag + "c")
    ctx.barrier()
    phase_ffn0_mid(ctx, C, dr["HB"], dr["H2WT"], colv, cols.map["h2_s"], dr["HA"], tag + "d")
    ctx.barrier()
    phase_out_ffn(ctx, C, dr["HA"], dr["W3T"], False, colv, cols.map["w3_s"], None,
                  x_src, None, x_dst, tag + "e")
    ctx.barrier()



def phase_ln2(ctx: Ctx, C: Consts, x_src, yt_dst, tag, y1_dst=None, double_ln=False,
              colv=None, w_cols=None, b_cols=None, ytok_dst=None):
    nc = ctx.nc
    with ExitStack() as st_:
        xb = [st_.enter_context(nc.sbuf_tensor(f"l2x{i}_{tag}", [128, D], F32)) for i in range(2)]
        y1b = [st_.enter_context(nc.sbuf_tensor(f"l2y1{i}_{tag}", [128, D], F32)) for i in range(2)] \
            if double_ln else None
        yb = [st_.enter_context(nc.sbuf_tensor(f"l2y{i}_{tag}", [128, D], BF16)) for i in range(2)]
        yts = [st_.enter_context(nc.sbuf_tensor(f"l2t{i}_{tag}", [128, 8 * 512], BF16)) for i in range(2)]
        ps = [st_.enter_context(nc.psum_tensor(f"l2p{i}_{tag}", [128, 1024], BF16)) for i in range(2)]
        lb = LnBufs(_StackCtx(ctx, st_), tag + "ln")
        lb2 = LnBufs(_StackCtx(ctx, st_), tag + "l2") if double_ln else None
        norm_tok = [None, None]
        evac_tok = [None, None]
        tr_tok = [None, None]
        st_tok = [None, None]
        sty_tok = [None, None]
        stk_tok = [None, None]
        ctx.wait("pe", C.ready)
        nsup = NT // 512
        for s in range(nsup):
            for j in range(4):
                it = s * 4 + j
                i2 = it % 2
                ctx.wait("sp", norm_tok[i2])
                ldx = ctx.dma(f"spx{i2}", out=xb[i2][:], in_=x_src[it * 128:(it + 1) * 128, :])
                ctx.wait("dve", tr_tok[i2])
                if double_ln:
                    ctx.wait("dve", sty_tok[i2])
                    t1, _, _ = emit_ln(ctx, lb, i2, xb[i2][:], y1b[i2][:], ldx)
                    ctx.wait("pool", t1)
                    sty_tok[i2] = ctx.dma(f"gy1{i2}", out=y1_dst[it * 128:(it + 1) * 128, :],
                                          in_=y1b[i2][:])
                    tnorm, _, _ = emit_ln(ctx, lb2, i2, y1b[i2][:], yb[i2][:], t1)
                else:
                    tnorm, _, _ = emit_ln(ctx, lb, i2, xb[i2][:], yb[i2][:], ldx)
                norm_tok[i2] = tnorm
                if ytok_dst is not None:
                    ctx.wait("pool", tnorm)
                    stk_tok[i2] = ctx.dma(f"gyk{i2}", out=ytok_dst[it * 128:(it + 1) * 128, :],
                                          in_=yb[i2][:])
                ctx.wait("pe", tnorm, evac_tok[i2])
                for kc in range(8):
                    pt = ctx.pe("transpose", ps[i2][:, kc * 128:(kc + 1) * 128],
                                yb[i2][:, kc * 128:(kc + 1) * 128], C.ident_bf[:],
                                mark=(kc == 7))
                tr_tok[i2] = pt
                ctx.wait("dve", pt, st_tok[s % 2])
                pv = ps[i2][:].rearrange("p (kc t) -> p kc t", kc=8)
                yv = yts[s % 2][:].rearrange("p (kc t) -> p kc t", kc=8)
                if w_cols is not None:
                    for kc in range(8):
                        evac_tok[i2] = ctx.op(
                            "dve", "tensor_scalar",
                            out=yv[:, kc, j * 128:(j + 1) * 128],
                            in0=pv[:, kc, :],
                            scalar1=colv[:, w_cols + kc:w_cols + kc + 1],
                            scalar2=colv[:, b_cols + kc:b_cols + kc + 1],
                            op0=ALU.mult, op1=ALU.add)
                else:
                    evac_tok[i2] = ctx.op("dve", "tensor_copy",
                                          yv[:, :, j * 128:(j + 1) * 128], pv[:, :, :])
            for kc in range(8):
                ctx.wait("pool", evac_tok[0], evac_tok[1])
                st_tok[s % 2] = ctx.dma(
                    f"gyt{s % 2}",
                    out=yt_dst[kc * 128:(kc + 1) * 128, s * 512:(s + 1) * 512],
                    in_=yts[s % 2][:, kc * 512:(kc + 1) * 512])



def emit_phi(ctx, psum_ap, bias_col, out_ap, tmpe, tmpm, phi_scale=1.0):
    ctx.op("dve", "tensor_scalar", out=tmpm, in0=psum_ap, scalar1=bias_col,
           scalar2=None, op0=ALU.add)
    t = ctx.tok("dve")
    ctx.wait("act", t)
    if phi_scale != 1.0:
        ctx.op("dve", "tensor_scalar", out=tmpm, in0=tmpm, scalar1=float(phi_scale),
               scalar2=None, op0=ALU.mult)
    ctx.op("dve", "tensor_scalar", out=tmpe, in0=tmpm, scalar1=0.0, scalar2=None, op0=ALU.min)
    t = ctx.tok("dve")
    ctx.wait("act", t)
    ctx.op("act", "activation", out=tmpe, in_=tmpe, func=AF.Exp)
    ta = ctx.tok("act")
    ctx.wait("dve", ta)
    ctx.op("dve", "tensor_scalar", out=tmpm, in0=tmpm, scalar1=0.0, scalar2=None, op0=ALU.max)
    return ctx.op("dve", "tensor_tensor", out=out_ap, in0=tmpe, in1=tmpm, op=ALU.add)


def phase_proj_fm(ctx: Ctx, C: Consts, xnt, w_src, dst, tag, colv=None, bias_cols=None,
                  phi=False, phi_scale=1.0, w_transposed_src=False):
    nc = ctx.nc
    O = 1024
    with ExitStack() as st_:
        wsb = st_.enter_context(nc.sbuf_tensor(f"pj_w_{tag}", [128, 8 * O], BF16))
        xtb = [st_.enter_context(nc.sbuf_tensor(f"pj_x{i}_{tag}", [128, 8 * 512], BF16)) for i in range(2)]
        ob = [st_.enter_context(nc.sbuf_tensor(f"pj_o{i}_{tag}", [128, 512], BF16)) for i in range(4)]
        te = [st_.enter_context(nc.sbuf_tensor(f"pj_e{i}_{tag}", [128, 512], F32)) for i in range(4)]
        tm = [st_.enter_context(nc.sbuf_tensor(f"pj_m{i}_{tag}", [128, 512], F32)) for i in range(4)]
        ps = [st_.enter_context(nc.psum_tensor(f"pj_p{i}_{tag}", [128, 512], F32)) for i in range(4)]
        wv = wsb[:].rearrange("p (kc o) -> p kc o", kc=8)
        if not w_transposed_src:
            wts = [ctx.dma(f"gw{kc % 2}", out=wv[:, kc, :], in_=w_src[kc * 128:(kc + 1) * 128, :])
                   for kc in range(8)]
            ctx.wait("pe", *wts)
        else:
            with ExitStack() as st2:
                blk = [st2.enter_context(nc.sbuf_tensor(f"pj_tb{i}_{tag}", [128, 128], BF16))
                       for i in range(2)]
                pst = [st2.enter_context(nc.psum_tensor(f"pj_tp{i}_{tag}", [128, 128], BF16))
                       for i in range(2)]
                ctx.wait("pe", C.ready)
                ev = [None, None]
                tt = [None, None]
                for n in range(64):
                    r, c = n // 8, n % 8
                    i = n % 2
                    ctx.wait("pool", tt[i])
                    lt = ctx.dma(f"gw{i}", out=blk[i][:],
                                 in_=w_src[r * 128:(r + 1) * 128, c * 128:(c + 1) * 128])
                    ctx.wait("pe", lt, ev[i])
                    tt[i] = ctx.pe("transpose", pst[i][:], blk[i][:], C.ident_bf[:], mark=True)
                    ctx.wait("dve", tt[i])
                    ev[i] = ctx.op("dve", "tensor_copy", wv[:, c, r * 128:(r + 1) * 128], pst[i][:])
                ctx.wait("pe", ev[0], ev[1])
        ld_tok = [None, None]
        e_tok = [None] * 4
        st_tok = [None] * 4
        pe_last = [None, None]
        nsup = NT // 512
        for s in range(nsup):
            i2 = s % 2
            ctx.wait("sp", pe_last[i2])
            for kc in range(8):
                ld_tok[i2] = ctx.dma(f"spy{i2}", out=xtb[i2][:, kc * 512:(kc + 1) * 512],
                                     in_=xnt[kc * 128:(kc + 1) * 128, s * 512:(s + 1) * 512])
            xv = xtb[i2][:].rearrange("p (kc t) -> p kc t", kc=8)
            for oc in range(8):
                o2 = (s * 8 + oc) % 4
                ctx.wait("pe", ld_tok[i2], e_tok[o2])
                for kc in range(8):
                    mm = ctx.pe("matmul", ps[o2][:], wv[:, kc, oc * 128:(oc + 1) * 128],
                                xv[:, kc, :], start=(kc == 0), stop=(kc == 7), mark=(kc == 7))
                pe_last[i2] = mm
                bias_col = colv[:, bias_cols + oc:bias_cols + oc + 1] if bias_cols is not None else 0.0
                ctx.wait("dve", mm, st_tok[o2])
                if phi:
                    e_tok[o2] = emit_phi(ctx, ps[o2][:], bias_col, ob[o2][:],
                                         te[o2][:], tm[o2][:], phi_scale)
                else:
                    e_tok[o2] = ctx.op("dve", "tensor_scalar", out=ob[o2][:], in0=ps[o2][:],
                                       scalar1=bias_col, scalar2=None, op0=ALU.add)
                ctx.wait("pool", e_tok[o2])
                st_tok[o2] = ctx.dma(f"go{o2}", out=dst[oc * 128:(oc + 1) * 128,
                                                       s * 512:(s + 1) * 512],
                                     in_=ob[o2][:])



def phase_proj_tok(ctx: Ctx, C: Consts, xnt, wv_src, vbias_row, dst_v, wd_src, dbias_row,
                   lam_sb, tag, wv_transposed=False, wd_transposed=False):
    nc = ctx.nc
    with ExitStack() as st_:
        wsb = st_.enter_context(nc.sbuf_tensor(f"pt_w_{tag}", [128, 8 * 1024], BF16))
        wdb = st_.enter_context(nc.sbuf_tensor(f"pt_wd_{tag}", [128, 8 * 16], BF16)) \
            if wd_src is not None else None
        xtb = [st_.enter_context(nc.sbuf_tensor(f"pt_x{i}_{tag}", [128, 8 * 128], BF16)) for i in range(2)]
        vb = [st_.enter_context(nc.sbuf_tensor(f"pt_v{i}_{tag}", [128, 512], BF16)) for i in range(2)]
        ps = [st_.enter_context(nc.psum_tensor(f"pt_p{i}_{tag}", [128, 512], F32)) for i in range(2)]
        psd = st_.enter_context(nc.psum_tensor(f"pt_pd_{tag}", [128, 16], F32)) \
            if wd_src is not None else None
        wv_ = wsb[:].rearrange("p (kc o) -> p kc o", kc=8)
        if not wv_transposed:
            wts = [ctx.dma(f"gw{kc % 2}", out=wv_[:, kc, :], in_=wv_src[kc * 128:(kc + 1) * 128, :])
                   for kc in range(8)]
        else:
            wts = []
            _load_transposed_into(ctx, C, wv_src, wv_, 1024, tag + "v")
        if wd_src is not None:
            wd_ = wdb[:].rearrange("p (kc o) -> p kc o", kc=8)
            if not wd_transposed:
                wts += [ctx.dma("gwd", out=wd_[:, kc, :], in_=wd_src[kc * 128:(kc + 1) * 128, :])
                        for kc in range(8)]
            else:
                _load_transposed_into(ctx, C, wd_src, wd_, 16, tag + "d")
        ctx.wait("pe", *wts)
        ld_tok = [None, None]
        e_tok = [None, None]
        d_tok = None
        st_tok = [None, None]
        pe_last = [None, None]
        for tt in range(NTILE):
            i2 = tt % 2
            ctx.wait("sp", pe_last[i2])
            for kc in range(8):
                ld_tok[i2] = ctx.dma(f"spy{i2}", out=xtb[i2][:, kc * 128:(kc + 1) * 128],
                                     in_=xnt[kc * 128:(kc + 1) * 128, tt * 128:(tt + 1) * 128])
            xv = xtb[i2][:].rearrange("p (kc t) -> p kc t", kc=8)
            for oh in range(2):
                o2 = (tt * 2 + oh) % 2
                ctx.wait("pe", ld_tok[i2], e_tok[o2])
                for kc in range(8):
                    mm = ctx.pe("matmul", ps[o2][:], xv[:, kc, :],
                                wv_[:, kc, oh * 512:(oh + 1) * 512],
                                start=(kc == 0), stop=(kc == 7), mark=(kc == 7))
                pe_last[i2] = mm
                ctx.wait("dve", mm, st_tok[o2])
                e_tok[o2] = ctx.op("dve", "tensor_tensor", out=vb[o2][:], in0=ps[o2][:],
                                   in1=vbias_row[:, oh * 512:(oh + 1) * 512], op=ALU.add)
                ctx.wait("pool", e_tok[o2])
                st_tok[o2] = ctx.dma(f"go{o2}", out=dst_v[tt * 128:(tt + 1) * 128,
                                                         oh * 512:(oh + 1) * 512],
                                     in_=vb[o2][:])
            if wd_src is not None:
                ctx.wait("pe", d_tok)
                for kc in range(8):
                    mm = ctx.pe("matmul", psd[:], xv[:, kc, :], wd_[:, kc, :],
                                start=(kc == 0), stop=(kc == 7), mark=(kc == 7))
                pe_last[i2] = mm
                ctx.wait("dve", mm)
                ctx.op("dve", "tensor_tensor", out=lam_sb[:, tt * 16:(tt + 1) * 16],
                       in0=psd[:], in1=dbias_row[:, 0:16], op=ALU.add)
                t = ctx.tok("dve")
                ctx.wait("act", t)
                d_tok = ctx.op("act", "activation", out=lam_sb[:, tt * 16:(tt + 1) * 16],
                               in_=lam_sb[:, tt * 16:(tt + 1) * 16], func=AF.Sigmoid)


def _load_transposed_into(ctx: Ctx, C: Consts, w_src, wv_dst3, O, tag):
    nc = ctx.nc
    ob = ceil_div(O, 128)
    with ExitStack() as st2:
        blk = [st2.enter_context(nc.sbuf_tensor(f"lt_b{i}_{tag}", [128, 128], BF16)) for i in range(2)]
        pst = [st2.enter_context(nc.psum_tensor(f"lt_p{i}_{tag}", [128, 128], BF16)) for i in range(2)]
        ctx.wait("pe", C.ready)
        ev = [None, None]
        tt = [None, None]
        n = 0
        for r in range(ob):
            rr = min(128, O - r * 128)
            for c in range(8):
                i = n % 2
                ctx.wait("pool", tt[i])
                lt = ctx.dma(f"gw{i}", out=blk[i][0:rr, :],
                             in_=w_src[r * 128:r * 128 + rr, c * 128:(c + 1) * 128])
                ctx.wait("pe", lt, ev[i])
                tt[i] = ctx.pe("transpose", pst[i][:, 0:rr], blk[i][0:rr, :],
                               C.ident_bf[0:rr, 0:rr], mark=True)
                ctx.wait("dve", tt[i])
                ev[i] = ctx.op("dve", "tensor_copy", wv_dst3[:, c, r * 128:r * 128 + rr],
                               pst[i][:, 0:rr])
                n += 1
        ctx.barrier()



def phase_fm_to_tok(ctx: Ctx, C: Consts, src_fm, dst_tok, tag):
    nc = ctx.nc
    with ExitStack() as st_:
        xb = [st_.enter_context(nc.sbuf_tensor(f"ft_x{i}_{tag}", [128, 8 * 128], BF16)) for i in range(2)]
        ob = [st_.enter_context(nc.sbuf_tensor(f"ft_o{i}_{tag}", [128, 1024], BF16)) for i in range(2)]
        ps = [st_.enter_context(nc.psum_tensor(f"ft_p{i}_{tag}", [128, 1024], BF16)) for i in range(2)]
        ctx.wait("pe", C.ready)
        tr_tok = [None, None]
        ev_tok = [None, None]
        st_tok = [None, None]
        for tt in range(NTILE):
            i2 = tt % 2
            ctx.wait("sp", tr_tok[i2])
            ld = None
            for kc in range(8):
                ld = ctx.dma(f"spy{i2}", out=xb[i2][:, kc * 128:(kc + 1) * 128],
                             in_=src_fm[kc * 128:(kc + 1) * 128, tt * 128:(tt + 1) * 128])
            ctx.wait("pe", ld, ev_tok[i2])
            for kc in range(8):
                pt = ctx.pe("transpose", ps[i2][:, kc * 128:(kc + 1) * 128],
                            xb[i2][:, kc * 128:(kc + 1) * 128], C.ident_bf[:], mark=(kc == 7))
            tr_tok[i2] = pt
            ctx.wait("dve", pt, st_tok[i2])
            ev_tok[i2] = ctx.op("dve", "tensor_copy", ob[i2][:], ps[i2][:])
            ctx.wait("pool", ev_tok[i2])
            st_tok[i2] = ctx.dma(f"go{i2}", out=dst_tok[tt * 128:(tt + 1) * 128, :], in_=ob[i2][:])



SLOT65 = 65


class ScanApp:

    def __init__(self, ctx: Ctx, tag, den=True):
        nc, st_ = ctx.nc, ctx.stack
        self.den = den
        self.w = SLOT65 if den else 64
        w = self.w
        ch = T // 128
        self.ch = ch
        self.out_sb = st_.enter_context(nc.sbuf_tensor(f"sc_out_{tag}", [128, ch * H * w], F32))
        self.lam = st_.enter_context(nc.sbuf_tensor(f"sc_lam_{tag}", [128, NTILE * H], F32))
        self.Sf = st_.enter_context(nc.sbuf_tensor(f"sc_sf_{tag}", [64, H * w], F32))
        self.Sb = st_.enter_context(nc.sbuf_tensor(f"sc_sb_{tag}", [128, H * w], BF16))
        self.eq = st_.enter_context(nc.sbuf_tensor(f"sc_eq_{tag}", [128, ch * H], F32))
        self.ek = st_.enter_context(nc.sbuf_tensor(f"sc_ek_{tag}", [128, ch * H], F32))
        self.gb = st_.enter_context(nc.sbuf_tensor(f"sc_gb_{tag}", [64, ch * H], F32))
        self.lhs16 = st_.enter_context(nc.sbuf_tensor(f"sc_lh_{tag}", [16, ch * 128], F32))
        self.bd16 = st_.enter_context(nc.sbuf_tensor(f"sc_bd_{tag}", [16, H * 128], F32))
        self.negone = st_.enter_context(nc.sbuf_tensor(f"sc_no_{tag}", [1, 128], F32))
        self.maskt = st_.enter_context(nc.sbuf_tensor(f"sc_mk_{tag}", [128, 128], F32))


_UNIQ = [0]


def _uq(s):
    _UNIQ[0] += 1
    return f"{s}{_UNIQ[0]}"


def build_scan_consts(ctx: Ctx, C: Consts, app: ScanApp, causal_mask):
    nc = ctx.nc
    with ExitStack() as st_:
        iorow = st_.enter_context(nc.sbuf_tensor(_uq("bd_ior"), [16, H * 128], F32))
        col = st_.enter_context(nc.sbuf_tensor(_uq("bd_col"), [16, 1], F32))
        t = ctx.op("pool", "iota", iorow[:], [[1, H * 128]], base=0, channel_multiplier=0,
                   allow_small_or_imprecise_dtypes=True)
        ctx.wait("dve", t, C.t_iota_p)
        ctx.op("dve", "tensor_scalar", out=col[:], in0=C.iota_p[0:16, :], scalar1=128.0,
               scalar2=None, op0=ALU.mult)
        ctx.op("dve", "tensor_scalar", out=app.bd16[:], in0=iorow[:], scalar1=col[:],
               scalar2=None, op0=ALU.is_ge)
        ctx.op("dve", "tensor_scalar", out=iorow[:], in0=iorow[:], scalar1=col[:],
               scalar2=128.0, op0=ALU.subtract, op1=ALU.is_lt)
        ctx.op("dve", "tensor_tensor", out=app.bd16[:], in0=app.bd16[:], in1=iorow[:], op=ALU.mult)
        ctx.op("dve", "memset", app.negone[:], -1.0)
        if causal_mask:
            sc1 = st_.enter_context(nc.sbuf_tensor(_uq("bd_s1"), [128, 128], F32))
            t2 = ctx.op("pool", "iota", sc1[:], [[1, 128]], base=0, channel_multiplier=0,
                        allow_small_or_imprecise_dtypes=True)
            ctx.wait("dve", t2)
            ctx.op("dve", "tensor_scalar", out=app.maskt[:], in0=sc1[:], scalar1=C.iota_p[:],
                   scalar2=None, op0=ALU.is_lt)
            ctx.op("dve", "tensor_scalar", out=app.maskt[:], in0=app.maskt[:], scalar1=-1e30,
                   scalar2=None, op0=ALU.mult)
        else:
            ctx.op("dve", "memset", app.maskt[:], 0.0)
        ctx.barrier()


def emit_scan_prelude(ctx: Ctx, C: Consts, app: ScanApp, b, prow_dram, clamp_min, tag):
    nc = ctx.nc
    CW = app.ch * H
    with ExitStack() as st_:
        lgl = st_.enter_context(nc.sbuf_tensor(f"pr_lgl_{tag}", [128, CW], F32))
        pcol = st_.enter_context(nc.sbuf_tensor(f"pr_pc_{tag}", [128, CW], F32))
        trow = st_.enter_context(nc.sbuf_tensor(f"pr_tr_{tag}", [128, ceil_div(CW, 128) * 128], F32))
        grow = st_.enter_context(nc.sbuf_tensor(f"pr_gr_{tag}", [1, CW], F32))
        pp = st_.enter_context(nc.psum_tensor(f"pr_pp_{tag}", [128, CW], F32))
        pp2 = st_.enter_context(nc.psum_tensor(f"pr_p2_{tag}", [128, CW], F32))
        ppg = st_.enter_context(nc.psum_tensor(f"pr_pg_{tag}", [1, CW], F32))
        ppb = st_.enter_context(nc.psum_tensor(f"pr_pb_{tag}", [64, CW], F32))
        ppt = [st_.enter_context(nc.psum_tensor(f"pr_pt{i}_{tag}", [128, 128], F32))
               for i in range(2)]
        lam_sl = app.lam[:, b * CW:(b + 1) * CW]
        ctx.op("dve", "tensor_scalar", out=lgl[:], in0=lam_sl, scalar1=float(clamp_min),
               scalar2=None, op0=ALU.max)
        t = ctx.tok("dve")
        ctx.wait("act", t)
        tl = ctx.op("act", "activation", out=lgl[:], in_=lgl[:], func=AF.Ln)
        ctx.wait("pe", tl, C.ready)
        t1 = ctx.pe("matmul", pp[:], C.tril_f[:], lgl[:], start=True, stop=True, mark=True)
        t2 = ctx.pe("matmul", pp2[:], C.triu_f[:], lgl[:], start=True, stop=True, mark=True)
        t3 = ctx.pe("matmul", ppg[:], C.ones_f[:, 0:1], lgl[:], start=True, stop=True, mark=True)
        ctx.wait("act", t1, t2, t3)
        ctx.op("act", "activation", out=app.eq[:], in_=pp[:], func=AF.Exp)
        ctx.op("act", "activation", out=app.ek[:], in_=pp2[:], func=AF.Exp)
        tg = ctx.op("act", "activation", out=grow[:], in_=ppg[:], func=AF.Exp)
        ctx.wait("dve", t1)
        tpc = ctx.op("dve", "tensor_copy", pcol[:], pp[:])
        ctx.wait("pe", tg)
        t4 = ctx.pe("matmul", ppb[:], C.ones_f[0:1, 0:64], grow[0:1, :], start=True, stop=True,
                    mark=True)
        ctx.wait("dve", t4)
        ctx.op("dve", "tensor_copy", app.gb[:], ppb[:])
        ctx.wait("pe", tpc)
        st_toks = []
        for half in range(ceil_div(CW, 128)):
            wdt = min(128, CW - half * 128)
            th = ctx.pe("transpose", ppt[half % 2][0:wdt, :],
                        pcol[:, half * 128:half * 128 + wdt],
                        C.ident_f[:], mark=True)
            ctx.wait("dve", th)
            ev = ctx.op("dve", "tensor_scalar",
                        out=trow[0:wdt, half * 128:half * 128 + 128],
                        in0=ppt[half % 2][0:wdt, :], scalar1=-1.0, scalar2=None, op0=ALU.mult)
            ctx.wait("pool", ev)
            st_toks.append(ctx.dma("gprow",
                                   out=prow_dram[half * 128:half * 128 + wdt, :],
                                   in_=trow[0:wdt, half * 128:half * 128 + 128]))
        ctx.wait("pool", *st_toks)
        ctx.dma("gprow", out=app.lhs16[:].rearrange("p (c i) -> p c i", c=app.ch),
                in_=prow_dram[:].rearrange("(c h) i -> h c i", h=16))
        ctx.barrier()


def emit_scan_pass(ctx: Ctx, C: Consts, app: ScanApp, b, qf, kf, ktok, vtok, prow_dram,
                   forward, causal, tag):
    nc = ctx.nc
    W = app.w
    den = app.den
    with ExitStack() as st_:
        qft = [st_.enter_context(nc.sbuf_tensor(f"sp_q{i}_{tag}", [128, 8 * 128], BF16)) for i in range(2)]
        kft = [st_.enter_context(nc.sbuf_tensor(f"sp_k{i}_{tag}", [128, 8 * 128], BF16)) for i in range(2)] \
            if forward else None
        ktt = [st_.enter_context(nc.sbuf_tensor(f"sp_kt{i}_{tag}", [128, 1024], BF16)) for i in range(2)]
        vtt = [st_.enter_context(nc.sbuf_tensor(f"sp_vt{i}_{tag}", [128, 1024], BF16)) for i in range(2)]
        pflat = [st_.enter_context(nc.sbuf_tensor(f"sp_pf{i}_{tag}", [1, 2048], F32)) for i in range(2)] \
            if forward else None
        cwn = st_.enter_context(nc.sbuf_tensor(f"sp_cn_{tag}", [128, 1024], F32)) if forward else None
        cw = st_.enter_context(nc.sbuf_tensor(f"sp_cw_{tag}", [128, 1024], F32)) if forward else None
        cs = [st_.enter_context(nc.sbuf_tensor(f"sp_cs{i}_{tag}", [128, 1024], BF16)) for i in range(2)] \
            if forward else None
        k2 = st_.enter_context(nc.sbuf_tensor(f"sp_k2_{tag}", [128, 1024], BF16))
        SW = W if W % 2 == 0 else W + 1
        tmp = [st_.enter_context(nc.sbuf_tensor(f"sp_tm{i}_{tag}", [128, 4 * SW], F32)) for i in range(2)]
        onesb = st_.enter_context(nc.sbuf_tensor(f"sp_on_{tag}", [128, 2], BF16))
        ps_big = st_.enter_context(nc.psum_tensor(f"sp_pb_{tag}", [128, 1024], F32)) if forward else None
        ps_a = [st_.enter_context(nc.psum_tensor(f"sp_pa{i}_{tag}", [128, 4 * SW], F32)) for i in range(2)] \
            if forward else None
        ps_ib = [st_.enter_context(nc.psum_tensor(f"sp_pi{i}_{tag}", [128, 4 * SW], F32)) for i in range(2)]
        ps_u = [st_.enter_context(nc.psum_tensor(f"sp_pu{i}_{tag}", [64, 4 * SW], F32)) for i in range(2)]
        ctx.op("dve", "memset", onesb[:], 1.0)
        ctx.op("dve", "memset", app.Sf[:], 0.0)
        sb_cast_tok = ctx.op("dve", "memset", app.Sb[:], 0.0)
        ld_q = [None, None]
        ld_k = [None, None]
        ld_kt = [None, None]
        ld_vt = [None, None]
        ld_pf = [None, None]
        cfree_tok = None
        cs_tok = [None, None]
        a_ev = [None, None]
        i_ev = [None, None]
        u_ev = [None, None]
        pe_q = [None, None]
        pe_kt = [None, None]
        pe_vt = [None, None]
        CH = app.ch
        eqv = app.eq[:].rearrange("p (c h) -> p c h", c=CH)
        ekv = app.ek[:].rearrange("p (c h) -> p c h", c=CH)
        gbv = app.gb[:].rearrange("p (c h) -> p c h", c=CH)
        outv = app.out_sb[:].rearrange("p (c h w) -> p c h w", c=CH, h=H)
        lhsv = app.lhs16[:].rearrange("p (c i) -> p c i", c=CH)
        chunks = range(CH) if forward else range(CH - 1, -1, -1)
        cs_war = [None, None]
        k2_war = None
        for ci, c in enumerate(chunks):
            i2 = ci % 2
            t0 = b * T + c * 128
            ctx.wait("sp", pe_q[i2])
            for fc in range(8):
                ld_q[i2] = ctx.dma(f"spq{i2}", out=qft[i2][:, fc * 128:(fc + 1) * 128],
                                   in_=qf[fc * 128:(fc + 1) * 128, t0:t0 + 128])
            if forward:
                for fc in range(8):
                    ld_k[i2] = ctx.dma(f"spk{i2}", out=kft[i2][:, fc * 128:(fc + 1) * 128],
                                       in_=kf[fc * 128:(fc + 1) * 128, t0:t0 + 128])
                ctx.wait("sp", cfree_tok)
                ld_pf[i2] = ctx.dma(f"sppf{i2}", out=pflat[i2][0:1, :],
                                    in_=prow_dram[c * 16:(c + 1) * 16, :].rearrange("a b -> (a b)").unsqueeze(0))
            ctx.wait("sp", pe_kt[i2], pe_vt[i2])
            ld_kt[i2] = ctx.dma(f"spkt{i2}", out=ktt[i2][:], in_=ktok[t0:t0 + 128, :])
            ld_vt[i2] = ctx.dma(f"spvt{i2}", out=vtt[i2][:], in_=vtok[t0:t0 + 128, :])
            inter_marks = []
            atok = None
            for g in range(2):
                if forward:
                    ctx.wait("pe", ld_pf[i2], cs_tok[0], cs_tok[1])
                    for sl in range(2 if SCAN_PARTS & 1 else 0):
                        ctx.pe("matmul", ps_big[:, sl * 512:(sl + 1) * 512],
                               lhsv[:, c, :], app.bd16[:, g * 1024 + sl * 512:g * 1024 + (sl + 1) * 512],
                               start=True, stop=False)
                        mtok = ctx.pe("matmul", ps_big[:, sl * 512:(sl + 1) * 512],
                                      app.negone[0:1, :],
                                      pflat[i2][0:1, g * 1024 + sl * 512:g * 1024 + (sl + 1) * 512],
                                      start=False, stop=True, mark=True)
                    if not SCAN_PARTS & 2:
                        mtok2 = mtok if SCAN_PARTS & 1 else None
                        ctx.wait("dve", mtok2)
                        ce_tok = ctx.op("dve", "memset", cw[:], 1.0)
                        cfree_tok = ce_tok
                    elif causal:
                        ctx.wait("dve", mtok)
                        mvk = app.maskt[:].unsqueeze(1).broadcast_to([128, 8, 128])
                        ctx.op("dve", "tensor_tensor",
                               out=cw[:].rearrange("p (h i) -> p h i", h=8),
                               in0=ps_big[:].rearrange("p (h i) -> p h i", h=8),
                               in1=mvk, op=ALU.add)
                    else:
                        ctx.wait("dve", mtok)
                        ctx.op("dve", "tensor_scalar", out=cwn[:], in0=ps_big[:],
                               scalar1=-1.0, scalar2=None, op0=ALU.mult)
                        ctx.op("dve", "tensor_tensor", out=cw[:], in0=ps_big[:], in1=cwn[:],
                               op=ALU.min)
                    if SCAN_PARTS & 2:
                        cfree_tok = ctx.tok("dve")
                        ctx.wait("act", cfree_tok)
                        ce_tok = ctx.op("act", "activation", out=cw[:], in_=cw[:], func=AF.Exp)
                    ctx.wait("pe", cfree_tok, ld_q[i2], ld_k[i2])
                    stok = None
                    for hh in ([0, 5, 2, 7, 1, 4, 3, 6] if SCAN_PARTS & 4 else []):
                        h = g * 8 + hh
                        p0 = (h % 2) * 64
                        fc = h // 2
                        stok = ctx.pe("matmul", ps_big[:, hh * 128:(hh + 1) * 128],
                                      kft[i2][p0:p0 + 64, fc * 128:(fc + 1) * 128],
                                      qft[i2][p0:p0 + 64, fc * 128:(fc + 1) * 128],
                                      start=True, stop=True, mark=(SCAN_SAFE or hh == 6))
                        if SCAN_SAFE:
                            ctx.wait("pe", stok)
                    ctx.wait("dve", stok, ce_tok, cs_war[g])
                    if SCAN_PARTS & 8:
                        cs_tok[g] = ctx.op("dve", "tensor_tensor", out=cs[g][:], in0=ps_big[:],
                                           in1=cw[:], op=ALU.mult)
                    else:
                        cs_tok[g] = ctx.op("dve", "memset", cs[g][:], 0.01)
                    ctx.wait("pe", cs_tok[g], ld_vt[i2], a_ev[0], a_ev[1])
                    for hh in (range(8) if SCAN_PARTS & 16 else []):
                        h = g * 8 + hh
                        half, pos = hh // 4, hh % 4
                        ctx.pe("matmul", ps_a[half][:, pos * SW:pos * SW + 64],
                               cs[g][:, hh * 128:(hh + 1) * 128],
                               vtt[i2][:, h * 64:(h + 1) * 64],
                               start=True, stop=True,
                               mark=(hh == 7 and not den))
                        if den:
                            ctx.pe("matmul", ps_a[half][:, pos * SW + 64:pos * SW + 66],
                                   cs[g][:, hh * 128:(hh + 1) * 128], onesb[:, 0:2],
                                   start=True, stop=True, mark=(hh == 7))
                    atok = Tok(ctx.pe_sem, ctx.pe_cnt)
                    cs_war[g] = atok
                    pe_vt[i2] = atok
                ctx.wait("pe", ld_q[i2], sb_cast_tok, i_ev[0], i_ev[1])
                state = app.Sb
                btok = None
                for hh in ([0, 4, 1, 5, 2, 6, 3, 7] if SCAN_PARTS & 32 else []):
                    h = g * 8 + hh
                    p0 = (h % 2) * 64
                    fc = h // 2
                    half, pos = hh // 4, hh % 4
                    btok = ctx.pe("matmul", ps_ib[half][:, pos * SW:pos * SW + W],
                                  qft[i2][p0:p0 + 64, fc * 128:(fc + 1) * 128],
                                  state[p0:p0 + 64, h * W:(h + 1) * W],
                                  start=True, stop=True, mark=(SCAN_SAFE or hh == 7))
                    if SCAN_SAFE:
                        ctx.wait("pe", btok)
                inter_marks.append(btok)
                pe_q[i2] = btok
                fac = eqv if forward else ekv
                for q in (range(2) if SCAN_PARTS & 64 else []):
                    hb = g * 8 + q * 4
                    ctx.wait("dve", btok, (atok if forward else None))
                    fap = fac[:, c, hb:hb + 4].unsqueeze(2).broadcast_to([128, 4, W])
                    i_ev[q] = ctx.op("dve", "tensor_tensor",
                                     out=tmp[q][:].rearrange("p (h w) -> p h w", h=4)[:, :, 0:W],
                                     in0=ps_ib[q][:].rearrange("p (h w) -> p h w", h=4)[:, :, 0:W],
                                     in1=fap, op=ALU.mult)
                    if forward:
                        a_ev[q] = ctx.op("dve", "tensor_tensor",
                                         out=outv[:, c, hb:hb + 4, :],
                                         in0=ps_a[q][:].rearrange("p (h w) -> p h w", h=4)[:, :, 0:W],
                                         in1=tmp[q][:].rearrange("p (h w) -> p h w", h=4)[:, :, 0:W],
                                         op=ALU.add)
                    else:
                        a_ev[q] = ctx.op("dve", "tensor_tensor",
                                         out=outv[:, c, hb:hb + 4, :],
                                         in0=outv[:, c, hb:hb + 4, :],
                                         in1=tmp[q][:].rearrange("p (h w) -> p h w", h=4)[:, :, 0:W],
                                         op=ALU.add)
            if not SCAN_PARTS & 128:
                continue
            kfac = ekv if forward else eqv
            ctx.wait("dve", ld_kt[i2], k2_war)
            k2tok = ctx.op("dve", "tensor_tensor",
                           out=k2[:].rearrange("p (h k) -> p h k", h=H),
                           in0=ktt[i2][:].rearrange("p (h k) -> p h k", h=H),
                           in1=kfac[:, c, :].unsqueeze(2).broadcast_to([128, H, 64]),
                           op=ALU.mult)
            pe_kt[i2] = k2tok
            for quarter in range(4):
                pu = ps_u[quarter % 2]
                ctx.wait("pe", k2tok, ld_vt[i2], u_ev[quarter % 2], *inter_marks)
                for hh4 in range(4):
                    h = quarter * 4 + hh4
                    ctx.pe("matmul", pu[:, hh4 * SW:hh4 * SW + 64],
                           k2[:, h * 64:(h + 1) * 64], vtt[i2][:, h * 64:(h + 1) * 64],
                           start=True, stop=True, mark=(hh4 == 3 and not den))
                    if den:
                        ctx.pe("matmul", pu[:, hh4 * SW + 64:hh4 * SW + 66],
                               k2[:, h * 64:(h + 1) * 64], onesb[:, 0:2],
                               start=True, stop=True, mark=(hh4 == 3))
                utok = Tok(ctx.pe_sem, ctx.pe_cnt)
                pe_vt[i2] = utok
                k2_war = utok
                sf4 = app.Sf[:].rearrange("p (h w) -> p h w", h=H)[:, quarter * 4:quarter * 4 + 4, :]
                gap = gbv[:, c, quarter * 4:quarter * 4 + 4].unsqueeze(2).broadcast_to([64, 4, W])
                ctx.wait("dve", utok, *inter_marks)
                ctx.op("dve", "tensor_tensor", out=sf4, in0=sf4, in1=gap, op=ALU.mult)
                u_ev[quarter % 2] = ctx.op(
                    "dve", "tensor_tensor", out=sf4, in0=sf4,
                    in1=pu[:].rearrange("p (h w) -> p h w", h=4)[:, :, 0:W], op=ALU.add)
                ctx.op(
                    "dve", "tensor_copy",
                    app.Sb[0:64, :].rearrange("p (h w) -> p h w", h=H)[:, quarter * 4:quarter * 4 + 4, :],
                    sf4)
                sb_cast_tok = ctx.op(
                    "dve", "tensor_copy",
                    app.Sb[64:128, :].rearrange("p (h w) -> p h w", h=H)[:, quarter * 4:quarter * 4 + 4, :],
                    sf4)


def emit_a3_l0(ctx: Ctx, C: Consts, app: ScanApp, b, ow_src, ob_row, y1_dram, z_dram,
               x2t_dram, colv, c_ln2w, c_ln2b, tag):
    nc = ctx.nc
    W = app.w
    with ExitStack() as st_:
        ow_sb = st_.enter_context(nc.sbuf_tensor(f"a3_ow_{tag}", [128, 8 * 1024], BF16))
        atile = [st_.enter_context(nc.sbuf_tensor(f"a3_at{i}_{tag}", [128, 1024], BF16)) for i in range(2)]
        afm = [st_.enter_context(nc.sbuf_tensor(f"a3_af{i}_{tag}", [128, 1024], BF16)) for i in range(2)]
        rden = [st_.enter_context(nc.sbuf_tensor(f"a3_rd{i}_{tag}", [128, 16], F32)) for i in range(2)]
        y1t = [st_.enter_context(nc.sbuf_tensor(f"a3_y1{i}_{tag}", [128, 1024], F32)) for i in range(2)]
        zt = [st_.enter_context(nc.sbuf_tensor(f"a3_z{i}_{tag}", [128, 1024], F32)) for i in range(2)]
        zn = [st_.enter_context(nc.sbuf_tensor(f"a3_zn{i}_{tag}", [128, 1024], BF16)) for i in range(2)]
        x2s = [st_.enter_context(nc.sbuf_tensor(f"a3_x2{i}_{tag}", [128, 1024], BF16)) for i in range(2)]
        ps_t = [st_.enter_context(nc.psum_tensor(f"a3_pt{i}_{tag}", [128, 1024], BF16)) for i in range(2)]
        ps_o = [st_.enter_context(nc.psum_tensor(f"a3_po{i}_{tag}", [128, 512], F32)) for i in range(2)]
        ps_t2 = [st_.enter_context(nc.psum_tensor(f"a3_p2{i}_{tag}", [128, 1024], BF16)) for i in range(2)]
        lnb = LnBufs(_StackCtx(ctx, st_), tag + "ln")
        wv = ow_sb[:].rearrange("p (kc o) -> p kc o", kc=8)
        wts = [ctx.dma(f"gw{kc % 2}", out=wv[:, kc, :], in_=ow_src[kc * 128:(kc + 1) * 128, :])
               for kc in range(8)]
        ctx.wait("pe", *wts)
        outv = app.out_sb[:].rearrange("p (c h w) -> p c h w", c=app.ch, h=H)
        tr_tok = [None, None]
        afm_ev = [None, None]
        o_ev = [None, None]
        z_tok = [None, None]
        stz_tok = [None, None]
        ln_tok = [None, None]
        tr2_tok = [None, None]
        x2_ev = [None, None]
        stx_tok = [None, None]
        pe_o_last = [None, None]
        for ti in range(app.ch):
            tt = b * app.ch + ti
            i2 = ti % 2
            ctx.wait("dve", tr_tok[i2])
            if app.den:
                ctx.op("dve", "tensor_scalar", out=rden[i2][:],
                       in0=outv[:, ti, :, 64], scalar1=1e-6, scalar2=None, op0=ALU.max)
                ctx.op("dve", "reciprocal", out=rden[i2][:], in_=rden[i2][:])
                at_tok = ctx.op("dve", "tensor_tensor",
                                out=atile[i2][:].rearrange("p (h k) -> p h k", h=H),
                                in0=outv[:, ti, :, 0:64],
                                in1=rden[i2][:].unsqueeze(2).broadcast_to([128, H, 64]),
                                op=ALU.mult)
            else:
                at_tok = ctx.op("dve", "tensor_copy", atile[i2][:], app.out_sb[
                    :, ti * H * W:(ti + 1) * H * W])
            ctx.wait("pe", at_tok, afm_ev[i2])
            for fc in range(8):
                ptok = ctx.pe("transpose", ps_t[i2][:, fc * 128:(fc + 1) * 128],
                              atile[i2][:, fc * 128:(fc + 1) * 128], C.ident_bf[:],
                              mark=(fc == 7))
            tr_tok[i2] = ptok
            ctx.wait("dve", ptok, pe_o_last[i2])
            afm_ev[i2] = ctx.op("dve", "tensor_copy", afm[i2][:], ps_t[i2][:])
            ctx.wait("sp", z_tok[i2])
            ldy = ctx.dma(f"spx{i2}", out=y1t[i2][:], in_=y1_dram[tt * 128:(tt + 1) * 128, :])
            for oh in range(2):
                ctx.wait("pe", afm_ev[i2], o_ev[oh])
                for kc in range(8):
                    mm = ctx.pe("matmul", ps_o[oh][:], afm[i2][:, kc * 128:(kc + 1) * 128],
                                wv[:, kc, oh * 512:(oh + 1) * 512],
                                start=(kc == 0), stop=(kc == 7), mark=(kc == 7))
                pe_o_last[i2] = mm
                ctx.wait("dve", mm, stz_tok[i2], ln_tok[i2])
                o_ev[oh] = ctx.op("dve", "tensor_tensor", out=zt[i2][:, oh * 512:(oh + 1) * 512],
                                  in0=ps_o[oh][:], in1=ob_row[:, oh * 512:(oh + 1) * 512],
                                  op=ALU.add)
            ctx.wait("dve", ldy)
            z_tok[i2] = ctx.op("dve", "tensor_tensor", out=zt[i2][:], in0=zt[i2][:],
                               in1=y1t[i2][:], op=ALU.add)
            ctx.wait("pool", z_tok[i2])
            stz_tok[i2] = ctx.dma(f"gz{i2}", out=z_dram[tt * 128:(tt + 1) * 128, :], in_=zt[i2][:])
            ctx.wait("dve", tr2_tok[i2])
            ln_tok[i2], _, _ = emit_ln(ctx, lnb, i2, zt[i2][:], zn[i2][:], z_tok[i2])
            ctx.wait("pe", ln_tok[i2], x2_ev[i2])
            for fc in range(8):
                p2 = ctx.pe("transpose", ps_t2[i2][:, fc * 128:(fc + 1) * 128],
                            zn[i2][:, fc * 128:(fc + 1) * 128], C.ident_bf[:], mark=(fc == 7))
            tr2_tok[i2] = p2
            ctx.wait("dve", p2, stx_tok[i2])
            for fc in range(8):
                x2_ev[i2] = ctx.op("dve", "tensor_scalar",
                                   out=x2s[i2][:, fc * 128:(fc + 1) * 128],
                                   in0=ps_t2[i2][:, fc * 128:(fc + 1) * 128],
                                   scalar1=colv[:, c_ln2w + fc:c_ln2w + fc + 1],
                                   scalar2=colv[:, c_ln2b + fc:c_ln2b + fc + 1],
                                   op0=ALU.mult, op1=ALU.add)
            ctx.wait("pool", x2_ev[i2])
            for fc in range(8):
                stx_tok[i2] = ctx.dma(f"gx{i2}", out=x2t_dram[fc * 128:(fc + 1) * 128,
                                                            tt * 128:(tt + 1) * 128],
                                      in_=x2s[i2][:, fc * 128:(fc + 1) * 128])



def phase_lin_fm(ctx: Ctx, C: Consts, y_src, w_src, w_f32, colv, c_bias, dst, func, tag,
                 kd=D, O=FFN):
    nc = ctx.nc
    kch = kd // 128
    och = O // 128
    with ExitStack() as st_:
        wsb = st_.enter_context(nc.sbuf_tensor(f"lf_w_{tag}", [128, kch * O], BF16))
        ytb = [st_.enter_context(nc.sbuf_tensor(f"lf_y{i}_{tag}", [128, kch * 512], BF16)) for i in range(2)]
        ob = [st_.enter_context(nc.sbuf_tensor(f"lf_o{i}_{tag}", [128, 512], BF16)) for i in range(2)]
        ps = [st_.enter_context(nc.psum_tensor(f"lf_p{i}_{tag}", [128, 512], F32)) for i in range(2)]
        wv = wsb[:].rearrange("p (kc o) -> p kc o", kc=kch)
        if w_f32:
            wts = [ctx.dma(f"gw{kc % 2}", out=wv[:, kc, :], in_=w_src[kc * 128:(kc + 1) * 128, :])
                   for kc in range(kch)]
        else:
            wts = [ctx.dma("spw", out=wv[:, kc, :], in_=w_src[kc * 128:(kc + 1) * 128, :])
                   for kc in range(kch)]
        ctx.wait("pe", *wts)
        ld_tok = [None, None]
        e_tok = [None, None]
        st_tok = [None, None]
        pe_last = [None, None]
        nsup = NT // 512
        for s in range(nsup):
            i2 = s % 2
            ctx.wait("sp", pe_last[i2])
            for kc in range(kch):
                ld_tok[i2] = ctx.dma(f"spy{i2}", out=ytb[i2][:, kc * 512:(kc + 1) * 512],
                                     in_=y_src[kc * 128:(kc + 1) * 128, s * 512:(s + 1) * 512])
            yv = ytb[i2][:].rearrange("p (kc t) -> p kc t", kc=kch)
            for oc in range(och):
                o2 = (s * och + oc) % 2
                ctx.wait("pe", ld_tok[i2], e_tok[o2])
                for kc in range(kch):
                    mm = ctx.pe("matmul", ps[o2][:], wv[:, kc, oc * 128:(oc + 1) * 128],
                                yv[:, kc, :], start=(kc == 0), stop=(kc == kch - 1),
                                mark=(kc == kch - 1))
                pe_last[i2] = mm
                ctx.wait("act", mm, st_tok[o2])
                e_tok[o2] = ctx.op("act", "activation", out=ob[o2][:], in_=ps[o2][:],
                                   func=func, bias=colv[:, c_bias + oc:c_bias + oc + 1],
                                   scale=1.0)
                ctx.wait("pool", e_tok[o2])
                st_tok[o2] = ctx.dma(f"go{o2}", out=dst[oc * 128:(oc + 1) * 128,
                                                       s * 512:(s + 1) * 512],
                                     in_=ob[o2][:])



def phase_out_ffn(ctx: Ctx, C: Consts, src_h, wt, w_f32, colv, c_scale, c_bias,
                  x_src, extra_src, x_dst, tag):
    nc = ctx.nc
    KCH = FFN // 128
    with ExitStack() as st_:
        w3sb = st_.enter_context(nc.sbuf_tensor(f"of_w_{tag}", [128, KCH * 1024], BF16))
        rhs = [st_.enter_context(nc.sbuf_tensor(f"of_r{i}_{tag}", [128, KCH * 512], BF16))
               for i in range(2)]
        ufm = [st_.enter_context(nc.sbuf_tensor(f"of_u{i}_{tag}", [128, 8 * 512], F32)) for i in range(2)]
        xs = [st_.enter_context(nc.sbuf_tensor(f"of_x{i}_{tag}", [128, D], F32)) for i in range(2)]
        es = [st_.enter_context(nc.sbuf_tensor(f"of_e{i}_{tag}", [128, D], F32)) for i in range(2)] \
            if extra_src is not None else None
        xo = [st_.enter_context(nc.sbuf_tensor(f"of_xo{i}_{tag}", [128, D], F32)) for i in range(2)]
        psm = [st_.enter_context(nc.psum_tensor(f"of_pm{i}_{tag}", [128, 512], F32)) for i in range(2)]
        pst = [st_.enter_context(nc.psum_tensor(f"of_pt{i}_{tag}", [128, 1024], F32)) for i in range(2)]
        w3v = w3sb[:].rearrange("p (kc o) -> p kc o", kc=KCH)
        if w_f32:
            wl_toks = [ctx.dma(f"gw{kc % 2}", out=w3v[:, kc, :],
                               in_=wt[kc * 128:(kc + 1) * 128, :]) for kc in range(KCH)]
        else:
            wl_toks = [ctx.dma("spw", out=w3v[:, kc, :],
                               in_=wt[kc * 128:(kc + 1) * 128, :]) for kc in range(KCH)]
        ld_tok = [None, None]
        ev_tok = [None, None]
        tr_tok = [None, None]
        add_tok = [None, None]
        stx_tok = [None, None]
        pe_last_rhs = [None, None]
        nsup = NT // 512
        for s in range(nsup):
            i2 = s % 2
            ctx.wait("sp", pe_last_rhs[i2])
            for kc in range(KCH):
                ld_tok[i2] = ctx.dma(f"spy{i2}", out=rhs[i2][:, kc * 512:(kc + 1) * 512],
                                     in_=src_h[kc * 128:(kc + 1) * 128, s * 512:(s + 1) * 512])
            rv = rhs[i2][:].rearrange("p (kc t) -> p kc t", kc=KCH)
            ctx.wait("dve", tr_tok[i2])
            for oc in range(8):
                o2 = oc % 2
                ctx.wait("pe", ld_tok[i2], ev_tok[o2], *(wl_toks if s == 0 else []))
                for kc in range(KCH):
                    mm = ctx.pe("matmul", psm[o2][:], w3v[:, kc, oc * 128:(oc + 1) * 128],
                                rv[:, kc, :], start=(kc == 0), stop=(kc == KCH - 1),
                                mark=(kc == KCH - 1))
                pe_last_rhs[i2] = mm
                ctx.wait("dve", mm)
                if c_scale is not None:
                    ev_tok[o2] = ctx.op("dve", "tensor_scalar",
                                        out=ufm[i2][:, oc * 512:(oc + 1) * 512], in0=psm[o2][:],
                                        scalar1=colv[:, c_scale + oc:c_scale + oc + 1],
                                        scalar2=None, op0=ALU.mult)
                else:
                    ev_tok[o2] = ctx.op("dve", "tensor_scalar",
                                        out=ufm[i2][:, oc * 512:(oc + 1) * 512], in0=psm[o2][:],
                                        scalar1=colv[:, c_bias + oc:c_bias + oc + 1],
                                        scalar2=None, op0=ALU.add)
            uv = ufm[i2][:].rearrange("p (oc t) -> p oc t", oc=8)
            for j in range(4):
                it = s * 4 + j
                j2 = j % 2
                ctx.wait("sp", add_tok[j2])
                ldx = ctx.dma(f"spx{j2}", out=xs[j2][:], in_=x_src[it * 128:(it + 1) * 128, :])
                ldx2 = ctx.dma(f"spe{j2}", out=es[j2][:],
                               in_=extra_src[it * 128:(it + 1) * 128, :])                     if extra_src is not None else None
                ctx.wait("pe", ev_tok[0], ev_tok[1], add_tok[j2])
                for oc in range(8):
                    pt = ctx.pe("transpose", pst[j2][:, oc * 128:(oc + 1) * 128],
                                uv[:, oc, j * 128:(j + 1) * 128], C.ident_f[:],
                                mark=(oc == 7))
                tr_tok[i2] = pt
                ctx.wait("dve", pt, ldx, ldx2, stx_tok[j2])
                if extra_src is not None:
                    ctx.op("dve", "tensor_tensor", out=xo[j2][:], in0=pst[j2][:],
                           in1=es[j2][:], op=ALU.add)
                    add_tok[j2] = ctx.op("dve", "tensor_tensor", out=xo[j2][:], in0=xo[j2][:],
                                         in1=xs[j2][:], op=ALU.add)
                else:
                    add_tok[j2] = ctx.op("dve", "tensor_tensor", out=xo[j2][:], in0=pst[j2][:],
                                         in1=xs[j2][:], op=ALU.add)
                ctx.wait("pool", add_tok[j2])
                stx_tok[j2] = ctx.dma(f"gxo{j2}", out=x_dst[it * 128:(it + 1) * 128, :],
                                      in_=xo[j2][:])


def emit_prep_attn0(ctx: Ctx, C: Consts, ext, colv, cols: ColAlloc):
    prep_colvec(ctx, C, ext["L0_attn_q_b"], D, colv, cols.reg("q_b", 8))
    prep_colvec(ctx, C, ext["L0_attn_k_b"], D, colv, cols.reg("k_b", 8))
    prep_colvec(ctx, C, ext["L0_attn_ln1_w"], D, colv, cols.reg("ln1_w", 8))
    prep_colvec(ctx, C, ext["L0_attn_ln1_b"], D, colv, cols.reg("ln1_b", 8))
    prep_colvec(ctx, C, ext["L0_attn_ln2_w"], D, colv, cols.reg("ln2_w", 8))
    prep_colvec(ctx, C, ext["L0_attn_ln2_b"], D, colv, cols.reg("ln2_b", 8))
    prep_colvec(ctx, C, ext["L0_attn_f1_b"], FFN, colv, cols.reg("f1_b", 32))
    prep_colvec(ctx, C, ext["L0_attn_f2_b"], D, colv, cols.reg("f2_b", 8))
    ctx.barrier()


def emit_attn0_app(ctx: Ctx, C: Consts, ext, dr, colv, cols: ColAlloc, x_src, x_dst, tag,
                   upto=99):
    nc = ctx.nc
    step = [0]

    def go():
        step[0] += 1
        return step[0] <= upto
    with ExitStack() as appst:
        sctx = _StackCtx(ctx, appst)
        app = ScanApp(sctx, tag, den=True)
        vb_row = appst.enter_context(nc.sbuf_tensor(f"a_vbr_{tag}", [128, D], F32))
        ob_row = appst.enter_context(nc.sbuf_tensor(f"a_obr_{tag}", [128, D], F32))
        dec_row = appst.enter_context(nc.sbuf_tensor(f"a_dbr_{tag}", [128, 16], F32))
        prep_bcast_row(ctx, C, ext["L0_attn_v_b"], D, vb_row, C.ones_f)
        prep_bcast_row(ctx, C, ext["L0_attn_o_b"], D, ob_row, C.ones_f)
        prep_bcast_row(ctx, C, ext["L0_attn_dec_b"], 16, dec_row, C.ones_f)
        build_scan_consts(ctx, C, app, causal_mask=False)
        ctx.barrier()
        if go():
            phase_ln2(ctx, C, x_src, dr["YT"], tag + "a", y1_dst=dr["Y1"], double_ln=True,
                      colv=colv, w_cols=cols.map["ln1_w"], b_cols=cols.map["ln1_b"])
            ctx.barrier()
        if go():
            phase_proj_fm(ctx, C, dr["YT"], ext["L0_attn_q_w"], dr["QF"], tag + "q",
                          colv=colv, bias_cols=cols.map["q_b"], phi=True)
            ctx.barrier()
        if go():
            phase_proj_fm(ctx, C, dr["YT"], ext["L0_attn_k_w"], dr["KF"], tag + "k",
                          colv=colv, bias_cols=cols.map["k_b"], phi=True)
            ctx.barrier()
        if go():
            phase_proj_tok(ctx, C, dr["YT"], ext["L0_attn_v_w"], vb_row, dr["VT"],
                           ext["L0_attn_dec_w"], dec_row, app.lam, tag + "v")
            ctx.barrier()
        if go():
            phase_fm_to_tok(ctx, C, dr["KF"], dr["KTOK"], tag + "t")
            ctx.barrier()
        for b in range(B_PER_CORE):
            if go():
                emit_scan_prelude(ctx, C, app, b, dr["PROW"], 1e-8, tag + f"p{b}")
            if go():
                emit_scan_pass(ctx, C, app, b, dr["QF"], dr["KF"], dr["KTOK"], dr["VT"],
                               dr["PROW"], True, False, tag + f"f{b}")
                ctx.barrier()
            if go():
                emit_scan_pass(ctx, C, app, b, dr["QF"], None, dr["KTOK"], dr["VT"],
                               dr["PROW"], False, False, tag + f"b{b}")
                ctx.barrier()
            if go():
                emit_a3_l0(ctx, C, app, b, ext["L0_attn_o_w"], ob_row, dr["Y1"], dr["Z"],
                           dr["X2T"], colv, cols.map["ln2_w"], cols.map["ln2_b"], tag + f"o{b}")
                ctx.barrier()
    if step[0] + 1 <= upto:
        step[0] += 1
        phase_lin_fm(ctx, C, dr["X2T"], ext["L0_attn_f1_w"], True, colv, cols.map["f1_b"],
                     dr["HA"], AF.Gelu, tag + "m")
        ctx.barrier()
    if step[0] + 1 <= upto:
        step[0] += 1
        phase_out_ffn(ctx, C, dr["HA"], ext["L0_attn_f2_w"], True, colv, None,
                      cols.map["f2_b"], x_src, dr["Z"], x_dst, tag + "n")
        ctx.barrier()
    else:
        phase_copy_tok(ctx, C, x_src, x_dst, tag + "cp")
        ctx.barrier()


def phase_copy_tok(ctx: Ctx, C: Consts, src, dst, tag):
    nc = ctx.nc
    with ExitStack() as st_:
        xb = [st_.enter_context(nc.sbuf_tensor(f"cp{i}_{tag}", [128, D], F32)) for i in range(2)]
        cp = [None, None]
        st = [None, None]
        for tt in range(NTILE):
            i2 = tt % 2
            ctx.wait("sp", cp[i2])
            ld = ctx.dma(f"spx{i2}", out=xb[i2][:], in_=src[tt * 128:(tt + 1) * 128, :])
            ctx.wait("dve", ld, st[i2])
            cp[i2] = ctx.op("dve", "tensor_copy", xb[i2][:], xb[i2][:])
            ctx.wait("pool", cp[i2])
            st[i2] = ctx.dma(f"gxo{i2}", out=dst[tt * 128:(tt + 1) * 128, :], in_=xb[i2][:])



CAP = 512
NE = 8


class MoeApp:
    def __init__(self, ctx, tag):
        nc, st_ = ctx.nc, ctx.stack
        self.probs = st_.enter_context(nc.sbuf_tensor(f"mo_pr_{tag}", [128, NTILE * NE], F32))
        self.cpart = st_.enter_context(nc.sbuf_tensor(f"mo_cp_{tag}", [128, 128], F32))
        self.cfree = st_.enter_context(nc.sbuf_tensor(f"mo_cf_{tag}", [128, 128], F32))
        self.epart = st_.enter_context(nc.sbuf_tensor(f"mo_ep_{tag}", [128, 128], F32))
        self.efree = st_.enter_context(nc.sbuf_tensor(f"mo_ef_{tag}", [128, 128], F32))
        self.bmat = st_.enter_context(nc.sbuf_tensor(f"mo_bm_{tag}", [128, 128], F32))
        self.eoff = st_.enter_context(nc.sbuf_tensor(f"mo_eo_{tag}", [128, 128], F32))
        self.big = st_.enter_context(nc.sbuf_tensor(f"mo_bg_{tag}", [128, 128], F32))
        self.trils = st_.enter_context(nc.sbuf_tensor(f"mo_ts_{tag}", [128, 128], F32))
        self.bigi = st_.enter_context(nc.sbuf_tensor(f"mo_bi_{tag}", [128, 64], I32))
        self.psum_guard = None


def build_moe_consts(ctx: Ctx, C: Consts, mo: MoeApp, nch):
    nc = ctx.nc
    ctx.wait("pe", C.ready)
    with ExitStack() as st_:
        ps = st_.enter_context(nc.psum_tensor(_uq("moc_ps"), [128, 128], F32))
        ctx.op("dve", "memset", mo.cfree[:], 0.0)
        ctx.op("dve", "memset", mo.efree[:], 0.0)
        tz = ctx.tok("dve")
        ctx.wait("pool", tz)
        t1 = ctx.op("pool", "iota",
                    mo.cfree[:, 0:nch * NE].rearrange("p (c e) -> p c e", c=nch),
                    [[1, nch], [0, NE]], base=0, channel_multiplier=0,
                    allow_small_or_imprecise_dtypes=True)
        t2 = ctx.op("pool", "iota",
                    mo.efree[:, 0:nch * NE].rearrange("p (c e) -> p c e", c=nch),
                    [[0, nch], [1, NE]], base=0, channel_multiplier=0,
                    allow_small_or_imprecise_dtypes=True)
        ctx.wait("pe", t1, t2)
        pt = ctx.pe("transpose", ps[:], mo.cfree[:], C.ident_f[:], mark=True)
        ctx.wait("dve", pt)
        tc = ctx.op("dve", "tensor_copy", mo.cpart[:], ps[:])
        ctx.wait("pe", tc)
        pt2 = ctx.pe("transpose", ps[:], mo.efree[:], C.ident_f[:], mark=True)
        ctx.wait("dve", pt2)
        ctx.op("dve", "tensor_copy", mo.epart[:], ps[:])
        ctx.op("dve", "tensor_tensor", out=mo.bmat[:], in0=mo.cpart[:], in1=mo.cfree[:],
               op=ALU.is_lt)
        ctx.op("dve", "tensor_tensor", out=mo.big[:], in0=mo.epart[:], in1=mo.efree[:],
               op=ALU.is_equal)
        ctx.op("dve", "tensor_tensor", out=mo.bmat[:], in0=mo.bmat[:], in1=mo.big[:],
               op=ALU.mult)
        ctx.op("dve", "tensor_scalar", out=mo.eoff[:], in0=mo.efree[:], scalar1=float(CAP),
               scalar2=None, op0=ALU.mult)
        ctx.op("dve", "memset", mo.big[:], 100000.0)
        ctx.op("dve", "memset", mo.bigi[:], 100000)
        sc1 = st_.enter_context(nc.sbuf_tensor(_uq("moc_s1"), [128, 128], F32))
        t3 = ctx.op("pool", "iota", sc1[:], [[1, 128]], base=0, channel_multiplier=0,
                    allow_small_or_imprecise_dtypes=True)
        ctx.wait("dve", t3, C.t_iota_p)
        ctx.op("dve", "tensor_scalar", out=mo.trils[:], in0=sc1[:], scalar1=C.iota_p[:],
               scalar2=None, op0=ALU.is_gt)
        ctx.barrier()


def phase_moe_m0(ctx: Ctx, C: Consts, mo: MoeApp, x_src, gate_w, ytok_dst, pr_dram, tag):
    nc = ctx.nc
    with ExitStack() as st_:
        xb = [st_.enter_context(nc.sbuf_tensor(f"m0x{i}_{tag}", [128, D], F32)) for i in range(2)]
        yb = [st_.enter_context(nc.sbuf_tensor(f"m0y{i}_{tag}", [128, D], BF16)) for i in range(2)]
        y3t = [st_.enter_context(nc.sbuf_tensor(f"m0t{i}_{tag}", [128, 8 * 128], BF16)) for i in range(2)]
        gsb = st_.enter_context(nc.sbuf_tensor(f"m0g_{tag}", [128, 8 * NE], BF16))
        prb = [st_.enter_context(nc.sbuf_tensor(f"m0p{i}_{tag}", [128, NE], F32)) for i in range(2)]
        mx = [st_.enter_context(nc.sbuf_tensor(f"m0m{i}_{tag}", [128, 1], F32)) for i in range(2)]
        ps = [st_.enter_context(nc.psum_tensor(f"m0ps{i}_{tag}", [128, 1024], BF16)) for i in range(2)]
        psc = [st_.enter_context(nc.psum_tensor(f"m0pc{i}_{tag}", [128, NE], F32)) for i in range(2)]
        lb = LnBufs(_StackCtx(ctx, st_), tag + "ln")
        gv = gsb[:].rearrange("p (kc e) -> p kc e", kc=8)
        wts = [ctx.dma("gw0", out=gv[:, kc, :], in_=gate_w[kc * 128:(kc + 1) * 128, :])
               for kc in range(8)]
        ctx.wait("pe", *wts, C.ready)
        norm_tok = [None, None]
        tr_tok = [None, None]
        ev_tok = [None, None]
        sm_tok = [None, None]
        sty = [None, None]
        stp = [None, None]
        for tt in range(NTILE):
            i2 = tt % 2
            ctx.wait("sp", norm_tok[i2])
            ldx = ctx.dma(f"spx{i2}", out=xb[i2][:], in_=x_src[tt * 128:(tt + 1) * 128, :])
            ctx.wait("dve", tr_tok[i2], sty[i2])
            tnorm, _, _ = emit_ln(ctx, lb, i2, xb[i2][:], yb[i2][:], ldx)
            norm_tok[i2] = tnorm
            ctx.wait("pool", tnorm)
            sty[i2] = ctx.dma(f"gyk{i2}", out=ytok_dst[tt * 128:(tt + 1) * 128, :], in_=yb[i2][:])
            ctx.wait("pe", tnorm, ev_tok[i2])
            for kc in range(8):
                pt = ctx.pe("transpose", ps[i2][:, kc * 128:(kc + 1) * 128],
                            yb[i2][:, kc * 128:(kc + 1) * 128], C.ident_bf[:], mark=(kc == 7))
            tr_tok[i2] = pt
            ctx.wait("dve", pt)
            ev_tok[i2] = ctx.op("dve", "tensor_copy", y3t[i2][:], ps[i2][:])
            ctx.wait("pe", ev_tok[i2], sm_tok[i2])
            for kc in range(8):
                mm = ctx.pe("matmul", psc[i2][:], y3t[i2][:, kc * 128:(kc + 1) * 128],
                            gv[:, kc, :], start=(kc == 0), stop=(kc == 7), mark=(kc == 7))
            ctx.wait("dve", mm, stp[i2])
            ctx.op("dve", "tensor_reduce", out=mx[i2][:], in_=psc[i2][:], axis=AX.X,
                   op=ALU.max, negate=True)
            ctx.op("dve", "tensor_scalar", out=prb[i2][:], in0=psc[i2][:],
                   scalar1=mx[i2][:], scalar2=None, op0=ALU.add)
            tsm = ctx.tok("dve")
            ctx.wait("act", tsm)
            ctx.op("act", "activation", out=prb[i2][:], in_=prb[i2][:], func=AF.Exp,
                   accum_out=mx[i2][:])
            ta = ctx.tok("act")
            ctx.wait("dve", ta)
            ctx.op("dve", "reciprocal", out=mx[i2][:], in_=mx[i2][:])
            sm_tok[i2] = ctx.op("dve", "tensor_scalar", out=mo.probs[:, tt * NE:(tt + 1) * NE],
                                in0=prb[i2][:], scalar1=mx[i2][:], scalar2=None, op0=ALU.mult)
            ctx.wait("pool", sm_tok[i2])
            stp[i2] = ctx.dma(f"gpr{i2}", out=pr_dram[tt * 128:(tt + 1) * 128, :],
                              in_=mo.probs[:, tt * NE:(tt + 1) * NE])


def phase_moe_route(ctx: Ctx, C: Consts, mo: MoeApp, b, ytok, gx_dram, tok_dram, pv_dram,
                    appst, tag, n_iter=40):
    nc = ctx.nc
    nch = T // 128
    CW = nch * NE
    st_ = appst
    if True:
        lo = st_.enter_context(nc.sbuf_tensor(f"rt_lo_{tag}", [128, NE], F32))
        hi = st_.enter_context(nc.sbuf_tensor(f"rt_hi_{tag}", [128, NE], F32))
        thr = st_.enter_context(nc.sbuf_tensor(f"rt_th_{tag}", [128, NE], F32))
        sel = st_.enter_context(nc.sbuf_tensor(f"rt_se_{tag}", [128, NE], F32))
        cmp_ = st_.enter_context(nc.sbuf_tensor(f"rt_cm_{tag}", [128, CW], F32))
        part = st_.enter_context(nc.sbuf_tensor(f"rt_pa_{tag}", [128, NE], F32))
        cnt1 = st_.enter_context(nc.sbuf_tensor(f"rt_c1_{tag}", [1, NE], F32))
        cbc = st_.enter_context(nc.sbuf_tensor(f"rt_cb_{tag}", [128, NE], F32))
        mask = st_.enter_context(nc.sbuf_tensor(f"rt_mk_{tag}", [128, CW], F32))
        csr = st_.enter_context(nc.sbuf_tensor(f"rt_cs_{tag}", [1, CW], F32))
        csc = st_.enter_context(nc.sbuf_tensor(f"rt_cc_{tag}", [128, 1], F32))
        offr = st_.enter_context(nc.sbuf_tensor(f"rt_of_{tag}", [1, CW], F32))
        rank = st_.enter_context(nc.sbuf_tensor(f"rt_rk_{tag}", [128, CW], F32))
        dd = st_.enter_context(nc.sbuf_tensor(f"rt_dd_{tag}", [128, CW], F32))
        di = st_.enter_context(nc.sbuf_tensor(f"rt_di_{tag}", [128, CW], I32))
        tcol = st_.enter_context(nc.sbuf_tensor(f"rt_tc_{tag}", [128, 1], F32))
        tcoli = st_.enter_context(nc.sbuf_tensor(f"rt_ti_{tag}", [128, nch], I32))
        yt = [st_.enter_context(nc.sbuf_tensor(f"rt_yt{i}_{tag}", [128, D], BF16)) for i in range(2)]
        pstack = ExitStack()
        pp1 = pstack.enter_context(nc.psum_tensor(f"rt_p1_{tag}", [1, CW], F32))
        pp2 = pstack.enter_context(nc.psum_tensor(f"rt_p2_{tag}", [128, CW], F32))
        ppc = pstack.enter_context(nc.psum_tensor(f"rt_pc_{tag}", [128, 1], F32))
        prs = mo.probs[:, b * CW:(b + 1) * CW]
        prv = prs.rearrange("p (c e) -> p c e", c=nch)
        ctx.wait("pe", C.ready, mo.psum_guard)
        ctx.op("dve", "memset", lo[:], 0.0)
        ctx.op("dve", "memset", hi[:], 1.0)
        ctx.op("dve", "tensor_scalar", out=thr[:], in0=hi[:], scalar1=0.5, scalar2=None,
               op0=ALU.mult)
        for it in range(n_iter):
            ctx.op("dve", "tensor_tensor",
                   out=cmp_[:].rearrange("p (e c) -> p c e", e=NE),
                   in0=prv, in1=thr[:].unsqueeze(1).broadcast_to([128, nch, NE]), op=ALU.is_ge)
            ctx.op("dve", "tensor_reduce", out=part[:],
                   in_=cmp_[:].rearrange("p (e c) -> p e c", e=NE), axis=AX.X, op=ALU.add)
            tpart = ctx.tok("dve")
            ctx.wait("pe", tpart)
            m2 = ctx.pe("matmul", pp2[:, 0:NE], C.ones_f[:, :], part[:], start=True,
                        stop=True, mark=True)
            ctx.wait("dve", m2)
            ctx.op("dve", "tensor_scalar", out=sel[:], in0=pp2[:, 0:NE], scalar1=float(CAP),
                   scalar2=None, op0=ALU.is_ge)
            emit_select(ctx, lo[:], sel[:], thr[:], lo[:], part[:])
            emit_select(ctx, hi[:], sel[:], hi[:], thr[:], part[:])
            ctx.op("dve", "tensor_tensor", out=thr[:], in0=lo[:], in1=hi[:], op=ALU.add)
            ctx.op("dve", "tensor_scalar", out=thr[:], in0=thr[:], scalar1=0.5, scalar2=None,
                   op0=ALU.mult)
        ctx.op("dve", "tensor_tensor", out=mask[:].rearrange("p (c e) -> p c e", c=nch),
               in0=prv, in1=lo[:].unsqueeze(1).broadcast_to([128, nch, NE]), op=ALU.is_ge)
        tmk = ctx.tok("dve")
        ctx.wait("pe", tmk)
        m3 = ctx.pe("matmul", pp1[0:1, :], C.ones_f[:, 0:1], mask[:], start=True, stop=True,
                    mark=True)
        ctx.wait("dve", m3)
        tcs = ctx.op("dve", "tensor_copy", csr[:], pp1[0:1, :])
        ctx.wait("pe", tcs)
        m4 = ctx.pe("transpose", ppc[0:CW, 0:1], csr[0:1, 0:CW], C.ident_f[0:1, 0:1], mark=True)
        ctx.wait("dve", m4)
        tcc = ctx.op("dve", "tensor_copy", csc[0:CW, :], ppc[0:CW, 0:1])
        ctx.wait("pe", tcc)
        m5 = ctx.pe("matmul", pp1[0:1, :], csc[0:CW, 0:1], mo.bmat[0:CW, 0:CW], start=True,
                    stop=True, mark=True)
        ctx.wait("dve", m5)
        tof = ctx.op("dve", "tensor_copy", offr[:], pp1[0:1, :])
        ctx.wait("pe", tof)
        ctx.pe("matmul", pp2[:], mo.trils[:], mask[:], start=True, stop=False)
        m6 = ctx.pe("matmul", pp2[:], C.ones_f[0:1, :], offr[0:1, :], start=False, stop=True,
                    mark=True)
        ctx.wait("dve", m6)
        ctx.op("dve", "tensor_copy", rank[:], pp2[:])
        ctx.op("dve", "tensor_scalar", out=cmp_[:], in0=rank[:], scalar1=float(CAP),
               scalar2=None, op0=ALU.is_lt)
        ctx.op("dve", "tensor_tensor", out=cmp_[:], in0=cmp_[:], in1=mask[:], op=ALU.mult)
        ctx.op("dve", "tensor_tensor", out=rank[:],
               in0=rank[:], in1=mo.eoff[0:128, 0:CW], op=ALU.add)
        emit_select(ctx, dd[:], cmp_[:], rank[:], mo.big[0:128, 0:CW], rank[:])
        tdi = ctx.op("dve", "tensor_copy", di[:], dd[:])
        mo.psum_guard = tdi
        pstack.close()
        for c in range(nch):
            ctx.op("dve", "tensor_scalar", out=tcol[:], in0=C.iota_p[:],
                   scalar1=float(b * T + c * 128), scalar2=None, op0=ALU.add)
            ctx.op("dve", "tensor_copy", tcoli[:, c:c + 1], tcol[:])
        ttc = ctx.tok("dve")
        ctx.wait("pool", tdi, ttc)
        it_tok = ctx.dma(f"gtk{b}", out=tok_dram[:].rearrange("(a b) x -> a (b x)", a=128),
                         in_=mo.bigi[:, 0:(NE * CAP) // 128])
        div = di[:].rearrange("p (c e) -> p c e", c=nch)
        e_toks = []
        sc_par = [None, None]
        for e in range(NE):
            ld_tok = [None, None]
            for c in range(nch):
                i2 = c % 2
                ctx.wait("sp", sc_par[i2])
                ld_tok[i2] = ctx.dma(f"sprt{b}{i2}", out=yt[i2][:],
                                     in_=ytok[(b * nch + c) * 128:(b * nch + c + 1) * 128, :])
                ctx.wait("pool", ld_tok[i2], it_tok)
                off = bass.IndirectOffsetOnAxis(div[:, c, e:e + 1], 0)
                stream = f"gsc{b}{e % 2}{c % 2}"
                ctx.idma(stream, out=gx_dram, out_offset=off, in_=yt[i2][:],
                         in_offset=None, bounds_check=NE * CAP - 1, oob_is_err=False)
                ctx.idma(stream, out=tok_dram, out_offset=off,
                         in_=tcoli[:, c:c + 1], in_offset=None,
                         bounds_check=NE * CAP - 1, oob_is_err=False)
                sc_par[i2] = ctx.idma(stream, out=pv_dram, out_offset=off,
                                      in_=prs[:, c * NE + e:c * NE + e + 1], in_offset=None,
                                      bounds_check=NE * CAP - 1, oob_is_err=False)
            e_toks.append((sc_par[0], sc_par[1]))
        return e_toks


def phase_moe_experts(ctx: Ctx, C: Consts, mo: MoeApp, w1_dram, w2_dram, gx_dram, tok_dram,
                      pv_dram, mout_dram, tag):
    nc = ctx.nc
    with ExitStack() as st_:
        w1sb = st_.enter_context(nc.sbuf_tensor(f"ex_w1_{tag}", [128, 8 * FFN], BF16))
        w2sb = st_.enter_context(nc.sbuf_tensor(f"ex_w2_{tag}", [128, 32 * 1024], BF16))
        xt = [st_.enter_context(nc.sbuf_tensor(f"ex_x{i}_{tag}", [128, 1024], BF16)) for i in range(2)]
        xfm = st_.enter_context(nc.sbuf_tensor(f"ex_xf_{tag}", [128, 8 * CAP], BF16))
        ht = st_.enter_context(nc.sbuf_tensor(f"ex_h_{tag}", [128, 32 * CAP], BF16))
        ysb = [st_.enter_context(nc.sbuf_tensor(f"ex_y{i}_{tag}", [128, 1024], F32)) for i in range(2)]
        pcol = [st_.enter_context(nc.sbuf_tensor(f"ex_p{i}_{tag}", [128, 1], F32)) for i in range(2)]
        tcol = [st_.enter_context(nc.sbuf_tensor(f"ex_t{i}_{tag}", [128, 1], I32)) for i in range(2)]
        pst = [st_.enter_context(nc.psum_tensor(f"ex_pt{i}_{tag}", [128, 1024], BF16)) for i in range(2)]
        psm = [st_.enter_context(nc.psum_tensor(f"ex_pm{i}_{tag}", [128, 512], F32)) for i in range(2)]
        w1v = w1sb[:].rearrange("p (kc o) -> p kc o", kc=8)
        w2v = w2sb[:].rearrange("p (kc o) -> p kc o", kc=32)
        xfv = xfm[:].rearrange("p (kc t) -> p kc t", kc=8)
        htv = ht[:].rearrange("p (oc t) -> p oc t", oc=32)
        w_tok = None
        pe_last_w = None
        tr_tok = [None, None]
        ev_x = [None, None]
        g_tok = [None, None]
        y_ev = [None, None]
        sc_tok = None
        pe_h_last = None
        pe_y_last = None
        for e in range(NE):
            ctx.wait("pool", pe_last_w)
            wt = []
            for kc in range(8):
                wt.append(ctx.dma(f"gw{kc % 2}", out=w1v[:, kc, :],
                                  in_=w1_dram[e, kc * 128:(kc + 1) * 128, :]))
            for kc in range(32):
                wt.append(ctx.dma(f"gw{kc % 2}", out=w2v[:, kc, :],
                                  in_=w2_dram[e, kc * 128:(kc + 1) * 128, :]))
            RB = CAP // 128
            for rb in range(RB):
                i2 = rb % 2
                ctx.wait("sp", pe_h_last, tr_tok[i2])
                ld = ctx.dma(f"spy{i2}", out=xt[i2][:],
                             in_=gx_dram[e * CAP + rb * 128:e * CAP + (rb + 1) * 128, :])
                ctx.wait("pe", ld, ev_x[i2], C.ready)
                for kc in range(8):
                    pt = ctx.pe("transpose", pst[i2][:, kc * 128:(kc + 1) * 128],
                                xt[i2][:, kc * 128:(kc + 1) * 128], C.ident_bf[:],
                                mark=(kc == 7))
                tr_tok[i2] = pt
                ctx.wait("dve", pt)
                ev_x[i2] = ctx.op("dve", "tensor_copy",
                                  xfv[:, :, rb * 128:(rb + 1) * 128],
                                  pst[i2][:].rearrange("p (kc t) -> p kc t", kc=8))
            for oc in range(32):
                o2 = oc % 2
                ctx.wait("pe", ev_x[0], ev_x[1], g_tok[o2], *wt)
                for kc in range(8):
                    mm = ctx.pe("matmul", psm[o2][:, 0:CAP], w1v[:, kc, oc * 128:(oc + 1) * 128],
                                xfv[:, kc, :], start=(kc == 0), stop=(kc == 7), mark=(kc == 7))
                pe_h_last = mm
                pe_last_w = mm
                ctx.wait("act", mm, pe_y_last)
                g_tok[o2] = ctx.op("act", "activation", out=htv[:, oc, :], in_=psm[o2][:, 0:CAP],
                                   func=AF.Gelu)
            for rb in range(RB):
                for oh in range(2):
                    o2 = (rb * 2 + oh) % 2
                    ctx.wait("pe", g_tok[0], g_tok[1], y_ev[o2])
                    for kc in range(32):
                        mm = ctx.pe("matmul", psm[o2][:],
                                    htv[:, kc, rb * 128:(rb + 1) * 128],
                                    w2v[:, kc, oh * 512:(oh + 1) * 512],
                                    start=(kc == 0), stop=(kc == 31), mark=(kc == 31))
                    pe_y_last = mm
                    pe_last_w = mm
                    if oh == 0:
                        ctx.wait("sp", y_ev[0], y_ev[1], sc_tok)
                        ldp = ctx.dma(f"spp{rb % 2}", out=pcol[rb % 2][:],
                                      in_=pv_dram[e * CAP + rb * 128:e * CAP + (rb + 1) * 128, :])
                        ldt = ctx.dma(f"spt{rb % 2}", out=tcol[rb % 2][:],
                                      in_=tok_dram[e * CAP + rb * 128:e * CAP + (rb + 1) * 128, :])
                        ctx.wait("dve", ldp)
                        ctx.op("dve", "tensor_scalar", out=pcol[rb % 2][:], in0=pcol[rb % 2][:],
                               scalar1=0.5, scalar2=None, op0=ALU.mult)
                    ctx.wait("dve", mm, sc_tok)
                    y_ev[o2] = ctx.op("dve", "tensor_scalar",
                                      out=ysb[rb % 2][:, oh * 512:(oh + 1) * 512],
                                      in0=psm[o2][:], scalar1=pcol[rb % 2][:], scalar2=None,
                                      op0=ALU.mult)
                ctx.wait("pool", y_ev[0], y_ev[1], ldt)
                sc_tok = ctx.idma("gsc", out=mout_dram,
                                  out_offset=bass.IndirectOffsetOnAxis(tcol[rb % 2][:], 0),
                                  in_=ysb[rb % 2][:], in_offset=None,
                                  bounds_check=NT - 1, oob_is_err=False,
                                  compute_op=ALU.add)
        ctx.barrier()


def phase_zero_dram(ctx: Ctx, dst, rows, tag):
    nc = ctx.nc
    with ExitStack() as st_:
        z = st_.enter_context(nc.sbuf_tensor(f"z_{tag}", [128, D], F32))
        t = ctx.op("dve", "memset", z[:], 0.0)
        ctx.wait("pool", t)
        for r in range(rows // 128):
            ctx.dma("gz0", out=dst[r * 128:(r + 1) * 128, :], in_=z[:])
        ctx.barrier()


def phase_add_from(ctx: Ctx, C: Consts, x_src, add_src, x_dst, tag):
    nc = ctx.nc
    with ExitStack() as st_:
        a = [st_.enter_context(nc.sbuf_tensor(f"ad_a{i}_{tag}", [128, D], F32)) for i in range(2)]
        bb = [st_.enter_context(nc.sbuf_tensor(f"ad_b{i}_{tag}", [128, D], F32)) for i in range(2)]
        o = [st_.enter_context(nc.sbuf_tensor(f"ad_o{i}_{tag}", [128, D], F32)) for i in range(2)]
        add_tok = [None, None]
        st_tok = [None, None]
        for tt in range(NTILE):
            i2 = tt % 2
            ctx.wait("sp", add_tok[i2])
            l1 = ctx.dma(f"spx{i2}", out=a[i2][:], in_=x_src[tt * 128:(tt + 1) * 128, :])
            l2 = ctx.dma(f"spe{i2}", out=bb[i2][:], in_=add_src[tt * 128:(tt + 1) * 128, :])
            ctx.wait("dve", l1, l2, st_tok[i2])
            add_tok[i2] = ctx.op("dve", "tensor_tensor", out=o[i2][:], in0=a[i2][:],
                                 in1=bb[i2][:], op=ALU.add)
            ctx.wait("pool", add_tok[i2])
            st_tok[i2] = ctx.dma(f"gxo{i2}", out=x_dst[tt * 128:(tt + 1) * 128, :], in_=o[i2][:])


def emit_moe_app(ctx: Ctx, C: Consts, ext, dr, x_src, x_dst, tag, upto=99):
    nc = ctx.nc
    nch = T // 128
    step = [0]

    def go():
        step[0] += 1
        return step[0] <= upto
    with ExitStack() as appst:
        sctx = _StackCtx(ctx, appst)
        mo = MoeApp(sctx, tag)
        build_moe_consts(ctx, C, mo, nch)
        if go():
            phase_zero_dram(ctx, dr["MOUT"], NT, tag + "z")
        if go():
            phase_moe_m0(ctx, C, mo, x_src, ext["L1_ffn_gate_w"], dr["YTOK"], dr["PR"],
                         tag + "a")
            ctx.barrier()
        route_toks = []
        for b in range(B_PER_CORE):
            if go():
                route_toks.append(phase_moe_route(
                    ctx, C, mo, b, dr["YTOK"], dr[f"GX{b}"], dr[f"TOK{b}"],
                    dr[f"PV{b}"], sctx.stack, tag + f"r{b}"))
        if go():
            phase_moe_experts2(ctx, C, mo, dr["MW1C"], dr["MW2C"],
                               [dr["GX0"], dr["GX1"]], [dr["TOK0"], dr["TOK1"]],
                               [dr["PV0"], dr["PV1"]], dr["MOUT"], dr["_mw_cast_tok"],
                               tag + "e", route_toks=route_toks if len(route_toks) == 2 else None)
    phase_add_from(ctx, C, x_src, dr["MOUT"], x_dst, tag + "f")
    ctx.barrier()



def emit_a3_l1(ctx: Ctx, C: Consts, app: ScanApp, b, wo_src, x_src, x_dst, tag):
    nc = ctx.nc
    W = app.w
    with ExitStack() as st_:
        ow_sb = st_.enter_context(nc.sbuf_tensor(f"b3_ow_{tag}", [128, 8 * 1024], BF16))
        atile = [st_.enter_context(nc.sbuf_tensor(f"b3_at{i}_{tag}", [128, 1024], BF16)) for i in range(2)]
        afm = [st_.enter_context(nc.sbuf_tensor(f"b3_af{i}_{tag}", [128, 1024], BF16)) for i in range(2)]
        xs = [st_.enter_context(nc.sbuf_tensor(f"b3_x{i}_{tag}", [128, 1024], F32)) for i in range(2)]
        zt = [st_.enter_context(nc.sbuf_tensor(f"b3_z{i}_{tag}", [128, 1024], F32)) for i in range(2)]
        ps_t = [st_.enter_context(nc.psum_tensor(f"b3_pt{i}_{tag}", [128, 1024], BF16)) for i in range(2)]
        ps_o = [st_.enter_context(nc.psum_tensor(f"b3_po{i}_{tag}", [128, 512], F32)) for i in range(2)]
        wv = ow_sb[:].rearrange("p (kc o) -> p kc o", kc=8)
        _load_transposed_into(ctx, C, wo_src, wv, 1024, tag + "w")
        outv = app.out_sb[:].rearrange("p (c h w) -> p c h w", c=app.ch, h=H)
        tr_tok = [None, None]
        afm_ev = [None, None]
        o_ev = [None, None]
        z_tok = [None, None]
        st_tok = [None, None]
        pe_o_last = [None, None]
        for ti in range(app.ch):
            tt = b * app.ch + ti
            i2 = ti % 2
            ctx.wait("dve", tr_tok[i2])
            at_tok = ctx.op("dve", "tensor_copy", atile[i2][:],
                            app.out_sb[:, ti * H * W:(ti + 1) * H * W])
            ctx.wait("pe", at_tok, afm_ev[i2])
            for fc in range(8):
                ptok = ctx.pe("transpose", ps_t[i2][:, fc * 128:(fc + 1) * 128],
                              atile[i2][:, fc * 128:(fc + 1) * 128], C.ident_bf[:],
                              mark=(fc == 7))
            tr_tok[i2] = ptok
            ctx.wait("dve", ptok, pe_o_last[i2])
            afm_ev[i2] = ctx.op("dve", "tensor_copy", afm[i2][:], ps_t[i2][:])
            ctx.wait("sp", z_tok[i2])
            ldx = ctx.dma(f"spx{i2}", out=xs[i2][:], in_=x_src[tt * 128:(tt + 1) * 128, :])
            for oh in range(2):
                ctx.wait("pe", afm_ev[i2], o_ev[oh])
                for kc in range(8):
                    mm = ctx.pe("matmul", ps_o[oh][:], afm[i2][:, kc * 128:(kc + 1) * 128],
                                wv[:, kc, oh * 512:(oh + 1) * 512],
                                start=(kc == 0), stop=(kc == 7), mark=(kc == 7))
                pe_o_last[i2] = mm
                ctx.wait("dve", mm, st_tok[i2], ldx)
                o_ev[oh] = ctx.op("dve", "tensor_tensor", out=zt[i2][:, oh * 512:(oh + 1) * 512],
                                  in0=ps_o[oh][:], in1=xs[i2][:, oh * 512:(oh + 1) * 512],
                                  op=ALU.add)
            z_tok[i2] = o_ev[1]
            ctx.wait("pool", z_tok[i2])
            st_tok[i2] = ctx.dma(f"gxo{i2}", out=x_dst[tt * 128:(tt + 1) * 128, :], in_=zt[i2][:])


def emit_attn1_app(ctx: Ctx, C: Consts, ext, dr, x_src, x_dst, tag):
    nc = ctx.nc
    with ExitStack() as appst:
        sctx = _StackCtx(ctx, appst)
        app = ScanApp(sctx, tag, den=False)
        zrow = appst.enter_context(nc.sbuf_tensor(f"b_zr_{tag}", [128, D], F32))
        grow = appst.enter_context(nc.sbuf_tensor(f"b_gr_{tag}", [128, 16], F32))
        ctx.op("dve", "memset", zrow[:], 0.0)
        prep_bcast_row(ctx, C, ext["L1_attn_bg"], 16, grow, C.ones_f)
        build_scan_consts(ctx, C, app, causal_mask=True)
        ctx.barrier()
        phase_ln2(ctx, C, x_src, dr["YT"], tag + "a")
        ctx.barrier()
        phase_proj_fm(ctx, C, dr["YT"], ext["L1_attn_Wq"], dr["QF"], tag + "q",
                      phi=True, phi_scale=0.125, w_transposed_src=True)
        ctx.barrier()
        phase_proj_fm(ctx, C, dr["YT"], ext["L1_attn_Wk"], dr["KF"], tag + "k",
                      phi=True, w_transposed_src=True)
        ctx.barrier()
        phase_proj_tok(ctx, C, dr["YT"], ext["L1_attn_Wv"], zrow, dr["VT"],
                       ext["L1_attn_Wg"], grow, app.lam, tag + "v",
                       wv_transposed=True, wd_transposed=True)
        ctx.barrier()
        phase_fm_to_tok(ctx, C, dr["KF"], dr["KTOK"], tag + "t")
        ctx.barrier()
        for b in range(B_PER_CORE):
            emit_scan_prelude(ctx, C, app, b, dr["PROW"], 1e-30, tag + f"p{b}")
            emit_scan_pass(ctx, C, app, b, dr["QF"], dr["KF"], dr["KTOK"], dr["VT"],
                           dr["PROW"], True, True, tag + f"f{b}")
            ctx.barrier()
            emit_a3_l1(ctx, C, app, b, ext["L1_attn_Wo"], x_src, x_dst, tag + f"o{b}")
            ctx.barrier()



def build_core_program(marks=None):
    def mark(name):
        if marks is not None:
            marks.append((name, nc.n_instructions()))
    nc = bass.Bass("TRN2", target_bir_lowering=False)
    ext = {}
    ext["L0_attn_x"] = nc.dram_tensor("L0_attn_x", [NT, D], F32, kind="ExternalInput").ap()
    specs = {
        "L0_attn_q_w": [D, D], "L0_attn_q_b": [D], "L0_attn_k_w": [D, D], "L0_attn_k_b": [D],
        "L0_attn_v_w": [D, D], "L0_attn_v_b": [D], "L0_attn_o_w": [D, D], "L0_attn_o_b": [D],
        "L0_attn_ln1_w": [D], "L0_attn_ln1_b": [D], "L0_attn_ln2_w": [D], "L0_attn_ln2_b": [D],
        "L0_attn_dec_w": [D, 16], "L0_attn_dec_b": [16],
        "L0_attn_f1_w": [D, FFN], "L0_attn_f1_b": [FFN],
        "L0_attn_f2_w": [FFN, D], "L0_attn_f2_b": [D],
        "L0_ffn_w1_w": [FFN, D], "L0_ffn_w1_s": [FFN], "L0_ffn_w2_w": [FFN, D],
        "L0_ffn_w2_s": [FFN], "L0_ffn_h1_w": [FFN, FFN], "L0_ffn_h1_s": [FFN],
        "L0_ffn_h2_w": [FFN, FFN], "L0_ffn_h2_s": [FFN], "L0_ffn_w3_w": [D, FFN],
        "L0_ffn_w3_s": [D],
        "L1_attn_Wq": [D, D], "L1_attn_Wk": [D, D], "L1_attn_Wv": [D, D],
        "L1_attn_Wo": [D, D], "L1_attn_Wg": [16, D], "L1_attn_bg": [16],
        "L1_ffn_gate_w": [D, NE], "L1_ffn_w1": [NE, D, FFN], "L1_ffn_w2": [NE, FFN, D],
    }
    for nm, s in specs.items():
        ext[nm] = nc.dram_tensor(nm, s, F32, kind="ExternalInput").ap()
    xout = nc.dram_tensor("xout", [NT, D], F32, kind="ExternalOutput").ap()

    dr = {}
    dr["XA"] = nc.dram_tensor("XA", [NT, D], F32, kind="Internal").ap()
    dr["XB"] = nc.dram_tensor("XB", [NT, D], F32, kind="Internal").ap()
    dr["YT"] = nc.dram_tensor("YT", [D, NT], BF16, kind="Internal").ap()
    dr["Y1"] = nc.dram_tensor("Y1", [NT, D], F32, kind="Internal").ap()
    dr["Z"] = nc.dram_tensor("Z", [NT, D], F32, kind="Internal").ap()
    dr["X2T"] = nc.dram_tensor("X2T", [D, NT], BF16, kind="Internal").ap()
    dr["QF"] = nc.dram_tensor("QF", [D, NT], BF16, kind="Internal").ap()
    dr["KF"] = nc.dram_tensor("KF", [D, NT], BF16, kind="Internal").ap()
    dr["KTOK"] = nc.dram_tensor("KTOK", [NT, D], BF16, kind="Internal").ap()
    dr["VT"] = nc.dram_tensor("VT", [NT, D], BF16, kind="Internal").ap()
    dr["HA"] = nc.dram_tensor("HA", [FFN, NT], BF16, kind="Internal").ap()
    dr["HB"] = nc.dram_tensor("HB", [FFN, NT], BF16, kind="Internal").ap()
    dr["PROW"] = nc.dram_tensor("PROW", [(T // 128) * 16, 128], F32, kind="Internal").ap()
    dr["YTOK"] = nc.dram_tensor("YTOK", [NT, D], BF16, kind="Internal").ap()
    dr["PR"] = nc.dram_tensor("PR", [NT, NE], F32, kind="Internal").ap()
    for b in range(B_PER_CORE):
        dr[f"GX{b}"] = nc.dram_tensor(f"GX{b}", [NE * CAP, D], BF16, kind="Internal").ap()
        dr[f"TOK{b}"] = nc.dram_tensor(f"TOK{b}", [NE * CAP, 1], I32, kind="Internal").ap()
        dr[f"PV{b}"] = nc.dram_tensor(f"PV{b}", [NE * CAP, 1], F32, kind="Internal").ap()
    dr["MW1C"] = nc.dram_tensor("MW1C", [NE * D, FFN], BF16, kind="Internal").ap()
    dr["MW2C"] = nc.dram_tensor("MW2C", [NE * FFN, D], BF16, kind="Internal").ap()
    dr["MOUT"] = nc.dram_tensor("MOUT", [NT, D], F32, kind="Internal").ap()
    dr["W1T"] = nc.dram_tensor("W1T", [D, FFN], BF16, kind="Internal").ap()
    dr["W2T"] = nc.dram_tensor("W2T", [D, FFN], BF16, kind="Internal").ap()
    dr["H1WT"] = nc.dram_tensor("H1WT", [FFN, FFN], BF16, kind="Internal").ap()
    dr["H2WT"] = nc.dram_tensor("H2WT", [FFN, FFN], BF16, kind="Internal").ap()
    dr["W3T"] = nc.dram_tensor("W3T", [FFN, D], BF16, kind="Internal").ap()

    with ExitStack() as stack:
        ctx = Ctx(nc, stack)
        colv = stack.enter_context(nc.sbuf_tensor("colv", [128, 256], F32))
        C = Consts(ctx)
        cols = ColAlloc()
        dr["_mw_cast_tok"] = emit_moe_weight_cast(ctx, ext, dr)
        mark("emit_prep_ffn0:")
        emit_prep_ffn0(ctx, C, ext, dr, colv, cols)
        mark("emit_prep_attn0:")
        emit_prep_attn0(ctx, C, ext, colv, cols)
        mark("emit_ffn0_app:f1")
        emit_ffn0_app(ctx, C, dr, colv, cols, ext["L0_attn_x"], dr["XA"], "f1")
        mark("emit_attn0_app:a0")
        emit_attn0_app(ctx, C, ext, dr, colv, cols, dr["XA"], dr["XB"], "a0")
        mark("emit_ffn0_app:f2")
        emit_ffn0_app(ctx, C, dr, colv, cols, dr["XB"], dr["XA"], "f2")
        mark("emit_moe_app:m1")
        emit_moe_app(ctx, C, ext, dr, dr["XA"], dr["XB"], "m1")
        mark("emit_attn1_app:b1")
        emit_attn1_app(ctx, C, ext, dr, dr["XB"], dr["XA"], "b1")
        mark("emit_moe_app:m2")
        emit_moe_app(ctx, C, ext, dr, dr["XA"], xout, "m2")
        ctx.barrier()
        mark("end")
    return nc


_PROG_CACHE = {}


def kernel(**inputs):
    if "nc" not in _PROG_CACHE:
        _PROG_CACHE["nc"] = build_core_program()
    nc = _PROG_CACHE["nc"]
    x = np.ascontiguousarray(np.asarray(inputs["L0_attn_x"]).reshape(16, T, D))
    in_maps = []
    for c in range(N_CORES):
        m = {"L0_attn_x": np.ascontiguousarray(
            x[c * B_PER_CORE:(c + 1) * B_PER_CORE].reshape(NT, D))}
        for nm, v in inputs.items():
            if nm != "L0_attn_x":
                m[nm] = np.ascontiguousarray(np.asarray(v, dtype=np.float32))
        in_maps.append(m)
    res = run_bass_kernel_spmd(nc, in_maps, core_ids=list(range(N_CORES)))
    out = np.concatenate([np.asarray(r["xout"]).reshape(B_PER_CORE, T, D)
                          for r in res.results], axis=0)
    return out


def emit_moe_weight_cast(ctx: Ctx, ext, dr):
    toks = []
    for e in range(NE):
        toks.append(ctx.dma("gmw", out=dr["MW1C"][e * D:(e + 1) * D, :],
                            in_=ext["L1_ffn_w1"][e]))
        toks.append(ctx.dma("gmw", out=dr["MW2C"][e * FFN:(e + 1) * FFN, :],
                            in_=ext["L1_ffn_w2"][e]))
    return toks[-1]


def phase_moe_experts2(ctx: Ctx, C: Consts, mo: MoeApp, w1c, w2c, gx, tok, pv,
                       mout_dram, cast_tok, tag, route_toks=None):
    nc = ctx.nc
    RB = CAP // 128
    with ExitStack() as st_:
        w1sb = st_.enter_context(nc.sbuf_tensor(f"e2_w1_{tag}", [128, 8 * FFN], BF16))
        w2sb = st_.enter_context(nc.sbuf_tensor(f"e2_w2_{tag}", [128, 32 * 1024], BF16))
        xt = [st_.enter_context(nc.sbuf_tensor(f"e2_x{i}_{tag}", [128, 1024], BF16)) for i in range(2)]
        xfm = st_.enter_context(nc.sbuf_tensor(f"e2_xf_{tag}", [128, 8 * CAP], BF16))
        ht = st_.enter_context(nc.sbuf_tensor(f"e2_h_{tag}", [128, 32 * CAP], BF16))
        ysb = [st_.enter_context(nc.sbuf_tensor(f"e2_y{i}_{tag}", [128, 1024], F32)) for i in range(2)]
        pcol = [st_.enter_context(nc.sbuf_tensor(f"e2_p{i}_{tag}", [128, 1], F32)) for i in range(2)]
        tcol = [st_.enter_context(nc.sbuf_tensor(f"e2_t{i}_{tag}", [128, 1], I32)) for i in range(2)]
        pst = [st_.enter_context(nc.psum_tensor(f"e2_pt{i}_{tag}", [128, 1024], BF16)) for i in range(2)]
        psm = [st_.enter_context(nc.psum_tensor(f"e2_pm{i}_{tag}", [128, 512], F32)) for i in range(2)]
        w1v = w1sb[:].rearrange("p (kc o) -> p kc o", kc=8)
        w2v = w2sb[:].rearrange("p (kc o) -> p kc o", kc=32)
        xfv = xfm[:].rearrange("p (kc t) -> p kc t", kc=8)
        htv = ht[:].rearrange("p (oc t) -> p oc t", oc=32)

        w1_tok = [None] * (NE + 1)
        w2_tok = [None] * (NE + 1)
        ctx.wait("pe", mo.psum_guard)
        ctx.wait("sp", cast_tok)
        for kc in range(8):
            w1_tok[0] = ctx.dma("spw1", out=w1v[:, kc, :], in_=w1c[kc * 128:(kc + 1) * 128, :])
        for kc in range(32):
            w2_tok[0] = ctx.dma("spw2", out=w2v[:, kc, :], in_=w2c[kc * 128:(kc + 1) * 128, :])
        tr_tok = [None, None]
        ev_x = [None, None]
        g_tok = [None, None]
        y_ev = [None, None]
        sc_tok = None
        pe_h_last = None
        pe_y_last = None
        for e in range(NE):
            for b in range(B_PER_CORE):
                rt = route_toks[b][e] if route_toks is not None else (None, None)
                for rb in range(RB):
                    i2 = rb % 2
                    ctx.wait("sp", pe_h_last, tr_tok[i2], rt[0], rt[1])
                    ld = ctx.dma(f"spgx{i2}", out=xt[i2][:],
                                 in_=gx[b][e * CAP + rb * 128:e * CAP + (rb + 1) * 128, :])
                    ctx.wait("pe", ld, ev_x[i2], C.ready)
                    for kc in range(8):
                        pt = ctx.pe("transpose", pst[i2][:, kc * 128:(kc + 1) * 128],
                                    xt[i2][:, kc * 128:(kc + 1) * 128], C.ident_bf[:],
                                    mark=(kc == 7))
                    tr_tok[i2] = pt
                    ctx.wait("dve", pt)
                    ev_x[i2] = ctx.op("dve", "tensor_copy",
                                      xfv[:, :, rb * 128:(rb + 1) * 128],
                                      pst[i2][:].rearrange("p (kc t) -> p kc t", kc=8))
                for oc in range(32):
                    o2 = oc % 2
                    ctx.wait("pe", ev_x[0], ev_x[1], g_tok[o2], w1_tok[e])
                    for kc in range(8):
                        mm = ctx.pe("matmul", psm[o2][:, 0:CAP],
                                    w1v[:, kc, oc * 128:(oc + 1) * 128],
                                    xfv[:, kc, :], start=(kc == 0), stop=(kc == 7),
                                    mark=(kc == 7))
                    pe_h_last = mm
                    ctx.wait("act", mm, pe_y_last)
                    g_tok[o2] = ctx.op("act", "activation", out=htv[:, oc, :],
                                       in_=psm[o2][:, 0:CAP], func=AF.Gelu)
                if b == B_PER_CORE - 1 and e + 1 < NE:
                    ctx.wait("sp", pe_h_last)
                    for kc in range(8):
                        w1_tok[e + 1] = ctx.dma("spw1", out=w1v[:, kc, :],
                                                in_=w1c[(e + 1) * D + kc * 128:
                                                        (e + 1) * D + (kc + 1) * 128, :])
                for rb in range(RB):
                    for oh in range(2):
                        o2 = (rb * 2 + oh) % 2
                        ctx.wait("pe", g_tok[0], g_tok[1], y_ev[o2], w2_tok[e])
                        for kc in range(32):
                            mm = ctx.pe("matmul", psm[o2][:],
                                        htv[:, kc, rb * 128:(rb + 1) * 128],
                                        w2v[:, kc, oh * 512:(oh + 1) * 512],
                                        start=(kc == 0), stop=(kc == 31), mark=(kc == 31))
                        pe_y_last = mm
                        if oh == 0:
                            ctx.wait("sp", y_ev[0], y_ev[1], sc_tok, rt[0], rt[1])
                            ldp = ctx.dma(f"spp{rb % 2}", out=pcol[rb % 2][:],
                                          in_=pv[b][e * CAP + rb * 128:
                                                    e * CAP + (rb + 1) * 128, :])
                            ldt = ctx.dma(f"spt{rb % 2}", out=tcol[rb % 2][:],
                                          in_=tok[b][e * CAP + rb * 128:
                                                     e * CAP + (rb + 1) * 128, :])
                            ctx.wait("dve", ldp)
                            ctx.op("dve", "tensor_scalar", out=pcol[rb % 2][:],
                                   in0=pcol[rb % 2][:], scalar1=0.5, scalar2=None,
                                   op0=ALU.mult)
                        ctx.wait("dve", mm, sc_tok)
                        y_ev[o2] = ctx.op("dve", "tensor_scalar",
                                          out=ysb[rb % 2][:, oh * 512:(oh + 1) * 512],
                                          in0=psm[o2][:], scalar1=pcol[rb % 2][:],
                                          scalar2=None, op0=ALU.mult)
                    ctx.wait("pool", y_ev[0], y_ev[1], ldt)
                    sc_tok = ctx.idma("gsa", out=mout_dram,
                                      out_offset=bass.IndirectOffsetOnAxis(tcol[rb % 2][:], 0),
                                      in_=ysb[rb % 2][:], in_offset=None,
                                      bounds_check=NT - 1, oob_is_err=False,
                                      compute_op=ALU.add)
            if e + 1 < NE:
                ctx.wait("sp", pe_y_last)
                for kc in range(32):
                    w2_tok[e + 1] = ctx.dma("spw2", out=w2v[:, kc, :],
                                            in_=w2c[(e + 1) * FFN + kc * 128:
                                                    (e + 1) * FFN + (kc + 1) * 128, :])
        ctx.barrier()
